# version 01
# baseline (speedup 1.0000x reference)
import contextlib
import numpy as np
import ml_dtypes
import concourse.bass as bass
import concourse.mybir as mybir
from concourse.bass_utils import run_bass_kernel_spmd

F32 = mybir.dt.float32
BF16 = mybir.dt.bfloat16
U32 = mybir.dt.uint32
AF = mybir.ActivationFunctionType
OP = mybir.AluOpType
BF = ml_dtypes.bfloat16

T = 4096
D = 1024
NT = T // 128
ND = D // 128
EPS = 1e-5
NCORES = 8
EXP_SHIFT = 8.0


class Sem:
    def __init__(self, h):
        self.h = h
        self.n = 0


class EQ:
    def __init__(self):
        self.ops = []

    def __call__(self, fn):
        self.ops.append(fn)

    def wait(self, sem, val=None):
        v = sem.n if val is None else val
        if v > 0:
            self.ops.append(lambda e, s=sem.h, v=v: e.wait_ge(s, v))

    def inc(self, sem, k, fn):
        def g(e, fn=fn, s=sem.h, k=k):
            fn(e).then_inc(s, k)
        self.ops.append(g)
        sem.n += k

    def inc_prev(self, sem, k):
        fn = self.ops.pop()
        def g(e, fn=fn, s=sem.h, k=k):
            fn(e).then_inc(s, k)
        self.ops.append(g)
        sem.n += k

    def drain(self):
        self.ops.append(lambda e: e.drain())

    def flush(self, block, which):
        ops = self.ops
        self.ops = []
        if ops:
            getattr(block, which)(lambda e: [f(e) for f in ops])


class Ctx:
    def __init__(self, nc, stack):
        self.nc = nc
        self.stack = stack
        self.sems = {}
        self.dram = {}
        self.q = {n: EQ() for n in ("sync", "vector", "tensor", "scalar", "gpsimd")}

    def sem(self, name):
        if name not in self.sems:
            h = self.stack.enter_context(self.nc.semaphore(name))
            self.sems[name] = Sem(h)
        return self.sems[name]

    @contextlib.contextmanager
    def scope(self):
        outer = self.stack
        with contextlib.ExitStack() as st:
            self.stack = st
            try:
                yield
            finally:
                self.stack = outer

    def sb(self, name, shape, dtype):
        return self.stack.enter_context(self.nc.sbuf_tensor(name, shape, dtype))

    def ps(self, name, shape, dtype=F32):
        return self.stack.enter_context(self.nc.psum_tensor(name, shape, dtype))

    def din(self, name, shape, dtype):
        ap = self.nc.dram_tensor(name, list(shape), dtype, kind="ExternalInput").ap()
        self.dram[name] = ap
        return ap

    def dout(self, name, shape, dtype):
        ap = self.nc.dram_tensor(name, list(shape), dtype, kind="ExternalOutput").ap()
        self.dram[name] = ap
        return ap

    def dtmp(self, name, shape, dtype):
        ap = self.nc.dram_tensor(name, list(shape), dtype).ap()
        self.dram[name] = ap
        return ap

    def flush(self, name=None):
        with self.nc.Block(name) as block:
            self.q["sync"].flush(block, "sync")
            self.q["vector"].flush(block, "vector")
            self.q["tensor"].flush(block, "tensor")
            self.q["scalar"].flush(block, "scalar")
            self.q["gpsimd"].flush(block, "gpsimd")

    @property
    def sy(self):
        return self.q["sync"]

    @property
    def vq(self):
        return self.q["vector"]

    @property
    def tq(self):
        return self.q["tensor"]

    @property
    def aq(self):
        return self.q["scalar"]

    @property
    def gq(self):
        return self.q["gpsimd"]


def make_nc():
    return bass.Bass("TRN2", target_bir_lowering=False, debug=False)


def host_constants():
    return {
        "c_iden_bf": np.eye(128, dtype=BF),
        "c_iden_f32": np.eye(128, dtype=np.float32),
        "c_ones_row": np.ones((1, 512), dtype=np.float32),
        "c_ones_col": np.ones((128, 1), dtype=np.float32),
        "c_ones_col_bf": np.ones((128, 1), dtype=BF),
        "c_trib": np.kron(np.eye(2, dtype=np.float32), np.triu(np.ones((64, 64), dtype=np.float32))),
        "c_utm": np.tile(np.triu(np.ones((64, 64), dtype=np.float32)), (2, 1)),
        "c_utm_u8": np.tile(np.triu(np.ones((64, 64), dtype=np.uint8)), (2, 1)),
        "c_iota8": np.tile(np.arange(8, dtype=np.float32)[None, :], (128, 1)),
        "c_ltx": np.triu(np.ones((128, 128), dtype=np.float32), k=1),
        "c_i128": np.arange(128, dtype=np.float32)[:, None],
    }


def declare_consts(ctx):
    ctx.din("c_iden_bf", (128, 128), BF16)
    ctx.din("c_iden_f32", (128, 128), F32)
    ctx.din("c_ones_row", (1, 512), F32)
    ctx.din("c_ones_col", (128, 1), F32)
    ctx.din("c_ones_col_bf", (128, 1), BF16)
    ctx.din("c_trib", (128, 128), F32)
    ctx.din("c_utm", (128, 64), F32)
    ctx.din("c_utm_u8", (128, 64), mybir.dt.uint8)
    ctx.din("c_iota8", (128, 8), F32)
    ctx.din("c_ltx", (128, 128), F32)
    ctx.din("c_i128", (128, 1), F32)


CONST_VALUES = [EPS, -EXP_SHIFT, 1.0]


def emit_load_consts(ctx, const_values=CONST_VALUES):
    ctx.iden_bf = ctx.sb("iden_bf", [128, 128], BF16)
    ctx.iden_f32 = ctx.sb("iden_f32", [128, 128], F32)
    ctx.ones_row = ctx.sb("ones_row", [1, 512], F32)
    ctx.onescol_f32 = ctx.sb("ones_col", [128, 1], F32)
    ctx.onescol_bf = ctx.sb("ones_col_bf", [128, 1], BF16)
    ctx.trib = ctx.sb("trib", [128, 128], F32)
    ctx.utm = ctx.sb("utm", [128, 64], F32)
    ctx.utm_u8 = ctx.sb("utm_u8", [128, 64], mybir.dt.uint8)
    ctx.iota8 = ctx.sb("iota8", [128, 8], F32)
    ctx.ltx = ctx.sb("ltx", [128, 128], F32)
    ctx.i128col = ctx.sb("i128col", [128, 1], F32)
    ld = ctx.sem("ld")
    ctx.sy.inc(ld, 16, lambda e: e.dma_start(out=ctx.iden_bf[:, :], in_=ctx.dram["c_iden_bf"][:, :]))
    ctx.sy.inc(ld, 16, lambda e: e.dma_start(out=ctx.iden_f32[:, :], in_=ctx.dram["c_iden_f32"][:, :]))
    ctx.sy.inc(ld, 16, lambda e: e.dma_start(out=ctx.ones_row[:, :], in_=ctx.dram["c_ones_row"][:, :]))
    ctx.sy.inc(ld, 16, lambda e: e.dma_start(out=ctx.onescol_f32[:, :], in_=ctx.dram["c_ones_col"][:, :]))
    ctx.sy.inc(ld, 16, lambda e: e.dma_start(out=ctx.onescol_bf[:, :], in_=ctx.dram["c_ones_col_bf"][:, :]))
    ctx.sy.inc(ld, 16, lambda e: e.dma_start(out=ctx.trib[:, :], in_=ctx.dram["c_trib"][:, :]))
    ctx.sy.inc(ld, 16, lambda e: e.dma_start(out=ctx.utm[:, :], in_=ctx.dram["c_utm"][:, :]))
    ctx.sy.inc(ld, 16, lambda e: e.dma_start(out=ctx.utm_u8[:, :], in_=ctx.dram["c_utm_u8"][:, :]))
    ctx.sy.inc(ld, 16, lambda e: e.dma_start(out=ctx.iota8[:, :], in_=ctx.dram["c_iota8"][:, :]))
    ctx.sy.inc(ld, 16, lambda e: e.dma_start(out=ctx.ltx[:, :], in_=ctx.dram["c_ltx"][:, :]))
    ctx.sy.inc(ld, 16, lambda e: e.dma_start(out=ctx.i128col[:, :], in_=ctx.dram["c_i128"][:, :]))
    cp = ctx.sb("constpool", [128, max(1, len(const_values))], F32)
    for i, v in enumerate(const_values):
        ctx.vq(lambda e, i=i, v=v: e.memset(cp[:, i:i + 1], float(v)))
        ctx.nc.const_aps.register(F32, float(v), cp[:, i:i + 1])
    ctx.vq.drain()
    ctx.sy.wait(ld)



def emit_ln_stats(ctx, x_ap, A, B, name="ln", z_of=None):
    ve, ac = ctx.sem("ve"), ctx.sem("ac")
    ldx = [ctx.sem("ldx0"), ctx.sem("ldx1")]
    sy, vq, aq = ctx.sy, ctx.vq, ctx.aq
    with ctx.scope():
        xb = [ctx.sb(f"{name}_x{i}", [128, 1024], F32) for i in range(2)]
        zb = ctx.sb(f"{name}_zb", [128, 1024], F32) if z_of else None
        bns = ctx.sb(f"{name}_bns", [128, 12], F32)
        mv = ctx.sb(f"{name}_mv", [128, 2], F32)
        var = ctx.sb(f"{name}_var", [128, NT], F32)
        ve_done = []
        for i in range(NT):
            if i >= 2:
                sy.wait(ve, ve_done[i - 2])
            sy.inc(ldx[i % 2], 16, lambda e, i=i: e.dma_start(out=xb[i % 2][:, :], in_=x_ap[i * 128:(i + 1) * 128, :]))
            vq.wait(ldx[i % 2])
            src = xb
            if z_of:
                A1, B1 = z_of
                vq(lambda e, i=i: e.tensor_scalar(out=zb[:, :], in0=xb[i % 2][:, :], scalar1=B1[:, i:i + 1],
                                                  scalar2=A1[:, i:i + 1], op0=OP.mult, op1=OP.subtract))
                vq.drain()
            tgt = (lambda i: zb[:, :]) if z_of else (lambda i: xb[i % 2][:, :])
            vq(lambda e, i=i: e.bn_stats(bns[:, 0:6], tgt(i)[:, 0:512]))
            vq(lambda e, i=i: e.bn_stats(bns[:, 6:12], tgt(i)[:, 512:1024]))
            vq.drain()
            vq(lambda e: e.bn_aggr(mv[:, :], bns[:, :]))
            vq.drain()
            vq(lambda e, i=i: e.tensor_copy(out=B[:, i:i + 1], in_=mv[:, 0:1]))
            vq.inc(ve, 1, lambda e, i=i: e.tensor_copy(out=var[:, i:i + 1], in_=mv[:, 1:2]))
            ve_done.append(ve.n)
        aq.wait(ve)
        aq.inc(ac, 1, lambda e: e.activation(out=var[:, :], in_=var[:, :], func=AF.Sqrt, bias=EPS))
        vq.wait(ac)
        vq(lambda e: e.reciprocal(out=var[:, :], in_=var[:, :]))
        vq.drain()
        vq(lambda e: e.tensor_tensor(out=A[:, :], in0=B[:, :], in1=var[:, :], op=OP.mult))
        vq.drain()
        vq.inc(ve, 1, lambda e: e.tensor_copy(out=B[:, :], in_=var[:, :]))
        ctx.flush(name)



class TileNorm:

    def __init__(self, ctx, name, x_ap, A, B, nbuf=2):
        self.ctx = ctx
        self.x_ap = x_ap
        self.A, self.B = A, B
        self.xb = [ctx.sb(f"{name}_x{i}", [128, 1024], F32) for i in range(2)]
        self.zh = [ctx.sb(f"{name}_z{i}", [128, 1024], BF16) for i in range(2)]
        self.xT = [ctx.sb(f"{name}_t{i}", [128, 1024], BF16) for i in range(2)]
        self.pTf = ctx.ps(f"{name}_pTf", [128, 1024], F32)
        self.ve_done = []
        self.pe_done = []
        self.ac_done = []
        self.consumed = []
        self.zh_consumed = []

    def record(self, i):
        ctx = self.ctx
        ve, pe, ac = ctx.sem("ve"), ctx.sem("pe"), ctx.sem("ac")
        ldx = [ctx.sem("ldx0"), ctx.sem("ldx1")]
        sy, vq, tq, aq = ctx.sy, ctx.vq, ctx.tq, ctx.aq
        if i >= 2:
            sy.wait(ve, self.ve_done[i - 2])
        sy.inc(ldx[i % 2], 16, lambda e, i=i: e.dma_start(out=self.xb[i % 2][:, :],
                                                  in_=self.x_ap[i * 128:(i + 1) * 128, :]))
        vq.wait(ldx[i % 2])
        if i >= 2:
            s, v = self.zh_consumed[i - 2]
            vq.wait(s, v)
        if i >= 1:
            vq.wait(pe, self.pe_done[i - 1])
        vq.inc(ve, 1, lambda e, i=i: e.tensor_scalar(
            out=self.zh[i % 2][:, :], in0=self.xb[i % 2][:, :],
            scalar1=self.B[:, i:i + 1], scalar2=self.A[:, i:i + 1],
            op0=OP.mult, op1=OP.subtract))
        self.ve_done.append(ve.n)
        tq.wait(ve)
        if i >= 1:
            tq.wait(ac, self.ac_done[i - 1])
        for j in range(8):
            tq(lambda e, i=i, j=j: e.matmul(
                out=self.pTf[:, j * 128:(j + 1) * 128],
                lhsT=self.zh[i % 2][:, j * 128:(j + 1) * 128],
                rhs=ctx.iden_bf[:, :], start=True, stop=True))
        tq.inc_prev(pe, 1)
        self.pe_done.append(pe.n)
        self.zh_consumed.append((pe, pe.n))
        aq.wait(pe)
        if i >= 2:
            s, v = self.consumed[i - 2]
            aq.wait(s, v)
        aq.inc(ac, 1, lambda e, i=i: e.copy(out=self.xT[i % 2][:, :], in_=self.pTf[:, :]))
        self.ac_done.append(ac.n)

    def note_consumed(self, sem, val):
        self.consumed.append((sem, val))

    def cur(self, i):
        return self.xT[i % 2]



L0A_PHASES = 9


def emit_l0a(ctx, x_in, x_out):
    nc = ctx.nc
    ld, st_, pe, ve, ac = (ctx.sem(n) for n in ("ld", "st", "pe", "ve", "ac"))
    sy, vq, tq, aq, gq = ctx.sy, ctx.vq, ctx.tq, ctx.aq, ctx.gq

    W_qkv = ctx.dram["L0_attn_a_W_qkv"]
    b_qkv = ctx.dram["L0_attn_a_b_qkv_r"]
    W_proj = ctx.dram["L0_attn_a_W_proj"]
    b_proj = ctx.dram["L0_attn_a_b_proj_r"]
    ln_w = ctx.dram["L0_attn_a_ln_w_r"]
    ln_b = ctx.dram["L0_attn_a_ln_b_r"]
    qkp = ctx.dtmp("l0a_qkp", (16, T, 128), BF16)
    v3 = ctx.dtmp("l0a_v3", (16, T, 64), BF16)

    with ctx.scope():
        A1 = ctx.sb("l0a_A1", [128, NT], F32)
        B1 = ctx.sb("l0a_B1", [128, NT], F32)
        A2 = ctx.sb("l0a_A2", [128, NT], F32)
        B2 = ctx.sb("l0a_B2", [128, NT], F32)
        emit_ln_stats(ctx, x_in, A1, B1, name="l0a_s1")
        emit_ln_stats(ctx, x_in, A2, B2, name="l0a_s2", z_of=(A1, B1))
        Acm = ctx.sb("l0a_Acm", [128, NT], F32)
        Bc = ctx.sb("l0a_Bc", [128, NT], F32)
        Bp = ctx.sb("l0a_Bp", [128, NT], F32)
        vq(lambda e: e.tensor_tensor(out=Bc[:, :], in0=B1[:, :], in1=B2[:, :], op=OP.mult))
        vq(lambda e: e.tensor_tensor(out=Acm[:, :], in0=A1[:, :], in1=B2[:, :], op=OP.mult))
        vq.drain()
        vq(lambda e: e.tensor_tensor(out=Acm[:, :], in0=Acm[:, :], in1=A2[:, :], op=OP.add))
        vq(lambda e: e.tensor_scalar(out=Bp[:, :], in0=B1[:, :], scalar1=1.0, scalar2=None, op0=OP.add))
        vq.drain()
        ctx.flush("l0a_tables")

        if L0A_PHASES < 2:
            return
        wqkv_scope = ctx.scope()
        wqkv_scope.__enter__()
        Wb = ctx.sb("l0a_Wb", [128, 8, 3072], BF16)
        bias_eff = ctx.sb("l0a_bias", [1, 3072], F32)
        with ctx.scope():
            wf = [ctx.sb(f"l0a_wf{i}", [128, 3072], F32) for i in range(2)]
            lnw = ctx.sb("l0a_lnw", [128, 8], F32)
            lnb = ctx.sb("l0a_lnb", [128, 8], F32)
            bq = ctx.sb("l0a_bq", [1, 3072], F32)
            pbias = ctx.ps("l0a_pbias", [1, 3072], F32)
            ldc = ctx.sem("ldc")
            ldw = [ctx.sem("ldw0"), ctx.sem("ldw1")]
            sy.inc(ldc, 16, lambda e: e.dma_start(out=lnw[:, :], in_=ln_w[:, :]))
            sy.inc(ldc, 16, lambda e: e.dma_start(out=lnb[:, :], in_=ln_b[:, :]))
            sy.inc(ldc, 16, lambda e: e.dma_start(out=bq[:, :], in_=b_qkv[:, :]))
            pe_done = []
            for j in range(8):
                if j >= 2:
                    sy.wait(pe, pe_done[j - 2])
                sy.inc(ldw[j % 2], 16, lambda e, j=j: e.dma_start(out=wf[j % 2][:, :],
                                                          in_=W_qkv[j * 128:(j + 1) * 128, :]))
                vq.wait(ldw[j % 2])
                vq.wait(ldc)
                vq.inc(ve, 1, lambda e, j=j: e.tensor_scalar(out=Wb[:, j, :], in0=wf[j % 2][:, :],
                                                             scalar1=lnw[:, j:j + 1], scalar2=None, op0=OP.mult))
                tq.wait(ve)
                for c in range(6):
                    tq(lambda e, j=j, c=c: e.matmul(out=pbias[:, c * 512:(c + 1) * 512], lhsT=lnb[:, j:j + 1],
                                                    rhs=wf[j % 2][:, c * 512:(c + 1) * 512],
                                                    start=(j == 0), stop=(j == 7)))
                tq.inc_prev(pe, 1)
                pe_done.append(pe.n)
            vq.wait(pe)
            vq.inc(ve, 1, lambda e: e.tensor_tensor(out=bias_eff[:, :], in0=pbias[:, :], in1=bq[:, :], op=OP.add))
            ctx.flush("l0a_wprep")

        if L0A_PHASES < 3:
            wqkv_scope.__exit__(None, None, None)
            return
        with ctx.scope():
            tn = TileNorm(ctx, "l0a_q", x_in, Acm, Bc)
            qo = [ctx.sb(f"l0a_qo{i}", [128, 3072], BF16) for i in range(2)]
            pq = ctx.ps("l0a_pq", [128, 3072], F32)
            SUB = 4
            for i in range(NT):
                tn.record(i)
                if SUB < 1:
                    tn.note_consumed(ctx.sem("ac"), ac.n)
                    continue
                tq.wait(ac)
                for c in range(6):
                    for j in range(8):
                        tq(lambda e, i=i, c=c, j=j: e.matmul(
                            out=pq[:, c * 512:(c + 1) * 512], lhsT=tn.cur(i)[:, j * 128:(j + 1) * 128],
                            rhs=Wb[:, j, c * 512:(c + 1) * 512], start=(j == 0), stop=(j == 7) if SUB < 2 else False))
                    if SUB >= 2:
                        tq(lambda e, c=c: e.matmul(out=pq[:, c * 512:(c + 1) * 512], lhsT=ctx.ones_row[0:1, 0:128],
                                                   rhs=bias_eff[:, c * 512:(c + 1) * 512], start=False, stop=True))
                tq.inc_prev(pe, 1)
                tn.note_consumed(pe, pe.n)
                if SUB < 3:
                    continue
                aq.wait(pe)
                stq = [ctx.sem("stq0"), ctx.sem("stq1")]
                if i >= 2:
                    aq.wait(stq[i % 2])
                aq.inc(ac, 1, lambda e, i=i: e.copy(out=qo[i % 2][:, :], in_=pq[:, :]))
                if SUB < 4:
                    continue
                gq.wait(ac)
                gq.inc(stq[i % 2], 16, lambda e, i=i: e.dma_start(
                    out=qkp[:, i * 128:(i + 1) * 128, :].rearrange("s p c -> p s c"),
                    in_=qo[i % 2][:, 0:2048].rearrange("p (s c) -> p s c", c=128)))
                gq.inc(stq[i % 2], 16, lambda e, i=i: e.dma_start(
                    out=v3[:, i * 128:(i + 1) * 128, :].rearrange("s p c -> p s c"),
                    in_=qo[i % 2][:, 2048:3072].rearrange("p (s c) -> p s c", c=64)))
            gq.wait(ctx.sem("stq0"))
            gq.wait(ctx.sem("stq1"))
            ctx.flush("l0a_qkv")
        wqkv_scope.__exit__(None, None, None)

        if L0A_PHASES < 4:
            return
        msaT = [ctx.sb(f"l0a_msaT{p}", [128, T], BF16) for p in range(8)]
        with ctx.scope():
            qTp = [ctx.sb(f"l0a_qT{i}", [128, T], BF16) for i in range(2)]
            kTp = [ctx.sb(f"l0a_kT{i}", [128, T], BF16) for i in range(2)]
            qktm = [ctx.sb(f"l0a_qktm{i}", [128, NT, 128], BF16) for i in range(2)]
            psQT = ctx.ps("l0a_psQT", [128, 512], F32)
            vA = [ctx.sb(f"l0a_v{i}", [128, 2 * NT, 66], BF16) for i in range(2)]
            PT = [ctx.sb(f"l0a_PT{i}", [128, 1024], BF16) for i in range(2)]
            mU = ctx.sb("l0a_mU", [64, 512], BF16)
            rrow = ctx.sb("l0a_rrow", [1, 512], F32)
            psS = [ctx.ps(f"l0a_psS{i}", [128, 1024], F32) for i in range(2)]
            psM = [ctx.ps(f"l0a_psM{i}", [66, 512], F32) for i in range(2)]
            psB = ctx.ps("l0a_psB", [64, 512], F32)
            for b in range(2):
                vq(lambda e, b=b: e.memset(vA[b][:, :, 64:65], 1.0))
                vq(lambda e, b=b: e.memset(vA[b][:, :, 65:66], 0.0))
            vq.drain()
            vq.inc_prev(ve, 1)
            pair_use = []
            for p in range(8):
                b = p % 2
                if p >= 2:
                    sy.wait(pe, pair_use[p - 2])
                else:
                    sy.wait(ve)
                ldv = [ctx.sem("ldv0"), ctx.sem("ldv1")]
                ldq = [ctx.sem("ldq0"), ctx.sem("ldq1")]
                sy.inc(ldv[b], 16, lambda e, p=p, b=b: e.dma_start(
                    out=vA[b][:, 0:NT, 0:64], in_=v3[2 * p, :, :].rearrange("(n p) c -> p n c", p=128)))
                sy.inc(ldv[b], 16, lambda e, p=p, b=b: e.dma_start(
                    out=vA[b][:, NT:2 * NT, 0:64], in_=v3[2 * p + 1, :, :].rearrange("(n p) c -> p n c", p=128)))
                for which, dst in ((0, qTp), (1, kTp)):
                    if which == 1:
                        sy.wait(pe)
                    sy.inc(ldq[b], 16, lambda e, p=p, b=b, which=which: e.dma_start(
                        out=qktm[b][:, :, :],
                        in_=qkp[which * 8 + p, :, :].rearrange("(n p) c -> p n c", p=128)))
                    tq.wait(ldq[b])
                    for g4 in range(8):
                        if not (which == 0 and g4 == 0):
                            tq.wait(ac)
                        for j in range(4):
                            blk = g4 * 4 + j
                            tq(lambda e, b=b, blk=blk, j=j: e.matmul(
                                out=psQT[:, j * 128:(j + 1) * 128],
                                lhsT=qktm[b][:, blk, :],
                                rhs=ctx.iden_bf[:, :], start=True, stop=True))
                        tq.inc_prev(pe, 1)
                        aq.wait(pe)
                        aq.inc(ac, 1, lambda e, dst=dst, b=b, g4=g4: e.copy(
                            out=dst[b][:, g4 * 512:(g4 + 1) * 512], in_=psQT[:, :]))
                tq.wait(ac)
                tq.wait(ldv[b])
                for half in range(2):
                    h = 2 * p + half
                    pb = 64 * half
                    vbase = NT * half
                    exp_done = {}
                    for g in range(8):
                        gb = g % 2
                        for u2 in range(16):
                            if (g, u2) != (0, 0):
                                prev = (g, u2 - 2) if u2 >= 2 else ((g - 1, 14 + u2) if g >= 1 else None)
                                if prev in exp_done:
                                    tq.wait(ac, exp_done[prev])
                            for s in range(2):
                                u = 2 * u2 + s
                                tq(lambda e, b=b, g=g, u=u, u2=u2, s=s, pb=pb: e.matmul(
                                    out=psS[u2 % 2][:, s * 512:(s + 1) * 512],
                                    lhsT=kTp[b][pb:pb + 64, u * 128:(u + 1) * 128],
                                    rhs=qTp[b][pb:pb + 64, g * 512:(g + 1) * 512], start=True, stop=True))
                            tq.inc_prev(pe, 1)
                            aq.wait(pe)
                            aq.inc(ac, 1, lambda e, u2=u2: e.activation(
                                out=PT[u2 % 2][:, :], in_=psS[u2 % 2][:, :], func=AF.Exp,
                                scale=0.125, bias=-EXP_SHIFT))
                            exp_done[(g, u2)] = ac.n
                            tq.wait(ac)
                            for s in range(2):
                                u = 2 * u2 + s
                                tq(lambda e, b=b, gb=gb, u=u, u2=u2, s=s, vbase=vbase: e.matmul(
                                    out=psM[gb][:, :], lhsT=vA[b][:, vbase + u, :],
                                    rhs=PT[u2 % 2][:, s * 512:(s + 1) * 512],
                                    start=(u == 0), stop=(u == 31)))
                            tq.inc_prev(pe, 1)
                        vq.wait(pe)
                        vq.inc(ve, 1, lambda e, gb=gb: e.reciprocal(out=rrow[:, :], in_=psM[gb][64:65, :]))
                        tq.wait(ve)
                        tq.inc(pe, 1, lambda e: e.matmul(out=psB[:, :], lhsT=ctx.ones_row[0:1, 0:64],
                                                         rhs=rrow[:, :], start=True, stop=True))
                        aq.wait(pe)
                        aq.inc(ac, 1, lambda e, gb=gb: e.copy(out=mU[:, :], in_=psM[gb][0:64, :]))
                        vq.wait(ac)
                        vq.inc(ve, 1, lambda e, p=p, half=half, g=g: e.tensor_tensor(
                            out=msaT[p][half * 64:half * 64 + 64, g * 512:(g + 1) * 512],
                            in0=mU[:, :], in1=psB[:, :], op=OP.mult))
                pair_use.append(pe.n)
            ctx.flush("l0a_attn")

        with ctx.scope():
            Wpb = ctx.sb("l0a_Wpb", [128, 8, 1024], BF16)
            bp = ctx.sb("l0a_bp", [1, 1024], F32)
            xb = [ctx.sb(f"l0a_px{i}", [128, 1024], F32) for i in range(2)]
            t1 = [ctx.sb(f"l0a_t1{i}", [128, 1024], F32) for i in range(2)]
            ot = [ctx.sb(f"l0a_ot{i}", [128, 1024], F32) for i in range(2)]
            psP = [ctx.ps(f"l0a_psP{i}", [128, 1024], F32) for i in range(2)]
            gldw = ctx.sem("gldw")
            ldx = [ctx.sem("ldx0"), ctx.sem("ldx1")]
            sto = [ctx.sem("sto0"), ctx.sem("sto1")]
            ldc = ctx.sem("ldc")
            for j in range(8):
                gq.inc(gldw, 16, lambda e, j=j: e.dma_start(out=Wpb[:, j, :], in_=W_proj[j * 128:(j + 1) * 128, :]))
            sy.inc(ldc, 16, lambda e: e.dma_start(out=bp[:, :], in_=b_proj[:, :]))
            tq.wait(gldw)
            tq.wait(ldc)
            ve_done = []
            for i in range(NT):
                ib = i % 2
                if i >= 2:
                    sy.wait(ve, ve_done[i - 2])
                sy.inc(ldx[ib], 16, lambda e, i=i, ib=ib: e.dma_start(out=xb[ib][:, :], in_=x_in[i * 128:(i + 1) * 128, :]))
                if i >= 2:
                    tq.wait(ve, ve_done[i - 2])
                for half in range(2):
                    for p in range(8):
                        tq(lambda e, i=i, ib=ib, half=half, p=p: e.matmul(
                            out=psP[ib][:, half * 512:(half + 1) * 512],
                            lhsT=msaT[p][:, i * 128:(i + 1) * 128],
                            rhs=Wpb[:, p, half * 512:(half + 1) * 512], start=(p == 0), stop=False))
                    tq(lambda e, ib=ib, half=half: e.matmul(
                        out=psP[ib][:, half * 512:(half + 1) * 512], lhsT=ctx.ones_row[0:1, 0:128],
                        rhs=bp[:, half * 512:(half + 1) * 512], start=False, stop=True))
                tq.inc_prev(pe, 1)
                vq.wait(ldx[ib])
                vq.wait(pe)
                if i >= 2:
                    vq.wait(sto[ib])
                vq(lambda e, i=i, ib=ib: e.tensor_scalar(out=t1[ib][:, :], in0=xb[ib][:, :],
                                                         scalar1=Bp[:, i:i + 1], scalar2=A1[:, i:i + 1],
                                                         op0=OP.mult, op1=OP.subtract))
                vq.drain()
                vq.inc(ve, 1, lambda e, ib=ib: e.tensor_tensor(out=ot[ib][:, :], in0=t1[ib][:, :],
                                                               in1=psP[ib][:, :], op=OP.add))
                ve_done.append(ve.n)
                gq.wait(ve)
                gq.inc(sto[ib], 16, lambda e, i=i, ib=ib: e.dma_start(out=x_out[i * 128:(i + 1) * 128, :],
                                                                      in_=ot[ib][:, :]))
            gq.wait(ctx.sem("sto0"))
            gq.wait(ctx.sem("sto1"))
            ctx.flush("l0a_proj")



def build_l0a_launch():
    nc = make_nc()
    stack = contextlib.ExitStack()
    ctx = Ctx(nc, stack)
    with stack:
        declare_consts(ctx)
        x_in = ctx.din("x_in", (T, D), F32)
        x_out = ctx.dout("x_out", (T, D), F32)
        ctx.din("L0_attn_a_W_qkv", (1024, 3072), F32)
        ctx.din("L0_attn_a_b_qkv_r", (1, 3072), F32)
        ctx.din("L0_attn_a_W_proj", (1024, 1024), F32)
        ctx.din("L0_attn_a_b_proj_r", (1, 1024), F32)
        ctx.din("L0_attn_a_ln_w_r", (128, 8), F32)
        ctx.din("L0_attn_a_ln_b_r", (128, 8), F32)
        emit_load_consts(ctx)
        emit_l0a(ctx, x_in, x_out)
        ctx.flush("fin")
    return nc


def l0a_inmaps(inp, xs):
    c = host_constants()
    base = {
        **{k: v for k, v in c.items()},
        "L0_attn_a_W_qkv": np.ascontiguousarray(inp["L0_attn_a_W_qkv"]),
        "L0_attn_a_b_qkv_r": np.ascontiguousarray(inp["L0_attn_a_b_qkv"].reshape(1, -1)),
        "L0_attn_a_W_proj": np.ascontiguousarray(inp["L0_attn_a_W_proj"]),
        "L0_attn_a_b_proj_r": np.ascontiguousarray(inp["L0_attn_a_b_proj"].reshape(1, -1)),
        "L0_attn_a_ln_w_r": np.ascontiguousarray(inp["L0_attn_a_ln_w"].reshape(8, 128).T),
        "L0_attn_a_ln_b_r": np.ascontiguousarray(inp["L0_attn_a_ln_b"].reshape(8, 128).T),
    }
    return [{**base, "x_in": np.ascontiguousarray(xs[i])} for i in range(NCORES)]


def run_l0a(inp, xs):
    nc = build_l0a_launch()
    res = run_bass_kernel_spmd(nc, l0a_inmaps(inp, xs), core_ids=list(range(NCORES)))
    return np.stack([r["x_out"] for r in res.results])



def emit_tern_prep(ctx, W_T, Wt_sb, nchunk, nfree, name):
    nc = ctx.nc
    ve, ac, pe = ctx.sem("ve"), ctx.sem("ac"), ctx.sem("pe")
    ldw = [ctx.sem("ldw0"), ctx.sem("ldw1")]
    sy, vq, tq, aq = ctx.sy, ctx.vq, ctx.tq, ctx.aq
    nelem = float(nchunk * 128 * nfree)
    with ctx.scope():
        wf = [ctx.sb(f"{name}_wf{i}", [128, nfree], F32) for i in range(2)]
        absw = ctx.sb(f"{name}_absw", [128, nfree], F32)
        parts = ctx.sb(f"{name}_parts", [128, nchunk], F32)
        thr = ctx.sb(f"{name}_thr", [128, 1], F32)
        srow = ctx.sb(f"{name}_srow", [1, nchunk], F32)
        sgn = ctx.sb(f"{name}_sgn", [128, nfree], F32)
        psr = ctx.ps(f"{name}_psr", [1, nchunk], F32)
        psb = ctx.ps(f"{name}_psb", [128, 1], F32)
        ve_done = []
        for j in range(nchunk):
            if j >= 2:
                sy.wait(ve, ve_done[j - 2])
            sy.inc(ldw[j % 2], 16, lambda e, j=j: e.dma_start(out=wf[j % 2][:, :],
                                                             in_=W_T[j * 128:(j + 1) * 128, :]))
            vq.wait(ldw[j % 2])
            vq.inc(ve, 1, lambda e, j=j: e.tensor_reduce(out=parts[:, j:j + 1], in_=wf[j % 2][:, :],
                                                         axis=mybir.AxisListType.X, op=OP.add,
                                                         apply_absolute_value=True))
            ve_done.append(ve.n)
        tq.wait(ve)
        tq.inc(pe, 1, lambda e: e.matmul(out=psr[:, :], lhsT=ctx.onescol_f32[:, 0:1], rhs=parts[:, :],
                                         start=True, stop=True))
        vq.wait(pe)
        vq(lambda e: e.tensor_reduce(out=srow[:, 0:1], in_=psr[:, :], axis=mybir.AxisListType.X, op=OP.add))
        vq.drain()
        vq(lambda e: e.tensor_scalar(out=srow[:, 0:1], in0=srow[:, 0:1], scalar1=1.0 / nelem,
                                     scalar2=1e-8, op0=OP.mult, op1=OP.max))
        vq.drain()
        vq.inc(ve, 1, lambda e: e.tensor_scalar(out=srow[:, 0:1], in0=srow[:, 0:1], scalar1=0.5,
                                                scalar2=None, op0=OP.mult))
        tq.wait(ve)
        tq.inc(pe, 1, lambda e: e.matmul(out=psb[:, :], lhsT=ctx.ones_row[0:1, 0:128], rhs=srow[:, 0:1],
                                         start=True, stop=True))
        aq.wait(pe)
        aq.inc(ac, 1, lambda e: e.copy(out=thr[:, :], in_=psb[:, :]))
        sy.wait(ac)
        ac_done = []
        for j in range(nchunk):
            if j >= 2:
                sy.wait(ac, ac_done[j - 2])
            sy.inc(ldw[j % 2], 16, lambda e, j=j: e.dma_start(out=wf[j % 2][:, :],
                                                             in_=W_T[j * 128:(j + 1) * 128, :]))
            aq.wait(ldw[j % 2])
            aq.wait(ve)
            aq(lambda e, j=j: e.activation(out=absw[:, :], in_=wf[j % 2][:, :], func=AF.Abs))
            aq.drain()
            aq.inc(ac, 1, lambda e, j=j: e.activation(out=sgn[:, :], in_=wf[j % 2][:, :], func=AF.Sign))
            ac_done.append(ac.n)
            vq.wait(ac)
            vq(lambda e, j=j: e.tensor_scalar(out=absw[:, :], in0=absw[:, :], scalar1=thr[:, 0:1],
                                              scalar2=None, op0=OP.is_gt))
            vq.drain()
            vq.inc(ve, 1, lambda e, j=j: e.tensor_tensor(out=Wt_sb[:, j, :], in0=absw[:, :], in1=sgn[:, :],
                                                         op=OP.mult))
        ctx.flush(f"{name}_prep")


def emit_ffn(ctx, x_in, x_out, tag):
    nc = ctx.nc
    ve, ac, pe = ctx.sem("ve"), ctx.sem("ac"), ctx.sem("pe")
    ldc = ctx.sem("ldc")
    sy, vq, tq, aq, gq = ctx.sy, ctx.vq, ctx.tq, ctx.aq, ctx.gq
    W1T = ctx.dram["L0_ffn_W1T"]
    W2T = ctx.dram["L0_ffn_W2T"]
    b1r = ctx.dram["L0_ffn_b1_r"]
    b2r = ctx.dram["L0_ffn_b2_r"]

    with ctx.scope():
        A1 = ctx.sb(f"{tag}_A1", [128, NT], F32)
        B1 = ctx.sb(f"{tag}_B1", [128, NT], F32)
        emit_ln_stats(ctx, x_in, A1, B1, name=f"{tag}_s1")

        W1t = ctx.sb(f"{tag}_W1t", [128, 8, 4096], BF16)
        W2t = ctx.sb(f"{tag}_W2t", [128, 32, 1024], BF16)
        emit_tern_prep(ctx, W1T, W1t, 8, 4096, f"{tag}_w1")
        emit_tern_prep(ctx, W2T, W2t, 32, 1024, f"{tag}_w2")

        with ctx.scope():
            b1c = ctx.sb(f"{tag}_b1c", [128, 32], F32)
            b2 = ctx.sb(f"{tag}_b2", [1, 1024], F32)
            sy.inc(ldc, 16, lambda e: e.dma_start(out=b1c[:, :], in_=b1r[:, :]))
            sy.inc(ldc, 16, lambda e: e.dma_start(out=b2[:, :], in_=b2r[:, :]))
            sy.wait(ldc)
            ctx.flush(f"{tag}_bias")

            tn = TileNorm(ctx, f"{tag}_m", x_in, A1, B1)
            hb = [ctx.sb(f"{tag}_hb{i}", [128, 128], BF16) for i in range(2)]
            ot = [ctx.sb(f"{tag}_ot{i}", [128, 1024], F32) for i in range(2)]
            psH = [ctx.ps(f"{tag}_psH{i}", [128, 128], F32) for i in range(2)]
            psO = [ctx.ps(f"{tag}_psO{i}", [128, 1024], F32) for i in range(2)]
            sto = [ctx.sem("sto0"), ctx.sem("sto1")]
            ldx = [ctx.sem("ldx0"), ctx.sem("ldx1")]
            ve_done = []
            for i in range(NT):
                ib = i % 2
                tn.record(i)
                tq.wait(ac)
                if i >= 2:
                    tq.wait(ve, ve_done[i - 2])
                for c in range(32):
                    cb = c % 2
                    if c >= 2:
                        tq.wait(ac)
                    for j in range(8):
                        tq(lambda e, i=i, c=c, j=j, cb=cb: e.matmul(
                            out=psH[cb][:, :], lhsT=W1t[:, j, c * 128:(c + 1) * 128],
                            rhs=tn.cur(i)[:, j * 128:(j + 1) * 128], start=(j == 0), stop=(j == 7)))
                    tq.inc_prev(pe, 1)
                    aq.wait(pe)
                    aq.inc(ac, 1, lambda e, c=c, cb=cb: e.activation(
                        out=hb[cb][:, :], in_=psH[cb][:, :], func=AF.Gelu, bias=b1c[:, c:c + 1]))
                    tq.wait(ac)
                    for half in range(2):
                        tq(lambda e, ib=ib, c=c, cb=cb, half=half: e.matmul(
                            out=psO[ib][:, half * 512:(half + 1) * 512], lhsT=hb[cb][:, :],
                            rhs=W2t[:, c, half * 512:(half + 1) * 512], start=(c == 0), stop=False))
                    tq.inc_prev(pe, 1)
                for half in range(2):
                    tq(lambda e, ib=ib, half=half: e.matmul(
                        out=psO[ib][:, half * 512:(half + 1) * 512], lhsT=ctx.ones_row[0:1, 0:128],
                        rhs=b2[:, half * 512:(half + 1) * 512], start=False, stop=True))
                tq.inc_prev(pe, 1)
                tn.note_consumed(pe, pe.n)
                vq.wait(pe)
                if i >= 2:
                    vq.wait(sto[ib])
                vq.inc(ve, 1, lambda e, i=i, ib=ib: e.tensor_tensor(
                    out=ot[ib][:, :], in0=tn.xb[ib][:, :], in1=psO[ib][:, :], op=OP.add))
                tn.ve_done[i] = ve.n
                ve_done.append(ve.n)
                gq.wait(ve)
                gq.inc(sto[ib], 16, lambda e, i=i, ib=ib: e.dma_start(
                    out=x_out[i * 128:(i + 1) * 128, :], in_=ot[ib][:, :]))
            gq.wait(ctx.sem("sto0"))
            gq.wait(ctx.sem("sto1"))
            ctx.flush(f"{tag}_main")


def build_ffn_launch():
    nc = make_nc()
    stack = contextlib.ExitStack()
    ctx = Ctx(nc, stack)
    with stack:
        declare_consts(ctx)
        x_in = ctx.din("x_in", (T, D), F32)
        x_out = ctx.dout("x_out", (T, D), F32)
        ctx.din("L0_ffn_W1T", (1024, 4096), F32)
        ctx.din("L0_ffn_W2T", (4096, 1024), F32)
        ctx.din("L0_ffn_b1_r", (128, 32), F32)
        ctx.din("L0_ffn_b2_r", (1, 1024), F32)
        emit_load_consts(ctx)
        emit_ffn(ctx, x_in, x_out, "ffn")
        ctx.flush("fin")
    return nc


def ffn_inmaps(inp, xs):
    c = host_constants()
    base = {
        **c,
        "L0_ffn_W1T": np.ascontiguousarray(inp["L0_ffn_W1"].T),
        "L0_ffn_W2T": np.ascontiguousarray(inp["L0_ffn_W2"].T),
        "L0_ffn_b1_r": np.ascontiguousarray(inp["L0_ffn_b1"].reshape(32, 128).T),
        "L0_ffn_b2_r": np.ascontiguousarray(inp["L0_ffn_b2"].reshape(1, -1)),
    }
    return [{**base, "x_in": np.ascontiguousarray(xs[i])} for i in range(NCORES)]


def run_ffn(inp, xs):
    nc = build_ffn_launch()
    res = run_bass_kernel_spmd(nc, ffn_inmaps(inp, xs), core_ids=list(range(NCORES)))
    return np.stack([r["x_out"] for r in res.results])



def emit_ret(ctx, x_in, x_out, tag="ret"):
    nc = ctx.nc
    ve, ac, pe = ctx.sem("ve"), ctx.sem("ac"), ctx.sem("pe")
    sy, vq, tq, aq, gq = ctx.sy, ctx.vq, ctx.tq, ctx.aq, ctx.gq
    WQ, WK, WV = (ctx.dram[f"L0_attn_b_W_{x}"] for x in "QKV")
    sc_q, ss_q = ctx.dram["ret_sc_q"], ctx.dram["ret_ss_q"]
    sc_k, ss_k = ctx.dram["ret_sc_k"], ctx.dram["ret_ss_k"]
    Dt = ctx.dram["ret_Dt"]
    QTd = ctx.dtmp(f"{tag}_QTd", (8, 128, T), BF16)
    KTd = ctx.dtmp(f"{tag}_KTd", (8, 128, T), BF16)
    Vd = ctx.dtmp(f"{tag}_Vd", (T, D), BF16)

    with ctx.scope():
        A1 = ctx.sb(f"{tag}_A1", [128, NT], F32)
        B1 = ctx.sb(f"{tag}_B1", [128, NT], F32)
        emit_ln_stats(ctx, x_in, A1, B1, name=f"{tag}_s1")

        with ctx.scope():
            Wb = ctx.sb(f"{tag}_Wb", [128, 8, 3072], BF16)
            gldw = ctx.sem("gldw")
            for m, W in enumerate((WQ, WK, WV)):
                for j in range(8):
                    gq.inc(gldw, 16, lambda e, m=m, j=j, W=W: e.dma_start(
                        out=Wb[:, j, m * 1024:(m + 1) * 1024], in_=W[j * 128:(j + 1) * 128, :]))
            gq.wait(gldw)
            ctx.flush(f"{tag}_wload")

            tn = TileNorm(ctx, f"{tag}_m", x_in, A1, B1)
            ldt = [ctx.sem("ldt0"), ctx.sem("ldt1")]
            stq = [ctx.sem("stq0"), ctx.sem("stq1")]
            tab = [ctx.sb(f"{tag}_tab{i}", [128, 4, 1024], F32) for i in range(2)]
            Qf = ctx.sb(f"{tag}_Qf", [128, 1024], F32)
            rb = ctx.sb(f"{tag}_rb", [128, 1024], F32)
            ta = ctx.sb(f"{tag}_ta", [128, 1024], F32)
            qkb = [ctx.sb(f"{tag}_qkb{i}", [128, 2048], BF16) for i in range(2)]
            vb = [ctx.sb(f"{tag}_vb{i}", [128, 1024], BF16) for i in range(2)]
            qkT = [ctx.sb(f"{tag}_qkT{i}", [128, 2048], BF16) for i in range(2)]
            psA = ctx.ps(f"{tag}_psA", [128, 1024], F32)
            psT = ctx.ps(f"{tag}_psT", [128, 2048], F32)
            ve_done2 = []
            ac_done2 = []
            tp_done = []
            for i in range(NT):
                ib = i % 2
                tn.record(i)
                if i >= 2:
                    sy.wait(ve, ve_done2[-1])
                sy.inc(ldt[ib], 16, lambda e, i=i, ib=ib: e.dma_start(
                    out=tab[ib][:, 0, :], in_=sc_q[i * 128:(i + 1) * 128, :]))
                sy.inc(ldt[ib], 16, lambda e, i=i, ib=ib: e.dma_start(
                    out=tab[ib][:, 1, :], in_=ss_q[i * 128:(i + 1) * 128, :]))
                sy.inc(ldt[ib], 16, lambda e, i=i, ib=ib: e.dma_start(
                    out=tab[ib][:, 2, :], in_=sc_k[i * 128:(i + 1) * 128, :]))
                sy.inc(ldt[ib], 16, lambda e, i=i, ib=ib: e.dma_start(
                    out=tab[ib][:, 3, :], in_=ss_k[i * 128:(i + 1) * 128, :]))
                tq.wait(ac)
                for m in range(3):
                    if not (i == 0 and m == 0):
                        tq.wait(ac)
                    for half in range(2):
                        for j in range(8):
                            tq(lambda e, i=i, m=m, half=half, j=j: e.matmul(
                                out=psA[:, half * 512:(half + 1) * 512],
                                lhsT=tn.cur(i)[:, j * 128:(j + 1) * 128],
                                rhs=Wb[:, j, m * 1024 + half * 512:m * 1024 + (half + 1) * 512],
                                start=(j == 0), stop=(j == 7)))
                    tq.inc_prev(pe, 1)
                    aq.wait(pe)
                    if m < 2:
                        if ve_done2:
                            aq.wait(ve, ve_done2[-1])
                        aq.inc(ac, 1, lambda e: e.copy(out=Qf[:, :], in_=psA[:, :]))
                        vq.wait(ac)
                        vq.wait(ldt[ib])
                        q3o = Qf.rearrange("p (n two) -> p n two", two=2)
                        r3o = rb.rearrange("p (n two) -> p n two", two=2)
                        vq(lambda e, q3=q3o, r3=r3o: e.tensor_scalar(
                            out=r3[:, :, 0], in0=q3[:, :, 1], scalar1=-1.0, scalar2=None, op0=OP.mult))
                        vq(lambda e, q3=q3o, r3=r3o: e.tensor_copy(out=r3[:, :, 1], in_=q3[:, :, 0]))
                        vq.drain()
                        vq(lambda e, ib=ib, m=m: e.tensor_tensor(
                            out=ta[:, :], in0=Qf[:, :], in1=tab[ib][:, 2 * m, :], op=OP.mult))
                        vq(lambda e, ib=ib, m=m: e.tensor_tensor(
                            out=rb[:, :], in0=rb[:, :], in1=tab[ib][:, 2 * m + 1, :], op=OP.mult))
                        vq.drain()
                        if i >= 2 and m == 0:
                            vq.wait(pe, tp_done[i - 2])
                        vq.inc(ve, 1, lambda e, ib=ib, m=m: e.tensor_tensor(
                            out=qkb[ib][:, m * 1024:(m + 1) * 1024], in0=ta[:, :], in1=rb[:, :], op=OP.add))
                        ve_done2.append(ve.n)
                    else:
                        if i >= 2:
                            aq.wait(stq[ib])
                        aq.inc(ac, 1, lambda e, ib=ib: e.copy(out=vb[ib][:, :], in_=psA[:, :]))
                        gq.wait(ac)
                        gq.inc(stq[ib], 16, lambda e, i=i, ib=ib: e.dma_start(
                            out=Vd[i * 128:(i + 1) * 128, :], in_=vb[ib][:, :]))
                tq.wait(ve)
                if ac_done2:
                    tq.wait(ac, ac_done2[-1])
                for j in range(16):
                    tq(lambda e, ib=ib, j=j: e.matmul(
                        out=psT[:, j * 128:(j + 1) * 128],
                        lhsT=qkb[ib][:, j * 128:(j + 1) * 128],
                        rhs=ctx.iden_bf[:, :], start=True, stop=True))
                tq.inc_prev(pe, 1)
                tp_done.append(pe.n)
                tn.note_consumed(pe, pe.n)
                aq.wait(pe)
                if i >= 2:
                    aq.wait(stq[ib])
                aq.inc(ac, 1, lambda e, ib=ib: e.copy(out=qkT[ib][:, :], in_=psT[:, :]))
                ac_done2.append(ac.n)
                gq.wait(ac)
                gq.inc(stq[ib], 16, lambda e, i=i, ib=ib: e.dma_start(
                    out=QTd[:, :, i * 128:(i + 1) * 128].rearrange("j p c -> p j c"),
                    in_=qkT[ib][:, 0:1024].rearrange("p (j c) -> p j c", c=128)))
                gq.inc(stq[ib], 16, lambda e, i=i, ib=ib: e.dma_start(
                    out=KTd[:, :, i * 128:(i + 1) * 128].rearrange("j p c -> p j c"),
                    in_=qkT[ib][:, 1024:2048].rearrange("p (j c) -> p j c", c=128)))
            gq.wait(ctx.sem("stq0"))
            gq.wait(ctx.sem("stq1"))
            ctx.flush(f"{tag}_r1")

        with ctx.scope():
            ldx = [ctx.sem("ldx0"), ctx.sem("ldx1")]
            ldt = [ctx.sem("ldt0"), ctx.sem("ldt1")]
            ldk = [ctx.sem("ldk0"), ctx.sem("ldk1")]
            sto = [ctx.sem("sto0"), ctx.sem("sto1")]
            ldv = ctx.sem("ldv0")
            Vr = ctx.sb(f"{tag}_Vr", [128, NT, 1024], BF16)
            QTg = [ctx.sb(f"{tag}_QTg{i}", [128, 8, 512], BF16) for i in range(2)]
            KTc = [ctx.sb(f"{tag}_KTc{i}", [128, 8, 128], BF16) for i in range(2)]
            Dtb = [ctx.sb(f"{tag}_Dtb{i}", [128, 512], F32) for i in range(2)]
            SDt = ctx.sb(f"{tag}_SDt", [128, NT, 512], BF16)
            xb = [ctx.sb(f"{tag}_rx{i}", [128, 1024], F32) for i in range(2)]
            ot = [ctx.sb(f"{tag}_rot{i}", [128, 1024], F32) for i in range(2)]
            psS = [ctx.ps(f"{tag}_psS{i}", [128, 512], F32) for i in range(2)]
            psO = [ctx.ps(f"{tag}_psO{i}", [128, 1024], F32) for i in range(2)]
            sy.inc(ldv, 16, lambda e: e.dma_start(
                out=Vr[:, :, :], in_=Vd.rearrange("(n p) c -> p n c", p=128)))
            tq.wait(ldv)
            ve_done3 = []
            qg_done = []
            mc_pe = []
            sd_ve = []
            for g in range(8):
                gb = g % 2
                nmc = 4 * (g + 1)
                if g >= 2:
                    sy.wait(pe, qg_done[g - 2])
                for j in range(8):
                    sy.inc(ldt[gb], 16, lambda e, g=g, gb=gb, j=j: e.dma_start(
                        out=QTg[gb][:, j, :], in_=QTd[j, :, g * 512:(g + 1) * 512]))
                tq.wait(ldt[gb])
                for mc in range(nmc):
                    mb = mc % 2
                    if len(mc_pe) >= 2:
                        sy.wait(pe, mc_pe[-2])
                    if len(sd_ve) >= 2:
                        sy.wait(ve, sd_ve[-2])
                    for j in range(8):
                        sy.inc(ldk[mb], 16, lambda e, mc=mc, mb=mb, j=j: e.dma_start(
                            out=KTc[mb][:, j, :], in_=KTd[j, :, mc * 128:(mc + 1) * 128]))
                    sy.inc(ldk[mb], 16, lambda e, g=g, mc=mc, mb=mb: e.dma_start(
                        out=Dtb[mb][:, :], in_=Dt[mc * 128:(mc + 1) * 128, g * 512:(g + 1) * 512]))
                    if mc >= 2:
                        tq.wait(ve)
                    tq.wait(ldk[mb])
                    for j in range(8):
                        tq(lambda e, gb=gb, mb=mb, j=j: e.matmul(
                            out=psS[mb][:, :], lhsT=KTc[mb][:, j, :], rhs=QTg[gb][:, j, :],
                            start=(j == 0), stop=(j == 7)))
                    tq.inc_prev(pe, 1)
                    mc_pe.append(pe.n)
                    vq.wait(pe)
                    vq.inc(ve, 1, lambda e, mc=mc, mb=mb: e.tensor_tensor(
                        out=SDt[:, mc, :], in0=psS[mb][:, :], in1=Dtb[mb][:, :], op=OP.mult))
                    sd_ve.append(ve.n)
                for tt in range(4):
                    i = g * 4 + tt
                    ib = i % 2
                    if ve_done3:
                        sy.wait(ve, ve_done3[-1])
                    sy.inc(ldx[ib], 16, lambda e, i=i, ib=ib: e.dma_start(
                        out=xb[ib][:, :], in_=x_in[i * 128:(i + 1) * 128, :]))
                    tq.wait(ve)
                    if len(ve_done3) >= 2:
                        tq.wait(ve, ve_done3[-2])
                    for half in range(2):
                        for mc in range(nmc):
                            tq(lambda e, tt=tt, mc=mc, ib=ib, half=half, nmc=nmc: e.matmul(
                                out=psO[ib][:, half * 512:(half + 1) * 512],
                                lhsT=SDt[:, mc, tt * 128:(tt + 1) * 128],
                                rhs=Vr[:, mc, half * 512:(half + 1) * 512],
                                start=(mc == 0), stop=(mc == nmc - 1)))
                    tq.inc_prev(pe, 1)
                    if tt == 3:
                        qg_done.append(pe.n)
                    vq.wait(pe)
                    vq.wait(ldx[ib])
                    if len(ve_done3) >= 2:
                        vq.wait(sto[ib])
                    vq.inc(ve, 1, lambda e, ib=ib: e.tensor_tensor(
                        out=ot[ib][:, :], in0=xb[ib][:, :], in1=psO[ib][:, :], op=OP.add))
                    ve_done3.append(ve.n)
                    gq.wait(ve)
                    gq.inc(sto[ib], 16, lambda e, i=i, ib=ib: e.dma_start(
                        out=x_out[i * 128:(i + 1) * 128, :], in_=ot[ib][:, :]))
            gq.wait(ctx.sem("sto0"))
            gq.wait(ctx.sem("sto1"))
            ctx.flush(f"{tag}_r2")


def ret_tables():
    SCALE_BASE = 512.0
    GAMMA = 0.96875
    d = D
    half = d // 2
    scale_vec = (np.arange(0, d, 2, dtype=np.float64) + 0.4 * d) / (1.4 * d)
    pos = np.arange(T, dtype=np.float64)
    scale = scale_vec[None, :] ** (pos[:, None] / SCALE_BASE)
    inv_freq = 1.0 / 10000.0 ** (np.arange(half, dtype=np.float64) / half)
    sinusoid = pos[:, None] * inv_freq[None, :]
    sin_s = np.repeat(np.sin(sinusoid) * scale, 2, axis=-1)
    cos_s = np.repeat(np.cos(sinusoid) * scale, 2, axis=-1)
    sin_d = np.repeat(np.sin(sinusoid) / scale, 2, axis=-1)
    cos_d = np.repeat(np.cos(sinusoid) / scale, 2, axis=-1)
    n = np.arange(T)[:, None]
    m = np.arange(T)[None, :]
    diff = (n - m).astype(np.float64)
    Dm = np.where(diff >= 0, GAMMA ** np.maximum(diff, 0), 0.0)
    return {
        "ret_sc_q": cos_s.astype(np.float32), "ret_ss_q": sin_s.astype(np.float32),
        "ret_sc_k": cos_d.astype(np.float32), "ret_ss_k": sin_d.astype(np.float32),
        "ret_Dt": np.ascontiguousarray(Dm.T.astype(np.float32)),
    }


def build_ret_launch():
    nc = make_nc()
    stack = contextlib.ExitStack()
    ctx = Ctx(nc, stack)
    with stack:
        declare_consts(ctx)
        x_in = ctx.din("x_in", (T, D), F32)
        x_out = ctx.dout("x_out", (T, D), F32)
        for x in "QKV":
            ctx.din(f"L0_attn_b_W_{x}", (1024, 1024), F32)
        for t in ("ret_sc_q", "ret_ss_q", "ret_sc_k", "ret_ss_k"):
            ctx.din(t, (T, D), F32)
        ctx.din("ret_Dt", (T, T), F32)
        emit_load_consts(ctx)
        emit_ret(ctx, x_in, x_out)
        ctx.flush("fin")
    return nc


def ret_inmaps(inp, xs):
    c = host_constants()
    tabs = ret_tables()
    base = {
        **c, **tabs,
        "L0_attn_b_W_Q": np.ascontiguousarray(inp["L0_attn_b_W_Q"]),
        "L0_attn_b_W_K": np.ascontiguousarray(inp["L0_attn_b_W_K"]),
        "L0_attn_b_W_V": np.ascontiguousarray(inp["L0_attn_b_W_V"]),
    }
    return [{**base, "x_in": np.ascontiguousarray(xs[i])} for i in range(NCORES)]


def run_ret(inp, xs):
    nc = build_ret_launch()
    res = run_bass_kernel_spmd(nc, ret_inmaps(inp, xs), core_ids=list(range(NCORES)))
    return np.stack([r["x_out"] for r in res.results])



def emit_gla(ctx, x_in, x_out, tag="gla"):
    nc = ctx.nc
    ve, ac, pe = ctx.sem("ve"), ctx.sem("ac"), ctx.sem("pe")
    sy, vq, tq, aq, gq = ctx.sy, ctx.vq, ctx.tq, ctx.aq, ctx.gq
    Wq, Wk, Wv, Wg = (ctx.dram[f"L1_attn_a_W{x}"] for x in ("q", "k", "v", "g"))
    Wo = ctx.dram["L1_attn_a_Wo"]
    qtTd = ctx.dtmp(f"{tag}_qtTd", (8, 128, T), BF16)
    ktTd = ctx.dtmp(f"{tag}_ktTd", (8, 128, T), BF16)
    ktd = ctx.dtmp(f"{tag}_ktd", (T, D), BF16)
    vtd = ctx.dtmp(f"{tag}_vtd", (T, D), BF16)
    hvd = ctx.dtmp(f"{tag}_hvd", (NT, 128, 16), F32)

    with ctx.scope():
        A1 = ctx.sb(f"{tag}_A1", [128, NT], F32)
        B1 = ctx.sb(f"{tag}_B1", [128, NT], F32)
        emit_ln_stats(ctx, x_in, A1, B1, name=f"{tag}_s1")

        with ctx.scope():
            Wb = ctx.sb(f"{tag}_Wb", [128, 8, 4096], BF16)
            gldw = ctx.sem("gldw")
            for m, W in enumerate((Wq, Wk, Wv, Wg)):
                for j in range(8):
                    gq.inc(gldw, 16, lambda e, m=m, j=j, W=W: e.dma_start(
                        out=Wb[:, j, m * 1024:(m + 1) * 1024], in_=W[j * 128:(j + 1) * 128, :]))
            gq.wait(gldw)
            ctx.flush(f"{tag}_wload")

            tn = TileNorm(ctx, f"{tag}_m", x_in, A1, B1)
            stq = [ctx.sem("stq0"), ctx.sem("stq1")]
            qkf = [ctx.sb(f"{tag}_qkf{i}", [128, 2048], F32) for i in range(2)]
            gf = [ctx.sb(f"{tag}_gf{i}", [128, 1024], F32) for i in range(2)]
            tmp1 = [ctx.sb(f"{tag}_tmp1{i}", [128, 1024], F32) for i in range(2)]
            lna = [ctx.sb(f"{tag}_lna{i}", [128, 1024], F32) for i in range(2)]
            ss = [ctx.sb(f"{tag}_ss{i}", [128, 16], F32) for i in range(2)]
            qkn = [ctx.sb(f"{tag}_qkn{i}", [128, 2048], BF16) for i in range(2)]
            qknT = [ctx.sb(f"{tag}_qknT{i}", [128, 2048], BF16) for i in range(2)]
            lnaT = ctx.sb(f"{tag}_lnaT", [128, 1024], F32)
            csT = ctx.sb(f"{tag}_csT", [128, 8, 128], F32)
            hv = [ctx.sb(f"{tag}_hv{i}", [128, 16], F32) for i in range(2)]
            earg = ctx.sb(f"{tag}_earg", [128, 2048], F32)
            BB = ctx.sb(f"{tag}_BB", [128, 2048], BF16)
            qkt = [ctx.sb(f"{tag}_qkt{i}", [128, 2048], BF16) for i in range(2)]
            kvb = [ctx.sb(f"{tag}_kvb{i}", [128, 2048], BF16) for i in range(2)]
            psA = ctx.ps(f"{tag}_psA", [128, 2048], F32)
            psT = ctx.ps(f"{tag}_psT", [128, 1024], F32)
            done = []
            for i in range(NT):
                ib = i % 2
                d = {}
                tn.record(i)
                tq.wait(ac)
                if i >= 2:
                    tq.wait(ve, done[i - 2]["all_ve"])
                for m in range(2):
                    for half in range(2):
                        for j in range(8):
                            tq(lambda e, i=i, m=m, half=half, j=j: e.matmul(
                                out=psA[:, m * 1024 + half * 512:m * 1024 + (half + 1) * 512],
                                lhsT=tn.cur(i)[:, j * 128:(j + 1) * 128],
                                rhs=Wb[:, j, m * 1024 + half * 512:m * 1024 + (half + 1) * 512],
                                start=(j == 0), stop=(j == 7)))
                tq.inc_prev(pe, 1)
                aq.wait(pe)
                aq.inc(ac, 1, lambda e, ib=ib: e.copy(out=qkf[ib][:, :], in_=psA[:, :]))
                d["qk_evac"] = ac.n
                aq.drain()
                aq.inc(ac, 1, lambda e, ib=ib: e.activation(out=tmp1[ib][:, :], in_=qkf[ib][:, 0:1024], func=AF.Square))
                vq.wait(ac)
                for s in range(8):
                    vq(lambda e, ib=ib, s=s: e.tensor_reduce(out=ss[ib][:, s:s + 1],
                                                             in_=tmp1[ib][:, s * 128:(s + 1) * 128],
                                                             axis=mybir.AxisListType.X, op=OP.add))
                vq.drain()
                vq.inc(ve, 1, lambda e, ib=ib: e.tensor_scalar(
                    out=ss[ib][:, 0:8], in0=ss[ib][:, 0:8], scalar1=1e-24, scalar2=None, op0=OP.max))
                aq.wait(ve)
                aq.inc(ac, 1, lambda e, ib=ib: e.activation(out=tmp1[ib][:, :], in_=qkf[ib][:, 1024:2048], func=AF.Square))
                vq.wait(ac)
                for s in range(8):
                    vq(lambda e, ib=ib, s=s: e.tensor_reduce(out=ss[ib][:, 8 + s:9 + s],
                                                             in_=tmp1[ib][:, s * 128:(s + 1) * 128],
                                                             axis=mybir.AxisListType.X, op=OP.add))
                vq.drain()
                vq.inc(ve, 1, lambda e, ib=ib: e.tensor_scalar(
                    out=ss[ib][:, 8:16], in0=ss[ib][:, 8:16], scalar1=1e-24, scalar2=None, op0=OP.max))
                aq.wait(ve)
                aq.drain()
                aq(lambda e, ib=ib: e.activation(out=ss[ib][:, :], in_=ss[ib][:, :], func=AF.Ln))
                aq.drain()
                aq.inc(ac, 1, lambda e, ib=ib: e.activation(out=ss[ib][:, :], in_=ss[ib][:, :], func=AF.Exp, scale=-0.5))
                d["rstd"] = ac.n
                vq.wait(ac, d["rstd"])
                if i >= 2:
                    vq.wait(pe, done[i - 2]["qkn_tr"])
                for s in range(16):
                    vq(lambda e, ib=ib, s=s: e.tensor_scalar(
                        out=qkn[ib][:, s * 128:(s + 1) * 128], in0=qkf[ib][:, s * 128:(s + 1) * 128],
                        scalar1=ss[ib][:, s:s + 1], scalar2=None, op0=OP.mult))
                vq.drain()
                vq.inc(ve, 1, lambda e, ib=ib: e.tensor_copy(out=qkn[ib][:, 0:1], in_=qkn[ib][:, 0:1]))
                d["qkn"] = ve.n
                tq.wait(ac, d["qk_evac"])
                for m in range(2):
                    for half in range(2):
                        for j in range(8):
                            tq(lambda e, i=i, m=m, half=half, j=j: e.matmul(
                                out=psA[:, m * 1024 + half * 512:m * 1024 + (half + 1) * 512],
                                lhsT=tn.cur(i)[:, j * 128:(j + 1) * 128],
                                rhs=Wb[:, j, (2 + m) * 1024 + half * 512:(2 + m) * 1024 + (half + 1) * 512],
                                start=(j == 0), stop=(j == 7)))
                tq.inc_prev(pe, 1)
                tn.note_consumed(pe, pe.n)
                aq.wait(pe)
                if i >= 2:
                    aq.wait(stq[ib])
                aq.inc(ac, 1, lambda e, ib=ib: e.copy(out=kvb[ib][:, 1024:2048], in_=psA[:, 0:1024]))
                aq.inc(ac, 1, lambda e, ib=ib: e.copy(out=gf[ib][:, :], in_=psA[:, 1024:2048]))
                d["vg_evac"] = ac.n
                aq.drain()
                aq(lambda e, ib=ib: e.activation(out=tmp1[ib][:, :], in_=gf[ib][:, :], func=AF.Abs))
                aq.drain()
                aq(lambda e, ib=ib: e.activation(out=tmp1[ib][:, :], in_=tmp1[ib][:, :], func=AF.Exp, scale=-1.0))
                aq.drain()
                aq.inc(ac, 1, lambda e, ib=ib: e.activation(out=tmp1[ib][:, :], in_=tmp1[ib][:, :], func=AF.Ln, bias=1.0))
                vq.wait(ac)
                vq(lambda e, ib=ib: e.tensor_scalar(out=lna[ib][:, :], in0=gf[ib][:, :], scalar1=0.0,
                                                    scalar2=None, op0=OP.min))
                vq.drain()
                vq.inc(ve, 1, lambda e, ib=ib: e.tensor_tensor(out=lna[ib][:, :], in0=lna[ib][:, :],
                                                               in1=tmp1[ib][:, :], op=OP.subtract))
                d["lna"] = ve.n
                for which in range(2):
                    tq.wait(ve, d["qkn"])
                    tq.wait(ac)
                    for j in range(8):
                        tq(lambda e, ib=ib, which=which, j=j: e.matmul(
                            out=psT[:, j * 128:(j + 1) * 128],
                            lhsT=qkn[ib][:, which * 1024 + j * 128:which * 1024 + (j + 1) * 128],
                            rhs=ctx.iden_bf[:, :], start=True, stop=True))
                    tq.inc_prev(pe, 1)
                    aq.wait(pe)
                    aq.inc(ac, 1, lambda e, ib=ib, which=which: e.copy(
                        out=qknT[ib][:, which * 1024:(which + 1) * 1024], in_=psT[:, :]))
                d["qkn_tr"] = pe.n
                tq.wait(ve, d["lna"])
                tq.wait(ac)
                for j in range(8):
                    tq(lambda e, ib=ib, j=j: e.matmul(
                        out=psT[:, j * 128:(j + 1) * 128],
                        lhsT=lna[ib][:, j * 128:(j + 1) * 128],
                        rhs=ctx.iden_f32[:, :], start=True, stop=True))
                tq.inc_prev(pe, 1)
                d["lna_tr"] = pe.n
                aq.wait(pe)
                aq.inc(ac, 1, lambda e: e.copy(out=lnaT[:, :], in_=psT[:, :]))
                vq.wait(ac)
                for j in range(8):
                    for c in range(2):
                        vq(lambda e, j=j, c=c: e.tensor_tensor_scan(
                            out=csT[:, j, c * 64:(c + 1) * 64],
                            data0=lnaT[:, j * 128 + c * 64:j * 128 + c * 64 + 64],
                            data1=lnaT[:, j * 128 + c * 64:j * 128 + c * 64 + 64],
                            initial=0.0, op0=OP.add, op1=OP.bypass))
                vq.drain()
                if i >= 2:
                    vq.wait(stq[ib])
                vq(lambda e, ib=ib: e.tensor_scalar(
                    out=hv[ib][:, 0:8], in0=csT[:, :, 63:64].rearrange("p j one -> p (j one)"),
                    scalar1=0.5, scalar2=None, op0=OP.mult))
                vq.drain()
                vq.inc(ve, 1, lambda e, ib=ib: e.tensor_scalar(
                    out=hv[ib][:, 8:16], in0=csT[:, :, 127:128].rearrange("p j one -> p (j one)"),
                    scalar1=0.5, scalar2=None, op0=OP.mult))
                d["hv"] = ve.n
                vq.drain()
                for j in range(8):
                    for c in range(2):
                        vq(lambda e, ib=ib, j=j, c=c: e.tensor_scalar(
                            out=earg[:, j * 128 + c * 64:j * 128 + c * 64 + 64],
                            in0=csT[:, j, c * 64:(c + 1) * 64],
                            scalar1=hv[ib][:, c * 8 + j:c * 8 + j + 1], scalar2=85.0,
                            op0=OP.subtract, op1=OP.min))
                        vq(lambda e, ib=ib, j=j, c=c: e.tensor_scalar(
                            out=earg[:, 1024 + j * 128 + c * 64:1024 + j * 128 + c * 64 + 64],
                            in0=csT[:, j, c * 64:(c + 1) * 64],
                            scalar1=hv[ib][:, c * 8 + j:c * 8 + j + 1], scalar2=-85.0,
                            op0=OP.subtract, op1=OP.max))
                vq.drain()
                vq.inc(ve, 1, lambda e: e.tensor_copy(out=earg[:, 0:1], in_=earg[:, 0:1]))
                aq.wait(ve)
                aq(lambda e: e.activation(out=BB[:, 0:1024], in_=earg[:, 0:1024], func=AF.Exp))
                aq.drain()
                aq.inc(ac, 1, lambda e: e.activation(out=BB[:, 1024:2048], in_=earg[:, 1024:2048],
                                                     func=AF.Exp, scale=-1.0))
                vq.wait(ac)
                if i >= 2:
                    vq.wait(pe, done[i - 2]["ktok_tr"])
                vq(lambda e, ib=ib: e.tensor_tensor(out=qkt[ib][:, 0:1024], in0=qknT[ib][:, 0:1024],
                                                    in1=BB[:, 0:1024], op=OP.mult))
                vq.drain()
                vq.inc(ve, 1, lambda e, ib=ib: e.tensor_tensor(out=qkt[ib][:, 1024:2048], in0=qknT[ib][:, 1024:2048],
                                                               in1=BB[:, 1024:2048], op=OP.mult))
                d["qkt"] = ve.n
                tq.wait(ve, d["qkt"])
                tq.wait(ac)
                for j in range(8):
                    tq(lambda e, ib=ib, j=j: e.matmul(
                        out=psT[:, j * 128:(j + 1) * 128],
                        lhsT=qkt[ib][:, 1024 + j * 128:1024 + (j + 1) * 128],
                        rhs=ctx.iden_bf[:, :], start=True, stop=True))
                tq.inc_prev(pe, 1)
                d["ktok_tr"] = pe.n
                aq.wait(pe)
                if i >= 2:
                    aq.wait(stq[ib])
                aq.inc(ac, 1, lambda e, ib=ib: e.copy(out=kvb[ib][:, 0:1024], in_=psT[:, :]))
                d["all_ve"] = ve.n
                gq.wait(ac)
                gq.wait(ve, d["hv"])
                gq.inc(stq[ib], 16, lambda e, i=i, ib=ib: e.dma_start(
                    out=qtTd[:, :, i * 128:(i + 1) * 128].rearrange("j p c -> p j c"),
                    in_=qkt[ib][:, 0:1024].rearrange("p (j c) -> p j c", c=128)))
                gq.inc(stq[ib], 16, lambda e, i=i, ib=ib: e.dma_start(
                    out=ktTd[:, :, i * 128:(i + 1) * 128].rearrange("j p c -> p j c"),
                    in_=qkt[ib][:, 1024:2048].rearrange("p (j c) -> p j c", c=128)))
                gq.inc(stq[ib], 16, lambda e, i=i, ib=ib: e.dma_start(
                    out=ktd[i * 128:(i + 1) * 128, :], in_=kvb[ib][:, 0:1024]))
                gq.inc(stq[ib], 16, lambda e, i=i, ib=ib: e.dma_start(
                    out=vtd[i * 128:(i + 1) * 128, :], in_=kvb[ib][:, 1024:2048]))
                gq.inc(stq[ib], 16, lambda e, i=i, ib=ib: e.dma_start(
                    out=hvd[i, :, :], in_=hv[ib][:, :]))
                done.append(d)
            gq.wait(ctx.sem("stq0"))
            gq.wait(ctx.sem("stq1"))
            ctx.flush(f"{tag}_g1")

        with ctx.scope():
            ldx = [ctx.sem("ldx0"), ctx.sem("ldx1")]
            ldt = [ctx.sem("ldt0"), ctx.sem("ldt1")]
            sto = [ctx.sem("sto0"), ctx.sem("sto1")]
            ldv = ctx.sem("ldv0")
            gldw = ctx.sem("gldw")
            Wob = ctx.sb(f"{tag}_Wob", [128, 8, 1024], BF16)
            for j in range(8):
                gq.inc(gldw, 16, lambda e, j=j: e.dma_start(out=Wob[:, j, :], in_=Wo[j * 128:(j + 1) * 128, :]))
            S = [ctx.sb(f"{tag}_S{h}", [128, 128], F32) for h in range(8)]
            Sb = [ctx.sb(f"{tag}_Sb{h}", [128, 128], BF16) for h in range(8)]
            hva = ctx.sb(f"{tag}_hva", [128, NT, 2, 8], F32)
            benda = ctx.sb(f"{tag}_benda", [128, 64, 8], F32)
            qkTt = [ctx.sb(f"{tag}_qkTt{i}", [128, 2048], BF16) for i in range(2)]
            kvt = [ctx.sb(f"{tag}_kvt{i}", [128, 2048], BF16) for i in range(2)]
            msk = ctx.sb(f"{tag}_msk", [128, 64], BF16)
            zeros64 = ctx.sb(f"{tag}_z64", [128, 64], BF16)
            otile = [ctx.sb(f"{tag}_oti{i}", [128, 1024], BF16) for i in range(2)]
            oT = [ctx.sb(f"{tag}_oT{i}", [128, 1024], BF16) for i in range(2)]
            xb = [ctx.sb(f"{tag}_gx{i}", [128, 1024], F32) for i in range(2)]
            ot = [ctx.sb(f"{tag}_got{i}", [128, 1024], F32) for i in range(2)]
            psO = ctx.ps(f"{tag}_psO", [128, 128], F32)
            psSc = ctx.ps(f"{tag}_psSc", [128, 64], F32)
            psKV = ctx.ps(f"{tag}_psKV", [128, 128], F32)
            psTo = ctx.ps(f"{tag}_psTo", [128, 1024], F32)
            psO2 = ctx.ps(f"{tag}_psO2", [128, 1024], F32)
            for h in range(8):
                vq(lambda e, h=h: e.memset(S[h][:, :], 0.0))
                vq(lambda e, h=h: e.memset(Sb[h][:, :], 0.0))
            vq(lambda e: e.memset(zeros64[:, :], 0.0))
            sy.inc(ldv, 16, lambda e: e.dma_start(
                out=hva[:, :, :, :], in_=hvd.rearrange("n p c -> p n c").rearrange("p n (c j) -> p n c j", c=2)))
            vq.wait(ldv)
            hflat = hva.rearrange("p n c j -> p (n c) j")
            vq(lambda e: e.tensor_tensor(out=benda[:, 0:63, :], in0=hflat[:, 0:63, :],
                                         in1=hflat[:, 1:64, :], op=OP.add))
            vq(lambda e: e.tensor_copy(out=benda[:, 63:64, :], in_=hflat[:, 63:64, :]))
            vq.drain()
            vq.inc(ve, 1, lambda e: e.tensor_copy(out=benda[:, 0:1, 0:1], in_=benda[:, 0:1, 0:1]))
            aq.wait(ve)
            aq.inc(ac, 1, lambda e: e.activation(out=benda[:, :, :], in_=benda[:, :, :], func=AF.Exp))
            tq.wait(gldw)
            tq.wait(ac)
            done2 = []
            for i in range(NT):
                ib = i % 2
                d = {}
                if i >= 2:
                    sy.wait(pe, done2[i - 2]["used"])
                sy.inc(ldt[ib], 16, lambda e, i=i, ib=ib: e.dma_start(
                    out=qkTt[ib][:, 0:1024].rearrange("p (j c) -> p j c", c=128),
                    in_=qtTd[:, :, i * 128:(i + 1) * 128].rearrange("j p c -> p j c")))
                sy.inc(ldt[ib], 16, lambda e, i=i, ib=ib: e.dma_start(
                    out=qkTt[ib][:, 1024:2048].rearrange("p (j c) -> p j c", c=128),
                    in_=ktTd[:, :, i * 128:(i + 1) * 128].rearrange("j p c -> p j c")))
                sy.inc(ldt[ib], 16, lambda e, i=i, ib=ib: e.dma_start(
                    out=kvt[ib][:, 0:1024], in_=ktd[i * 128:(i + 1) * 128, :]))
                sy.inc(ldt[ib], 16, lambda e, i=i, ib=ib: e.dma_start(
                    out=kvt[ib][:, 1024:2048], in_=vtd[i * 128:(i + 1) * 128, :]))
                if i >= 2:
                    sy.wait(ve, done2[i - 2]["ep_ve"])
                sy.inc(ldx[ib], 16, lambda e, i=i, ib=ib: e.dma_start(
                    out=xb[ib][:, :], in_=x_in[i * 128:(i + 1) * 128, :]))
                tq.wait(ldt[ib])
                for c in range(2):
                    p0 = 64 * c
                    mchunk = 2 * i + c
                    for h in range(8):
                        if (c, h) != (0, 0):
                            tq.wait(ac)
                        tq(lambda e, ib=ib, c=c, h=h, p0=p0: e.matmul(
                            out=psO[p0:p0 + 64, :], lhsT=qkTt[ib][:, h * 128 + p0:h * 128 + p0 + 64],
                            rhs=Sb[h][:, :], start=True, stop=False))
                        tq(lambda e, ib=ib, c=c, h=h, p0=p0: e.matmul(
                            out=psSc[p0:p0 + 64, :], lhsT=qkTt[ib][:, 1024 + h * 128 + p0:1024 + h * 128 + p0 + 64],
                            rhs=qkTt[ib][:, h * 128 + p0:h * 128 + p0 + 64], start=True, stop=True))
                        tq.inc_prev(pe, 1)
                        vq.wait(pe)
                        vq.inc(ve, 1, lambda e, c=c, p0=p0: e.select(
                            out=msk[p0:p0 + 64, :], mask=ctx.utm_u8[p0:p0 + 64, :],
                            on_true=psSc[p0:p0 + 64, :], on_false=zeros64[p0:p0 + 64, :],
                            add_drain=True))
                        tq.wait(ve)
                        tq(lambda e, ib=ib, c=c, h=h, p0=p0: e.matmul(
                            out=psO[p0:p0 + 64, :], lhsT=msk[p0:p0 + 64, :],
                            rhs=kvt[ib][p0:p0 + 64, 1024 + h * 128:1024 + (h + 1) * 128],
                            start=False, stop=True))
                        tq(lambda e, ib=ib, c=c, h=h, p0=p0: e.matmul(
                            out=psKV[:, :], lhsT=kvt[ib][p0:p0 + 64, h * 128:(h + 1) * 128],
                            rhs=kvt[ib][p0:p0 + 64, 1024 + h * 128:1024 + (h + 1) * 128],
                            start=True, stop=True))
                        tq.inc_prev(pe, 1)
                        vq.wait(pe)
                        vq(lambda e, h=h: e.tensor_tensor(out=S[h][:, :], in0=S[h][:, :], in1=psKV[:, :], op=OP.add))
                        vq.drain()
                        vq.inc(ve, 1, lambda e, h=h, mchunk=mchunk: e.tensor_scalar(
                            out=S[h][:, :], in0=S[h][:, :], scalar1=benda[:, mchunk, h:h + 1],
                            scalar2=None, op0=OP.mult))
                        aq.wait(ve)
                        aq(lambda e, h=h: e.copy(out=Sb[h][:, :], in_=S[h][:, :]))
                        aq.drain()
                        aq.inc(ac, 1, lambda e, ib=ib, c=c, h=h, p0=p0: e.copy(
                            out=otile[ib][p0:p0 + 64, h * 128:(h + 1) * 128], in_=psO[p0:p0 + 64, :]))
                tq.wait(ac)
                if i >= 2:
                    tq.wait(ac, done2[i - 2]["oT_evac"])
                for j in range(8):
                    tq(lambda e, ib=ib, j=j: e.matmul(
                        out=psTo[:, j * 128:(j + 1) * 128], lhsT=otile[ib][:, j * 128:(j + 1) * 128],
                        rhs=ctx.iden_bf[:, :], start=True, stop=True))
                tq.inc_prev(pe, 1)
                d["used"] = pe.n
                aq.wait(pe)
                aq.inc(ac, 1, lambda e, ib=ib: e.copy(out=oT[ib][:, :], in_=psTo[:, :]))
                d["oT_evac"] = ac.n
                tq.wait(ac)
                if i >= 1:
                    tq.wait(ve, done2[i - 1]["ep_ve"])
                for half in range(2):
                    for j in range(8):
                        tq(lambda e, ib=ib, half=half, j=j: e.matmul(
                            out=psO2[:, half * 512:(half + 1) * 512], lhsT=oT[ib][:, j * 128:(j + 1) * 128],
                            rhs=Wob[:, j, half * 512:(half + 1) * 512], start=(j == 0), stop=(j == 7)))
                tq.inc_prev(pe, 1)
                vq.wait(pe)
                vq.wait(ldx[ib])
                if i >= 2:
                    vq.wait(sto[ib])
                vq.inc(ve, 1, lambda e, ib=ib: e.tensor_tensor(
                    out=ot[ib][:, :], in0=xb[ib][:, :], in1=psO2[:, :], op=OP.add))
                d["ep_ve"] = ve.n
                gq.wait(ve)
                gq.inc(sto[ib], 16, lambda e, i=i, ib=ib: e.dma_start(
                    out=x_out[i * 128:(i + 1) * 128, :], in_=ot[ib][:, :]))
                done2.append(d)
            gq.wait(ctx.sem("sto0"))
            gq.wait(ctx.sem("sto1"))
            ctx.flush(f"{tag}_g2")


def build_gla_launch():
    nc = make_nc()
    stack = contextlib.ExitStack()
    ctx = Ctx(nc, stack)
    with stack:
        declare_consts(ctx)
        x_in = ctx.din("x_in", (T, D), F32)
        x_out = ctx.dout("x_out", (T, D), F32)
        for x in ("q", "k", "v", "g", "o"):
            ctx.din(f"L1_attn_a_W{x}", (1024, 1024), F32)
        emit_load_consts(ctx)
        emit_gla(ctx, x_in, x_out)
        ctx.flush("fin")
    return nc


def gla_inmaps(inp, xs):
    c = host_constants()
    base = {**c}
    for x in ("q", "k", "v", "g", "o"):
        base[f"L1_attn_a_W{x}"] = np.ascontiguousarray(inp[f"L1_attn_a_W{x}"])
    return [{**base, "x_in": np.ascontiguousarray(xs[i])} for i in range(NCORES)]


def run_gla(inp, xs):
    nc = build_gla_launch()
    res = run_bass_kernel_spmd(nc, gla_inmaps(inp, xs), core_ids=list(range(NCORES)))
    return np.stack([r["x_out"] for r in res.results])



def emit_lra(ctx, x_in, x_out, tag="lra"):
    nc = ctx.nc
    ve, ac, pe = ctx.sem("ve"), ctx.sem("ac"), ctx.sem("pe")
    sy, vq, tq, aq, gq = ctx.sy, ctx.vq, ctx.tq, ctx.aq, ctx.gq
    Qt, Kt = ctx.dram["lra_Qt"], ctx.dram["lra_Kt"]
    VO = ctx.dram["L1_attn_b_VO"]
    xqTd = ctx.dtmp(f"{tag}_xqTd", (8, 128, T), BF16)
    xkTd = ctx.dtmp(f"{tag}_xkTd", (8, 128, T), BF16)
    xhTd = ctx.dtmp(f"{tag}_xhTd", (8, 128, T), BF16)
    PTds = [ctx.dtmp(f"{tag}_PTd{i}", (4, T, T), BF16) for i in range(4)]
    PTd_of = lambda h: PTds[h // 4][h % 4]

    with ctx.scope():
        A1 = ctx.sb(f"{tag}_A1", [128, NT], F32)
        B1 = ctx.sb(f"{tag}_B1", [128, NT], F32)
        emit_ln_stats(ctx, x_in, A1, B1, name=f"{tag}_s1")

        with ctx.scope():
            QKb = ctx.sb(f"{tag}_QKb", [128, 8, 2048], BF16)
            gldw = ctx.sem("gldw")
            for m, W in enumerate((Qt, Kt)):
                for j in range(8):
                    gq.inc(gldw, 16, lambda e, m=m, j=j, W=W: e.dma_start(
                        out=QKb[:, j, m * 1024:(m + 1) * 1024], in_=W[j * 128:(j + 1) * 128, :]))
            gq.wait(gldw)
            ctx.flush(f"{tag}_wload")

            tn = TileNorm(ctx, f"{tag}_m", x_in, A1, B1)
            stq = [ctx.sem("stq0"), ctx.sem("stq1")]
            qko = [ctx.sb(f"{tag}_qko{i}", [128, 2048], BF16) for i in range(2)]
            psQ = ctx.ps(f"{tag}_psQ", [128, 2048], F32)
            ac_done = []
            for i in range(NT):
                ib = i % 2
                tn.record(i)
                tq.wait(ac)
                for m in range(2):
                    for blk in range(8):
                        for j in range(8):
                            tq(lambda e, i=i, m=m, blk=blk, j=j: e.matmul(
                                out=psQ[:, m * 1024 + blk * 128:m * 1024 + (blk + 1) * 128],
                                lhsT=QKb[:, j, m * 1024 + blk * 128:m * 1024 + (blk + 1) * 128],
                                rhs=tn.cur(i)[:, j * 128:(j + 1) * 128],
                                start=(j == 0), stop=(j == 7)))
                tq.inc_prev(pe, 1)
                aq.wait(pe)
                if i >= 2:
                    aq.wait(stq[ib])
                aq.inc(ac, 1, lambda e, ib=ib: e.copy(out=qko[ib][:, :], in_=psQ[:, :]))
                ac_done.append(ac.n)
                gq.wait(ac)
                gq.inc(stq[ib], 16, lambda e, i=i, ib=ib: e.dma_start(
                    out=xqTd[:, :, i * 128:(i + 1) * 128].rearrange("j p c -> p j c"),
                    in_=qko[ib][:, 0:1024].rearrange("p (j c) -> p j c", c=128)))
                gq.inc(stq[ib], 16, lambda e, i=i, ib=ib: e.dma_start(
                    out=xkTd[:, :, i * 128:(i + 1) * 128].rearrange("j p c -> p j c"),
                    in_=qko[ib][:, 1024:2048].rearrange("p (j c) -> p j c", c=128)))
                gq.inc(stq[ib], 16, lambda e, i=i, ib=ib: e.dma_start(
                    out=xhTd[:, :, i * 128:(i + 1) * 128].rearrange("j p c -> p j c"),
                    in_=tn.cur(i).rearrange("p (j c) -> p j c", c=128)))
                tn.note_consumed(ctx.sems["stq0" if ib == 0 else "stq1"], ctx.sems["stq0" if ib == 0 else "stq1"].n)
            gq.wait(ctx.sem("stq0"))
            gq.wait(ctx.sem("stq1"))
            ctx.flush(f"{tag}_b1")

        with ctx.scope():
            ldx = [ctx.sem("ldx0"), ctx.sem("ldx1")]
            ldt = [ctx.sem("ldt0"), ctx.sem("ldt1")]
            ldk = [ctx.sem("ldk0"), ctx.sem("ldk1")]
            ldp = [ctx.sem("ldp0"), ctx.sem("ldp1")]
            ldv = ctx.sem("ldv0")
            sto = [ctx.sem("sto0"), ctx.sem("sto1")]
            stq = [ctx.sem("stq0"), ctx.sem("stq1")]
            gldw = ctx.sem("gldw")
            acc = ctx.sb(f"{tag}_acc", [128, NT, 512], F32)
            Yres = ctx.sb(f"{tag}_Yres", [128, NT, 512], BF16)
            xqh = ctx.sb(f"{tag}_xqh", [64, T], BF16)
            xkh = ctx.sb(f"{tag}_xkh", [64, T], BF16)
            VOb = [ctx.sb(f"{tag}_VOb{i}", [128, 8, 512], BF16) for i in range(2)]
            xhb = [ctx.sb(f"{tag}_xhb{i}", [128, 8, 128], BF16) for i in range(2)]
            PT = [ctx.sb(f"{tag}_PT{i}", [128, 512], BF16) for i in range(2)]
            dnrow = ctx.sb(f"{tag}_dnrow", [1, 512], F32)
            rden = ctx.sb(f"{tag}_rden", [128, 16, 8, 4], F32)
            xb = [ctx.sb(f"{tag}_lx{i}", [128, 512], F32) for i in range(2)]
            ot = [ctx.sb(f"{tag}_lot{i}", [128, 512], F32) for i in range(2)]
            psSY = [ctx.ps(f"{tag}_psSY{i}", [128, 512], F32) for i in range(2)]
            psO = [ctx.ps(f"{tag}_psO{i}", [128, 512], F32) for i in range(4)]
            psDen = ctx.ps(f"{tag}_psDen", [1, 512], F32)
            psDen2 = ctx.ps(f"{tag}_psDen2", [128, 4], F32)
            for eh in range(2):
                if eh == 1:
                    sy.wait(stq[0])
                    sy.wait(stq[1])
                vq.drain()
                vq(lambda e: e.memset(acc[:, :, :], 0.0))
                vq.drain()
                vq.inc(ve, 1, lambda e: e.tensor_copy(out=acc[:, 0:1, 0:1], in_=acc[:, 0:1, 0:1]))
                acc_zero = ve.n
                tq.wait(ve, acc_zero)
                for h in range(16):
                    hb = h % 2
                    sy.wait(pe)
                    sy.inc(ldk[0], 16, lambda e, h=h: e.dma_start(
                        out=xqh[:, :], in_=xqTd[h // 2, (h % 2) * 64:(h % 2) * 64 + 64, :]))
                    sy.inc(ldk[0], 16, lambda e, h=h: e.dma_start(
                        out=xkh[:, :], in_=xkTd[h // 2, (h % 2) * 64:(h % 2) * 64 + 64, :]))
                    gq.wait(pe)
                    for j in range(8):
                        gq.inc(gldw, 16, lambda e, eh=eh, h=h, hb=hb, j=j: e.dma_start(
                            out=VOb[hb][:, j, :], in_=VO[h, j * 128:(j + 1) * 128, eh * 512:(eh + 1) * 512]))
                    tq.wait(gldw)
                    ac_y = []
                    for u in range(NT):
                        ub = u % 2
                        if u >= 2:
                            sy.wait(pe)
                        sy.inc(ldt[ub], 16, lambda e, u=u, ub=ub: e.dma_start(
                            out=xhb[ub][:, :, :],
                            in_=xhTd[:, :, u * 128:(u + 1) * 128].rearrange("j p c -> p j c")))
                        tq.wait(ldt[ub])
                        if u >= 2:
                            tq.wait(ac, ac_y[u - 2])
                        for j in range(8):
                            tq(lambda e, ub=ub, j=j, hb=hb: e.matmul(
                                out=psSY[ub][:, :], lhsT=xhb[ub][:, j, :], rhs=VOb[hb][:, j, :],
                                start=(j == 0), stop=(j == 7)))
                        tq.inc_prev(pe, 1)
                        aq.wait(pe)
                        aq.inc(ac, 1, lambda e, u=u, ub=ub: e.copy(out=Yres[:, u, :], in_=psSY[ub][:, :]))
                        ac_y.append(ac.n)
                    tq.wait(ac)
                    tq.wait(ctx.sems["ldk0"])
                    pt_pe = []
                    for tg in range(8):
                        for u in range(NT):
                            ub = u % 2
                            if eh == 0:
                                if (tg, u) != (0, 0):
                                    tq.wait(ac)
                                tq(lambda e, ub=ub, u=u, tg=tg: e.matmul(
                                    out=psSY[ub][:, :], lhsT=xkh[:, u * 128:(u + 1) * 128],
                                    rhs=xqh[:, tg * 512:(tg + 1) * 512], start=True, stop=True))
                                tq.inc_prev(pe, 1)
                                aq.wait(pe)
                                aq.wait(stq[ub])
                                aq.inc(ac, 1, lambda e, ub=ub: e.activation(
                                    out=PT[ub][:, :], in_=psSY[ub][:, :], func=AF.Exp,
                                    scale=1.0 / 32.0, bias=-EXP_SHIFT))
                                gq.wait(ac)
                                gq.inc(stq[ub], 16, lambda e, h=h, u=u, tg=tg, ub=ub: e.dma_start(
                                    out=PTd_of(h)[u * 128:(u + 1) * 128, tg * 512:(tg + 1) * 512],
                                    in_=PT[ub][:, :]))
                                tq.wait(ac)
                            else:
                                if len(pt_pe) >= 2:
                                    sy.wait(pe, pt_pe[-2])
                                sy.inc(ldp[ub], 16, lambda e, h=h, u=u, tg=tg, ub=ub: e.dma_start(
                                    out=PT[ub][:, :], in_=PTd_of(h)[u * 128:(u + 1) * 128, tg * 512:(tg + 1) * 512]))
                                tq.wait(ldp[ub])
                            for ts in range(4):
                                tq(lambda e, ub=ub, ts=ts, hb=hb, u=u, h=h, tg=tg: e.matmul(
                                    out=psO[ts][:, :], lhsT=PT[ub][:, ts * 128:(ts + 1) * 128],
                                    rhs=Yres[:, u, :], start=(u == 0), stop=(u == NT - 1)))
                            if eh == 0:
                                tq(lambda e, ub=ub, u=u: e.matmul(
                                    out=psDen[:, :], lhsT=ctx.onescol_bf[:, 0:1], rhs=PT[ub][:, :],
                                    start=(u == 0), stop=(u == NT - 1)))
                            tq.inc_prev(pe, 1)
                            pt_pe.append(pe.n)
                        if eh == 0:
                            aq.wait(pe)
                            aq.inc(ac, 1, lambda e: e.copy(out=dnrow[:, :], in_=psDen[:, :]))
                            tq.wait(ac)
                            for ts in range(4):
                                tq(lambda e, ts=ts: e.transpose(out=psDen2[:, ts:ts + 1],
                                                                in_=dnrow[:, ts * 128:(ts + 1) * 128],
                                                                identity=ctx.iden_f32[0:1, 0:1]))
                            tq.inc_prev(pe, 1)
                            vq.wait(pe)
                            vq.inc(ve, 1, lambda e, h=h, tg=tg: e.reciprocal(
                                out=rden[:, h, tg, :], in_=psDen2[:, :]))
                        vq.wait(pe)
                        if eh == 0:
                            vq.wait(ve)
                        for ts in range(4):
                            vq(lambda e, h=h, tg=tg, ts=ts: e.scalar_tensor_tensor(
                                out=acc[:, tg * 4 + ts, :], in0=psO[ts][:, :],
                                scalar=rden[:, h, tg, ts:ts + 1], op0=OP.mult,
                                op1=OP.add, in1=acc[:, tg * 4 + ts, :]))
                        vq.drain()
                        vq.inc(ve, 1, lambda e: e.tensor_copy(out=acc[:, 0:1, 0:1], in_=acc[:, 0:1, 0:1]))
                        tq.wait(ve)
                vq.drain()
                for i in range(NT):
                    ib = i % 2
                    if i >= 2:
                        sy.wait(ve)
                    sy.inc(ldx[ib], 16, lambda e, i=i, ib=ib, eh=eh: e.dma_start(
                        out=xb[ib][:, :], in_=x_in[i * 128:(i + 1) * 128, eh * 512:(eh + 1) * 512]))
                    vq.wait(ldx[ib])
                    vq.wait(sto[ib])
                    vq.inc(ve, 1, lambda e, i=i, ib=ib: e.tensor_tensor(
                        out=ot[ib][:, :], in0=xb[ib][:, :], in1=acc[:, i, :], op=OP.add))
                    gq.wait(ve)
                    gq.inc(sto[ib], 16, lambda e, i=i, ib=ib, eh=eh: e.dma_start(
                        out=x_out[i * 128:(i + 1) * 128, eh * 512:(eh + 1) * 512], in_=ot[ib][:, :]))
            gq.wait(ctx.sem("sto0"))
            gq.wait(ctx.sem("sto1"))
            ctx.flush(f"{tag}_b2")


def build_lra_launch():
    nc = make_nc()
    stack = contextlib.ExitStack()
    ctx = Ctx(nc, stack)
    with stack:
        declare_consts(ctx)
        x_in = ctx.din("x_in", (T, D), F32)
        x_out = ctx.dout("x_out", (T, D), F32)
        ctx.din("lra_Qt", (1024, 1024), F32)
        ctx.din("lra_Kt", (1024, 1024), F32)
        ctx.din("L1_attn_b_VO", (16, 1024, 1024), F32)
        emit_load_consts(ctx)
        emit_lra(ctx, x_in, x_out)
        ctx.flush("fin")
    return nc


def lra_inmaps(inp, xs):
    c = host_constants()
    Q = inp["L1_attn_b_Q"]
    K = inp["L1_attn_b_K"]
    base = {
        **c,
        "lra_Qt": np.ascontiguousarray(Q.transpose(2, 0, 1).reshape(1024, 1024)),
        "lra_Kt": np.ascontiguousarray(K.transpose(2, 0, 1).reshape(1024, 1024)),
        "L1_attn_b_VO": np.ascontiguousarray(inp["L1_attn_b_VO"]),
    }
    return [{**base, "x_in": np.ascontiguousarray(xs[i])} for i in range(NCORES)]


def run_lra(inp, xs):
    nc = build_lra_launch()
    res = run_bass_kernel_spmd(nc, lra_inmaps(inp, xs), core_ids=list(range(NCORES)))
    return np.stack([r["x_out"] for r in res.results])



def emit_moe_pre(ctx, x_in, xh_out, route_out, cnt_out, tag):
    nc = ctx.nc
    ve, ac, pe = ctx.sem("ve"), ctx.sem("ac"), ctx.sem("pe")
    ldc = ctx.sem("ldc")
    sy, vq, tq, aq, gq = ctx.sy, ctx.vq, ctx.tq, ctx.aq, ctx.gq
    wgT = ctx.dram["moe_wgT"]

    with ctx.scope():
        A1 = ctx.sb(f"{tag}_A1", [128, NT], F32)
        B1 = ctx.sb(f"{tag}_B1", [128, NT], F32)
        emit_ln_stats(ctx, x_in, A1, B1, name=f"{tag}_s1")

        with ctx.scope():
            wg = ctx.sb(f"{tag}_wg", [128, 8, 8], F32)
            runrow = ctx.sb(f"{tag}_run", [1, 16], F32)
            sy.inc(ldc, 16, lambda e: e.dma_start(out=wg[:, :, :],
                                                  in_=wgT.rearrange("(j p) c -> p j c", p=128)))
            vq(lambda e: e.memset(runrow[:, :], 0.0))
            vq.drain()
            vq.inc(ve, 1, lambda e: e.tensor_copy(out=runrow[:, 0:1], in_=runrow[:, 0:1]))
            ctx.flush(f"{tag}_prep")

            tn = TileNorm(ctx, f"{tag}_m", x_in, A1, B1)
            stq = [ctx.sem("stq0"), ctx.sem("stq1")]
            zf = [ctx.sb(f"{tag}_zf{i}", [128, 1024], F32) for i in range(2)]
            xTf = ctx.sb(f"{tag}_xTf", [128, 1024], F32)
            lg = ctx.sb(f"{tag}_lg", [128, 8], F32)
            w1 = ctx.sb(f"{tag}_w1", [128, 8], F32)
            w2 = ctx.sb(f"{tag}_w2", [128, 8], F32)
            w3 = ctx.sb(f"{tag}_w3", [128, 8], F32)
            oh = ctx.sb(f"{tag}_oh", [128, 16], F32)
            mcol = ctx.sb(f"{tag}_mcol", [128, 8], F32)
            rt = [ctx.sb(f"{tag}_rt{i}", [128, 8], F32) for i in range(2)]
            psL = ctx.ps(f"{tag}_psL", [128, 8], F32)
            psR = ctx.ps(f"{tag}_psR", [128, 16], F32)
            psC = ctx.ps(f"{tag}_psC", [1, 16], F32)
            tq.wait(ldc)
            done = []
            for i in range(NT):
                ib = i % 2
                d = {}
                tn.record(i)
                tn.note_consumed(ctx.sem("ac"), 0)
                vq.wait(ctx.sems["ldx0" if ib == 0 else "ldx1"])
                if i >= 2:
                    vq.wait(pe, done[i - 2]["ztr"])
                vq.inc(ve, 1, lambda e, i=i, ib=ib: e.tensor_scalar(
                    out=zf[ib][:, :], in0=tn.xb[ib][:, :], scalar1=B1[:, i:i + 1],
                    scalar2=A1[:, i:i + 1], op0=OP.mult, op1=OP.subtract))
                tn.ve_done[i] = ve.n
                tq.wait(ve)
                tq.wait(ac)
                for j in range(8):
                    tq(lambda e, ib=ib, j=j: e.matmul(
                        out=tn.pTf[:, j * 128:(j + 1) * 128], lhsT=zf[ib][:, j * 128:(j + 1) * 128],
                        rhs=ctx.iden_f32[:, :], start=True, stop=True))
                tq.inc_prev(pe, 1)
                d["ztr"] = pe.n
                aq.wait(pe)
                aq.inc(ac, 1, lambda e: e.copy(out=xTf[:, :], in_=tn.pTf[:, :]))
                d["xTf"] = ac.n
                tn.ac_done[i] = ac.n
                tq.wait(ac)
                tq.wait(ve)
                for j in range(8):
                    tq(lambda e, j=j: e.matmul(out=psL[:, :], lhsT=xTf[:, j * 128:(j + 1) * 128],
                                               rhs=wg[:, j, :], start=(j == 0), stop=(j == 7)))
                tq.inc_prev(pe, 1)
                vq.wait(pe)
                vq(lambda e: e.tensor_reduce(out=mcol[:, 0:1], in_=psL[:, :], axis=mybir.AxisListType.X, op=OP.max))
                vq.drain()
                vq(lambda e: e.tensor_copy(out=lg[:, :], in_=psL[:, :]))
                vq.drain()
                if i >= 1:
                    vq.wait(pe, done[i - 1]["rkmm"])
                vq.wait(ctx.sems["stq0" if ib == 0 else "stq1"])
                vq(lambda e: e.tensor_scalar(out=w1[:, :], in0=lg[:, :], scalar1=mcol[:, 0:1],
                                             scalar2=None, op0=OP.is_ge))
                vq.drain()
                vq(lambda e: e.tensor_copy(out=w2[:, :], in_=w1[:, :]))
                vq.drain()
                vq(lambda e: e.tensor_tensor(out=w2[:, 1:8], in0=w2[:, 1:8], in1=w1[:, 0:7], op=OP.add))
                vq.drain()
                vq(lambda e: e.tensor_copy(out=w3[:, :], in_=w2[:, :]))
                vq.drain()
                vq(lambda e: e.tensor_tensor(out=w3[:, 2:8], in0=w3[:, 2:8], in1=w2[:, 0:6], op=OP.add))
                vq.drain()
                vq(lambda e: e.tensor_copy(out=w2[:, :], in_=w3[:, :]))
                vq.drain()
                vq(lambda e: e.tensor_tensor(out=w2[:, 4:8], in0=w2[:, 4:8], in1=w3[:, 0:4], op=OP.add))
                vq.drain()
                vq(lambda e: e.tensor_scalar(out=w2[:, :], in0=w2[:, :], scalar1=1.0, scalar2=None, op0=OP.is_le))
                vq.drain()
                vq(lambda e: e.tensor_tensor(out=oh[:, 0:8], in0=w1[:, :], in1=w2[:, :], op=OP.mult))
                vq.drain()
                vq(lambda e: e.tensor_scalar(out=w1[:, :], in0=oh[:, 0:8], scalar1=-1e30, scalar2=None, op0=OP.mult))
                vq.drain()
                vq(lambda e: e.tensor_tensor(out=w1[:, :], in0=w1[:, :], in1=lg[:, :], op=OP.add))
                vq.drain()
                vq(lambda e: e.tensor_reduce(out=mcol[:, 1:2], in_=w1[:, :], axis=mybir.AxisListType.X, op=OP.max))
                vq.drain()
                vq(lambda e: e.tensor_scalar(out=w2[:, :], in0=w1[:, :], scalar1=mcol[:, 1:2],
                                             scalar2=None, op0=OP.is_ge))
                vq.drain()
                vq(lambda e: e.tensor_copy(out=w3[:, :], in_=w2[:, :]))
                vq.drain()
                vq(lambda e: e.tensor_tensor(out=w3[:, 1:8], in0=w3[:, 1:8], in1=w2[:, 0:7], op=OP.add))
                vq.drain()
                vq(lambda e: e.tensor_copy(out=w1[:, :], in_=w3[:, :]))
                vq.drain()
                vq(lambda e: e.tensor_tensor(out=w1[:, 2:8], in0=w1[:, 2:8], in1=w3[:, 0:6], op=OP.add))
                vq.drain()
                vq(lambda e: e.tensor_copy(out=w3[:, :], in_=w1[:, :]))
                vq.drain()
                vq(lambda e: e.tensor_tensor(out=w3[:, 4:8], in0=w3[:, 4:8], in1=w1[:, 0:4], op=OP.add))
                vq.drain()
                vq(lambda e: e.tensor_scalar(out=w3[:, :], in0=w3[:, :], scalar1=1.0, scalar2=None, op0=OP.is_le))
                vq.drain()
                vq.inc(ve, 1, lambda e: e.tensor_tensor(out=oh[:, 8:16], in0=w2[:, :], in1=w3[:, :], op=OP.mult))
                d["oh"] = ve.n
                vq(lambda e: e.tensor_tensor(out=w1[:, :], in0=oh[:, 0:8], in1=ctx.iota8[:, :], op=OP.mult))
                vq.drain()
                vq(lambda e: e.tensor_reduce(out=mcol[:, 4:5], in_=w1[:, :], axis=mybir.AxisListType.X, op=OP.add))
                vq.drain()
                vq(lambda e: e.tensor_tensor(out=w1[:, :], in0=oh[:, 8:16], in1=ctx.iota8[:, :], op=OP.mult))
                vq.drain()
                vq(lambda e: e.tensor_reduce(out=mcol[:, 5:6], in_=w1[:, :], axis=mybir.AxisListType.X, op=OP.add))
                vq.drain()
                vq.inc(ve, 1, lambda e: e.tensor_scalar(out=mcol[:, 6:7], in0=mcol[:, 0:1],
                                                        scalar1=mcol[:, 1:2], scalar2=None, op0=OP.subtract))
                aq.wait(ve)
                aq.inc(ac, 1, lambda e: e.activation(out=mcol[:, 2:3], in_=mcol[:, 6:7], func=AF.Sigmoid))
                vq.wait(ac)
                vq.inc(ve, 1, lambda e: e.tensor_scalar(out=mcol[:, 3:4], in0=mcol[:, 2:3], scalar1=1.0,
                                                        scalar2=-1.0, op0=OP.subtract, op1=OP.mult))
                tq.wait(ve, d["oh"])
                if i >= 1:
                    tq.wait(ve)
                tq(lambda e: e.matmul(out=psR[:, 0:8], lhsT=ctx.ltx[:, :], rhs=oh[:, 0:8], start=True, stop=False))
                tq(lambda e: e.matmul(out=psR[:, 8:16], lhsT=ctx.ltx[:, :], rhs=oh[:, 8:16], start=False, stop=False))
                tq(lambda e: e.matmul(out=psR[:, 0:16], lhsT=ctx.ones_row[0:1, 0:128], rhs=runrow[:, :],
                                      start=False, stop=True))
                tq.inc_prev(pe, 1)
                tq(lambda e: e.matmul(out=psC[:, 0:8], lhsT=ctx.onescol_f32[:, :], rhs=oh[:, 0:8],
                                      start=True, stop=False))
                tq(lambda e: e.matmul(out=psC[:, 8:16], lhsT=ctx.onescol_f32[:, :], rhs=oh[:, 8:16],
                                      start=False, stop=True))
                tq.inc_prev(pe, 1)
                d["rkmm"] = pe.n
                vq.wait(pe)
                vq(lambda e: e.tensor_tensor(out=w1[:, :], in0=oh[:, 0:8], in1=psR[:, 0:8], op=OP.mult))
                vq.drain()
                vq(lambda e, ib=ib: e.tensor_reduce(out=rt[ib][:, 4:5], in_=w1[:, :], axis=mybir.AxisListType.X, op=OP.add))
                vq.drain()
                vq(lambda e: e.tensor_tensor(out=w1[:, :], in0=oh[:, 8:16], in1=psR[:, 8:16], op=OP.mult))
                vq.drain()
                vq(lambda e, ib=ib: e.tensor_reduce(out=rt[ib][:, 5:6], in_=w1[:, :], axis=mybir.AxisListType.X, op=OP.add))
                vq.drain()
                vq(lambda e: e.tensor_tensor(out=runrow[:, :], in0=runrow[:, :], in1=psC[:, :], op=OP.add))
                vq.drain()
                vq(lambda e, ib=ib: e.tensor_copy(out=rt[ib][:, 0:1], in_=mcol[:, 4:5]))
                vq(lambda e, ib=ib: e.tensor_copy(out=rt[ib][:, 1:2], in_=mcol[:, 5:6]))
                vq(lambda e, ib=ib: e.tensor_copy(out=rt[ib][:, 2:3], in_=mcol[:, 2:3]))
                vq(lambda e, ib=ib: e.tensor_copy(out=rt[ib][:, 3:4], in_=mcol[:, 3:4]))
                vq(lambda e, ib=ib: e.memset(rt[ib][:, 6:8], 0.0))
                vq.drain()
                vq.inc(ve, 1, lambda e, ib=ib: e.tensor_copy(out=rt[ib][:, 0:1], in_=rt[ib][:, 0:1]))
                d["rt"] = ve.n
                gq.wait(ve)
                if i >= 2:
                    gq.wait(stq[ib])
                gq.inc(stq[ib], 16, lambda e, i=i, ib=ib: e.dma_start(
                    out=xh_out[i * 128:(i + 1) * 128, :], in_=tn.zh[ib][:, :]))
                gq.inc(stq[ib], 16, lambda e, i=i, ib=ib: e.dma_start(
                    out=route_out[i * 128:(i + 1) * 128, :], in_=rt[ib][:, :]))
                tn.zh_consumed[i] = (ctx.sems["stq0" if ib == 0 else "stq1"],
                                     ctx.sems["stq0" if ib == 0 else "stq1"].n)
                done.append(d)
            gq.wait(ve)
            gq.inc(ctx.sem("sto0"), 16, lambda e: e.dma_start(out=cnt_out[:, :], in_=runrow[:, :]))
            gq.wait(ctx.sem("sto0"))
            gq.wait(ctx.sem("stq0"))
            gq.wait(ctx.sem("stq1"))
            ctx.flush(f"{tag}_main")


def build_moepre_launch():
    nc = make_nc()
    stack = contextlib.ExitStack()
    ctx = Ctx(nc, stack)
    with stack:
        declare_consts(ctx)
        x_in = ctx.din("x_in", (T, D), F32)
        ctx.din("moe_wgT", (1024, 8), F32)
        xh_out = ctx.dout("xh_out", (T, D), BF16)
        route_out = ctx.dout("route_out", (T, 8), F32)
        cnt_out = ctx.dout("cnt_out", (1, 16), F32)
        emit_load_consts(ctx)
        emit_moe_pre(ctx, x_in, xh_out, route_out, cnt_out, "mp")
        ctx.flush("fin")
    return nc


def moepre_inmaps(inp, xs):
    c = host_constants()
    base = {**c, "moe_wgT": np.ascontiguousarray(inp["L1_ffn_w_g"].T)}
    return [{**base, "x_in": np.ascontiguousarray(xs[i])} for i in range(NCORES)]



NGRAN = 15
GRAN = 1024


def emit_moe_main(ctx, x_in, x_out, xh_in, route_in, tag):
    nc = ctx.nc
    ve, ac, pe = ctx.sem("ve"), ctx.sem("ac"), ctx.sem("pe")
    ldc = ctx.sem("ldc")
    sy, vq, tq, aq, gq = ctx.sy, ctx.vq, ctx.tq, ctx.aq, ctx.gq
    allcnt = ctx.dram["moe_allcnt"]
    mycnt = ctx.dram["moe_mycnt"]
    mprev = ctx.dram["moe_mprev"]
    c_jp8 = ctx.dram["c_jp8"]
    c_jp32 = ctx.dram["c_jp32"]
    c_i128 = ctx.dram["c_i128"]
    cfc = ctx.dram["L1_ffn_c_fc"]
    cpj = ctx.dram["L1_ffn_c_proj"]
    cfcv = cfc.rearrange("e d f -> (e d) f")
    cpjv = cpj.rearrange("e f d -> (e f) d")
    xpkd = ctx.dtmp(f"{tag}_xpkd", (NGRAN * GRAN + 2 * T, 1024), BF16)
    opk = ctx.dtmp(f"{tag}_opk", (NGRAN * GRAN, 1024), BF16)
    dstd = ctx.dtmp(f"{tag}_dstd", (T, 2), U32)
    pmd = ctx.dtmp(f"{tag}_pmd", (T, 2), F32)

    with ctx.scope():
        srowb = ctx.sb(f"{tag}_srowb", [128, 48], F32)
        ebc = ctx.sb(f"{tag}_ebc", [128, 15], F32)
        idxfc = ctx.sb(f"{tag}_idxfc", [128, NGRAN, 8], U32)
        idxpj = ctx.sb(f"{tag}_idxpj", [128, NGRAN, 32], U32)
        zz = ctx.sb(f"{tag}_zz", [128, 1024], BF16)
        pfx = ctx.sem("pfx")
        with ctx.scope():
            ac8 = ctx.sb(f"{tag}_ac8", [8, 16], F32)
            mc1 = ctx.sb(f"{tag}_mc1", [1, 16], F32)
            mpv = ctx.sb(f"{tag}_mpv", [8, 1], F32)
            srow = ctx.sb(f"{tag}_srow", [1, 64], F32)
            w8 = ctx.sb(f"{tag}_w8", [1, 8], F32)
            w8b = ctx.sb(f"{tag}_w8b", [1, 8], F32)
            expr = ctx.sb(f"{tag}_expr", [1, 128], F32)
            ecol = ctx.sb(f"{tag}_ecol", [128, 1], F32)
            jp8 = ctx.sb(f"{tag}_jp8", [128, 8], F32)
            jp32 = ctx.sb(f"{tag}_jp32", [128, 32], F32)
            psS = ctx.ps(f"{tag}_psS", [1, 32], F32)
            psB = ctx.ps(f"{tag}_psB", [128, 48], F32)
            psE = ctx.ps(f"{tag}_psE", [128, 15], F32)
            psT1 = ctx.ps(f"{tag}_psT1", [128, 128], F32)
            sy.inc(ldc, 16, lambda e: e.dma_start(out=ac8[:, :], in_=allcnt[:, :]))
            sy.inc(ldc, 16, lambda e: e.dma_start(out=mc1[:, :], in_=mycnt[:, :]))
            sy.inc(ldc, 16, lambda e: e.dma_start(out=mpv[:, :], in_=mprev[:, :]))
            sy.inc(ldc, 16, lambda e: e.dma_start(out=jp8[:, :], in_=c_jp8[:, :]))
            sy.inc(ldc, 16, lambda e: e.dma_start(out=jp32[:, :], in_=c_jp32[:, :]))
            vq(lambda e: e.memset(zz[:, :], 0.0))
            vq.drain()
            vq.inc(ve, 1, lambda e: e.tensor_copy(out=zz[:, 0:1], in_=zz[:, 0:1]))
            gq.wait(ve)
            for z in range(NGRAN * GRAN // 128):
                gq.inc(pfx, 16, lambda e, z=z: e.dma_start(
                    out=xpkd[z * 128:(z + 1) * 128, :], in_=zz[:, :]))
            tq.wait(ldc)
            tq(lambda e: e.matmul(out=psS[:, 0:16], lhsT=mpv[:, :], rhs=ac8[:, :], start=True, stop=False))
            tq(lambda e: e.matmul(out=psS[:, 16:32], lhsT=ctx.onescol_f32[0:8, :], rhs=ac8[:, :],
                                  start=False, stop=True))
            tq.inc_prev(pe, 1)
            vq.wait(pe)
            vq(lambda e: e.memset(srow[:, 40:48], 0.0))
            vq(lambda e: e.tensor_copy(out=srow[:, 0:16], in_=psS[:, 0:16]))
            vq.drain()
            vq(lambda e: e.tensor_tensor(out=srow[:, 8:16], in0=srow[:, 8:16], in1=psS[:, 16:24], op=OP.add))
            vq.drain()
            vq(lambda e: e.tensor_scalar(out=srow[:, 16:32], in0=srow[:, 0:16], scalar1=10240.0, scalar2=-1.0,
                                         op0=OP.subtract, op1=OP.mult))
            vq.drain()
            vq(lambda e: e.tensor_scalar(out=srow[:, 16:32], in0=srow[:, 16:32], scalar1=0.0, scalar2=None, op0=OP.max))
            vq.drain()
            vq(lambda e: e.tensor_tensor(out=srow[:, 16:32], in0=srow[:, 16:32], in1=mc1[:, :], op=OP.min))
            vq.drain()
            vq(lambda e: e.tensor_tensor(out=w8[:, :], in0=srow[:, 16:24], in1=srow[:, 24:32], op=OP.add))
            vq.drain()
            vq(lambda e: e.tensor_scalar(out=w8b[:, :], in0=w8[:, :], scalar1=0.0, scalar2=None, op0=OP.is_gt))
            vq.drain()
            for mth in range(1, 8):
                vq(lambda e, mth=mth: e.tensor_scalar(out=srow[:, 48:56], in0=w8[:, :],
                                                      scalar1=float(mth * 1024), scalar2=None, op0=OP.is_gt))
                vq.drain()
                vq(lambda e: e.tensor_tensor(out=w8b[:, :], in0=w8b[:, :], in1=srow[:, 48:56], op=OP.add))
                vq.drain()
            vq(lambda e: e.tensor_copy(out=srow[:, 48:56], in_=w8b[:, :]))
            vq.drain()
            vq(lambda e: e.tensor_tensor(out=srow[:, 49:56], in0=srow[:, 49:56], in1=w8b[:, 0:7], op=OP.add))
            vq.drain()
            vq(lambda e: e.tensor_copy(out=srow[:, 56:64], in_=srow[:, 48:56]))
            vq.drain()
            vq(lambda e: e.tensor_tensor(out=srow[:, 58:64], in0=srow[:, 58:64], in1=srow[:, 48:54], op=OP.add))
            vq.drain()
            vq(lambda e: e.tensor_copy(out=srow[:, 48:56], in_=srow[:, 56:64]))
            vq.drain()
            vq(lambda e: e.tensor_tensor(out=srow[:, 52:56], in0=srow[:, 52:56], in1=srow[:, 56:60], op=OP.add))
            vq.drain()
            vq(lambda e: e.tensor_tensor(out=srow[:, 32:40], in0=srow[:, 48:56], in1=w8b[:, :], op=OP.subtract))
            vq.drain()
            vq.inc(ve, 1, lambda e: e.tensor_scalar(out=srow[:, 32:40], in0=srow[:, 32:40],
                                                    scalar1=1024.0, scalar2=None, op0=OP.mult))
            tq.wait(ve)
            tq.inc(pe, 1, lambda e: e.matmul(out=psB[:, :], lhsT=ctx.ones_row[0:1, 0:128],
                                             rhs=srow[:, 0:48], start=True, stop=True))
            aq.wait(pe)
            aq.inc(ac, 1, lambda e: e.copy(out=srowb[:, :], in_=psB[:, :]))
            tq.wait(ve)
            tq.inc(pe, 1, lambda e: e.matmul(out=psE[:, 0:8], lhsT=ctx.ones_row[0:1, 0:128],
                                             rhs=srow[:, 48:56], start=True, stop=True))
            vq.wait(pe)
            vq(lambda e: e.tensor_scalar(out=ebc[:, 0:8], in0=psE[:, 0:8], scalar1=ctx.i128col[:, 0:1],
                                         scalar2=None, op0=OP.is_le))
            vq.drain()
            vq.inc(ve, 1, lambda e: e.tensor_reduce(out=ecol[:, :], in_=ebc[:, 0:8],
                                                    axis=mybir.AxisListType.X, op=OP.add))
            tq.wait(ve)
            tq.inc(pe, 1, lambda e: e.transpose(out=psT1[0:1, :],
                                                in_=ecol[:, :], identity=ctx.iden_f32[:, :]))
            aq.wait(pe)
            aq.inc(ac, 1, lambda e: e.copy(out=expr[:, :], in_=psT1[0:1, 0:128]))
            tq.wait(ac)
            tq.inc(pe, 1, lambda e: e.matmul(out=psE[:, 0:15], lhsT=ctx.ones_row[0:1, 0:128],
                                             rhs=expr[:, 0:15], start=True, stop=True))
            aq.wait(pe)
            aq.inc(ac, 1, lambda e: e.copy(out=ebc[:, :], in_=psE[:, :]))
            vq.wait(ac)
            vq(lambda e: e.tensor_scalar(out=ebc[:, :], in0=ebc[:, :],
                                         scalar1=7.0, scalar2=None, op0=OP.min))
            vq.drain()
            for g in range(NGRAN):
                vq(lambda e, g=g: e.tensor_scalar(out=ebc[:, g:g + 1], in0=ebc[:, g:g + 1],
                                                  scalar1=1024.0, scalar2=None, op0=OP.mult))
            vq.drain()
            for g in range(NGRAN):
                vq(lambda e, g=g: e.tensor_scalar(out=idxfc[:, g, :], in0=jp8[:, :],
                                                  scalar1=ebc[:, g:g + 1], scalar2=None, op0=OP.add))
            vq.drain()
            for g in range(NGRAN):
                vq(lambda e, g=g: e.tensor_scalar(out=ebc[:, g:g + 1], in0=ebc[:, g:g + 1],
                                                  scalar1=4.0, scalar2=None, op0=OP.mult))
            vq.drain()
            for g in range(NGRAN):
                vq(lambda e, g=g: e.tensor_scalar(out=idxpj[:, g, :], in0=jp32[:, :],
                                                  scalar1=ebc[:, g:g + 1], scalar2=None, op0=OP.add))
            vq.drain()
            vq.inc(ve, 1, lambda e: e.tensor_copy(out=ebc[:, 0:1], in_=ebc[:, 0:1]))
            ctx.flush(f"{tag}_m0")

        with ctx.scope():
            ldx = [ctx.sem("ldx0"), ctx.sem("ldx1")]
            ldh = [ctx.sem("ldh0"), ctx.sem("ldh1")]
            rt = [ctx.sb(f"{tag}_rt{i}", [128, 8], F32) for i in range(2)]
            xhb = [ctx.sb(f"{tag}_xhb{i}", [128, 1024], BF16) for i in range(2)]
            ohb = ctx.sb(f"{tag}_ohb", [128, 8], F32)
            wk = ctx.sb(f"{tag}_wk", [128, 8], F32)
            pk = ctx.sb(f"{tag}_pk", [128, 6], F32)
            pm2 = [ctx.sb(f"{tag}_pm2{i}", [128, 2], F32) for i in range(2)]
            dstf = ctx.sb(f"{tag}_dstf", [128, 2], F32)
            dstg = ctx.sb(f"{tag}_dstg", [128, 2], F32)
            dst = ctx.sb(f"{tag}_dst", [128, 2], U32)
            dsc = ctx.sb(f"{tag}_dsc", [128, 2], U32)
            stv = [ctx.sem("stq0"), ctx.sem("stq1")]
            gq.wait(pfx)
            for i in range(NT):
                ib = i % 2
                sy.wait(ve)
                sy.wait(ctx.sems["stq0"])
                sy.wait(ctx.sems["stq1"])
                sy.inc(ldx[ib], 16, lambda e, i=i, ib=ib: e.dma_start(
                    out=rt[ib][:, :], in_=route_in[i * 128:(i + 1) * 128, :]))
                sy.inc(ldh[ib], 16, lambda e, i=i, ib=ib: e.dma_start(
                    out=xhb[ib][:, :], in_=xh_in[i * 128:(i + 1) * 128, :]))
                vq.wait(ldx[ib])
                vq.wait(ctx.sems["stq0"])
                vq.wait(ctx.sems["stq1"])
                for k in range(2):
                    lrcol = 4 + k
                    vq(lambda e, ib=ib, k=k: e.tensor_scalar(out=ohb[:, :], in0=ctx.iota8[:, :],
                                                             scalar1=rt[ib][:, k:k + 1], scalar2=None,
                                                             op0=OP.is_equal))
                    vq.drain()
                    vq(lambda e, k=k: e.tensor_tensor(out=wk[:, :], in0=ohb[:, :],
                                                      in1=srowb[:, k * 8:(k + 1) * 8], op=OP.mult))
                    vq.drain()
                    vq(lambda e: e.tensor_reduce(out=pk[:, 0:1], in_=wk[:, :], axis=mybir.AxisListType.X, op=OP.add))
                    vq.drain()
                    vq(lambda e: e.tensor_tensor(out=wk[:, :], in0=ohb[:, :], in1=srowb[:, 32:40], op=OP.mult))
                    vq.drain()
                    vq(lambda e: e.tensor_reduce(out=pk[:, 1:2], in_=wk[:, :], axis=mybir.AxisListType.X, op=OP.add))
                    vq.drain()
                    vq(lambda e: e.tensor_tensor(out=wk[:, :], in0=ohb[:, :], in1=srowb[:, 16:24], op=OP.mult))
                    vq.drain()
                    vq(lambda e: e.tensor_reduce(out=pk[:, 2:3], in_=wk[:, :], axis=mybir.AxisListType.X, op=OP.add))
                    vq.drain()
                    vq(lambda e, k=k: e.tensor_tensor(out=wk[:, :], in0=ohb[:, :],
                                                      in1=srowb[:, (2 + k) * 8:(3 + k) * 8], op=OP.mult))
                    vq.drain()
                    vq(lambda e: e.tensor_reduce(out=pk[:, 3:4], in_=wk[:, :], axis=mybir.AxisListType.X, op=OP.add))
                    vq.drain()
                    vq(lambda e, ib=ib, k=k, lrcol=lrcol: e.tensor_scalar(
                        out=pk[:, 4:5], in0=rt[ib][:, lrcol:lrcol + 1], scalar1=pk[:, 0:1],
                        scalar2=None, op0=OP.add))
                    vq.drain()
                    vq(lambda e: e.tensor_scalar(out=pk[:, 5:6], in0=pk[:, 4:5], scalar1=10240.0,
                                                 scalar2=None, op0=OP.is_lt))
                    vq.drain()
                    vq(lambda e, ib=ib, k=k: e.tensor_scalar(out=pm2[ib][:, k:k + 1],
                                                             in0=rt[ib][:, 2 + k:3 + k],
                                                             scalar1=pk[:, 5:6], scalar2=None, op0=OP.mult))
                    vq.drain()
                    if k == 1:
                        vq(lambda e: e.tensor_scalar(out=pk[:, 1:2], in0=pk[:, 2:3], scalar1=pk[:, 1:2],
                                                     scalar2=None, op0=OP.add))
                        vq.drain()
                    vq(lambda e, ib=ib, k=k, lrcol=lrcol: e.tensor_scalar(
                        out=dstf[:, k:k + 1], in0=rt[ib][:, lrcol:lrcol + 1], scalar1=pk[:, 1:2],
                        scalar2=None, op0=OP.add))
                    vq.drain()
                    vq(lambda e, ib=ib, lrcol=lrcol: e.tensor_scalar(
                        out=pk[:, 3:4], in0=rt[ib][:, lrcol:lrcol + 1], scalar1=pk[:, 3:4],
                        scalar2=None, op0=OP.is_ge))
                    vq(lambda e, i=i, k=k: e.tensor_scalar(
                        out=pk[:, 4:5], in0=ctx.i128col[:, :], scalar1=2.0,
                        scalar2=float(NGRAN * GRAN + k + 256 * i), op0=OP.mult, op1=OP.add))
                    vq.drain()
                    vq(lambda e, k=k: e.tensor_tensor(out=pk[:, 4:5], in0=pk[:, 4:5],
                                                      in1=dstf[:, k:k + 1], op=OP.subtract))
                    vq.drain()
                    vq(lambda e, k=k: e.scalar_tensor_tensor(
                        out=dstf[:, k:k + 1], in0=pk[:, 4:5], scalar=pk[:, 3:4], op0=OP.mult,
                        op1=OP.add, in1=dstf[:, k:k + 1]))
                    vq.drain()
                vq(lambda e: e.tensor_scalar(out=dstf[:, :], in0=dstf[:, :],
                                             scalar1=float(NGRAN * GRAN + 2 * T - 1), scalar2=None, op0=OP.min))
                vq.drain()
                vq(lambda e: e.tensor_scalar(out=dstg[:, :], in0=dstf[:, :],
                                             scalar1=float(NGRAN * GRAN - 1), scalar2=None, op0=OP.min))
                vq.drain()
                vq(lambda e: e.tensor_scalar(out=dst[:, :], in0=dstf[:, :], scalar1=0.0,
                                             scalar2=None, op0=OP.add))
                vq.drain()
                vq.inc(ve, 1, lambda e: e.tensor_scalar(out=dsc[:, :], in0=dstg[:, :], scalar1=0.0,
                                                        scalar2=None, op0=OP.add))
                gq.wait(ve)
                gq.wait(ldh[ib])
                for k in range(2):
                    gq.inc(stv[k], 16, lambda e, k=k, ib=ib: e.indirect_dma_start(
                        out=xpkd[:, :],
                        out_offset=bass.IndirectOffsetOnAxis(ap=dst[:, k:k + 1], axis=0),
                        in_=xhb[ib][:, :], in_offset=None,
                        bounds_check=None))
                gq.inc(stv[ib], 16, lambda e, i=i: e.dma_start(
                    out=dstd[i * 128:(i + 1) * 128, :], in_=dsc[:, :]))
                gq.inc(stv[ib], 16, lambda e, i=i, ib=ib: e.dma_start(
                    out=pmd[i * 128:(i + 1) * 128, :], in_=pm2[ib][:, :]))
            gq.wait(ctx.sem("stq0"))
            gq.wait(ctx.sem("stq1"))
            ctx.flush(f"{tag}_m1")

        with ctx.scope():
            gldA = ctx.sem("gldA")
            gldB = ctx.sem("gldB")
            gldX = ctx.sem("gldX")
            stq = [ctx.sem("stq0"), ctx.sem("stq1")]
            wfcb = ctx.sb(f"{tag}_wfcb", [128, 8, 4096], BF16)
            wpjb = ctx.sb(f"{tag}_wpjb", [128, 32, 1024], BF16)
            xgT = ctx.sb(f"{tag}_xgT", [128, 8, 1024], BF16)
            xgs = [ctx.sb(f"{tag}_xgs{i}", [128, 1024], BF16) for i in range(2)]
            hbT = ctx.sb(f"{tag}_hbT", [128, 32, 512], BF16)
            hrl = ctx.sb(f"{tag}_hrl", [128, 512], F32)
            osb = [ctx.sb(f"{tag}_osb{i}", [128, 1024], BF16) for i in range(2)]
            psH = [ctx.ps(f"{tag}_psH{i}", [128, 512], F32) for i in range(2)]
            psO = [ctx.ps(f"{tag}_psO{i}", [128, 1024], F32) for i in range(2)]
            psT = ctx.ps(f"{tag}_psT2", [128, 1024], F32)
            done = []
            for g in range(NGRAN):
                d = {}
                for s in range(8):
                    sb2 = s % 2
                    gq.wait(pe)
                    gq.inc(gldX, 16, lambda e, g=g, s=s, sb2=sb2: e.dma_start(
                        out=xgs[sb2][:, :],
                        in_=xpkd[g * 1024 + s * 128:g * 1024 + (s + 1) * 128, :]))
                    tq.wait(gldX)
                    tq.wait(ac)
                    for j in range(8):
                        tq(lambda e, sb2=sb2, j=j: e.matmul(
                            out=psT[:, j * 128:(j + 1) * 128], lhsT=xgs[sb2][:, j * 128:(j + 1) * 128],
                            rhs=ctx.iden_bf[:, :], start=True, stop=True))
                    tq.inc_prev(pe, 1)
                    aq.wait(pe)
                    aq.inc(ac, 1, lambda e, s=s: e.copy(
                        out=xgT[:, :, s * 128:(s + 1) * 128],
                        in_=psT[:, :].rearrange("p (j c) -> p j c", c=128)))
                d["xtr"] = pe.n
                gq.wait(pe, done[-1]["h1_done"] if done else 0)
                for j in range(8):
                    for half in range(2):
                        gq.inc(gldA, 16, lambda e, g=g, j=j, half=half: e.indirect_dma_start(
                            out=wfcb[:, j, half * 2048:(half + 1) * 2048], out_offset=None,
                            in_=cfcv[:, :],
                            in_offset=bass.IndirectOffsetOnAxis(ap=idxfc[:, g, j:j + 1], axis=0),
                            element_offset=half * 2048,
                            bounds_check=None))
                for th in range(2):
                    tq.wait(gldA)
                    for fc in range(32):
                        fb = fc % 2
                        if not (th == 0 and fc < 2):
                            tq.wait(ac)
                        for j in range(8):
                            tq(lambda e, g=g, th=th, fc=fc, fb=fb, j=j: e.matmul(
                                out=psH[fb][:, :], lhsT=wfcb[:, j, fc * 128:(fc + 1) * 128],
                                rhs=xgT[:, j, th * 512:(th + 1) * 512], start=(j == 0), stop=(j == 7)))
                        tq.inc_prev(pe, 1)
                        aq.wait(pe)
                        aq(lambda e, fb=fb: e.activation(out=hrl[:, :], in_=psH[fb][:, :], func=AF.Relu))
                        aq.drain()
                        aq.inc(ac, 1, lambda e, fc=fc: e.activation(out=hbT[:, fc, :], in_=hrl[:, :],
                                                                    func=AF.Square))
                        aq.drain()
                    if th == 0:
                        d["h0_done"] = pe.n
                    else:
                        d["h1_done"] = pe.n
                    if th == 0:
                        gq.wait(pe, done[-1]["o_done"] if done else 0)
                        for jj in range(32):
                            gq.inc(gldB, 16, lambda e, g=g, jj=jj: e.indirect_dma_start(
                                out=wpjb[:, jj, :], out_offset=None,
                                in_=cpjv[:, :],
                                in_offset=bass.IndirectOffsetOnAxis(ap=idxpj[:, g, jj:jj + 1], axis=0),
                                bounds_check=None))
                    tq.wait(ac)
                    tq.wait(gldB)
                    for tt in range(4):
                        tb = tt % 2
                        if not (th == 0 and tt < 2):
                            tq.wait(ac)
                        for half in range(2):
                            for fc in range(32):
                                tq(lambda e, tt=tt, tb=tb, th=th, fc=fc, half=half: e.matmul(
                                    out=psO[tb][:, half * 512:(half + 1) * 512],
                                    lhsT=hbT[:, fc, tt * 128:(tt + 1) * 128],
                                    rhs=wpjb[:, fc, half * 512:(half + 1) * 512],
                                    start=(fc == 0), stop=(fc == 31)))
                        tq.inc_prev(pe, 1)
                        aq.wait(pe)
                        aq.wait(stq[tb])
                        aq.inc(ac, 1, lambda e, tb=tb: e.copy(out=osb[tb][:, :], in_=psO[tb][:, :]))
                        gq.wait(ac)
                        gq.inc(stq[tb], 16, lambda e, g=g, th=th, tt=tt, tb=tb: e.dma_start(
                            out=opk[g * 1024 + th * 512 + tt * 128:
                                    g * 1024 + th * 512 + (tt + 1) * 128, :],
                            in_=osb[tb][:, :]))
                d["o_done"] = pe.n
                done.append(d)
            gq.wait(ctx.sem("stq0"))
            gq.wait(ctx.sem("stq1"))
            ctx.flush(f"{tag}_m2")

        with ctx.scope():
            ldx = [ctx.sem("ldx0"), ctx.sem("ldx1")]
            ldd = [ctx.sem("ldd0"), ctx.sem("ldd1")]
            glo = [ctx.sem("glo0"), ctx.sem("glo1")]
            sto = [ctx.sem("sto0"), ctx.sem("sto1")]
            xb = [ctx.sb(f"{tag}_cx{i}", [128, 1024], F32) for i in range(2)]
            o0 = [ctx.sb(f"{tag}_co0{i}", [128, 1024], BF16) for i in range(2)]
            o1 = [ctx.sb(f"{tag}_co1{i}", [128, 1024], BF16) for i in range(2)]
            dsb = [ctx.sb(f"{tag}_dsb{i}", [128, 2], U32) for i in range(2)]
            pmb = [ctx.sb(f"{tag}_cpm{i}", [128, 2], F32) for i in range(2)]
            t1 = [ctx.sb(f"{tag}_ct1{i}", [128, 1024], F32) for i in range(2)]
            ot = [ctx.sb(f"{tag}_cot{i}", [128, 1024], F32) for i in range(2)]
            ve_done = []
            glo_done = []
            for i in range(NT):
                ib = i % 2
                if i >= 2:
                    sy.wait(ve, ve_done[i - 2])
                    sy.wait(glo[ib], glo_done[i - 2])
                sy.inc(ldx[ib], 16, lambda e, i=i, ib=ib: e.dma_start(
                    out=xb[ib][:, :], in_=x_in[i * 128:(i + 1) * 128, :]))
                sy.inc(ldd[ib], 16, lambda e, i=i, ib=ib: e.dma_start(
                    out=dsb[ib][:, :], in_=dstd[i * 128:(i + 1) * 128, :]))
                sy.inc(ldd[ib], 16, lambda e, i=i, ib=ib: e.dma_start(
                    out=pmb[ib][:, :], in_=pmd[i * 128:(i + 1) * 128, :]))
                gq.wait(ldd[ib])
                if i >= 2:
                    gq.wait(ve, ve_done[i - 2])
                gq.inc(glo[ib], 16, lambda e, ib=ib: e.indirect_dma_start(
                    out=o0[ib][:, :], out_offset=None,
                    in_=opk[:, :],
                    in_offset=bass.IndirectOffsetOnAxis(ap=dsb[ib][:, 0:1], axis=0),
                    bounds_check=None))
                gq.inc(glo[ib], 16, lambda e, ib=ib: e.indirect_dma_start(
                    out=o1[ib][:, :], out_offset=None,
                    in_=opk[:, :],
                    in_offset=bass.IndirectOffsetOnAxis(ap=dsb[ib][:, 1:2], axis=0),
                    bounds_check=None))
                glo_done.append(glo[ib].n)
                vq.wait(ldx[ib])
                vq.wait(glo[ib])
                vq.wait(ldd[ib])
                if i >= 2:
                    vq.wait(sto[ib])
                vq(lambda e, ib=ib: e.scalar_tensor_tensor(
                    out=t1[ib][:, :], in0=o0[ib][:, :], scalar=pmb[ib][:, 0:1], op0=OP.mult,
                    op1=OP.add, in1=xb[ib][:, :]))
                vq.drain()
                vq.inc(ve, 1, lambda e, ib=ib: e.scalar_tensor_tensor(
                    out=ot[ib][:, :], in0=o1[ib][:, :], scalar=pmb[ib][:, 1:2], op0=OP.mult,
                    op1=OP.add, in1=t1[ib][:, :]))
                ve_done.append(ve.n)
                gq.wait(ve)
                gq.inc(sto[ib], 16, lambda e, i=i, ib=ib: e.dma_start(
                    out=x_out[i * 128:(i + 1) * 128, :], in_=ot[ib][:, :]))
            gq.wait(ctx.sem("sto0"))
            gq.wait(ctx.sem("sto1"))
            ctx.flush(f"{tag}_m3")


def moe_consts():
    jp8 = (np.arange(8, dtype=np.float32)[None, :] * 128 + np.arange(128, dtype=np.float32)[:, None])
    jp32 = (np.arange(32, dtype=np.float32)[None, :] * 128 + np.arange(128, dtype=np.float32)[:, None])
    return {
        "c_jp8": jp8.astype(np.float32),
        "c_jp32": jp32.astype(np.float32),
    }


def build_moemain_launch():
    nc = make_nc()
    stack = contextlib.ExitStack()
    ctx = Ctx(nc, stack)
    with stack:
        declare_consts(ctx)
        x_in = ctx.din("x_in", (T, D), F32)
        xh_in = ctx.din("xh_in", (T, D), BF16)
        route_in = ctx.din("route_in", (T, 8), F32)
        ctx.din("moe_allcnt", (8, 16), F32)
        ctx.din("moe_mycnt", (1, 16), F32)
        ctx.din("moe_mprev", (8, 1), F32)
        ctx.din("c_jp8", (128, 8), F32)
        ctx.din("c_jp32", (128, 32), F32)
        ctx.din("L1_ffn_c_fc", (8, 1024, 4096), F32)
        ctx.din("L1_ffn_c_proj", (8, 4096, 1024), F32)
        x_out = ctx.dout("x_out", (T, D), F32)
        emit_load_consts(ctx)
        emit_moe_main(ctx, x_in, x_out, xh_in, route_in, "mm")
        ctx.flush("fin")
    return nc


def moemain_inmaps(inp, xs, xh, route, cnts):
    c = host_constants()
    mc = moe_consts()
    allc = np.ascontiguousarray(cnts.reshape(8, 16))
    base = {
        **c, **mc,
        "moe_allcnt": allc,
        "L1_ffn_c_fc": np.ascontiguousarray(inp["L1_ffn_c_fc"]),
        "L1_ffn_c_proj": np.ascontiguousarray(inp["L1_ffn_c_proj"]),
    }
    maps = []
    for i in range(NCORES):
        mprev = np.zeros((8, 1), np.float32)
        mprev[:i] = 1.0
        maps.append({**base, "x_in": np.ascontiguousarray(xs[i]),
                     "xh_in": np.ascontiguousarray(xh[i]),
                     "route_in": np.ascontiguousarray(route[i]),
                     "moe_mycnt": allc[i:i + 1].copy(),
                     "moe_mprev": mprev})
    return maps


def run_moe(inp, xs):
    nc1 = build_moepre_launch()
    res1 = run_bass_kernel_spmd(nc1, moepre_inmaps(inp, xs), core_ids=list(range(NCORES)))
    xh = np.stack([r["xh_out"] for r in res1.results])
    route = np.stack([r["route_out"] for r in res1.results])
    cnts = np.stack([r["cnt_out"] for r in res1.results])
    nc2 = build_moemain_launch()
    res2 = run_bass_kernel_spmd(nc2, moemain_inmaps(inp, xs, xh, route, cnts), core_ids=list(range(NCORES)))
    return np.stack([r["x_out"] for r in res2.results])



_BUILT = {}


def _nc_for(name, builder):
    if name not in _BUILT:
        _BUILT[name] = builder()
    return _BUILT[name]


def _launch(name, builder, maps, outs):
    res = run_bass_kernel_spmd(_nc_for(name, builder), maps, core_ids=list(range(NCORES)))
    return [np.stack([r[o] for r in res.results]) for o in outs]


def _moe_step(inp, x):
    xh, route, cnts = _launch("moepre", build_moepre_launch, moepre_inmaps(inp, x),
                              ["xh_out", "route_out", "cnt_out"])
    x, = _launch("moemain", build_moemain_launch,
                 moemain_inmaps(inp, x, xh, route, cnts), ["x_out"])
    return x


def kernel(**inputs):
    inp = {k: np.asarray(v) for k, v in inputs.items()}
    x = np.ascontiguousarray(np.asarray(inp["L0_attn_a_z"], dtype=np.float32))
    x, = _launch("l0a", build_l0a_launch, l0a_inmaps(inp, x), ["x_out"])
    x, = _launch("ffn", build_ffn_launch, ffn_inmaps(inp, x), ["x_out"])
    x, = _launch("ret", build_ret_launch, ret_inmaps(inp, x), ["x_out"])
    x, = _launch("ffn", build_ffn_launch, ffn_inmaps(inp, x), ["x_out"])
    x, = _launch("gla", build_gla_launch, gla_inmaps(inp, x), ["x_out"])
    x = _moe_step(inp, x)
    x, = _launch("lra", build_lra_launch, lra_inmaps(inp, x), ["x_out"])
    x = _moe_step(inp, x)
    return np.ascontiguousarray(x.astype(np.float32))


# version 02
# speedup vs baseline: 1.0043x; 1.0043x over previous
import contextlib
import numpy as np
import ml_dtypes
import concourse.bass as bass
import concourse.mybir as mybir
from concourse.bass_utils import run_bass_kernel_spmd

F32 = mybir.dt.float32
BF16 = mybir.dt.bfloat16
U32 = mybir.dt.uint32
AF = mybir.ActivationFunctionType
OP = mybir.AluOpType
BF = ml_dtypes.bfloat16

T = 4096
D = 1024
NT = T // 128
ND = D // 128
EPS = 1e-5
NCORES = 8
EXP_SHIFT = 8.0


class Sem:
    def __init__(self, h):
        self.h = h
        self.n = 0


class EQ:
    def __init__(self):
        self.ops = []

    def __call__(self, fn):
        self.ops.append(fn)

    def wait(self, sem, val=None):
        v = sem.n if val is None else val
        if v > 0:
            self.ops.append(lambda e, s=sem.h, v=v: e.wait_ge(s, v))

    def inc(self, sem, k, fn):
        def g(e, fn=fn, s=sem.h, k=k):
            fn(e).then_inc(s, k)
        self.ops.append(g)
        sem.n += k

    def inc_prev(self, sem, k):
        fn = self.ops.pop()
        def g(e, fn=fn, s=sem.h, k=k):
            fn(e).then_inc(s, k)
        self.ops.append(g)
        sem.n += k

    def drain(self):
        self.ops.append(lambda e: e.drain())

    def flush(self, block, which):
        ops = self.ops
        self.ops = []
        if ops:
            getattr(block, which)(lambda e: [f(e) for f in ops])


class Ctx:
    def __init__(self, nc, stack):
        self.nc = nc
        self.stack = stack
        self.sems = {}
        self.dram = {}
        self.q = {n: EQ() for n in ("sync", "vector", "tensor", "scalar", "gpsimd")}

    def sem(self, name):
        if name not in self.sems:
            h = self.stack.enter_context(self.nc.semaphore(name))
            self.sems[name] = Sem(h)
        return self.sems[name]

    @contextlib.contextmanager
    def scope(self):
        outer = self.stack
        with contextlib.ExitStack() as st:
            self.stack = st
            try:
                yield
            finally:
                self.stack = outer

    def sb(self, name, shape, dtype):
        return self.stack.enter_context(self.nc.sbuf_tensor(name, shape, dtype))

    def ps(self, name, shape, dtype=F32):
        return self.stack.enter_context(self.nc.psum_tensor(name, shape, dtype))

    def din(self, name, shape, dtype):
        ap = self.nc.dram_tensor(name, list(shape), dtype, kind="ExternalInput").ap()
        self.dram[name] = ap
        return ap

    def dout(self, name, shape, dtype):
        ap = self.nc.dram_tensor(name, list(shape), dtype, kind="ExternalOutput").ap()
        self.dram[name] = ap
        return ap

    def dtmp(self, name, shape, dtype):
        ap = self.nc.dram_tensor(name, list(shape), dtype).ap()
        self.dram[name] = ap
        return ap

    def flush(self, name=None):
        with self.nc.Block(name) as block:
            self.q["sync"].flush(block, "sync")
            self.q["vector"].flush(block, "vector")
            self.q["tensor"].flush(block, "tensor")
            self.q["scalar"].flush(block, "scalar")
            self.q["gpsimd"].flush(block, "gpsimd")

    @property
    def sy(self):
        return self.q["sync"]

    @property
    def vq(self):
        return self.q["vector"]

    @property
    def tq(self):
        return self.q["tensor"]

    @property
    def aq(self):
        return self.q["scalar"]

    @property
    def gq(self):
        return self.q["gpsimd"]


def make_nc():
    return bass.Bass("TRN2", target_bir_lowering=False, debug=False)


def host_constants():
    return {
        "c_iden_bf": np.eye(128, dtype=BF),
        "c_iden_f32": np.eye(128, dtype=np.float32),
        "c_ones_row": np.ones((1, 512), dtype=np.float32),
        "c_ones_col": np.ones((128, 1), dtype=np.float32),
        "c_ones_col_bf": np.ones((128, 1), dtype=BF),
        "c_trib": np.kron(np.eye(2, dtype=np.float32), np.triu(np.ones((64, 64), dtype=np.float32))),
        "c_utm": np.tile(np.triu(np.ones((64, 64), dtype=np.float32)), (2, 1)),
        "c_utm_u8": np.tile(np.triu(np.ones((64, 64), dtype=np.uint8)), (2, 1)),
        "c_iota8": np.tile(np.arange(8, dtype=np.float32)[None, :], (128, 1)),
        "c_ltx": np.triu(np.ones((128, 128), dtype=np.float32), k=1),
        "c_i128": np.arange(128, dtype=np.float32)[:, None],
    }


def declare_consts(ctx):
    ctx.din("c_iden_bf", (128, 128), BF16)
    ctx.din("c_iden_f32", (128, 128), F32)
    ctx.din("c_ones_row", (1, 512), F32)
    ctx.din("c_ones_col", (128, 1), F32)
    ctx.din("c_ones_col_bf", (128, 1), BF16)
    ctx.din("c_trib", (128, 128), F32)
    ctx.din("c_utm", (128, 64), F32)
    ctx.din("c_utm_u8", (128, 64), mybir.dt.uint8)
    ctx.din("c_iota8", (128, 8), F32)
    ctx.din("c_ltx", (128, 128), F32)
    ctx.din("c_i128", (128, 1), F32)


CONST_VALUES = [EPS, -EXP_SHIFT, 1.0]


def emit_load_consts(ctx, const_values=CONST_VALUES):
    ctx.iden_bf = ctx.sb("iden_bf", [128, 128], BF16)
    ctx.iden_f32 = ctx.sb("iden_f32", [128, 128], F32)
    ctx.ones_row = ctx.sb("ones_row", [1, 512], F32)
    ctx.onescol_f32 = ctx.sb("ones_col", [128, 1], F32)
    ctx.onescol_bf = ctx.sb("ones_col_bf", [128, 1], BF16)
    ctx.trib = ctx.sb("trib", [128, 128], F32)
    ctx.utm = ctx.sb("utm", [128, 64], F32)
    ctx.utm_u8 = ctx.sb("utm_u8", [128, 64], mybir.dt.uint8)
    ctx.iota8 = ctx.sb("iota8", [128, 8], F32)
    ctx.ltx = ctx.sb("ltx", [128, 128], F32)
    ctx.i128col = ctx.sb("i128col", [128, 1], F32)
    ld = ctx.sem("ld")
    ctx.sy.inc(ld, 16, lambda e: e.dma_start(out=ctx.iden_bf[:, :], in_=ctx.dram["c_iden_bf"][:, :]))
    ctx.sy.inc(ld, 16, lambda e: e.dma_start(out=ctx.iden_f32[:, :], in_=ctx.dram["c_iden_f32"][:, :]))
    ctx.sy.inc(ld, 16, lambda e: e.dma_start(out=ctx.ones_row[:, :], in_=ctx.dram["c_ones_row"][:, :]))
    ctx.sy.inc(ld, 16, lambda e: e.dma_start(out=ctx.onescol_f32[:, :], in_=ctx.dram["c_ones_col"][:, :]))
    ctx.sy.inc(ld, 16, lambda e: e.dma_start(out=ctx.onescol_bf[:, :], in_=ctx.dram["c_ones_col_bf"][:, :]))
    ctx.sy.inc(ld, 16, lambda e: e.dma_start(out=ctx.trib[:, :], in_=ctx.dram["c_trib"][:, :]))
    ctx.sy.inc(ld, 16, lambda e: e.dma_start(out=ctx.utm[:, :], in_=ctx.dram["c_utm"][:, :]))
    ctx.sy.inc(ld, 16, lambda e: e.dma_start(out=ctx.utm_u8[:, :], in_=ctx.dram["c_utm_u8"][:, :]))
    ctx.sy.inc(ld, 16, lambda e: e.dma_start(out=ctx.iota8[:, :], in_=ctx.dram["c_iota8"][:, :]))
    ctx.sy.inc(ld, 16, lambda e: e.dma_start(out=ctx.ltx[:, :], in_=ctx.dram["c_ltx"][:, :]))
    ctx.sy.inc(ld, 16, lambda e: e.dma_start(out=ctx.i128col[:, :], in_=ctx.dram["c_i128"][:, :]))
    cp = ctx.sb("constpool", [128, max(1, len(const_values))], F32)
    for i, v in enumerate(const_values):
        ctx.vq(lambda e, i=i, v=v: e.memset(cp[:, i:i + 1], float(v)))
        ctx.nc.const_aps.register(F32, float(v), cp[:, i:i + 1])
    ctx.vq.drain()
    ctx.sy.wait(ld)



def emit_ln_stats(ctx, x_ap, A, B, name="ln", z_of=None):
    ve, ac = ctx.sem("ve"), ctx.sem("ac")
    ldx = [ctx.sem("ldx0"), ctx.sem("ldx1")]
    sy, vq, aq = ctx.sy, ctx.vq, ctx.aq
    with ctx.scope():
        xb = [ctx.sb(f"{name}_x{i}", [128, 1024], F32) for i in range(2)]
        zb = ctx.sb(f"{name}_zb", [128, 1024], F32) if z_of else None
        bns = ctx.sb(f"{name}_bns", [128, 12], F32)
        mv = ctx.sb(f"{name}_mv", [128, 2], F32)
        var = ctx.sb(f"{name}_var", [128, NT], F32)
        ve_done = []
        for i in range(NT):
            if i >= 2:
                sy.wait(ve, ve_done[i - 2])
            sy.inc(ldx[i % 2], 16, lambda e, i=i: e.dma_start(out=xb[i % 2][:, :], in_=x_ap[i * 128:(i + 1) * 128, :]))
            vq.wait(ldx[i % 2])
            src = xb
            if z_of:
                A1, B1 = z_of
                vq(lambda e, i=i: e.tensor_scalar(out=zb[:, :], in0=xb[i % 2][:, :], scalar1=B1[:, i:i + 1],
                                                  scalar2=A1[:, i:i + 1], op0=OP.mult, op1=OP.subtract))
                vq.drain()
            tgt = (lambda i: zb[:, :]) if z_of else (lambda i: xb[i % 2][:, :])
            vq(lambda e, i=i: e.bn_stats(bns[:, 0:6], tgt(i)[:, 0:512]))
            vq(lambda e, i=i: e.bn_stats(bns[:, 6:12], tgt(i)[:, 512:1024]))
            vq.drain()
            vq(lambda e: e.bn_aggr(mv[:, :], bns[:, :]))
            vq.drain()
            vq(lambda e, i=i: e.tensor_copy(out=B[:, i:i + 1], in_=mv[:, 0:1]))
            vq.inc(ve, 1, lambda e, i=i: e.tensor_copy(out=var[:, i:i + 1], in_=mv[:, 1:2]))
            ve_done.append(ve.n)
        aq.wait(ve)
        aq.inc(ac, 1, lambda e: e.activation(out=var[:, :], in_=var[:, :], func=AF.Sqrt, bias=EPS))
        vq.wait(ac)
        vq(lambda e: e.reciprocal(out=var[:, :], in_=var[:, :]))
        vq.drain()
        vq(lambda e: e.tensor_tensor(out=A[:, :], in0=B[:, :], in1=var[:, :], op=OP.mult))
        vq.drain()
        vq.inc(ve, 1, lambda e: e.tensor_copy(out=B[:, :], in_=var[:, :]))
        ctx.flush(name)



class TileNorm:

    def __init__(self, ctx, name, x_ap, A, B, nbuf=2):
        self.ctx = ctx
        self.x_ap = x_ap
        self.A, self.B = A, B
        self.xb = [ctx.sb(f"{name}_x{i}", [128, 1024], F32) for i in range(2)]
        self.zh = [ctx.sb(f"{name}_z{i}", [128, 1024], BF16) for i in range(2)]
        self.xT = [ctx.sb(f"{name}_t{i}", [128, 1024], BF16) for i in range(2)]
        self.pTf = ctx.ps(f"{name}_pTf", [128, 1024], F32)
        self.ve_done = []
        self.pe_done = []
        self.ac_done = []
        self.consumed = []
        self.zh_consumed = []

    def record(self, i):
        ctx = self.ctx
        ve, pe, ac = ctx.sem("ve"), ctx.sem("pe"), ctx.sem("ac")
        ldx = [ctx.sem("ldx0"), ctx.sem("ldx1")]
        sy, vq, tq, aq = ctx.sy, ctx.vq, ctx.tq, ctx.aq
        if i >= 2:
            sy.wait(ve, self.ve_done[i - 2])
        sy.inc(ldx[i % 2], 16, lambda e, i=i: e.dma_start(out=self.xb[i % 2][:, :],
                                                  in_=self.x_ap[i * 128:(i + 1) * 128, :]))
        vq.wait(ldx[i % 2])
        if i >= 2:
            s, v = self.zh_consumed[i - 2]
            vq.wait(s, v)
        if i >= 1:
            vq.wait(pe, self.pe_done[i - 1])
        vq.inc(ve, 1, lambda e, i=i: e.tensor_scalar(
            out=self.zh[i % 2][:, :], in0=self.xb[i % 2][:, :],
            scalar1=self.B[:, i:i + 1], scalar2=self.A[:, i:i + 1],
            op0=OP.mult, op1=OP.subtract))
        self.ve_done.append(ve.n)
        tq.wait(ve)
        if i >= 1:
            tq.wait(ac, self.ac_done[i - 1])
        for j in range(8):
            tq(lambda e, i=i, j=j: e.matmul(
                out=self.pTf[:, j * 128:(j + 1) * 128],
                lhsT=self.zh[i % 2][:, j * 128:(j + 1) * 128],
                rhs=ctx.iden_bf[:, :], start=True, stop=True))
        tq.inc_prev(pe, 1)
        self.pe_done.append(pe.n)
        self.zh_consumed.append((pe, pe.n))
        aq.wait(pe)
        if i >= 2:
            s, v = self.consumed[i - 2]
            aq.wait(s, v)
        aq.inc(ac, 1, lambda e, i=i: e.copy(out=self.xT[i % 2][:, :], in_=self.pTf[:, :]))
        self.ac_done.append(ac.n)

    def note_consumed(self, sem, val):
        self.consumed.append((sem, val))

    def cur(self, i):
        return self.xT[i % 2]



L0A_PHASES = 9


def emit_l0a(ctx, x_in, x_out):
    nc = ctx.nc
    ld, st_, pe, ve, ac = (ctx.sem(n) for n in ("ld", "st", "pe", "ve", "ac"))
    sy, vq, tq, aq, gq = ctx.sy, ctx.vq, ctx.tq, ctx.aq, ctx.gq

    W_qkv = ctx.dram["L0_attn_a_W_qkv"]
    b_qkv = ctx.dram["L0_attn_a_b_qkv_r"]
    W_proj = ctx.dram["L0_attn_a_W_proj"]
    b_proj = ctx.dram["L0_attn_a_b_proj_r"]
    ln_w = ctx.dram["L0_attn_a_ln_w_r"]
    ln_b = ctx.dram["L0_attn_a_ln_b_r"]
    qkp = ctx.dtmp("l0a_qkp", (16, T, 128), BF16)
    v3 = ctx.dtmp("l0a_v3", (16, T, 64), BF16)

    with ctx.scope():
        A1 = ctx.sb("l0a_A1", [128, NT], F32)
        B1 = ctx.sb("l0a_B1", [128, NT], F32)
        A2 = ctx.sb("l0a_A2", [128, NT], F32)
        B2 = ctx.sb("l0a_B2", [128, NT], F32)
        emit_ln_stats(ctx, x_in, A1, B1, name="l0a_s1")
        emit_ln_stats(ctx, x_in, A2, B2, name="l0a_s2", z_of=(A1, B1))
        Acm = ctx.sb("l0a_Acm", [128, NT], F32)
        Bc = ctx.sb("l0a_Bc", [128, NT], F32)
        Bp = ctx.sb("l0a_Bp", [128, NT], F32)
        vq(lambda e: e.tensor_tensor(out=Bc[:, :], in0=B1[:, :], in1=B2[:, :], op=OP.mult))
        vq(lambda e: e.tensor_tensor(out=Acm[:, :], in0=A1[:, :], in1=B2[:, :], op=OP.mult))
        vq.drain()
        vq(lambda e: e.tensor_tensor(out=Acm[:, :], in0=Acm[:, :], in1=A2[:, :], op=OP.add))
        vq(lambda e: e.tensor_scalar(out=Bp[:, :], in0=B1[:, :], scalar1=1.0, scalar2=None, op0=OP.add))
        vq.drain()
        ctx.flush("l0a_tables")

        if L0A_PHASES < 2:
            return
        wqkv_scope = ctx.scope()
        wqkv_scope.__enter__()
        Wb = ctx.sb("l0a_Wb", [128, 8, 3072], BF16)
        bias_eff = ctx.sb("l0a_bias", [1, 3072], F32)
        with ctx.scope():
            wf = [ctx.sb(f"l0a_wf{i}", [128, 3072], F32) for i in range(2)]
            lnw = ctx.sb("l0a_lnw", [128, 8], F32)
            lnb = ctx.sb("l0a_lnb", [128, 8], F32)
            bq = ctx.sb("l0a_bq", [1, 3072], F32)
            pbias = ctx.ps("l0a_pbias", [1, 3072], F32)
            ldc = ctx.sem("ldc")
            ldw = [ctx.sem("ldw0"), ctx.sem("ldw1")]
            sy.inc(ldc, 16, lambda e: e.dma_start(out=lnw[:, :], in_=ln_w[:, :]))
            sy.inc(ldc, 16, lambda e: e.dma_start(out=lnb[:, :], in_=ln_b[:, :]))
            sy.inc(ldc, 16, lambda e: e.dma_start(out=bq[:, :], in_=b_qkv[:, :]))
            pe_done = []
            for j in range(8):
                if j >= 2:
                    sy.wait(pe, pe_done[j - 2])
                sy.inc(ldw[j % 2], 16, lambda e, j=j: e.dma_start(out=wf[j % 2][:, :],
                                                          in_=W_qkv[j * 128:(j + 1) * 128, :]))
                vq.wait(ldw[j % 2])
                vq.wait(ldc)
                vq.inc(ve, 1, lambda e, j=j: e.tensor_scalar(out=Wb[:, j, :], in0=wf[j % 2][:, :],
                                                             scalar1=lnw[:, j:j + 1], scalar2=None, op0=OP.mult))
                tq.wait(ve)
                for c in range(6):
                    tq(lambda e, j=j, c=c: e.matmul(out=pbias[:, c * 512:(c + 1) * 512], lhsT=lnb[:, j:j + 1],
                                                    rhs=wf[j % 2][:, c * 512:(c + 1) * 512],
                                                    start=(j == 0), stop=(j == 7)))
                tq.inc_prev(pe, 1)
                pe_done.append(pe.n)
            vq.wait(pe)
            vq.inc(ve, 1, lambda e: e.tensor_tensor(out=bias_eff[:, :], in0=pbias[:, :], in1=bq[:, :], op=OP.add))
            ctx.flush("l0a_wprep")

        if L0A_PHASES < 3:
            wqkv_scope.__exit__(None, None, None)
            return
        with ctx.scope():
            tn = TileNorm(ctx, "l0a_q", x_in, Acm, Bc)
            qo = [ctx.sb(f"l0a_qo{i}", [128, 3072], BF16) for i in range(2)]
            pq = ctx.ps("l0a_pq", [128, 3072], F32)
            SUB = 4
            for i in range(NT):
                tn.record(i)
                if SUB < 1:
                    tn.note_consumed(ctx.sem("ac"), ac.n)
                    continue
                tq.wait(ac)
                for c in range(6):
                    for j in range(8):
                        tq(lambda e, i=i, c=c, j=j: e.matmul(
                            out=pq[:, c * 512:(c + 1) * 512], lhsT=tn.cur(i)[:, j * 128:(j + 1) * 128],
                            rhs=Wb[:, j, c * 512:(c + 1) * 512], start=(j == 0), stop=(j == 7) if SUB < 2 else False))
                    if SUB >= 2:
                        tq(lambda e, c=c: e.matmul(out=pq[:, c * 512:(c + 1) * 512], lhsT=ctx.ones_row[0:1, 0:128],
                                                   rhs=bias_eff[:, c * 512:(c + 1) * 512], start=False, stop=True))
                tq.inc_prev(pe, 1)
                tn.note_consumed(pe, pe.n)
                if SUB < 3:
                    continue
                aq.wait(pe)
                stq = [ctx.sem("stq0"), ctx.sem("stq1")]
                if i >= 2:
                    aq.wait(stq[i % 2])
                aq.inc(ac, 1, lambda e, i=i: e.copy(out=qo[i % 2][:, :], in_=pq[:, :]))
                if SUB < 4:
                    continue
                gq.wait(ac)
                gq.inc(stq[i % 2], 16, lambda e, i=i: e.dma_start(
                    out=qkp[:, i * 128:(i + 1) * 128, :].rearrange("s p c -> p s c"),
                    in_=qo[i % 2][:, 0:2048].rearrange("p (s c) -> p s c", c=128)))
                gq.inc(stq[i % 2], 16, lambda e, i=i: e.dma_start(
                    out=v3[:, i * 128:(i + 1) * 128, :].rearrange("s p c -> p s c"),
                    in_=qo[i % 2][:, 2048:3072].rearrange("p (s c) -> p s c", c=64)))
            gq.wait(ctx.sem("stq0"))
            gq.wait(ctx.sem("stq1"))
            ctx.flush("l0a_qkv")
        wqkv_scope.__exit__(None, None, None)

        if L0A_PHASES < 4:
            return
        msaT = [ctx.sb(f"l0a_msaT{p}", [128, T], BF16) for p in range(8)]
        with ctx.scope():
            qTp = [ctx.sb(f"l0a_qT{i}", [128, T], BF16) for i in range(2)]
            kTp = [ctx.sb(f"l0a_kT{i}", [128, T], BF16) for i in range(2)]
            qktm = [ctx.sb(f"l0a_qktm{i}", [128, NT, 128], BF16) for i in range(2)]
            psQT = ctx.ps("l0a_psQT", [128, 512], F32)
            vA = [ctx.sb(f"l0a_v{i}", [128, 2 * NT, 66], BF16) for i in range(2)]
            PT = [ctx.sb(f"l0a_PT{i}", [128, 1024], BF16) for i in range(2)]
            mU = ctx.sb("l0a_mU", [64, 512], BF16)
            rrow = ctx.sb("l0a_rrow", [1, 512], F32)
            psS = [ctx.ps(f"l0a_psS{i}", [128, 1024], F32) for i in range(2)]
            psM = [ctx.ps(f"l0a_psM{i}", [66, 512], F32) for i in range(2)]
            psB = ctx.ps("l0a_psB", [64, 512], F32)
            for b in range(2):
                vq(lambda e, b=b: e.memset(vA[b][:, :, 64:65], 1.0))
                vq(lambda e, b=b: e.memset(vA[b][:, :, 65:66], 0.0))
            vq.drain()
            vq.inc_prev(ve, 1)
            pair_use = []
            for p in range(8):
                b = p % 2
                if p >= 2:
                    sy.wait(pe, pair_use[p - 2])
                else:
                    sy.wait(ve)
                ldv = [ctx.sem("ldv0"), ctx.sem("ldv1")]
                ldq = [ctx.sem("ldq0"), ctx.sem("ldq1")]
                sy.inc(ldv[b], 16, lambda e, p=p, b=b: e.dma_start(
                    out=vA[b][:, 0:NT, 0:64], in_=v3[2 * p, :, :].rearrange("(n p) c -> p n c", p=128)))
                sy.inc(ldv[b], 16, lambda e, p=p, b=b: e.dma_start(
                    out=vA[b][:, NT:2 * NT, 0:64], in_=v3[2 * p + 1, :, :].rearrange("(n p) c -> p n c", p=128)))
                for which, dst in ((0, qTp), (1, kTp)):
                    if which == 1:
                        sy.wait(pe)
                    sy.inc(ldq[b], 16, lambda e, p=p, b=b, which=which: e.dma_start(
                        out=qktm[b][:, :, :],
                        in_=qkp[which * 8 + p, :, :].rearrange("(n p) c -> p n c", p=128)))
                    tq.wait(ldq[b])
                    for g4 in range(8):
                        if not (which == 0 and g4 == 0):
                            tq.wait(ac)
                        for j in range(4):
                            blk = g4 * 4 + j
                            tq(lambda e, b=b, blk=blk, j=j: e.matmul(
                                out=psQT[:, j * 128:(j + 1) * 128],
                                lhsT=qktm[b][:, blk, :],
                                rhs=ctx.iden_bf[:, :], start=True, stop=True))
                        tq.inc_prev(pe, 1)
                        aq.wait(pe)
                        aq.inc(ac, 1, lambda e, dst=dst, b=b, g4=g4: e.copy(
                            out=dst[b][:, g4 * 512:(g4 + 1) * 512], in_=psQT[:, :]))
                tq.wait(ac)
                tq.wait(ldv[b])
                for half in range(2):
                    h = 2 * p + half
                    pb = 64 * half
                    vbase = NT * half
                    exp_done = {}
                    for g in range(8):
                        gb = g % 2
                        for u2 in range(16):
                            if (g, u2) != (0, 0):
                                prev = (g, u2 - 2) if u2 >= 2 else ((g - 1, 14 + u2) if g >= 1 else None)
                                if prev in exp_done:
                                    tq.wait(ac, exp_done[prev])
                            for s in range(2):
                                u = 2 * u2 + s
                                tq(lambda e, b=b, g=g, u=u, u2=u2, s=s, pb=pb: e.matmul(
                                    out=psS[u2 % 2][:, s * 512:(s + 1) * 512],
                                    lhsT=kTp[b][pb:pb + 64, u * 128:(u + 1) * 128],
                                    rhs=qTp[b][pb:pb + 64, g * 512:(g + 1) * 512], start=True, stop=True))
                            tq.inc_prev(pe, 1)
                            aq.wait(pe)
                            aq.inc(ac, 1, lambda e, u2=u2: e.activation(
                                out=PT[u2 % 2][:, :], in_=psS[u2 % 2][:, :], func=AF.Exp,
                                scale=0.125, bias=-EXP_SHIFT))
                            exp_done[(g, u2)] = ac.n
                            tq.wait(ac)
                            for s in range(2):
                                u = 2 * u2 + s
                                tq(lambda e, b=b, gb=gb, u=u, u2=u2, s=s, vbase=vbase: e.matmul(
                                    out=psM[gb][:, :], lhsT=vA[b][:, vbase + u, :],
                                    rhs=PT[u2 % 2][:, s * 512:(s + 1) * 512],
                                    start=(u == 0), stop=(u == 31)))
                            tq.inc_prev(pe, 1)
                        vq.wait(pe)
                        vq.inc(ve, 1, lambda e, gb=gb: e.reciprocal(out=rrow[:, :], in_=psM[gb][64:65, :]))
                        tq.wait(ve)
                        tq.inc(pe, 1, lambda e: e.matmul(out=psB[:, :], lhsT=ctx.ones_row[0:1, 0:64],
                                                         rhs=rrow[:, :], start=True, stop=True))
                        aq.wait(pe)
                        aq.inc(ac, 1, lambda e, gb=gb: e.copy(out=mU[:, :], in_=psM[gb][0:64, :]))
                        vq.wait(ac)
                        vq.inc(ve, 1, lambda e, p=p, half=half, g=g: e.tensor_tensor(
                            out=msaT[p][half * 64:half * 64 + 64, g * 512:(g + 1) * 512],
                            in0=mU[:, :], in1=psB[:, :], op=OP.mult))
                pair_use.append(pe.n)
            ctx.flush("l0a_attn")

        with ctx.scope():
            Wpb = ctx.sb("l0a_Wpb", [128, 8, 1024], BF16)
            bp = ctx.sb("l0a_bp", [1, 1024], F32)
            xb = [ctx.sb(f"l0a_px{i}", [128, 1024], F32) for i in range(2)]
            t1 = [ctx.sb(f"l0a_t1{i}", [128, 1024], F32) for i in range(2)]
            ot = [ctx.sb(f"l0a_ot{i}", [128, 1024], F32) for i in range(2)]
            psP = [ctx.ps(f"l0a_psP{i}", [128, 1024], F32) for i in range(2)]
            gldw = ctx.sem("gldw")
            ldx = [ctx.sem("ldx0"), ctx.sem("ldx1")]
            sto = [ctx.sem("sto0"), ctx.sem("sto1")]
            ldc = ctx.sem("ldc")
            for j in range(8):
                gq.inc(gldw, 16, lambda e, j=j: e.dma_start(out=Wpb[:, j, :], in_=W_proj[j * 128:(j + 1) * 128, :]))
            sy.inc(ldc, 16, lambda e: e.dma_start(out=bp[:, :], in_=b_proj[:, :]))
            tq.wait(gldw)
            tq.wait(ldc)
            ve_done = []
            for i in range(NT):
                ib = i % 2
                if i >= 2:
                    sy.wait(ve, ve_done[i - 2])
                sy.inc(ldx[ib], 16, lambda e, i=i, ib=ib: e.dma_start(out=xb[ib][:, :], in_=x_in[i * 128:(i + 1) * 128, :]))
                if i >= 2:
                    tq.wait(ve, ve_done[i - 2])
                for half in range(2):
                    for p in range(8):
                        tq(lambda e, i=i, ib=ib, half=half, p=p: e.matmul(
                            out=psP[ib][:, half * 512:(half + 1) * 512],
                            lhsT=msaT[p][:, i * 128:(i + 1) * 128],
                            rhs=Wpb[:, p, half * 512:(half + 1) * 512], start=(p == 0), stop=False))
                    tq(lambda e, ib=ib, half=half: e.matmul(
                        out=psP[ib][:, half * 512:(half + 1) * 512], lhsT=ctx.ones_row[0:1, 0:128],
                        rhs=bp[:, half * 512:(half + 1) * 512], start=False, stop=True))
                tq.inc_prev(pe, 1)
                vq.wait(ldx[ib])
                vq.wait(pe)
                if i >= 2:
                    vq.wait(sto[ib])
                vq(lambda e, i=i, ib=ib: e.tensor_scalar(out=t1[ib][:, :], in0=xb[ib][:, :],
                                                         scalar1=Bp[:, i:i + 1], scalar2=A1[:, i:i + 1],
                                                         op0=OP.mult, op1=OP.subtract))
                vq.drain()
                vq.inc(ve, 1, lambda e, ib=ib: e.tensor_tensor(out=ot[ib][:, :], in0=t1[ib][:, :],
                                                               in1=psP[ib][:, :], op=OP.add))
                ve_done.append(ve.n)
                gq.wait(ve)
                gq.inc(sto[ib], 16, lambda e, i=i, ib=ib: e.dma_start(out=x_out[i * 128:(i + 1) * 128, :],
                                                                      in_=ot[ib][:, :]))
            gq.wait(ctx.sem("sto0"))
            gq.wait(ctx.sem("sto1"))
            ctx.flush("l0a_proj")



def build_l0a_launch():
    nc = make_nc()
    stack = contextlib.ExitStack()
    ctx = Ctx(nc, stack)
    with stack:
        declare_consts(ctx)
        x_in = ctx.din("x_in", (T, D), F32)
        x_out = ctx.dout("x_out", (T, D), F32)
        ctx.din("L0_attn_a_W_qkv", (1024, 3072), F32)
        ctx.din("L0_attn_a_b_qkv_r", (1, 3072), F32)
        ctx.din("L0_attn_a_W_proj", (1024, 1024), F32)
        ctx.din("L0_attn_a_b_proj_r", (1, 1024), F32)
        ctx.din("L0_attn_a_ln_w_r", (128, 8), F32)
        ctx.din("L0_attn_a_ln_b_r", (128, 8), F32)
        emit_load_consts(ctx)
        emit_l0a(ctx, x_in, x_out)
        ctx.flush("fin")
    return nc


def l0a_inmaps(inp, xs):
    c = host_constants()
    base = {
        **{k: v for k, v in c.items()},
        "L0_attn_a_W_qkv": np.ascontiguousarray(inp["L0_attn_a_W_qkv"]),
        "L0_attn_a_b_qkv_r": np.ascontiguousarray(inp["L0_attn_a_b_qkv"].reshape(1, -1)),
        "L0_attn_a_W_proj": np.ascontiguousarray(inp["L0_attn_a_W_proj"]),
        "L0_attn_a_b_proj_r": np.ascontiguousarray(inp["L0_attn_a_b_proj"].reshape(1, -1)),
        "L0_attn_a_ln_w_r": np.ascontiguousarray(inp["L0_attn_a_ln_w"].reshape(8, 128).T),
        "L0_attn_a_ln_b_r": np.ascontiguousarray(inp["L0_attn_a_ln_b"].reshape(8, 128).T),
    }
    return [{**base, "x_in": np.ascontiguousarray(xs[i])} for i in range(NCORES)]


def run_l0a(inp, xs):
    nc = build_l0a_launch()
    res = run_bass_kernel_spmd(nc, l0a_inmaps(inp, xs), core_ids=list(range(NCORES)))
    return np.stack([r["x_out"] for r in res.results])



def emit_tern_prep(ctx, W_T, Wt_sb, nchunk, nfree, name):
    nc = ctx.nc
    ve, ac, pe = ctx.sem("ve"), ctx.sem("ac"), ctx.sem("pe")
    ldw = [ctx.sem("ldw0"), ctx.sem("ldw1")]
    sy, vq, tq, aq = ctx.sy, ctx.vq, ctx.tq, ctx.aq
    nelem = float(nchunk * 128 * nfree)
    with ctx.scope():
        wf = [ctx.sb(f"{name}_wf{i}", [128, nfree], F32) for i in range(2)]
        absw = ctx.sb(f"{name}_absw", [128, nfree], F32)
        parts = ctx.sb(f"{name}_parts", [128, nchunk], F32)
        thr = ctx.sb(f"{name}_thr", [128, 1], F32)
        srow = ctx.sb(f"{name}_srow", [1, nchunk], F32)
        sgn = ctx.sb(f"{name}_sgn", [128, nfree], F32)
        psr = ctx.ps(f"{name}_psr", [1, nchunk], F32)
        psb = ctx.ps(f"{name}_psb", [128, 1], F32)
        ve_done = []
        for j in range(nchunk):
            if j >= 2:
                sy.wait(ve, ve_done[j - 2])
            sy.inc(ldw[j % 2], 16, lambda e, j=j: e.dma_start(out=wf[j % 2][:, :],
                                                             in_=W_T[j * 128:(j + 1) * 128, :]))
            vq.wait(ldw[j % 2])
            vq.inc(ve, 1, lambda e, j=j: e.tensor_reduce(out=parts[:, j:j + 1], in_=wf[j % 2][:, :],
                                                         axis=mybir.AxisListType.X, op=OP.add,
                                                         apply_absolute_value=True))
            ve_done.append(ve.n)
        tq.wait(ve)
        tq.inc(pe, 1, lambda e: e.matmul(out=psr[:, :], lhsT=ctx.onescol_f32[:, 0:1], rhs=parts[:, :],
                                         start=True, stop=True))
        vq.wait(pe)
        vq(lambda e: e.tensor_reduce(out=srow[:, 0:1], in_=psr[:, :], axis=mybir.AxisListType.X, op=OP.add))
        vq.drain()
        vq(lambda e: e.tensor_scalar(out=srow[:, 0:1], in0=srow[:, 0:1], scalar1=1.0 / nelem,
                                     scalar2=1e-8, op0=OP.mult, op1=OP.max))
        vq.drain()
        vq.inc(ve, 1, lambda e: e.tensor_scalar(out=srow[:, 0:1], in0=srow[:, 0:1], scalar1=0.5,
                                                scalar2=None, op0=OP.mult))
        tq.wait(ve)
        tq.inc(pe, 1, lambda e: e.matmul(out=psb[:, :], lhsT=ctx.ones_row[0:1, 0:128], rhs=srow[:, 0:1],
                                         start=True, stop=True))
        aq.wait(pe)
        aq.inc(ac, 1, lambda e: e.copy(out=thr[:, :], in_=psb[:, :]))
        sy.wait(ac)
        ac_done = []
        for j in range(nchunk):
            if j >= 2:
                sy.wait(ac, ac_done[j - 2])
            sy.inc(ldw[j % 2], 16, lambda e, j=j: e.dma_start(out=wf[j % 2][:, :],
                                                             in_=W_T[j * 128:(j + 1) * 128, :]))
            aq.wait(ldw[j % 2])
            aq.wait(ve)
            aq(lambda e, j=j: e.activation(out=absw[:, :], in_=wf[j % 2][:, :], func=AF.Abs))
            aq.drain()
            aq.inc(ac, 1, lambda e, j=j: e.activation(out=sgn[:, :], in_=wf[j % 2][:, :], func=AF.Sign))
            ac_done.append(ac.n)
            vq.wait(ac)
            vq(lambda e, j=j: e.tensor_scalar(out=absw[:, :], in0=absw[:, :], scalar1=thr[:, 0:1],
                                              scalar2=None, op0=OP.is_gt))
            vq.drain()
            vq.inc(ve, 1, lambda e, j=j: e.tensor_tensor(out=Wt_sb[:, j, :], in0=absw[:, :], in1=sgn[:, :],
                                                         op=OP.mult))
        ctx.flush(f"{name}_prep")


def emit_ffn(ctx, x_in, x_out, tag):
    nc = ctx.nc
    ve, ac, pe = ctx.sem("ve"), ctx.sem("ac"), ctx.sem("pe")
    ldc = ctx.sem("ldc")
    sy, vq, tq, aq, gq = ctx.sy, ctx.vq, ctx.tq, ctx.aq, ctx.gq
    W1T = ctx.dram["L0_ffn_W1T"]
    W2T = ctx.dram["L0_ffn_W2T"]
    b1r = ctx.dram["L0_ffn_b1_r"]
    b2r = ctx.dram["L0_ffn_b2_r"]

    with ctx.scope():
        A1 = ctx.sb(f"{tag}_A1", [128, NT], F32)
        B1 = ctx.sb(f"{tag}_B1", [128, NT], F32)
        emit_ln_stats(ctx, x_in, A1, B1, name=f"{tag}_s1")

        W1t = ctx.sb(f"{tag}_W1t", [128, 8, 4096], BF16)
        W2t = ctx.sb(f"{tag}_W2t", [128, 32, 1024], BF16)
        emit_tern_prep(ctx, W1T, W1t, 8, 4096, f"{tag}_w1")
        emit_tern_prep(ctx, W2T, W2t, 32, 1024, f"{tag}_w2")

        with ctx.scope():
            b1c = ctx.sb(f"{tag}_b1c", [128, 32], F32)
            b2 = ctx.sb(f"{tag}_b2", [1, 1024], F32)
            sy.inc(ldc, 16, lambda e: e.dma_start(out=b1c[:, :], in_=b1r[:, :]))
            sy.inc(ldc, 16, lambda e: e.dma_start(out=b2[:, :], in_=b2r[:, :]))
            sy.wait(ldc)
            ctx.flush(f"{tag}_bias")

            tn = TileNorm(ctx, f"{tag}_m", x_in, A1, B1)
            hb = [ctx.sb(f"{tag}_hb{i}", [128, 128], BF16) for i in range(2)]
            ot = [ctx.sb(f"{tag}_ot{i}", [128, 1024], F32) for i in range(2)]
            psH = [ctx.ps(f"{tag}_psH{i}", [128, 128], F32) for i in range(2)]
            psO = [ctx.ps(f"{tag}_psO{i}", [128, 1024], F32) for i in range(2)]
            sto = [ctx.sem("sto0"), ctx.sem("sto1")]
            ldx = [ctx.sem("ldx0"), ctx.sem("ldx1")]
            ve_done = []
            for i in range(NT):
                ib = i % 2
                tn.record(i)
                tq.wait(ac)
                if i >= 2:
                    tq.wait(ve, ve_done[i - 2])
                for c in range(32):
                    cb = c % 2
                    if c >= 2:
                        tq.wait(ac)
                    for j in range(8):
                        tq(lambda e, i=i, c=c, j=j, cb=cb: e.matmul(
                            out=psH[cb][:, :], lhsT=W1t[:, j, c * 128:(c + 1) * 128],
                            rhs=tn.cur(i)[:, j * 128:(j + 1) * 128], start=(j == 0), stop=(j == 7)))
                    tq.inc_prev(pe, 1)
                    aq.wait(pe)
                    aq.inc(ac, 1, lambda e, c=c, cb=cb: e.activation(
                        out=hb[cb][:, :], in_=psH[cb][:, :], func=AF.Gelu, bias=b1c[:, c:c + 1]))
                    tq.wait(ac)
                    for half in range(2):
                        tq(lambda e, ib=ib, c=c, cb=cb, half=half: e.matmul(
                            out=psO[ib][:, half * 512:(half + 1) * 512], lhsT=hb[cb][:, :],
                            rhs=W2t[:, c, half * 512:(half + 1) * 512], start=(c == 0), stop=False))
                    tq.inc_prev(pe, 1)
                for half in range(2):
                    tq(lambda e, ib=ib, half=half: e.matmul(
                        out=psO[ib][:, half * 512:(half + 1) * 512], lhsT=ctx.ones_row[0:1, 0:128],
                        rhs=b2[:, half * 512:(half + 1) * 512], start=False, stop=True))
                tq.inc_prev(pe, 1)
                tn.note_consumed(pe, pe.n)
                vq.wait(pe)
                if i >= 2:
                    vq.wait(sto[ib])
                vq.inc(ve, 1, lambda e, i=i, ib=ib: e.tensor_tensor(
                    out=ot[ib][:, :], in0=tn.xb[ib][:, :], in1=psO[ib][:, :], op=OP.add))
                tn.ve_done[i] = ve.n
                ve_done.append(ve.n)
                gq.wait(ve)
                gq.inc(sto[ib], 16, lambda e, i=i, ib=ib: e.dma_start(
                    out=x_out[i * 128:(i + 1) * 128, :], in_=ot[ib][:, :]))
            gq.wait(ctx.sem("sto0"))
            gq.wait(ctx.sem("sto1"))
            ctx.flush(f"{tag}_main")


def build_ffn_launch():
    nc = make_nc()
    stack = contextlib.ExitStack()
    ctx = Ctx(nc, stack)
    with stack:
        declare_consts(ctx)
        x_in = ctx.din("x_in", (T, D), F32)
        x_out = ctx.dout("x_out", (T, D), F32)
        ctx.din("L0_ffn_W1T", (1024, 4096), F32)
        ctx.din("L0_ffn_W2T", (4096, 1024), F32)
        ctx.din("L0_ffn_b1_r", (128, 32), F32)
        ctx.din("L0_ffn_b2_r", (1, 1024), F32)
        emit_load_consts(ctx)
        emit_ffn(ctx, x_in, x_out, "ffn")
        ctx.flush("fin")
    return nc


def ffn_inmaps(inp, xs):
    c = host_constants()
    base = {
        **c,
        "L0_ffn_W1T": np.ascontiguousarray(inp["L0_ffn_W1"].T),
        "L0_ffn_W2T": np.ascontiguousarray(inp["L0_ffn_W2"].T),
        "L0_ffn_b1_r": np.ascontiguousarray(inp["L0_ffn_b1"].reshape(32, 128).T),
        "L0_ffn_b2_r": np.ascontiguousarray(inp["L0_ffn_b2"].reshape(1, -1)),
    }
    return [{**base, "x_in": np.ascontiguousarray(xs[i])} for i in range(NCORES)]


def run_ffn(inp, xs):
    nc = build_ffn_launch()
    res = run_bass_kernel_spmd(nc, ffn_inmaps(inp, xs), core_ids=list(range(NCORES)))
    return np.stack([r["x_out"] for r in res.results])



def emit_ret(ctx, x_in, x_out, tag="ret"):
    nc = ctx.nc
    ve, ac, pe = ctx.sem("ve"), ctx.sem("ac"), ctx.sem("pe")
    sy, vq, tq, aq, gq = ctx.sy, ctx.vq, ctx.tq, ctx.aq, ctx.gq
    WQ, WK, WV = (ctx.dram[f"L0_attn_b_W_{x}"] for x in "QKV")
    sc_q, ss_q = ctx.dram["ret_sc_q"], ctx.dram["ret_ss_q"]
    sc_k, ss_k = ctx.dram["ret_sc_k"], ctx.dram["ret_ss_k"]
    Dt = ctx.dram["ret_Dt"]
    QTd = ctx.dtmp(f"{tag}_QTd", (8, 128, T), BF16)
    KTd = ctx.dtmp(f"{tag}_KTd", (8, 128, T), BF16)
    Vd = ctx.dtmp(f"{tag}_Vd", (T, D), BF16)

    with ctx.scope():
        A1 = ctx.sb(f"{tag}_A1", [128, NT], F32)
        B1 = ctx.sb(f"{tag}_B1", [128, NT], F32)
        emit_ln_stats(ctx, x_in, A1, B1, name=f"{tag}_s1")

        with ctx.scope():
            Wb = ctx.sb(f"{tag}_Wb", [128, 8, 3072], BF16)
            gldw = ctx.sem("gldw")
            for m, W in enumerate((WQ, WK, WV)):
                for j in range(8):
                    gq.inc(gldw, 16, lambda e, m=m, j=j, W=W: e.dma_start(
                        out=Wb[:, j, m * 1024:(m + 1) * 1024], in_=W[j * 128:(j + 1) * 128, :]))
            gq.wait(gldw)
            ctx.flush(f"{tag}_wload")

            tn = TileNorm(ctx, f"{tag}_m", x_in, A1, B1)
            ldt = [ctx.sem("ldt0"), ctx.sem("ldt1")]
            stq = [ctx.sem("stq0"), ctx.sem("stq1")]
            tab = [ctx.sb(f"{tag}_tab{i}", [128, 4, 1024], F32) for i in range(2)]
            Qf = ctx.sb(f"{tag}_Qf", [128, 1024], F32)
            rb = ctx.sb(f"{tag}_rb", [128, 1024], F32)
            ta = ctx.sb(f"{tag}_ta", [128, 1024], F32)
            qkb = [ctx.sb(f"{tag}_qkb{i}", [128, 2048], BF16) for i in range(2)]
            vb = [ctx.sb(f"{tag}_vb{i}", [128, 1024], BF16) for i in range(2)]
            qkT = [ctx.sb(f"{tag}_qkT{i}", [128, 2048], BF16) for i in range(2)]
            psA = ctx.ps(f"{tag}_psA", [128, 1024], F32)
            psT = ctx.ps(f"{tag}_psT", [128, 2048], F32)
            ve_done2 = []
            ac_done2 = []
            tp_done = []
            for i in range(NT):
                ib = i % 2
                tn.record(i)
                if i >= 2:
                    sy.wait(ve, ve_done2[-1])
                sy.inc(ldt[ib], 16, lambda e, i=i, ib=ib: e.dma_start(
                    out=tab[ib][:, 0, :], in_=sc_q[i * 128:(i + 1) * 128, :]))
                sy.inc(ldt[ib], 16, lambda e, i=i, ib=ib: e.dma_start(
                    out=tab[ib][:, 1, :], in_=ss_q[i * 128:(i + 1) * 128, :]))
                sy.inc(ldt[ib], 16, lambda e, i=i, ib=ib: e.dma_start(
                    out=tab[ib][:, 2, :], in_=sc_k[i * 128:(i + 1) * 128, :]))
                sy.inc(ldt[ib], 16, lambda e, i=i, ib=ib: e.dma_start(
                    out=tab[ib][:, 3, :], in_=ss_k[i * 128:(i + 1) * 128, :]))
                tq.wait(ac)
                for m in range(3):
                    if not (i == 0 and m == 0):
                        tq.wait(ac)
                    for half in range(2):
                        for j in range(8):
                            tq(lambda e, i=i, m=m, half=half, j=j: e.matmul(
                                out=psA[:, half * 512:(half + 1) * 512],
                                lhsT=tn.cur(i)[:, j * 128:(j + 1) * 128],
                                rhs=Wb[:, j, m * 1024 + half * 512:m * 1024 + (half + 1) * 512],
                                start=(j == 0), stop=(j == 7)))
                    tq.inc_prev(pe, 1)
                    aq.wait(pe)
                    if m < 2:
                        if ve_done2:
                            aq.wait(ve, ve_done2[-1])
                        aq.inc(ac, 1, lambda e: e.copy(out=Qf[:, :], in_=psA[:, :]))
                        vq.wait(ac)
                        vq.wait(ldt[ib])
                        q3o = Qf.rearrange("p (n two) -> p n two", two=2)
                        r3o = rb.rearrange("p (n two) -> p n two", two=2)
                        vq(lambda e, q3=q3o, r3=r3o: e.tensor_scalar(
                            out=r3[:, :, 0], in0=q3[:, :, 1], scalar1=-1.0, scalar2=None, op0=OP.mult))
                        vq(lambda e, q3=q3o, r3=r3o: e.tensor_copy(out=r3[:, :, 1], in_=q3[:, :, 0]))
                        vq.drain()
                        vq(lambda e, ib=ib, m=m: e.tensor_tensor(
                            out=ta[:, :], in0=Qf[:, :], in1=tab[ib][:, 2 * m, :], op=OP.mult))
                        vq(lambda e, ib=ib, m=m: e.tensor_tensor(
                            out=rb[:, :], in0=rb[:, :], in1=tab[ib][:, 2 * m + 1, :], op=OP.mult))
                        vq.drain()
                        if i >= 2 and m == 0:
                            vq.wait(pe, tp_done[i - 2])
                        vq.inc(ve, 1, lambda e, ib=ib, m=m: e.tensor_tensor(
                            out=qkb[ib][:, m * 1024:(m + 1) * 1024], in0=ta[:, :], in1=rb[:, :], op=OP.add))
                        ve_done2.append(ve.n)
                    else:
                        if i >= 2:
                            aq.wait(stq[ib])
                        aq.inc(ac, 1, lambda e, ib=ib: e.copy(out=vb[ib][:, :], in_=psA[:, :]))
                        gq.wait(ac)
                        gq.inc(stq[ib], 16, lambda e, i=i, ib=ib: e.dma_start(
                            out=Vd[i * 128:(i + 1) * 128, :], in_=vb[ib][:, :]))
                tq.wait(ve)
                if ac_done2:
                    tq.wait(ac, ac_done2[-1])
                for j in range(16):
                    tq(lambda e, ib=ib, j=j: e.matmul(
                        out=psT[:, j * 128:(j + 1) * 128],
                        lhsT=qkb[ib][:, j * 128:(j + 1) * 128],
                        rhs=ctx.iden_bf[:, :], start=True, stop=True))
                tq.inc_prev(pe, 1)
                tp_done.append(pe.n)
                tn.note_consumed(pe, pe.n)
                aq.wait(pe)
                if i >= 2:
                    aq.wait(stq[ib])
                aq.inc(ac, 1, lambda e, ib=ib: e.copy(out=qkT[ib][:, :], in_=psT[:, :]))
                ac_done2.append(ac.n)
                gq.wait(ac)
                gq.inc(stq[ib], 16, lambda e, i=i, ib=ib: e.dma_start(
                    out=QTd[:, :, i * 128:(i + 1) * 128].rearrange("j p c -> p j c"),
                    in_=qkT[ib][:, 0:1024].rearrange("p (j c) -> p j c", c=128)))
                gq.inc(stq[ib], 16, lambda e, i=i, ib=ib: e.dma_start(
                    out=KTd[:, :, i * 128:(i + 1) * 128].rearrange("j p c -> p j c"),
                    in_=qkT[ib][:, 1024:2048].rearrange("p (j c) -> p j c", c=128)))
            gq.wait(ctx.sem("stq0"))
            gq.wait(ctx.sem("stq1"))
            ctx.flush(f"{tag}_r1")

        with ctx.scope():
            ldx = [ctx.sem("ldx0"), ctx.sem("ldx1")]
            ldt = [ctx.sem("ldt0"), ctx.sem("ldt1")]
            ldk = [ctx.sem("ldk0"), ctx.sem("ldk1")]
            sto = [ctx.sem("sto0"), ctx.sem("sto1")]
            ldv = ctx.sem("ldv0")
            Vr = ctx.sb(f"{tag}_Vr", [128, NT, 1024], BF16)
            QTg = [ctx.sb(f"{tag}_QTg{i}", [128, 8, 512], BF16) for i in range(2)]
            KTc = [ctx.sb(f"{tag}_KTc{i}", [128, 8, 128], BF16) for i in range(2)]
            Dtb = [ctx.sb(f"{tag}_Dtb{i}", [128, 512], F32) for i in range(2)]
            SDt = ctx.sb(f"{tag}_SDt", [128, NT, 512], BF16)
            xb = [ctx.sb(f"{tag}_rx{i}", [128, 1024], F32) for i in range(2)]
            ot = [ctx.sb(f"{tag}_rot{i}", [128, 1024], F32) for i in range(2)]
            psS = [ctx.ps(f"{tag}_psS{i}", [128, 512], F32) for i in range(2)]
            psO = [ctx.ps(f"{tag}_psO{i}", [128, 1024], F32) for i in range(2)]
            sy.inc(ldv, 16, lambda e: e.dma_start(
                out=Vr[:, :, :], in_=Vd.rearrange("(n p) c -> p n c", p=128)))
            tq.wait(ldv)
            ve_done3 = []
            qg_done = []
            mc_pe = []
            sd_ve = []
            for g in range(8):
                gb = g % 2
                nmc = 4 * (g + 1)
                if g >= 2:
                    sy.wait(pe, qg_done[g - 2])
                for j in range(8):
                    sy.inc(ldt[gb], 16, lambda e, g=g, gb=gb, j=j: e.dma_start(
                        out=QTg[gb][:, j, :], in_=QTd[j, :, g * 512:(g + 1) * 512]))
                tq.wait(ldt[gb])
                for mc in range(nmc):
                    mb = mc % 2
                    if len(mc_pe) >= 2:
                        sy.wait(pe, mc_pe[-2])
                    if len(sd_ve) >= 2:
                        sy.wait(ve, sd_ve[-2])
                    for j in range(8):
                        sy.inc(ldk[mb], 16, lambda e, mc=mc, mb=mb, j=j: e.dma_start(
                            out=KTc[mb][:, j, :], in_=KTd[j, :, mc * 128:(mc + 1) * 128]))
                    sy.inc(ldk[mb], 16, lambda e, g=g, mc=mc, mb=mb: e.dma_start(
                        out=Dtb[mb][:, :], in_=Dt[mc * 128:(mc + 1) * 128, g * 512:(g + 1) * 512]))
                    if mc >= 2:
                        tq.wait(ve)
                    tq.wait(ldk[mb])
                    for j in range(8):
                        tq(lambda e, gb=gb, mb=mb, j=j: e.matmul(
                            out=psS[mb][:, :], lhsT=KTc[mb][:, j, :], rhs=QTg[gb][:, j, :],
                            start=(j == 0), stop=(j == 7)))
                    tq.inc_prev(pe, 1)
                    mc_pe.append(pe.n)
                    vq.wait(pe)
                    vq.inc(ve, 1, lambda e, mc=mc, mb=mb: e.tensor_tensor(
                        out=SDt[:, mc, :], in0=psS[mb][:, :], in1=Dtb[mb][:, :], op=OP.mult))
                    sd_ve.append(ve.n)
                for tt in range(4):
                    i = g * 4 + tt
                    ib = i % 2
                    if ve_done3:
                        sy.wait(ve, ve_done3[-1])
                    sy.inc(ldx[ib], 16, lambda e, i=i, ib=ib: e.dma_start(
                        out=xb[ib][:, :], in_=x_in[i * 128:(i + 1) * 128, :]))
                    tq.wait(ve)
                    if len(ve_done3) >= 2:
                        tq.wait(ve, ve_done3[-2])
                    for half in range(2):
                        for mc in range(nmc):
                            tq(lambda e, tt=tt, mc=mc, ib=ib, half=half, nmc=nmc: e.matmul(
                                out=psO[ib][:, half * 512:(half + 1) * 512],
                                lhsT=SDt[:, mc, tt * 128:(tt + 1) * 128],
                                rhs=Vr[:, mc, half * 512:(half + 1) * 512],
                                start=(mc == 0), stop=(mc == nmc - 1)))
                    tq.inc_prev(pe, 1)
                    if tt == 3:
                        qg_done.append(pe.n)
                    vq.wait(pe)
                    vq.wait(ldx[ib])
                    if len(ve_done3) >= 2:
                        vq.wait(sto[ib])
                    vq.inc(ve, 1, lambda e, ib=ib: e.tensor_tensor(
                        out=ot[ib][:, :], in0=xb[ib][:, :], in1=psO[ib][:, :], op=OP.add))
                    ve_done3.append(ve.n)
                    gq.wait(ve)
                    gq.inc(sto[ib], 16, lambda e, i=i, ib=ib: e.dma_start(
                        out=x_out[i * 128:(i + 1) * 128, :], in_=ot[ib][:, :]))
            gq.wait(ctx.sem("sto0"))
            gq.wait(ctx.sem("sto1"))
            ctx.flush(f"{tag}_r2")


def ret_tables():
    SCALE_BASE = 512.0
    GAMMA = 0.96875
    d = D
    half = d // 2
    scale_vec = (np.arange(0, d, 2, dtype=np.float64) + 0.4 * d) / (1.4 * d)
    pos = np.arange(T, dtype=np.float64)
    scale = scale_vec[None, :] ** (pos[:, None] / SCALE_BASE)
    inv_freq = 1.0 / 10000.0 ** (np.arange(half, dtype=np.float64) / half)
    sinusoid = pos[:, None] * inv_freq[None, :]
    sin_s = np.repeat(np.sin(sinusoid) * scale, 2, axis=-1)
    cos_s = np.repeat(np.cos(sinusoid) * scale, 2, axis=-1)
    sin_d = np.repeat(np.sin(sinusoid) / scale, 2, axis=-1)
    cos_d = np.repeat(np.cos(sinusoid) / scale, 2, axis=-1)
    n = np.arange(T)[:, None]
    m = np.arange(T)[None, :]
    diff = (n - m).astype(np.float64)
    Dm = np.where(diff >= 0, GAMMA ** np.maximum(diff, 0), 0.0)
    return {
        "ret_sc_q": cos_s.astype(np.float32), "ret_ss_q": sin_s.astype(np.float32),
        "ret_sc_k": cos_d.astype(np.float32), "ret_ss_k": sin_d.astype(np.float32),
        "ret_Dt": np.ascontiguousarray(Dm.T.astype(np.float32)),
    }


def build_ret_launch():
    nc = make_nc()
    stack = contextlib.ExitStack()
    ctx = Ctx(nc, stack)
    with stack:
        declare_consts(ctx)
        x_in = ctx.din("x_in", (T, D), F32)
        x_out = ctx.dout("x_out", (T, D), F32)
        for x in "QKV":
            ctx.din(f"L0_attn_b_W_{x}", (1024, 1024), F32)
        for t in ("ret_sc_q", "ret_ss_q", "ret_sc_k", "ret_ss_k"):
            ctx.din(t, (T, D), F32)
        ctx.din("ret_Dt", (T, T), F32)
        emit_load_consts(ctx)
        emit_ret(ctx, x_in, x_out)
        ctx.flush("fin")
    return nc


def ret_inmaps(inp, xs):
    c = host_constants()
    tabs = ret_tables()
    base = {
        **c, **tabs,
        "L0_attn_b_W_Q": np.ascontiguousarray(inp["L0_attn_b_W_Q"]),
        "L0_attn_b_W_K": np.ascontiguousarray(inp["L0_attn_b_W_K"]),
        "L0_attn_b_W_V": np.ascontiguousarray(inp["L0_attn_b_W_V"]),
    }
    return [{**base, "x_in": np.ascontiguousarray(xs[i])} for i in range(NCORES)]


def run_ret(inp, xs):
    nc = build_ret_launch()
    res = run_bass_kernel_spmd(nc, ret_inmaps(inp, xs), core_ids=list(range(NCORES)))
    return np.stack([r["x_out"] for r in res.results])



def emit_gla(ctx, x_in, x_out, tag="gla"):
    nc = ctx.nc
    ve, ac, pe = ctx.sem("ve"), ctx.sem("ac"), ctx.sem("pe")
    sy, vq, tq, aq, gq = ctx.sy, ctx.vq, ctx.tq, ctx.aq, ctx.gq
    Wq, Wk, Wv, Wg = (ctx.dram[f"L1_attn_a_W{x}"] for x in ("q", "k", "v", "g"))
    Wo = ctx.dram["L1_attn_a_Wo"]
    qtTd = ctx.dtmp(f"{tag}_qtTd", (8, 128, T), BF16)
    ktTd = ctx.dtmp(f"{tag}_ktTd", (8, 128, T), BF16)
    ktd = ctx.dtmp(f"{tag}_ktd", (T, D), BF16)
    vtd = ctx.dtmp(f"{tag}_vtd", (T, D), BF16)
    hvd = ctx.dtmp(f"{tag}_hvd", (NT, 128, 16), F32)

    with ctx.scope():
        A1 = ctx.sb(f"{tag}_A1", [128, NT], F32)
        B1 = ctx.sb(f"{tag}_B1", [128, NT], F32)
        emit_ln_stats(ctx, x_in, A1, B1, name=f"{tag}_s1")

        with ctx.scope():
            Wb = ctx.sb(f"{tag}_Wb", [128, 8, 4096], BF16)
            gldw = ctx.sem("gldw")
            for m, W in enumerate((Wq, Wk, Wv, Wg)):
                for j in range(8):
                    gq.inc(gldw, 16, lambda e, m=m, j=j, W=W: e.dma_start(
                        out=Wb[:, j, m * 1024:(m + 1) * 1024], in_=W[j * 128:(j + 1) * 128, :]))
            gq.wait(gldw)
            ctx.flush(f"{tag}_wload")

            tn = TileNorm(ctx, f"{tag}_m", x_in, A1, B1)
            stq = [ctx.sem("stq0"), ctx.sem("stq1")]
            qkf = [ctx.sb(f"{tag}_qkf{i}", [128, 2048], F32) for i in range(2)]
            gf = [ctx.sb(f"{tag}_gf{i}", [128, 1024], F32) for i in range(2)]
            tmp1 = [ctx.sb(f"{tag}_tmp1{i}", [128, 1024], F32) for i in range(2)]
            lna = [ctx.sb(f"{tag}_lna{i}", [128, 1024], F32) for i in range(2)]
            ss = [ctx.sb(f"{tag}_ss{i}", [128, 16], F32) for i in range(2)]
            qkn = [ctx.sb(f"{tag}_qkn{i}", [128, 2048], BF16) for i in range(2)]
            qknT = [ctx.sb(f"{tag}_qknT{i}", [128, 2048], BF16) for i in range(2)]
            lnaT = ctx.sb(f"{tag}_lnaT", [128, 1024], F32)
            csT = ctx.sb(f"{tag}_csT", [128, 8, 128], F32)
            hv = [ctx.sb(f"{tag}_hv{i}", [128, 16], F32) for i in range(2)]
            earg = ctx.sb(f"{tag}_earg", [128, 2048], F32)
            BB = ctx.sb(f"{tag}_BB", [128, 2048], BF16)
            qkt = [ctx.sb(f"{tag}_qkt{i}", [128, 2048], BF16) for i in range(2)]
            kvb = [ctx.sb(f"{tag}_kvb{i}", [128, 2048], BF16) for i in range(2)]
            psA = ctx.ps(f"{tag}_psA", [128, 2048], F32)
            psT = ctx.ps(f"{tag}_psT", [128, 1024], F32)
            done = []
            for i in range(NT):
                ib = i % 2
                d = {}
                tn.record(i)
                tq.wait(ac)
                if i >= 2:
                    tq.wait(ve, done[i - 2]["all_ve"])
                for m in range(2):
                    for half in range(2):
                        for j in range(8):
                            tq(lambda e, i=i, m=m, half=half, j=j: e.matmul(
                                out=psA[:, m * 1024 + half * 512:m * 1024 + (half + 1) * 512],
                                lhsT=tn.cur(i)[:, j * 128:(j + 1) * 128],
                                rhs=Wb[:, j, m * 1024 + half * 512:m * 1024 + (half + 1) * 512],
                                start=(j == 0), stop=(j == 7)))
                tq.inc_prev(pe, 1)
                aq.wait(pe)
                aq.inc(ac, 1, lambda e, ib=ib: e.copy(out=qkf[ib][:, :], in_=psA[:, :]))
                d["qk_evac"] = ac.n
                aq.drain()
                aq.inc(ac, 1, lambda e, ib=ib: e.activation(out=tmp1[ib][:, :], in_=qkf[ib][:, 0:1024], func=AF.Square))
                vq.wait(ac)
                for s in range(8):
                    vq(lambda e, ib=ib, s=s: e.tensor_reduce(out=ss[ib][:, s:s + 1],
                                                             in_=tmp1[ib][:, s * 128:(s + 1) * 128],
                                                             axis=mybir.AxisListType.X, op=OP.add))
                vq.drain()
                vq.inc(ve, 1, lambda e, ib=ib: e.tensor_scalar(
                    out=ss[ib][:, 0:8], in0=ss[ib][:, 0:8], scalar1=1e-24, scalar2=None, op0=OP.max))
                aq.wait(ve)
                aq.inc(ac, 1, lambda e, ib=ib: e.activation(out=tmp1[ib][:, :], in_=qkf[ib][:, 1024:2048], func=AF.Square))
                vq.wait(ac)
                for s in range(8):
                    vq(lambda e, ib=ib, s=s: e.tensor_reduce(out=ss[ib][:, 8 + s:9 + s],
                                                             in_=tmp1[ib][:, s * 128:(s + 1) * 128],
                                                             axis=mybir.AxisListType.X, op=OP.add))
                vq.drain()
                vq.inc(ve, 1, lambda e, ib=ib: e.tensor_scalar(
                    out=ss[ib][:, 8:16], in0=ss[ib][:, 8:16], scalar1=1e-24, scalar2=None, op0=OP.max))
                aq.wait(ve)
                aq.drain()
                aq(lambda e, ib=ib: e.activation(out=ss[ib][:, :], in_=ss[ib][:, :], func=AF.Ln))
                aq.drain()
                aq.inc(ac, 1, lambda e, ib=ib: e.activation(out=ss[ib][:, :], in_=ss[ib][:, :], func=AF.Exp, scale=-0.5))
                d["rstd"] = ac.n
                vq.wait(ac, d["rstd"])
                if i >= 2:
                    vq.wait(pe, done[i - 2]["qkn_tr"])
                for s in range(16):
                    vq(lambda e, ib=ib, s=s: e.tensor_scalar(
                        out=qkn[ib][:, s * 128:(s + 1) * 128], in0=qkf[ib][:, s * 128:(s + 1) * 128],
                        scalar1=ss[ib][:, s:s + 1], scalar2=None, op0=OP.mult))
                vq.drain()
                vq.inc(ve, 1, lambda e, ib=ib: e.tensor_copy(out=qkn[ib][:, 0:1], in_=qkn[ib][:, 0:1]))
                d["qkn"] = ve.n
                tq.wait(ac, d["qk_evac"])
                for m in range(2):
                    for half in range(2):
                        for j in range(8):
                            tq(lambda e, i=i, m=m, half=half, j=j: e.matmul(
                                out=psA[:, m * 1024 + half * 512:m * 1024 + (half + 1) * 512],
                                lhsT=tn.cur(i)[:, j * 128:(j + 1) * 128],
                                rhs=Wb[:, j, (2 + m) * 1024 + half * 512:(2 + m) * 1024 + (half + 1) * 512],
                                start=(j == 0), stop=(j == 7)))
                tq.inc_prev(pe, 1)
                tn.note_consumed(pe, pe.n)
                aq.wait(pe)
                if i >= 2:
                    aq.wait(stq[ib])
                aq.inc(ac, 1, lambda e, ib=ib: e.copy(out=kvb[ib][:, 1024:2048], in_=psA[:, 0:1024]))
                aq.inc(ac, 1, lambda e, ib=ib: e.copy(out=gf[ib][:, :], in_=psA[:, 1024:2048]))
                d["vg_evac"] = ac.n
                aq.drain()
                aq(lambda e, ib=ib: e.activation(out=tmp1[ib][:, :], in_=gf[ib][:, :], func=AF.Abs))
                aq.drain()
                aq(lambda e, ib=ib: e.activation(out=tmp1[ib][:, :], in_=tmp1[ib][:, :], func=AF.Exp, scale=-1.0))
                aq.drain()
                aq.inc(ac, 1, lambda e, ib=ib: e.activation(out=tmp1[ib][:, :], in_=tmp1[ib][:, :], func=AF.Ln, bias=1.0))
                vq.wait(ac)
                vq(lambda e, ib=ib: e.tensor_scalar(out=lna[ib][:, :], in0=gf[ib][:, :], scalar1=0.0,
                                                    scalar2=None, op0=OP.min))
                vq.drain()
                vq.inc(ve, 1, lambda e, ib=ib: e.tensor_tensor(out=lna[ib][:, :], in0=lna[ib][:, :],
                                                               in1=tmp1[ib][:, :], op=OP.subtract))
                d["lna"] = ve.n
                for which in range(2):
                    tq.wait(ve, d["qkn"])
                    tq.wait(ac)
                    for j in range(8):
                        tq(lambda e, ib=ib, which=which, j=j: e.matmul(
                            out=psT[:, j * 128:(j + 1) * 128],
                            lhsT=qkn[ib][:, which * 1024 + j * 128:which * 1024 + (j + 1) * 128],
                            rhs=ctx.iden_bf[:, :], start=True, stop=True))
                    tq.inc_prev(pe, 1)
                    aq.wait(pe)
                    aq.inc(ac, 1, lambda e, ib=ib, which=which: e.copy(
                        out=qknT[ib][:, which * 1024:(which + 1) * 1024], in_=psT[:, :]))
                d["qkn_tr"] = pe.n
                tq.wait(ve, d["lna"])
                tq.wait(ac)
                for j in range(8):
                    tq(lambda e, ib=ib, j=j: e.matmul(
                        out=psT[:, j * 128:(j + 1) * 128],
                        lhsT=lna[ib][:, j * 128:(j + 1) * 128],
                        rhs=ctx.iden_f32[:, :], start=True, stop=True))
                tq.inc_prev(pe, 1)
                d["lna_tr"] = pe.n
                aq.wait(pe)
                aq.inc(ac, 1, lambda e: e.copy(out=lnaT[:, :], in_=psT[:, :]))
                vq.wait(ac)
                for j in range(8):
                    for c in range(2):
                        vq(lambda e, j=j, c=c: e.tensor_tensor_scan(
                            out=csT[:, j, c * 64:(c + 1) * 64],
                            data0=lnaT[:, j * 128 + c * 64:j * 128 + c * 64 + 64],
                            data1=lnaT[:, j * 128 + c * 64:j * 128 + c * 64 + 64],
                            initial=0.0, op0=OP.add, op1=OP.bypass))
                vq.drain()
                if i >= 2:
                    vq.wait(stq[ib])
                vq(lambda e, ib=ib: e.tensor_scalar(
                    out=hv[ib][:, 0:8], in0=csT[:, :, 63:64].rearrange("p j one -> p (j one)"),
                    scalar1=0.5, scalar2=None, op0=OP.mult))
                vq.drain()
                vq.inc(ve, 1, lambda e, ib=ib: e.tensor_scalar(
                    out=hv[ib][:, 8:16], in0=csT[:, :, 127:128].rearrange("p j one -> p (j one)"),
                    scalar1=0.5, scalar2=None, op0=OP.mult))
                d["hv"] = ve.n
                vq.drain()
                for j in range(8):
                    for c in range(2):
                        vq(lambda e, ib=ib, j=j, c=c: e.tensor_scalar(
                            out=earg[:, j * 128 + c * 64:j * 128 + c * 64 + 64],
                            in0=csT[:, j, c * 64:(c + 1) * 64],
                            scalar1=hv[ib][:, c * 8 + j:c * 8 + j + 1], scalar2=85.0,
                            op0=OP.subtract, op1=OP.min))
                        vq(lambda e, ib=ib, j=j, c=c: e.tensor_scalar(
                            out=earg[:, 1024 + j * 128 + c * 64:1024 + j * 128 + c * 64 + 64],
                            in0=csT[:, j, c * 64:(c + 1) * 64],
                            scalar1=hv[ib][:, c * 8 + j:c * 8 + j + 1], scalar2=-85.0,
                            op0=OP.subtract, op1=OP.max))
                vq.drain()
                vq.inc(ve, 1, lambda e: e.tensor_copy(out=earg[:, 0:1], in_=earg[:, 0:1]))
                aq.wait(ve)
                aq(lambda e: e.activation(out=BB[:, 0:1024], in_=earg[:, 0:1024], func=AF.Exp))
                aq.drain()
                aq.inc(ac, 1, lambda e: e.activation(out=BB[:, 1024:2048], in_=earg[:, 1024:2048],
                                                     func=AF.Exp, scale=-1.0))
                vq.wait(ac)
                if i >= 2:
                    vq.wait(pe, done[i - 2]["ktok_tr"])
                vq(lambda e, ib=ib: e.tensor_tensor(out=qkt[ib][:, 0:1024], in0=qknT[ib][:, 0:1024],
                                                    in1=BB[:, 0:1024], op=OP.mult))
                vq.drain()
                vq.inc(ve, 1, lambda e, ib=ib: e.tensor_tensor(out=qkt[ib][:, 1024:2048], in0=qknT[ib][:, 1024:2048],
                                                               in1=BB[:, 1024:2048], op=OP.mult))
                d["qkt"] = ve.n
                tq.wait(ve, d["qkt"])
                tq.wait(ac)
                for j in range(8):
                    tq(lambda e, ib=ib, j=j: e.matmul(
                        out=psT[:, j * 128:(j + 1) * 128],
                        lhsT=qkt[ib][:, 1024 + j * 128:1024 + (j + 1) * 128],
                        rhs=ctx.iden_bf[:, :], start=True, stop=True))
                tq.inc_prev(pe, 1)
                d["ktok_tr"] = pe.n
                aq.wait(pe)
                if i >= 2:
                    aq.wait(stq[ib])
                aq.inc(ac, 1, lambda e, ib=ib: e.copy(out=kvb[ib][:, 0:1024], in_=psT[:, :]))
                d["all_ve"] = ve.n
                gq.wait(ac)
                gq.wait(ve, d["hv"])
                gq.inc(stq[ib], 16, lambda e, i=i, ib=ib: e.dma_start(
                    out=qtTd[:, :, i * 128:(i + 1) * 128].rearrange("j p c -> p j c"),
                    in_=qkt[ib][:, 0:1024].rearrange("p (j c) -> p j c", c=128)))
                gq.inc(stq[ib], 16, lambda e, i=i, ib=ib: e.dma_start(
                    out=ktTd[:, :, i * 128:(i + 1) * 128].rearrange("j p c -> p j c"),
                    in_=qkt[ib][:, 1024:2048].rearrange("p (j c) -> p j c", c=128)))
                gq.inc(stq[ib], 16, lambda e, i=i, ib=ib: e.dma_start(
                    out=ktd[i * 128:(i + 1) * 128, :], in_=kvb[ib][:, 0:1024]))
                gq.inc(stq[ib], 16, lambda e, i=i, ib=ib: e.dma_start(
                    out=vtd[i * 128:(i + 1) * 128, :], in_=kvb[ib][:, 1024:2048]))
                gq.inc(stq[ib], 16, lambda e, i=i, ib=ib: e.dma_start(
                    out=hvd[i, :, :], in_=hv[ib][:, :]))
                done.append(d)
            gq.wait(ctx.sem("stq0"))
            gq.wait(ctx.sem("stq1"))
            ctx.flush(f"{tag}_g1")

        with ctx.scope():
            ldx = [ctx.sem("ldx0"), ctx.sem("ldx1")]
            ldt = [ctx.sem("ldt0"), ctx.sem("ldt1")]
            sto = [ctx.sem("sto0"), ctx.sem("sto1")]
            ldv = ctx.sem("ldv0")
            gldw = ctx.sem("gldw")
            Wob = ctx.sb(f"{tag}_Wob", [128, 8, 1024], BF16)
            for j in range(8):
                gq.inc(gldw, 16, lambda e, j=j: e.dma_start(out=Wob[:, j, :], in_=Wo[j * 128:(j + 1) * 128, :]))
            S = [ctx.sb(f"{tag}_S{h}", [128, 128], F32) for h in range(8)]
            Sb = [ctx.sb(f"{tag}_Sb{h}", [128, 128], BF16) for h in range(8)]
            hva = ctx.sb(f"{tag}_hva", [128, NT, 2, 8], F32)
            benda = ctx.sb(f"{tag}_benda", [128, 64, 8], F32)
            qkTt = [ctx.sb(f"{tag}_qkTt{i}", [128, 2048], BF16) for i in range(2)]
            kvt = [ctx.sb(f"{tag}_kvt{i}", [128, 2048], BF16) for i in range(2)]
            msk = ctx.sb(f"{tag}_msk", [128, 64], BF16)
            zeros64 = ctx.sb(f"{tag}_z64", [128, 64], BF16)
            otile = [ctx.sb(f"{tag}_oti{i}", [128, 1024], BF16) for i in range(2)]
            oT = [ctx.sb(f"{tag}_oT{i}", [128, 1024], BF16) for i in range(2)]
            xb = [ctx.sb(f"{tag}_gx{i}", [128, 1024], F32) for i in range(2)]
            ot = [ctx.sb(f"{tag}_got{i}", [128, 1024], F32) for i in range(2)]
            psO = ctx.ps(f"{tag}_psO", [128, 128], F32)
            psSc = ctx.ps(f"{tag}_psSc", [128, 64], F32)
            psKV = ctx.ps(f"{tag}_psKV", [128, 128], F32)
            psTo = ctx.ps(f"{tag}_psTo", [128, 1024], F32)
            psO2 = ctx.ps(f"{tag}_psO2", [128, 1024], F32)
            for h in range(8):
                vq(lambda e, h=h: e.memset(S[h][:, :], 0.0))
                vq(lambda e, h=h: e.memset(Sb[h][:, :], 0.0))
            vq(lambda e: e.memset(zeros64[:, :], 0.0))
            sy.inc(ldv, 16, lambda e: e.dma_start(
                out=hva[:, :, :, :], in_=hvd.rearrange("n p c -> p n c").rearrange("p n (c j) -> p n c j", c=2)))
            vq.wait(ldv)
            hflat = hva.rearrange("p n c j -> p (n c) j")
            vq(lambda e: e.tensor_tensor(out=benda[:, 0:63, :], in0=hflat[:, 0:63, :],
                                         in1=hflat[:, 1:64, :], op=OP.add))
            vq(lambda e: e.tensor_copy(out=benda[:, 63:64, :], in_=hflat[:, 63:64, :]))
            vq.drain()
            vq.inc(ve, 1, lambda e: e.tensor_copy(out=benda[:, 0:1, 0:1], in_=benda[:, 0:1, 0:1]))
            aq.wait(ve)
            aq.inc(ac, 1, lambda e: e.activation(out=benda[:, :, :], in_=benda[:, :, :], func=AF.Exp))
            tq.wait(gldw)
            tq.wait(ac)
            done2 = []
            for i in range(NT):
                ib = i % 2
                d = {}
                if i >= 2:
                    sy.wait(pe, done2[i - 2]["used"])
                sy.inc(ldt[ib], 16, lambda e, i=i, ib=ib: e.dma_start(
                    out=qkTt[ib][:, 0:1024].rearrange("p (j c) -> p j c", c=128),
                    in_=qtTd[:, :, i * 128:(i + 1) * 128].rearrange("j p c -> p j c")))
                sy.inc(ldt[ib], 16, lambda e, i=i, ib=ib: e.dma_start(
                    out=qkTt[ib][:, 1024:2048].rearrange("p (j c) -> p j c", c=128),
                    in_=ktTd[:, :, i * 128:(i + 1) * 128].rearrange("j p c -> p j c")))
                sy.inc(ldt[ib], 16, lambda e, i=i, ib=ib: e.dma_start(
                    out=kvt[ib][:, 0:1024], in_=ktd[i * 128:(i + 1) * 128, :]))
                sy.inc(ldt[ib], 16, lambda e, i=i, ib=ib: e.dma_start(
                    out=kvt[ib][:, 1024:2048], in_=vtd[i * 128:(i + 1) * 128, :]))
                if i >= 2:
                    sy.wait(ve, done2[i - 2]["ep_ve"])
                sy.inc(ldx[ib], 16, lambda e, i=i, ib=ib: e.dma_start(
                    out=xb[ib][:, :], in_=x_in[i * 128:(i + 1) * 128, :]))
                tq.wait(ldt[ib])
                for c in range(2):
                    p0 = 64 * c
                    mchunk = 2 * i + c
                    for h in range(8):
                        if (c, h) != (0, 0):
                            tq.wait(ac)
                        tq(lambda e, ib=ib, c=c, h=h, p0=p0: e.matmul(
                            out=psO[p0:p0 + 64, :], lhsT=qkTt[ib][:, h * 128 + p0:h * 128 + p0 + 64],
                            rhs=Sb[h][:, :], start=True, stop=False))
                        tq(lambda e, ib=ib, c=c, h=h, p0=p0: e.matmul(
                            out=psSc[p0:p0 + 64, :], lhsT=qkTt[ib][:, 1024 + h * 128 + p0:1024 + h * 128 + p0 + 64],
                            rhs=qkTt[ib][:, h * 128 + p0:h * 128 + p0 + 64], start=True, stop=True))
                        tq.inc_prev(pe, 1)
                        vq.wait(pe)
                        vq.inc(ve, 1, lambda e, c=c, p0=p0: e.select(
                            out=msk[p0:p0 + 64, :], mask=ctx.utm_u8[p0:p0 + 64, :],
                            on_true=psSc[p0:p0 + 64, :], on_false=zeros64[p0:p0 + 64, :],
                            add_drain=True))
                        tq.wait(ve)
                        tq(lambda e, ib=ib, c=c, h=h, p0=p0: e.matmul(
                            out=psO[p0:p0 + 64, :], lhsT=msk[p0:p0 + 64, :],
                            rhs=kvt[ib][p0:p0 + 64, 1024 + h * 128:1024 + (h + 1) * 128],
                            start=False, stop=True))
                        tq(lambda e, ib=ib, c=c, h=h, p0=p0: e.matmul(
                            out=psKV[:, :], lhsT=kvt[ib][p0:p0 + 64, h * 128:(h + 1) * 128],
                            rhs=kvt[ib][p0:p0 + 64, 1024 + h * 128:1024 + (h + 1) * 128],
                            start=True, stop=True))
                        tq.inc_prev(pe, 1)
                        vq.wait(pe)
                        vq(lambda e, h=h: e.tensor_tensor(out=S[h][:, :], in0=S[h][:, :], in1=psKV[:, :], op=OP.add))
                        vq.drain()
                        vq.inc(ve, 1, lambda e, h=h, mchunk=mchunk: e.tensor_scalar(
                            out=S[h][:, :], in0=S[h][:, :], scalar1=benda[:, mchunk, h:h + 1],
                            scalar2=None, op0=OP.mult))
                        aq.wait(ve)
                        aq(lambda e, h=h: e.copy(out=Sb[h][:, :], in_=S[h][:, :]))
                        aq.drain()
                        aq.inc(ac, 1, lambda e, ib=ib, c=c, h=h, p0=p0: e.copy(
                            out=otile[ib][p0:p0 + 64, h * 128:(h + 1) * 128], in_=psO[p0:p0 + 64, :]))
                tq.wait(ac)
                if i >= 2:
                    tq.wait(ac, done2[i - 2]["oT_evac"])
                for j in range(8):
                    tq(lambda e, ib=ib, j=j: e.matmul(
                        out=psTo[:, j * 128:(j + 1) * 128], lhsT=otile[ib][:, j * 128:(j + 1) * 128],
                        rhs=ctx.iden_bf[:, :], start=True, stop=True))
                tq.inc_prev(pe, 1)
                d["used"] = pe.n
                aq.wait(pe)
                aq.inc(ac, 1, lambda e, ib=ib: e.copy(out=oT[ib][:, :], in_=psTo[:, :]))
                d["oT_evac"] = ac.n
                tq.wait(ac)
                if i >= 1:
                    tq.wait(ve, done2[i - 1]["ep_ve"])
                for half in range(2):
                    for j in range(8):
                        tq(lambda e, ib=ib, half=half, j=j: e.matmul(
                            out=psO2[:, half * 512:(half + 1) * 512], lhsT=oT[ib][:, j * 128:(j + 1) * 128],
                            rhs=Wob[:, j, half * 512:(half + 1) * 512], start=(j == 0), stop=(j == 7)))
                tq.inc_prev(pe, 1)
                vq.wait(pe)
                vq.wait(ldx[ib])
                if i >= 2:
                    vq.wait(sto[ib])
                vq.inc(ve, 1, lambda e, ib=ib: e.tensor_tensor(
                    out=ot[ib][:, :], in0=xb[ib][:, :], in1=psO2[:, :], op=OP.add))
                d["ep_ve"] = ve.n
                gq.wait(ve)
                gq.inc(sto[ib], 16, lambda e, i=i, ib=ib: e.dma_start(
                    out=x_out[i * 128:(i + 1) * 128, :], in_=ot[ib][:, :]))
                done2.append(d)
            gq.wait(ctx.sem("sto0"))
            gq.wait(ctx.sem("sto1"))
            ctx.flush(f"{tag}_g2")


def build_gla_launch():
    nc = make_nc()
    stack = contextlib.ExitStack()
    ctx = Ctx(nc, stack)
    with stack:
        declare_consts(ctx)
        x_in = ctx.din("x_in", (T, D), F32)
        x_out = ctx.dout("x_out", (T, D), F32)
        for x in ("q", "k", "v", "g", "o"):
            ctx.din(f"L1_attn_a_W{x}", (1024, 1024), F32)
        emit_load_consts(ctx)
        emit_gla(ctx, x_in, x_out)
        ctx.flush("fin")
    return nc


def gla_inmaps(inp, xs):
    c = host_constants()
    base = {**c}
    for x in ("q", "k", "v", "g", "o"):
        base[f"L1_attn_a_W{x}"] = np.ascontiguousarray(inp[f"L1_attn_a_W{x}"])
    return [{**base, "x_in": np.ascontiguousarray(xs[i])} for i in range(NCORES)]


def run_gla(inp, xs):
    nc = build_gla_launch()
    res = run_bass_kernel_spmd(nc, gla_inmaps(inp, xs), core_ids=list(range(NCORES)))
    return np.stack([r["x_out"] for r in res.results])



def emit_lra(ctx, x_in, x_out, tag="lra"):
    nc = ctx.nc
    ve, ac, pe = ctx.sem("ve"), ctx.sem("ac"), ctx.sem("pe")
    sy, vq, tq, aq, gq = ctx.sy, ctx.vq, ctx.tq, ctx.aq, ctx.gq
    Qt, Kt = ctx.dram["lra_Qt"], ctx.dram["lra_Kt"]
    VO = ctx.dram["L1_attn_b_VO"]
    xqTd = ctx.dtmp(f"{tag}_xqTd", (8, 128, T), BF16)
    xkTd = ctx.dtmp(f"{tag}_xkTd", (8, 128, T), BF16)
    xhTd = ctx.dtmp(f"{tag}_xhTd", (8, 128, T), BF16)
    PTds = [ctx.dtmp(f"{tag}_PTd{i}", (4, T, T), BF16) for i in range(4)]
    PTd_of = lambda h: PTds[h // 4][h % 4]

    with ctx.scope():
        A1 = ctx.sb(f"{tag}_A1", [128, NT], F32)
        B1 = ctx.sb(f"{tag}_B1", [128, NT], F32)
        emit_ln_stats(ctx, x_in, A1, B1, name=f"{tag}_s1")

        with ctx.scope():
            QKb = ctx.sb(f"{tag}_QKb", [128, 8, 2048], BF16)
            gldw = ctx.sem("gldw")
            for m, W in enumerate((Qt, Kt)):
                for j in range(8):
                    gq.inc(gldw, 16, lambda e, m=m, j=j, W=W: e.dma_start(
                        out=QKb[:, j, m * 1024:(m + 1) * 1024], in_=W[j * 128:(j + 1) * 128, :]))
            gq.wait(gldw)
            ctx.flush(f"{tag}_wload")

            tn = TileNorm(ctx, f"{tag}_m", x_in, A1, B1)
            stq = [ctx.sem("stq0"), ctx.sem("stq1")]
            qko = [ctx.sb(f"{tag}_qko{i}", [128, 2048], BF16) for i in range(2)]
            psQ = ctx.ps(f"{tag}_psQ", [128, 2048], F32)
            ac_done = []
            for i in range(NT):
                ib = i % 2
                tn.record(i)
                tq.wait(ac)
                for m in range(2):
                    for blk in range(8):
                        for j in range(8):
                            tq(lambda e, i=i, m=m, blk=blk, j=j: e.matmul(
                                out=psQ[:, m * 1024 + blk * 128:m * 1024 + (blk + 1) * 128],
                                lhsT=QKb[:, j, m * 1024 + blk * 128:m * 1024 + (blk + 1) * 128],
                                rhs=tn.cur(i)[:, j * 128:(j + 1) * 128],
                                start=(j == 0), stop=(j == 7)))
                tq.inc_prev(pe, 1)
                aq.wait(pe)
                if i >= 2:
                    aq.wait(stq[ib])
                aq.inc(ac, 1, lambda e, ib=ib: e.copy(out=qko[ib][:, :], in_=psQ[:, :]))
                ac_done.append(ac.n)
                gq.wait(ac)
                gq.inc(stq[ib], 16, lambda e, i=i, ib=ib: e.dma_start(
                    out=xqTd[:, :, i * 128:(i + 1) * 128].rearrange("j p c -> p j c"),
                    in_=qko[ib][:, 0:1024].rearrange("p (j c) -> p j c", c=128)))
                gq.inc(stq[ib], 16, lambda e, i=i, ib=ib: e.dma_start(
                    out=xkTd[:, :, i * 128:(i + 1) * 128].rearrange("j p c -> p j c"),
                    in_=qko[ib][:, 1024:2048].rearrange("p (j c) -> p j c", c=128)))
                gq.inc(stq[ib], 16, lambda e, i=i, ib=ib: e.dma_start(
                    out=xhTd[:, :, i * 128:(i + 1) * 128].rearrange("j p c -> p j c"),
                    in_=tn.cur(i).rearrange("p (j c) -> p j c", c=128)))
                tn.note_consumed(ctx.sems["stq0" if ib == 0 else "stq1"], ctx.sems["stq0" if ib == 0 else "stq1"].n)
            gq.wait(ctx.sem("stq0"))
            gq.wait(ctx.sem("stq1"))
            ctx.flush(f"{tag}_b1")

        with ctx.scope():
            ldx = [ctx.sem("ldx0"), ctx.sem("ldx1")]
            ldt = [ctx.sem("ldt0"), ctx.sem("ldt1")]
            ldk = [ctx.sem("ldk0"), ctx.sem("ldk1")]
            ldp = [ctx.sem("ldp0"), ctx.sem("ldp1")]
            ldv = ctx.sem("ldv0")
            sto = [ctx.sem("sto0"), ctx.sem("sto1")]
            stq = [ctx.sem("stq0"), ctx.sem("stq1")]
            gldw = ctx.sem("gldw")
            acc = ctx.sb(f"{tag}_acc", [128, NT, 512], F32)
            Yres = ctx.sb(f"{tag}_Yres", [128, NT, 512], BF16)
            xqh = ctx.sb(f"{tag}_xqh", [64, T], BF16)
            xkh = ctx.sb(f"{tag}_xkh", [64, T], BF16)
            VOb = [ctx.sb(f"{tag}_VOb{i}", [128, 8, 512], BF16) for i in range(2)]
            xhb = [ctx.sb(f"{tag}_xhb{i}", [128, 8, 128], BF16) for i in range(2)]
            PT = [ctx.sb(f"{tag}_PT{i}", [128, 512], BF16) for i in range(2)]
            dnrow = ctx.sb(f"{tag}_dnrow", [1, 512], F32)
            rden = ctx.sb(f"{tag}_rden", [128, 16, 8, 4], F32)
            xb = [ctx.sb(f"{tag}_lx{i}", [128, 512], F32) for i in range(2)]
            ot = [ctx.sb(f"{tag}_lot{i}", [128, 512], F32) for i in range(2)]
            psSY = [ctx.ps(f"{tag}_psSY{i}", [128, 512], F32) for i in range(2)]
            psO = [ctx.ps(f"{tag}_psO{i}", [128, 512], F32) for i in range(4)]
            psDen = ctx.ps(f"{tag}_psDen", [1, 512], F32)
            psDen2 = ctx.ps(f"{tag}_psDen2", [128, 4], F32)
            for eh in range(2):
                if eh == 1:
                    sy.wait(stq[0])
                    sy.wait(stq[1])
                vq.drain()
                vq(lambda e: e.memset(acc[:, :, :], 0.0))
                vq.drain()
                vq.inc(ve, 1, lambda e: e.tensor_copy(out=acc[:, 0:1, 0:1], in_=acc[:, 0:1, 0:1]))
                acc_zero = ve.n
                tq.wait(ve, acc_zero)
                for h in range(16):
                    hb = h % 2
                    sy.wait(pe)
                    sy.inc(ldk[0], 16, lambda e, h=h: e.dma_start(
                        out=xqh[:, :], in_=xqTd[h // 2, (h % 2) * 64:(h % 2) * 64 + 64, :]))
                    sy.inc(ldk[0], 16, lambda e, h=h: e.dma_start(
                        out=xkh[:, :], in_=xkTd[h // 2, (h % 2) * 64:(h % 2) * 64 + 64, :]))
                    gq.wait(pe)
                    for j in range(8):
                        gq.inc(gldw, 16, lambda e, eh=eh, h=h, hb=hb, j=j: e.dma_start(
                            out=VOb[hb][:, j, :], in_=VO[h, j * 128:(j + 1) * 128, eh * 512:(eh + 1) * 512]))
                    tq.wait(gldw)
                    ac_y = []
                    for u in range(NT):
                        ub = u % 2
                        if u >= 2:
                            sy.wait(pe)
                        sy.inc(ldt[ub], 16, lambda e, u=u, ub=ub: e.dma_start(
                            out=xhb[ub][:, :, :],
                            in_=xhTd[:, :, u * 128:(u + 1) * 128].rearrange("j p c -> p j c")))
                        tq.wait(ldt[ub])
                        if u >= 2:
                            tq.wait(ac, ac_y[u - 2])
                        for j in range(8):
                            tq(lambda e, ub=ub, j=j, hb=hb: e.matmul(
                                out=psSY[ub][:, :], lhsT=xhb[ub][:, j, :], rhs=VOb[hb][:, j, :],
                                start=(j == 0), stop=(j == 7)))
                        tq.inc_prev(pe, 1)
                        aq.wait(pe)
                        aq.inc(ac, 1, lambda e, u=u, ub=ub: e.copy(out=Yres[:, u, :], in_=psSY[ub][:, :]))
                        ac_y.append(ac.n)
                    tq.wait(ac)
                    tq.wait(ctx.sems["ldk0"])
                    pt_pe = []
                    for tg in range(8):
                        for u in range(NT):
                            ub = u % 2
                            if eh == 0:
                                if (tg, u) != (0, 0):
                                    tq.wait(ac)
                                tq(lambda e, ub=ub, u=u, tg=tg: e.matmul(
                                    out=psSY[ub][:, :], lhsT=xkh[:, u * 128:(u + 1) * 128],
                                    rhs=xqh[:, tg * 512:(tg + 1) * 512], start=True, stop=True))
                                tq.inc_prev(pe, 1)
                                aq.wait(pe)
                                aq.wait(stq[ub])
                                aq.inc(ac, 1, lambda e, ub=ub: e.activation(
                                    out=PT[ub][:, :], in_=psSY[ub][:, :], func=AF.Exp,
                                    scale=1.0 / 32.0, bias=-EXP_SHIFT))
                                gq.wait(ac)
                                gq.inc(stq[ub], 16, lambda e, h=h, u=u, tg=tg, ub=ub: e.dma_start(
                                    out=PTd_of(h)[u * 128:(u + 1) * 128, tg * 512:(tg + 1) * 512],
                                    in_=PT[ub][:, :]))
                                tq.wait(ac)
                            else:
                                if len(pt_pe) >= 2:
                                    sy.wait(pe, pt_pe[-2])
                                sy.inc(ldp[ub], 16, lambda e, h=h, u=u, tg=tg, ub=ub: e.dma_start(
                                    out=PT[ub][:, :], in_=PTd_of(h)[u * 128:(u + 1) * 128, tg * 512:(tg + 1) * 512]))
                                tq.wait(ldp[ub])
                            for ts in range(4):
                                tq(lambda e, ub=ub, ts=ts, hb=hb, u=u, h=h, tg=tg: e.matmul(
                                    out=psO[ts][:, :], lhsT=PT[ub][:, ts * 128:(ts + 1) * 128],
                                    rhs=Yres[:, u, :], start=(u == 0), stop=(u == NT - 1)))
                            if eh == 0:
                                tq(lambda e, ub=ub, u=u: e.matmul(
                                    out=psDen[:, :], lhsT=ctx.onescol_bf[:, 0:1], rhs=PT[ub][:, :],
                                    start=(u == 0), stop=(u == NT - 1)))
                            tq.inc_prev(pe, 1)
                            pt_pe.append(pe.n)
                        if eh == 0:
                            aq.wait(pe)
                            aq.inc(ac, 1, lambda e: e.copy(out=dnrow[:, :], in_=psDen[:, :]))
                            tq.wait(ac)
                            for ts in range(4):
                                tq(lambda e, ts=ts: e.transpose(out=psDen2[:, ts:ts + 1],
                                                                in_=dnrow[:, ts * 128:(ts + 1) * 128],
                                                                identity=ctx.iden_f32[0:1, 0:1]))
                            tq.inc_prev(pe, 1)
                            vq.wait(pe)
                            vq.inc(ve, 1, lambda e, h=h, tg=tg: e.reciprocal(
                                out=rden[:, h, tg, :], in_=psDen2[:, :]))
                        vq.wait(pe)
                        if eh == 0:
                            vq.wait(ve)
                        for ts in range(4):
                            vq(lambda e, h=h, tg=tg, ts=ts: e.scalar_tensor_tensor(
                                out=acc[:, tg * 4 + ts, :], in0=psO[ts][:, :],
                                scalar=rden[:, h, tg, ts:ts + 1], op0=OP.mult,
                                op1=OP.add, in1=acc[:, tg * 4 + ts, :]))
                        vq.drain()
                        vq.inc(ve, 1, lambda e: e.tensor_copy(out=acc[:, 0:1, 0:1], in_=acc[:, 0:1, 0:1]))
                        tq.wait(ve)
                vq.drain()
                for i in range(NT):
                    ib = i % 2
                    if i >= 2:
                        sy.wait(ve)
                    sy.inc(ldx[ib], 16, lambda e, i=i, ib=ib, eh=eh: e.dma_start(
                        out=xb[ib][:, :], in_=x_in[i * 128:(i + 1) * 128, eh * 512:(eh + 1) * 512]))
                    vq.wait(ldx[ib])
                    vq.wait(sto[ib])
                    vq.inc(ve, 1, lambda e, i=i, ib=ib: e.tensor_tensor(
                        out=ot[ib][:, :], in0=xb[ib][:, :], in1=acc[:, i, :], op=OP.add))
                    gq.wait(ve)
                    gq.inc(sto[ib], 16, lambda e, i=i, ib=ib, eh=eh: e.dma_start(
                        out=x_out[i * 128:(i + 1) * 128, eh * 512:(eh + 1) * 512], in_=ot[ib][:, :]))
            gq.wait(ctx.sem("sto0"))
            gq.wait(ctx.sem("sto1"))
            ctx.flush(f"{tag}_b2")


def build_lra_launch():
    nc = make_nc()
    stack = contextlib.ExitStack()
    ctx = Ctx(nc, stack)
    with stack:
        declare_consts(ctx)
        x_in = ctx.din("x_in", (T, D), F32)
        x_out = ctx.dout("x_out", (T, D), F32)
        ctx.din("lra_Qt", (1024, 1024), F32)
        ctx.din("lra_Kt", (1024, 1024), F32)
        ctx.din("L1_attn_b_VO", (16, 1024, 1024), F32)
        emit_load_consts(ctx)
        emit_lra(ctx, x_in, x_out)
        ctx.flush("fin")
    return nc


def lra_inmaps(inp, xs):
    c = host_constants()
    Q = inp["L1_attn_b_Q"]
    K = inp["L1_attn_b_K"]
    base = {
        **c,
        "lra_Qt": np.ascontiguousarray(Q.transpose(2, 0, 1).reshape(1024, 1024)),
        "lra_Kt": np.ascontiguousarray(K.transpose(2, 0, 1).reshape(1024, 1024)),
        "L1_attn_b_VO": np.ascontiguousarray(inp["L1_attn_b_VO"]),
    }
    return [{**base, "x_in": np.ascontiguousarray(xs[i])} for i in range(NCORES)]


def run_lra(inp, xs):
    nc = build_lra_launch()
    res = run_bass_kernel_spmd(nc, lra_inmaps(inp, xs), core_ids=list(range(NCORES)))
    return np.stack([r["x_out"] for r in res.results])



def emit_moe_pre(ctx, x_in, xh_out, route_out, cnt_out, tag):
    nc = ctx.nc
    ve, ac, pe = ctx.sem("ve"), ctx.sem("ac"), ctx.sem("pe")
    ldc = ctx.sem("ldc")
    sy, vq, tq, aq, gq = ctx.sy, ctx.vq, ctx.tq, ctx.aq, ctx.gq
    wgT = ctx.dram["moe_wgT"]

    with ctx.scope():
        A1 = ctx.sb(f"{tag}_A1", [128, NT], F32)
        B1 = ctx.sb(f"{tag}_B1", [128, NT], F32)
        emit_ln_stats(ctx, x_in, A1, B1, name=f"{tag}_s1")

        with ctx.scope():
            wg = ctx.sb(f"{tag}_wg", [128, 8, 8], F32)
            runrow = ctx.sb(f"{tag}_run", [1, 16], F32)
            sy.inc(ldc, 16, lambda e: e.dma_start(out=wg[:, :, :],
                                                  in_=wgT.rearrange("(j p) c -> p j c", p=128)))
            vq(lambda e: e.memset(runrow[:, :], 0.0))
            vq.drain()
            vq.inc(ve, 1, lambda e: e.tensor_copy(out=runrow[:, 0:1], in_=runrow[:, 0:1]))
            ctx.flush(f"{tag}_prep")

            tn = TileNorm(ctx, f"{tag}_m", x_in, A1, B1)
            stq = [ctx.sem("stq0"), ctx.sem("stq1")]
            zf = [ctx.sb(f"{tag}_zf{i}", [128, 1024], F32) for i in range(2)]
            xTf = ctx.sb(f"{tag}_xTf", [128, 1024], F32)
            lg = ctx.sb(f"{tag}_lg", [128, 8], F32)
            w1 = ctx.sb(f"{tag}_w1", [128, 8], F32)
            w2 = ctx.sb(f"{tag}_w2", [128, 8], F32)
            w3 = ctx.sb(f"{tag}_w3", [128, 8], F32)
            oh = ctx.sb(f"{tag}_oh", [128, 16], F32)
            mcol = ctx.sb(f"{tag}_mcol", [128, 8], F32)
            rt = [ctx.sb(f"{tag}_rt{i}", [128, 8], F32) for i in range(2)]
            psL = ctx.ps(f"{tag}_psL", [128, 8], F32)
            psR = ctx.ps(f"{tag}_psR", [128, 16], F32)
            psC = ctx.ps(f"{tag}_psC", [1, 16], F32)
            tq.wait(ldc)
            done = []
            for i in range(NT):
                ib = i % 2
                d = {}
                tn.record(i)
                tn.note_consumed(ctx.sem("ac"), 0)
                vq.wait(ctx.sems["ldx0" if ib == 0 else "ldx1"])
                if i >= 2:
                    vq.wait(pe, done[i - 2]["ztr"])
                vq.inc(ve, 1, lambda e, i=i, ib=ib: e.tensor_scalar(
                    out=zf[ib][:, :], in0=tn.xb[ib][:, :], scalar1=B1[:, i:i + 1],
                    scalar2=A1[:, i:i + 1], op0=OP.mult, op1=OP.subtract))
                tn.ve_done[i] = ve.n
                tq.wait(ve)
                tq.wait(ac)
                for j in range(8):
                    tq(lambda e, ib=ib, j=j: e.matmul(
                        out=tn.pTf[:, j * 128:(j + 1) * 128], lhsT=zf[ib][:, j * 128:(j + 1) * 128],
                        rhs=ctx.iden_f32[:, :], start=True, stop=True))
                tq.inc_prev(pe, 1)
                d["ztr"] = pe.n
                aq.wait(pe)
                aq.inc(ac, 1, lambda e: e.copy(out=xTf[:, :], in_=tn.pTf[:, :]))
                d["xTf"] = ac.n
                tn.ac_done[i] = ac.n
                tq.wait(ac)
                tq.wait(ve)
                for j in range(8):
                    tq(lambda e, j=j: e.matmul(out=psL[:, :], lhsT=xTf[:, j * 128:(j + 1) * 128],
                                               rhs=wg[:, j, :], start=(j == 0), stop=(j == 7)))
                tq.inc_prev(pe, 1)
                vq.wait(pe)
                vq(lambda e: e.tensor_reduce(out=mcol[:, 0:1], in_=psL[:, :], axis=mybir.AxisListType.X, op=OP.max))
                vq.drain()
                vq(lambda e: e.tensor_copy(out=lg[:, :], in_=psL[:, :]))
                vq.drain()
                if i >= 1:
                    vq.wait(pe, done[i - 1]["rkmm"])
                vq.wait(ctx.sems["stq0" if ib == 0 else "stq1"])
                vq(lambda e: e.tensor_scalar(out=w1[:, :], in0=lg[:, :], scalar1=mcol[:, 0:1],
                                             scalar2=None, op0=OP.is_ge))
                vq.drain()
                vq(lambda e: e.tensor_copy(out=w2[:, :], in_=w1[:, :]))
                vq.drain()
                vq(lambda e: e.tensor_tensor(out=w2[:, 1:8], in0=w2[:, 1:8], in1=w1[:, 0:7], op=OP.add))
                vq.drain()
                vq(lambda e: e.tensor_copy(out=w3[:, :], in_=w2[:, :]))
                vq.drain()
                vq(lambda e: e.tensor_tensor(out=w3[:, 2:8], in0=w3[:, 2:8], in1=w2[:, 0:6], op=OP.add))
                vq.drain()
                vq(lambda e: e.tensor_copy(out=w2[:, :], in_=w3[:, :]))
                vq.drain()
                vq(lambda e: e.tensor_tensor(out=w2[:, 4:8], in0=w2[:, 4:8], in1=w3[:, 0:4], op=OP.add))
                vq.drain()
                vq(lambda e: e.tensor_scalar(out=w2[:, :], in0=w2[:, :], scalar1=1.0, scalar2=None, op0=OP.is_le))
                vq.drain()
                vq(lambda e: e.tensor_tensor(out=oh[:, 0:8], in0=w1[:, :], in1=w2[:, :], op=OP.mult))
                vq.drain()
                vq(lambda e: e.tensor_scalar(out=w1[:, :], in0=oh[:, 0:8], scalar1=-1e30, scalar2=None, op0=OP.mult))
                vq.drain()
                vq(lambda e: e.tensor_tensor(out=w1[:, :], in0=w1[:, :], in1=lg[:, :], op=OP.add))
                vq.drain()
                vq(lambda e: e.tensor_reduce(out=mcol[:, 1:2], in_=w1[:, :], axis=mybir.AxisListType.X, op=OP.max))
                vq.drain()
                vq(lambda e: e.tensor_scalar(out=w2[:, :], in0=w1[:, :], scalar1=mcol[:, 1:2],
                                             scalar2=None, op0=OP.is_ge))
                vq.drain()
                vq(lambda e: e.tensor_copy(out=w3[:, :], in_=w2[:, :]))
                vq.drain()
                vq(lambda e: e.tensor_tensor(out=w3[:, 1:8], in0=w3[:, 1:8], in1=w2[:, 0:7], op=OP.add))
                vq.drain()
                vq(lambda e: e.tensor_copy(out=w1[:, :], in_=w3[:, :]))
                vq.drain()
                vq(lambda e: e.tensor_tensor(out=w1[:, 2:8], in0=w1[:, 2:8], in1=w3[:, 0:6], op=OP.add))
                vq.drain()
                vq(lambda e: e.tensor_copy(out=w3[:, :], in_=w1[:, :]))
                vq.drain()
                vq(lambda e: e.tensor_tensor(out=w3[:, 4:8], in0=w3[:, 4:8], in1=w1[:, 0:4], op=OP.add))
                vq.drain()
                vq(lambda e: e.tensor_scalar(out=w3[:, :], in0=w3[:, :], scalar1=1.0, scalar2=None, op0=OP.is_le))
                vq.drain()
                vq.inc(ve, 1, lambda e: e.tensor_tensor(out=oh[:, 8:16], in0=w2[:, :], in1=w3[:, :], op=OP.mult))
                d["oh"] = ve.n
                vq(lambda e: e.tensor_tensor(out=w1[:, :], in0=oh[:, 0:8], in1=ctx.iota8[:, :], op=OP.mult))
                vq.drain()
                vq(lambda e: e.tensor_reduce(out=mcol[:, 4:5], in_=w1[:, :], axis=mybir.AxisListType.X, op=OP.add))
                vq.drain()
                vq(lambda e: e.tensor_tensor(out=w1[:, :], in0=oh[:, 8:16], in1=ctx.iota8[:, :], op=OP.mult))
                vq.drain()
                vq(lambda e: e.tensor_reduce(out=mcol[:, 5:6], in_=w1[:, :], axis=mybir.AxisListType.X, op=OP.add))
                vq.drain()
                vq.inc(ve, 1, lambda e: e.tensor_scalar(out=mcol[:, 6:7], in0=mcol[:, 0:1],
                                                        scalar1=mcol[:, 1:2], scalar2=None, op0=OP.subtract))
                aq.wait(ve)
                aq.inc(ac, 1, lambda e: e.activation(out=mcol[:, 2:3], in_=mcol[:, 6:7], func=AF.Sigmoid))
                vq.wait(ac)
                vq.inc(ve, 1, lambda e: e.tensor_scalar(out=mcol[:, 3:4], in0=mcol[:, 2:3], scalar1=1.0,
                                                        scalar2=-1.0, op0=OP.subtract, op1=OP.mult))
                tq.wait(ve, d["oh"])
                if i >= 1:
                    tq.wait(ve)
                tq(lambda e: e.matmul(out=psR[:, 0:8], lhsT=ctx.ltx[:, :], rhs=oh[:, 0:8], start=True, stop=False))
                tq(lambda e: e.matmul(out=psR[:, 8:16], lhsT=ctx.ltx[:, :], rhs=oh[:, 8:16], start=False, stop=False))
                tq(lambda e: e.matmul(out=psR[:, 0:16], lhsT=ctx.ones_row[0:1, 0:128], rhs=runrow[:, :],
                                      start=False, stop=True))
                tq.inc_prev(pe, 1)
                tq(lambda e: e.matmul(out=psC[:, 0:8], lhsT=ctx.onescol_f32[:, :], rhs=oh[:, 0:8],
                                      start=True, stop=False))
                tq(lambda e: e.matmul(out=psC[:, 8:16], lhsT=ctx.onescol_f32[:, :], rhs=oh[:, 8:16],
                                      start=False, stop=True))
                tq.inc_prev(pe, 1)
                d["rkmm"] = pe.n
                vq.wait(pe)
                vq(lambda e: e.tensor_tensor(out=w1[:, :], in0=oh[:, 0:8], in1=psR[:, 0:8], op=OP.mult))
                vq.drain()
                vq(lambda e, ib=ib: e.tensor_reduce(out=rt[ib][:, 4:5], in_=w1[:, :], axis=mybir.AxisListType.X, op=OP.add))
                vq.drain()
                vq(lambda e: e.tensor_tensor(out=w1[:, :], in0=oh[:, 8:16], in1=psR[:, 8:16], op=OP.mult))
                vq.drain()
                vq(lambda e, ib=ib: e.tensor_reduce(out=rt[ib][:, 5:6], in_=w1[:, :], axis=mybir.AxisListType.X, op=OP.add))
                vq.drain()
                vq(lambda e: e.tensor_tensor(out=runrow[:, :], in0=runrow[:, :], in1=psC[:, :], op=OP.add))
                vq.drain()
                vq(lambda e, ib=ib: e.tensor_copy(out=rt[ib][:, 0:1], in_=mcol[:, 4:5]))
                vq(lambda e, ib=ib: e.tensor_copy(out=rt[ib][:, 1:2], in_=mcol[:, 5:6]))
                vq(lambda e, ib=ib: e.tensor_copy(out=rt[ib][:, 2:3], in_=mcol[:, 2:3]))
                vq(lambda e, ib=ib: e.tensor_copy(out=rt[ib][:, 3:4], in_=mcol[:, 3:4]))
                vq(lambda e, ib=ib: e.memset(rt[ib][:, 6:8], 0.0))
                vq.drain()
                vq.inc(ve, 1, lambda e, ib=ib: e.tensor_copy(out=rt[ib][:, 0:1], in_=rt[ib][:, 0:1]))
                d["rt"] = ve.n
                gq.wait(ve)
                if i >= 2:
                    gq.wait(stq[ib])
                gq.inc(stq[ib], 16, lambda e, i=i, ib=ib: e.dma_start(
                    out=xh_out[i * 128:(i + 1) * 128, :], in_=tn.zh[ib][:, :]))
                gq.inc(stq[ib], 16, lambda e, i=i, ib=ib: e.dma_start(
                    out=route_out[i * 128:(i + 1) * 128, :], in_=rt[ib][:, :]))
                tn.zh_consumed[i] = (ctx.sems["stq0" if ib == 0 else "stq1"],
                                     ctx.sems["stq0" if ib == 0 else "stq1"].n)
                done.append(d)
            gq.wait(ve)
            gq.inc(ctx.sem("sto0"), 16, lambda e: e.dma_start(out=cnt_out[:, :], in_=runrow[:, :]))
            gq.wait(ctx.sem("sto0"))
            gq.wait(ctx.sem("stq0"))
            gq.wait(ctx.sem("stq1"))
            ctx.flush(f"{tag}_main")


def build_moepre_launch():
    nc = make_nc()
    stack = contextlib.ExitStack()
    ctx = Ctx(nc, stack)
    with stack:
        declare_consts(ctx)
        x_in = ctx.din("x_in", (T, D), F32)
        ctx.din("moe_wgT", (1024, 8), F32)
        xh_out = ctx.dout("xh_out", (T, D), BF16)
        route_out = ctx.dout("route_out", (T, 8), F32)
        cnt_out = ctx.dout("cnt_out", (1, 16), F32)
        emit_load_consts(ctx)
        emit_moe_pre(ctx, x_in, xh_out, route_out, cnt_out, "mp")
        ctx.flush("fin")
    return nc


def moepre_inmaps(inp, xs):
    c = host_constants()
    base = {**c, "moe_wgT": np.ascontiguousarray(inp["L1_ffn_w_g"].T)}
    return [{**base, "x_in": np.ascontiguousarray(xs[i])} for i in range(NCORES)]



NGRAN = 15
GRAN = 1024


def emit_moe_main(ctx, x_in, x_out, xh_in, route_in, tag):
    nc = ctx.nc
    ve, ac, pe = ctx.sem("ve"), ctx.sem("ac"), ctx.sem("pe")
    ldc = ctx.sem("ldc")
    sy, vq, tq, aq, gq = ctx.sy, ctx.vq, ctx.tq, ctx.aq, ctx.gq
    allcnt = ctx.dram["moe_allcnt"]
    mycnt = ctx.dram["moe_mycnt"]
    mprev = ctx.dram["moe_mprev"]
    c_jp8 = ctx.dram["c_jp8"]
    c_jp32 = ctx.dram["c_jp32"]
    c_i128 = ctx.dram["c_i128"]
    cfc = ctx.dram["L1_ffn_c_fc"]
    cpj = ctx.dram["L1_ffn_c_proj"]
    cfcv = cfc.rearrange("e d f -> (e d) f")
    cpjv = cpj.rearrange("e f d -> (e f) d")
    xpkd = ctx.dtmp(f"{tag}_xpkd", (NGRAN * GRAN + 2 * T, 1024), BF16)
    opk = ctx.dtmp(f"{tag}_opk", (NGRAN * GRAN, 1024), BF16)
    dstd = ctx.dtmp(f"{tag}_dstd", (T, 2), U32)
    pmd = ctx.dtmp(f"{tag}_pmd", (T, 2), F32)

    with ctx.scope():
        srowb = ctx.sb(f"{tag}_srowb", [128, 48], F32)
        ebc = ctx.sb(f"{tag}_ebc", [128, 15], F32)
        idxfc = ctx.sb(f"{tag}_idxfc", [128, NGRAN, 8], U32)
        idxpj = ctx.sb(f"{tag}_idxpj", [128, NGRAN, 32], U32)
        zz = ctx.sb(f"{tag}_zz", [128, 1024], BF16)
        pfx = ctx.sem("pfx")
        with ctx.scope():
            ac8 = ctx.sb(f"{tag}_ac8", [8, 16], F32)
            mc1 = ctx.sb(f"{tag}_mc1", [1, 16], F32)
            mpv = ctx.sb(f"{tag}_mpv", [8, 1], F32)
            srow = ctx.sb(f"{tag}_srow", [1, 64], F32)
            w8 = ctx.sb(f"{tag}_w8", [1, 8], F32)
            w8b = ctx.sb(f"{tag}_w8b", [1, 8], F32)
            expr = ctx.sb(f"{tag}_expr", [1, 128], F32)
            ecol = ctx.sb(f"{tag}_ecol", [128, 1], F32)
            jp8 = ctx.sb(f"{tag}_jp8", [128, 8], F32)
            jp32 = ctx.sb(f"{tag}_jp32", [128, 32], F32)
            psS = ctx.ps(f"{tag}_psS", [1, 32], F32)
            psB = ctx.ps(f"{tag}_psB", [128, 48], F32)
            psE = ctx.ps(f"{tag}_psE", [128, 15], F32)
            psT1 = ctx.ps(f"{tag}_psT1", [128, 128], F32)
            sy.inc(ldc, 16, lambda e: e.dma_start(out=ac8[:, :], in_=allcnt[:, :]))
            sy.inc(ldc, 16, lambda e: e.dma_start(out=mc1[:, :], in_=mycnt[:, :]))
            sy.inc(ldc, 16, lambda e: e.dma_start(out=mpv[:, :], in_=mprev[:, :]))
            sy.inc(ldc, 16, lambda e: e.dma_start(out=jp8[:, :], in_=c_jp8[:, :]))
            sy.inc(ldc, 16, lambda e: e.dma_start(out=jp32[:, :], in_=c_jp32[:, :]))
            vq(lambda e: e.memset(zz[:, :], 0.0))
            vq.drain()
            vq.inc(ve, 1, lambda e: e.tensor_copy(out=zz[:, 0:1], in_=zz[:, 0:1]))
            gq.wait(ve)
            for z in range(NGRAN * GRAN // 128):
                gq.inc(pfx, 16, lambda e, z=z: e.dma_start(
                    out=xpkd[z * 128:(z + 1) * 128, :], in_=zz[:, :]))
            tq.wait(ldc)
            tq(lambda e: e.matmul(out=psS[:, 0:16], lhsT=mpv[:, :], rhs=ac8[:, :], start=True, stop=False))
            tq(lambda e: e.matmul(out=psS[:, 16:32], lhsT=ctx.onescol_f32[0:8, :], rhs=ac8[:, :],
                                  start=False, stop=True))
            tq.inc_prev(pe, 1)
            vq.wait(pe)
            vq(lambda e: e.memset(srow[:, 40:48], 0.0))
            vq(lambda e: e.tensor_copy(out=srow[:, 0:16], in_=psS[:, 0:16]))
            vq.drain()
            vq(lambda e: e.tensor_tensor(out=srow[:, 8:16], in0=srow[:, 8:16], in1=psS[:, 16:24], op=OP.add))
            vq.drain()
            vq(lambda e: e.tensor_scalar(out=srow[:, 16:32], in0=srow[:, 0:16], scalar1=10240.0, scalar2=-1.0,
                                         op0=OP.subtract, op1=OP.mult))
            vq.drain()
            vq(lambda e: e.tensor_scalar(out=srow[:, 16:32], in0=srow[:, 16:32], scalar1=0.0, scalar2=None, op0=OP.max))
            vq.drain()
            vq(lambda e: e.tensor_tensor(out=srow[:, 16:32], in0=srow[:, 16:32], in1=mc1[:, :], op=OP.min))
            vq.drain()
            vq(lambda e: e.tensor_tensor(out=w8[:, :], in0=srow[:, 16:24], in1=srow[:, 24:32], op=OP.add))
            vq.drain()
            vq(lambda e: e.tensor_scalar(out=w8b[:, :], in0=w8[:, :], scalar1=0.0, scalar2=None, op0=OP.is_gt))
            vq.drain()
            for mth in range(1, 8):
                vq(lambda e, mth=mth: e.tensor_scalar(out=srow[:, 48:56], in0=w8[:, :],
                                                      scalar1=float(mth * 1024), scalar2=None, op0=OP.is_gt))
                vq.drain()
                vq(lambda e: e.tensor_tensor(out=w8b[:, :], in0=w8b[:, :], in1=srow[:, 48:56], op=OP.add))
                vq.drain()
            vq(lambda e: e.tensor_copy(out=srow[:, 48:56], in_=w8b[:, :]))
            vq.drain()
            vq(lambda e: e.tensor_tensor(out=srow[:, 49:56], in0=srow[:, 49:56], in1=w8b[:, 0:7], op=OP.add))
            vq.drain()
            vq(lambda e: e.tensor_copy(out=srow[:, 56:64], in_=srow[:, 48:56]))
            vq.drain()
            vq(lambda e: e.tensor_tensor(out=srow[:, 58:64], in0=srow[:, 58:64], in1=srow[:, 48:54], op=OP.add))
            vq.drain()
            vq(lambda e: e.tensor_copy(out=srow[:, 48:56], in_=srow[:, 56:64]))
            vq.drain()
            vq(lambda e: e.tensor_tensor(out=srow[:, 52:56], in0=srow[:, 52:56], in1=srow[:, 56:60], op=OP.add))
            vq.drain()
            vq(lambda e: e.tensor_tensor(out=srow[:, 32:40], in0=srow[:, 48:56], in1=w8b[:, :], op=OP.subtract))
            vq.drain()
            vq.inc(ve, 1, lambda e: e.tensor_scalar(out=srow[:, 32:40], in0=srow[:, 32:40],
                                                    scalar1=1024.0, scalar2=None, op0=OP.mult))
            tq.wait(ve)
            tq.inc(pe, 1, lambda e: e.matmul(out=psB[:, :], lhsT=ctx.ones_row[0:1, 0:128],
                                             rhs=srow[:, 0:48], start=True, stop=True))
            aq.wait(pe)
            aq.inc(ac, 1, lambda e: e.copy(out=srowb[:, :], in_=psB[:, :]))
            tq.wait(ve)
            tq.inc(pe, 1, lambda e: e.matmul(out=psE[:, 0:8], lhsT=ctx.ones_row[0:1, 0:128],
                                             rhs=srow[:, 48:56], start=True, stop=True))
            vq.wait(pe)
            vq(lambda e: e.tensor_scalar(out=ebc[:, 0:8], in0=psE[:, 0:8], scalar1=ctx.i128col[:, 0:1],
                                         scalar2=None, op0=OP.is_le))
            vq.drain()
            vq.inc(ve, 1, lambda e: e.tensor_reduce(out=ecol[:, :], in_=ebc[:, 0:8],
                                                    axis=mybir.AxisListType.X, op=OP.add))
            tq.wait(ve)
            tq.inc(pe, 1, lambda e: e.transpose(out=psT1[0:1, :],
                                                in_=ecol[:, :], identity=ctx.iden_f32[:, :]))
            aq.wait(pe)
            aq.inc(ac, 1, lambda e: e.copy(out=expr[:, :], in_=psT1[0:1, 0:128]))
            tq.wait(ac)
            tq.inc(pe, 1, lambda e: e.matmul(out=psE[:, 0:15], lhsT=ctx.ones_row[0:1, 0:128],
                                             rhs=expr[:, 0:15], start=True, stop=True))
            aq.wait(pe)
            aq.inc(ac, 1, lambda e: e.copy(out=ebc[:, :], in_=psE[:, :]))
            vq.wait(ac)
            vq(lambda e: e.tensor_scalar(out=ebc[:, :], in0=ebc[:, :],
                                         scalar1=7.0, scalar2=None, op0=OP.min))
            vq.drain()
            for g in range(NGRAN):
                vq(lambda e, g=g: e.tensor_scalar(out=ebc[:, g:g + 1], in0=ebc[:, g:g + 1],
                                                  scalar1=1024.0, scalar2=None, op0=OP.mult))
            vq.drain()
            for g in range(NGRAN):
                vq(lambda e, g=g: e.tensor_scalar(out=idxfc[:, g, :], in0=jp8[:, :],
                                                  scalar1=ebc[:, g:g + 1], scalar2=None, op0=OP.add))
            vq.drain()
            for g in range(NGRAN):
                vq(lambda e, g=g: e.tensor_scalar(out=ebc[:, g:g + 1], in0=ebc[:, g:g + 1],
                                                  scalar1=4.0, scalar2=None, op0=OP.mult))
            vq.drain()
            for g in range(NGRAN):
                vq(lambda e, g=g: e.tensor_scalar(out=idxpj[:, g, :], in0=jp32[:, :],
                                                  scalar1=ebc[:, g:g + 1], scalar2=None, op0=OP.add))
            vq.drain()
            vq.inc(ve, 1, lambda e: e.tensor_copy(out=ebc[:, 0:1], in_=ebc[:, 0:1]))
            ctx.flush(f"{tag}_m0")

        with ctx.scope():
            ldx = [ctx.sem("ldx0"), ctx.sem("ldx1")]
            ldh = [ctx.sem("ldh0"), ctx.sem("ldh1")]
            rt = [ctx.sb(f"{tag}_rt{i}", [128, 8], F32) for i in range(2)]
            xhb = [ctx.sb(f"{tag}_xhb{i}", [128, 1024], BF16) for i in range(2)]
            ohb = ctx.sb(f"{tag}_ohb", [128, 8], F32)
            wk = ctx.sb(f"{tag}_wk", [128, 8], F32)
            pk = ctx.sb(f"{tag}_pk", [128, 6], F32)
            pm2 = [ctx.sb(f"{tag}_pm2{i}", [128, 2], F32) for i in range(2)]
            dstf = ctx.sb(f"{tag}_dstf", [128, 2], F32)
            dstg = ctx.sb(f"{tag}_dstg", [128, 2], F32)
            dst = ctx.sb(f"{tag}_dst", [128, 2], U32)
            dsc = ctx.sb(f"{tag}_dsc", [128, 2], U32)
            stv = [ctx.sem("stq0"), ctx.sem("stq1")]
            gq.wait(pfx)
            for i in range(NT):
                ib = i % 2
                sy.wait(ve)
                sy.wait(ctx.sems["stq0"])
                sy.wait(ctx.sems["stq1"])
                sy.inc(ldx[ib], 16, lambda e, i=i, ib=ib: e.dma_start(
                    out=rt[ib][:, :], in_=route_in[i * 128:(i + 1) * 128, :]))
                sy.inc(ldh[ib], 16, lambda e, i=i, ib=ib: e.dma_start(
                    out=xhb[ib][:, :], in_=xh_in[i * 128:(i + 1) * 128, :]))
                vq.wait(ldx[ib])
                vq.wait(ctx.sems["stq0"])
                vq.wait(ctx.sems["stq1"])
                for k in range(2):
                    lrcol = 4 + k
                    vq(lambda e, ib=ib, k=k: e.tensor_scalar(out=ohb[:, :], in0=ctx.iota8[:, :],
                                                             scalar1=rt[ib][:, k:k + 1], scalar2=None,
                                                             op0=OP.is_equal))
                    vq.drain()
                    vq(lambda e, k=k: e.tensor_tensor(out=wk[:, :], in0=ohb[:, :],
                                                      in1=srowb[:, k * 8:(k + 1) * 8], op=OP.mult))
                    vq.drain()
                    vq(lambda e: e.tensor_reduce(out=pk[:, 0:1], in_=wk[:, :], axis=mybir.AxisListType.X, op=OP.add))
                    vq.drain()
                    vq(lambda e: e.tensor_tensor(out=wk[:, :], in0=ohb[:, :], in1=srowb[:, 32:40], op=OP.mult))
                    vq.drain()
                    vq(lambda e: e.tensor_reduce(out=pk[:, 1:2], in_=wk[:, :], axis=mybir.AxisListType.X, op=OP.add))
                    vq.drain()
                    vq(lambda e: e.tensor_tensor(out=wk[:, :], in0=ohb[:, :], in1=srowb[:, 16:24], op=OP.mult))
                    vq.drain()
                    vq(lambda e: e.tensor_reduce(out=pk[:, 2:3], in_=wk[:, :], axis=mybir.AxisListType.X, op=OP.add))
                    vq.drain()
                    vq(lambda e, k=k: e.tensor_tensor(out=wk[:, :], in0=ohb[:, :],
                                                      in1=srowb[:, (2 + k) * 8:(3 + k) * 8], op=OP.mult))
                    vq.drain()
                    vq(lambda e: e.tensor_reduce(out=pk[:, 3:4], in_=wk[:, :], axis=mybir.AxisListType.X, op=OP.add))
                    vq.drain()
                    vq(lambda e, ib=ib, k=k, lrcol=lrcol: e.tensor_scalar(
                        out=pk[:, 4:5], in0=rt[ib][:, lrcol:lrcol + 1], scalar1=pk[:, 0:1],
                        scalar2=None, op0=OP.add))
                    vq.drain()
                    vq(lambda e: e.tensor_scalar(out=pk[:, 5:6], in0=pk[:, 4:5], scalar1=10240.0,
                                                 scalar2=None, op0=OP.is_lt))
                    vq.drain()
                    vq(lambda e, ib=ib, k=k: e.tensor_scalar(out=pm2[ib][:, k:k + 1],
                                                             in0=rt[ib][:, 2 + k:3 + k],
                                                             scalar1=pk[:, 5:6], scalar2=None, op0=OP.mult))
                    vq.drain()
                    if k == 1:
                        vq(lambda e: e.tensor_scalar(out=pk[:, 1:2], in0=pk[:, 2:3], scalar1=pk[:, 1:2],
                                                     scalar2=None, op0=OP.add))
                        vq.drain()
                    vq(lambda e, ib=ib, k=k, lrcol=lrcol: e.tensor_scalar(
                        out=dstf[:, k:k + 1], in0=rt[ib][:, lrcol:lrcol + 1], scalar1=pk[:, 1:2],
                        scalar2=None, op0=OP.add))
                    vq.drain()
                    vq(lambda e, ib=ib, lrcol=lrcol: e.tensor_scalar(
                        out=pk[:, 3:4], in0=rt[ib][:, lrcol:lrcol + 1], scalar1=pk[:, 3:4],
                        scalar2=None, op0=OP.is_ge))
                    vq(lambda e, i=i, k=k: e.tensor_scalar(
                        out=pk[:, 4:5], in0=ctx.i128col[:, :], scalar1=2.0,
                        scalar2=float(NGRAN * GRAN + k + 256 * i), op0=OP.mult, op1=OP.add))
                    vq.drain()
                    vq(lambda e, k=k: e.tensor_tensor(out=pk[:, 4:5], in0=pk[:, 4:5],
                                                      in1=dstf[:, k:k + 1], op=OP.subtract))
                    vq.drain()
                    vq(lambda e, k=k: e.scalar_tensor_tensor(
                        out=dstf[:, k:k + 1], in0=pk[:, 4:5], scalar=pk[:, 3:4], op0=OP.mult,
                        op1=OP.add, in1=dstf[:, k:k + 1]))
                    vq.drain()
                vq(lambda e: e.tensor_scalar(out=dstf[:, :], in0=dstf[:, :],
                                             scalar1=float(NGRAN * GRAN + 2 * T - 1), scalar2=None, op0=OP.min))
                vq.drain()
                vq(lambda e: e.tensor_scalar(out=dstg[:, :], in0=dstf[:, :],
                                             scalar1=float(NGRAN * GRAN - 1), scalar2=None, op0=OP.min))
                vq.drain()
                vq(lambda e: e.tensor_scalar(out=dst[:, :], in0=dstf[:, :], scalar1=0.0,
                                             scalar2=None, op0=OP.add))
                vq.drain()
                vq.inc(ve, 1, lambda e: e.tensor_scalar(out=dsc[:, :], in0=dstg[:, :], scalar1=0.0,
                                                        scalar2=None, op0=OP.add))
                gq.wait(ve)
                gq.wait(ldh[ib])
                for k in range(2):
                    gq.inc(stv[k], 16, lambda e, k=k, ib=ib: e.indirect_dma_start(
                        out=xpkd[:, :],
                        out_offset=bass.IndirectOffsetOnAxis(ap=dst[:, k:k + 1], axis=0),
                        in_=xhb[ib][:, :], in_offset=None,
                        bounds_check=None))
                gq.inc(stv[ib], 16, lambda e, i=i: e.dma_start(
                    out=dstd[i * 128:(i + 1) * 128, :], in_=dsc[:, :]))
                gq.inc(stv[ib], 16, lambda e, i=i, ib=ib: e.dma_start(
                    out=pmd[i * 128:(i + 1) * 128, :], in_=pm2[ib][:, :]))
            gq.wait(ctx.sem("stq0"))
            gq.wait(ctx.sem("stq1"))
            ctx.flush(f"{tag}_m1")

        with ctx.scope():
            gldA = ctx.sem("gldA")
            gldB = ctx.sem("gldB")
            gldX = ctx.sem("gldX")
            stq = [ctx.sem("stq0"), ctx.sem("stq1")]
            wfcb = ctx.sb(f"{tag}_wfcb", [128, 8, 4096], BF16)
            wpjb = ctx.sb(f"{tag}_wpjb", [128, 32, 1024], BF16)
            xgT = ctx.sb(f"{tag}_xgT", [128, 8, 1024], BF16)
            xgs = [ctx.sb(f"{tag}_xgs{i}", [128, 1024], BF16) for i in range(2)]
            hbT = ctx.sb(f"{tag}_hbT", [128, 32, 512], BF16)
            hrl = ctx.sb(f"{tag}_hrl", [128, 512], F32)
            osb = [ctx.sb(f"{tag}_osb{i}", [128, 1024], BF16) for i in range(2)]
            psH = [ctx.ps(f"{tag}_psH{i}", [128, 512], F32) for i in range(2)]
            psO = [ctx.ps(f"{tag}_psO{i}", [128, 1024], F32) for i in range(2)]
            psT = ctx.ps(f"{tag}_psT2", [128, 1024], F32)
            done = []
            for g in range(NGRAN):
                d = {}
                for s in range(8):
                    sb2 = s % 2
                    gq.wait(pe)
                    gq.inc(gldX, 16, lambda e, g=g, s=s, sb2=sb2: e.dma_start(
                        out=xgs[sb2][:, :],
                        in_=xpkd[g * 1024 + s * 128:g * 1024 + (s + 1) * 128, :]))
                    tq.wait(gldX)
                    tq.wait(ac)
                    for j in range(8):
                        tq(lambda e, sb2=sb2, j=j: e.matmul(
                            out=psT[:, j * 128:(j + 1) * 128], lhsT=xgs[sb2][:, j * 128:(j + 1) * 128],
                            rhs=ctx.iden_bf[:, :], start=True, stop=True))
                    tq.inc_prev(pe, 1)
                    aq.wait(pe)
                    aq.inc(ac, 1, lambda e, s=s: e.copy(
                        out=xgT[:, :, s * 128:(s + 1) * 128],
                        in_=psT[:, :].rearrange("p (j c) -> p j c", c=128)))
                d["xtr"] = pe.n
                gq.wait(pe, done[-1]["h1_done"] if done else 0)
                for j in range(8):
                    for half in range(2):
                        gq.inc(gldA, 16, lambda e, g=g, j=j, half=half: e.indirect_dma_start(
                            out=wfcb[:, j, half * 2048:(half + 1) * 2048], out_offset=None,
                            in_=cfcv[:, :],
                            in_offset=bass.IndirectOffsetOnAxis(ap=idxfc[:, g, j:j + 1], axis=0),
                            element_offset=half * 2048,
                            bounds_check=None))
                for th in range(2):
                    tq.wait(gldA)
                    for fc in range(32):
                        fb = fc % 2
                        if not (th == 0 and fc < 2):
                            tq.wait(ac)
                        for j in range(8):
                            tq(lambda e, g=g, th=th, fc=fc, fb=fb, j=j: e.matmul(
                                out=psH[fb][:, :], lhsT=wfcb[:, j, fc * 128:(fc + 1) * 128],
                                rhs=xgT[:, j, th * 512:(th + 1) * 512], start=(j == 0), stop=(j == 7)))
                        tq.inc_prev(pe, 1)
                        aq.wait(pe)
                        aq(lambda e, fb=fb: e.activation(out=hrl[:, :], in_=psH[fb][:, :], func=AF.Relu))
                        aq.drain()
                        aq.inc(ac, 1, lambda e, fc=fc: e.activation(out=hbT[:, fc, :], in_=hrl[:, :],
                                                                    func=AF.Square))
                        aq.drain()
                    if th == 0:
                        d["h0_done"] = pe.n
                    else:
                        d["h1_done"] = pe.n
                    if th == 0:
                        gq.wait(pe, done[-1]["o_done"] if done else 0)
                        for jj in range(32):
                            gq.inc(gldB, 16, lambda e, g=g, jj=jj: e.indirect_dma_start(
                                out=wpjb[:, jj, :], out_offset=None,
                                in_=cpjv[:, :],
                                in_offset=bass.IndirectOffsetOnAxis(ap=idxpj[:, g, jj:jj + 1], axis=0),
                                bounds_check=None))
                    tq.wait(ac)
                    tq.wait(gldB)
                    for tt in range(4):
                        tb = tt % 2
                        if not (th == 0 and tt < 2):
                            tq.wait(ac)
                        for half in range(2):
                            for fc in range(32):
                                tq(lambda e, tt=tt, tb=tb, th=th, fc=fc, half=half: e.matmul(
                                    out=psO[tb][:, half * 512:(half + 1) * 512],
                                    lhsT=hbT[:, fc, tt * 128:(tt + 1) * 128],
                                    rhs=wpjb[:, fc, half * 512:(half + 1) * 512],
                                    start=(fc == 0), stop=(fc == 31)))
                        tq.inc_prev(pe, 1)
                        aq.wait(pe)
                        aq.wait(stq[tb])
                        aq.inc(ac, 1, lambda e, tb=tb: e.copy(out=osb[tb][:, :], in_=psO[tb][:, :]))
                        gq.wait(ac)
                        gq.inc(stq[tb], 16, lambda e, g=g, th=th, tt=tt, tb=tb: e.dma_start(
                            out=opk[g * 1024 + th * 512 + tt * 128:
                                    g * 1024 + th * 512 + (tt + 1) * 128, :],
                            in_=osb[tb][:, :]))
                d["o_done"] = pe.n
                done.append(d)
            gq.wait(ctx.sem("stq0"))
            gq.wait(ctx.sem("stq1"))
            ctx.flush(f"{tag}_m2")

        with ctx.scope():
            ldx = [ctx.sem("ldx0"), ctx.sem("ldx1")]
            ldd = [ctx.sem("ldd0"), ctx.sem("ldd1")]
            glo = [ctx.sem("glo0"), ctx.sem("glo1")]
            sto = [ctx.sem("sto0"), ctx.sem("sto1")]
            xb = [ctx.sb(f"{tag}_cx{i}", [128, 1024], F32) for i in range(2)]
            o0 = [ctx.sb(f"{tag}_co0{i}", [128, 1024], BF16) for i in range(2)]
            o1 = [ctx.sb(f"{tag}_co1{i}", [128, 1024], BF16) for i in range(2)]
            dsb = [ctx.sb(f"{tag}_dsb{i}", [128, 2], U32) for i in range(2)]
            pmb = [ctx.sb(f"{tag}_cpm{i}", [128, 2], F32) for i in range(2)]
            t1 = [ctx.sb(f"{tag}_ct1{i}", [128, 1024], F32) for i in range(2)]
            ot = [ctx.sb(f"{tag}_cot{i}", [128, 1024], F32) for i in range(2)]
            ve_done = []
            glo_done = []
            for i in range(NT):
                ib = i % 2
                if i >= 2:
                    sy.wait(ve, ve_done[i - 2])
                    sy.wait(glo[ib], glo_done[i - 2])
                sy.inc(ldx[ib], 16, lambda e, i=i, ib=ib: e.dma_start(
                    out=xb[ib][:, :], in_=x_in[i * 128:(i + 1) * 128, :]))
                sy.inc(ldd[ib], 16, lambda e, i=i, ib=ib: e.dma_start(
                    out=dsb[ib][:, :], in_=dstd[i * 128:(i + 1) * 128, :]))
                sy.inc(ldd[ib], 16, lambda e, i=i, ib=ib: e.dma_start(
                    out=pmb[ib][:, :], in_=pmd[i * 128:(i + 1) * 128, :]))
                gq.wait(ldd[ib])
                if i >= 2:
                    gq.wait(ve, ve_done[i - 2])
                gq.inc(glo[ib], 16, lambda e, ib=ib: e.indirect_dma_start(
                    out=o0[ib][:, :], out_offset=None,
                    in_=opk[:, :],
                    in_offset=bass.IndirectOffsetOnAxis(ap=dsb[ib][:, 0:1], axis=0),
                    bounds_check=None))
                gq.inc(glo[ib], 16, lambda e, ib=ib: e.indirect_dma_start(
                    out=o1[ib][:, :], out_offset=None,
                    in_=opk[:, :],
                    in_offset=bass.IndirectOffsetOnAxis(ap=dsb[ib][:, 1:2], axis=0),
                    bounds_check=None))
                glo_done.append(glo[ib].n)
                vq.wait(ldx[ib])
                vq.wait(glo[ib])
                vq.wait(ldd[ib])
                if i >= 2:
                    vq.wait(sto[ib])
                vq(lambda e, ib=ib: e.scalar_tensor_tensor(
                    out=t1[ib][:, :], in0=o0[ib][:, :], scalar=pmb[ib][:, 0:1], op0=OP.mult,
                    op1=OP.add, in1=xb[ib][:, :]))
                vq.drain()
                vq.inc(ve, 1, lambda e, ib=ib: e.scalar_tensor_tensor(
                    out=ot[ib][:, :], in0=o1[ib][:, :], scalar=pmb[ib][:, 1:2], op0=OP.mult,
                    op1=OP.add, in1=t1[ib][:, :]))
                ve_done.append(ve.n)
                gq.wait(ve)
                gq.inc(sto[ib], 16, lambda e, i=i, ib=ib: e.dma_start(
                    out=x_out[i * 128:(i + 1) * 128, :], in_=ot[ib][:, :]))
            gq.wait(ctx.sem("sto0"))
            gq.wait(ctx.sem("sto1"))
            ctx.flush(f"{tag}_m3")


def moe_consts():
    jp8 = (np.arange(8, dtype=np.float32)[None, :] * 128 + np.arange(128, dtype=np.float32)[:, None])
    jp32 = (np.arange(32, dtype=np.float32)[None, :] * 128 + np.arange(128, dtype=np.float32)[:, None])
    return {
        "c_jp8": jp8.astype(np.float32),
        "c_jp32": jp32.astype(np.float32),
    }


def build_moemain_launch():
    nc = make_nc()
    stack = contextlib.ExitStack()
    ctx = Ctx(nc, stack)
    with stack:
        declare_consts(ctx)
        x_in = ctx.din("x_in", (T, D), F32)
        xh_in = ctx.din("xh_in", (T, D), BF16)
        route_in = ctx.din("route_in", (T, 8), F32)
        ctx.din("moe_allcnt", (8, 16), F32)
        ctx.din("moe_mycnt", (1, 16), F32)
        ctx.din("moe_mprev", (8, 1), F32)
        ctx.din("c_jp8", (128, 8), F32)
        ctx.din("c_jp32", (128, 32), F32)
        ctx.din("L1_ffn_c_fc", (8, 1024, 4096), F32)
        ctx.din("L1_ffn_c_proj", (8, 4096, 1024), F32)
        x_out = ctx.dout("x_out", (T, D), F32)
        emit_load_consts(ctx)
        emit_moe_main(ctx, x_in, x_out, xh_in, route_in, "mm")
        ctx.flush("fin")
    return nc


def moemain_inmaps(inp, xs, xh, route, cnts):
    c = host_constants()
    mc = moe_consts()
    allc = np.ascontiguousarray(cnts.reshape(8, 16))
    base = {
        **c, **mc,
        "moe_allcnt": allc,
        "L1_ffn_c_fc": np.ascontiguousarray(inp["L1_ffn_c_fc"]),
        "L1_ffn_c_proj": np.ascontiguousarray(inp["L1_ffn_c_proj"]),
    }
    maps = []
    for i in range(NCORES):
        mprev = np.zeros((8, 1), np.float32)
        mprev[:i] = 1.0
        maps.append({**base, "x_in": np.ascontiguousarray(xs[i]),
                     "xh_in": np.ascontiguousarray(xh[i]),
                     "route_in": np.ascontiguousarray(route[i]),
                     "moe_mycnt": allc[i:i + 1].copy(),
                     "moe_mprev": mprev})
    return maps


def run_moe(inp, xs):
    nc1 = build_moepre_launch()
    res1 = run_bass_kernel_spmd(nc1, moepre_inmaps(inp, xs), core_ids=list(range(NCORES)))
    xh = np.stack([r["xh_out"] for r in res1.results])
    route = np.stack([r["route_out"] for r in res1.results])
    cnts = np.stack([r["cnt_out"] for r in res1.results])
    nc2 = build_moemain_launch()
    res2 = run_bass_kernel_spmd(nc2, moemain_inmaps(inp, xs, xh, route, cnts), core_ids=list(range(NCORES)))
    return np.stack([r["x_out"] for r in res2.results])



def build_A_launch():
    nc = make_nc()
    stack = contextlib.ExitStack()
    ctx = Ctx(nc, stack)
    with stack:
        declare_consts(ctx)
        x_in = ctx.din("x_in", (T, D), F32)
        ctx.din("L0_attn_a_W_qkv", (1024, 3072), F32)
        ctx.din("L0_attn_a_b_qkv_r", (1, 3072), F32)
        ctx.din("L0_attn_a_W_proj", (1024, 1024), F32)
        ctx.din("L0_attn_a_b_proj_r", (1, 1024), F32)
        ctx.din("L0_attn_a_ln_w_r", (128, 8), F32)
        ctx.din("L0_attn_a_ln_b_r", (128, 8), F32)
        ctx.din("L0_ffn_W1T", (1024, 4096), F32)
        ctx.din("L0_ffn_W2T", (4096, 1024), F32)
        ctx.din("L0_ffn_b1_r", (128, 32), F32)
        ctx.din("L0_ffn_b2_r", (1, 1024), F32)
        for x in "QKV":
            ctx.din(f"L0_attn_b_W_{x}", (1024, 1024), F32)
        for t in ("ret_sc_q", "ret_ss_q", "ret_sc_k", "ret_ss_k"):
            ctx.din(t, (T, D), F32)
        ctx.din("ret_Dt", (T, T), F32)
        for x in ("q", "k", "v", "g", "o"):
            ctx.din(f"L1_attn_a_W{x}", (1024, 1024), F32)
        ctx.din("moe_wgT", (1024, 8), F32)
        x_out = ctx.dout("x_out", (T, D), F32)
        xh_out = ctx.dout("xh_out", (T, D), BF16)
        route_out = ctx.dout("route_out", (T, 8), F32)
        cnt_out = ctx.dout("cnt_out", (1, 16), F32)
        emit_load_consts(ctx)
        xm1 = ctx.dtmp("xm1", (T, D), F32)
        xm2 = ctx.dtmp("xm2", (T, D), F32)
        xm3 = ctx.dtmp("xm3", (T, D), F32)
        xm4 = ctx.dtmp("xm4", (T, D), F32)
        emit_l0a(ctx, x_in, xm1)
        emit_ffn(ctx, xm1, xm2, "f1")
        emit_ret(ctx, xm2, xm3)
        emit_ffn(ctx, xm3, xm4, "f2")
        emit_gla(ctx, xm4, x_out)
        emit_moe_pre(ctx, x_out, xh_out, route_out, cnt_out, "mp")
        ctx.flush("fin")
    return nc


def a_inmaps(inp, xs):
    c = host_constants()
    tabs = ret_tables()
    base = {
        **c, **tabs,
        "L0_attn_a_W_qkv": np.ascontiguousarray(inp["L0_attn_a_W_qkv"]),
        "L0_attn_a_b_qkv_r": np.ascontiguousarray(inp["L0_attn_a_b_qkv"].reshape(1, -1)),
        "L0_attn_a_W_proj": np.ascontiguousarray(inp["L0_attn_a_W_proj"]),
        "L0_attn_a_b_proj_r": np.ascontiguousarray(inp["L0_attn_a_b_proj"].reshape(1, -1)),
        "L0_attn_a_ln_w_r": np.ascontiguousarray(inp["L0_attn_a_ln_w"].reshape(8, 128).T),
        "L0_attn_a_ln_b_r": np.ascontiguousarray(inp["L0_attn_a_ln_b"].reshape(8, 128).T),
        "L0_ffn_W1T": np.ascontiguousarray(inp["L0_ffn_W1"].T),
        "L0_ffn_W2T": np.ascontiguousarray(inp["L0_ffn_W2"].T),
        "L0_ffn_b1_r": np.ascontiguousarray(inp["L0_ffn_b1"].reshape(32, 128).T),
        "L0_ffn_b2_r": np.ascontiguousarray(inp["L0_ffn_b2"].reshape(1, -1)),
        "L0_attn_b_W_Q": np.ascontiguousarray(inp["L0_attn_b_W_Q"]),
        "L0_attn_b_W_K": np.ascontiguousarray(inp["L0_attn_b_W_K"]),
        "L0_attn_b_W_V": np.ascontiguousarray(inp["L0_attn_b_W_V"]),
        "moe_wgT": np.ascontiguousarray(inp["L1_ffn_w_g"].T),
    }
    for x in ("q", "k", "v", "g", "o"):
        base[f"L1_attn_a_W{x}"] = np.ascontiguousarray(inp[f"L1_attn_a_W{x}"])
    return [{**base, "x_in": np.ascontiguousarray(xs[i])} for i in range(NCORES)]


def build_B_launch():
    nc = make_nc()
    stack = contextlib.ExitStack()
    ctx = Ctx(nc, stack)
    with stack:
        declare_consts(ctx)
        x_in = ctx.din("x_in", (T, D), F32)
        xh_in = ctx.din("xh_in", (T, D), BF16)
        route_in = ctx.din("route_in", (T, 8), F32)
        ctx.din("moe_allcnt", (8, 16), F32)
        ctx.din("moe_mycnt", (1, 16), F32)
        ctx.din("moe_mprev", (8, 1), F32)
        ctx.din("c_jp8", (128, 8), F32)
        ctx.din("c_jp32", (128, 32), F32)
        ctx.din("L1_ffn_c_fc", (8, 1024, 4096), F32)
        ctx.din("L1_ffn_c_proj", (8, 4096, 1024), F32)
        ctx.din("lra_Qt", (1024, 1024), F32)
        ctx.din("lra_Kt", (1024, 1024), F32)
        ctx.din("L1_attn_b_VO", (16, 1024, 1024), F32)
        ctx.din("moe_wgT", (1024, 8), F32)
        x_out = ctx.dout("x_out", (T, D), F32)
        xh_out = ctx.dout("xh_out", (T, D), BF16)
        route_out = ctx.dout("route_out", (T, 8), F32)
        cnt_out = ctx.dout("cnt_out", (1, 16), F32)
        emit_load_consts(ctx)
        xm6 = ctx.dtmp("xm6", (T, D), F32)
        emit_moe_main(ctx, x_in, xm6, xh_in, route_in, "mm")
        emit_lra(ctx, xm6, x_out)
        emit_moe_pre(ctx, x_out, xh_out, route_out, cnt_out, "mp")
        ctx.flush("fin")
    return nc


def b_inmaps(inp, xs, xh, route, cnts):
    c = host_constants()
    mc = moe_consts()
    allc = np.ascontiguousarray(cnts.reshape(8, 16))
    Q = inp["L1_attn_b_Q"]
    K = inp["L1_attn_b_K"]
    base = {
        **c, **mc,
        "moe_allcnt": allc,
        "L1_ffn_c_fc": np.ascontiguousarray(inp["L1_ffn_c_fc"]),
        "L1_ffn_c_proj": np.ascontiguousarray(inp["L1_ffn_c_proj"]),
        "lra_Qt": np.ascontiguousarray(Q.transpose(2, 0, 1).reshape(1024, 1024)),
        "lra_Kt": np.ascontiguousarray(K.transpose(2, 0, 1).reshape(1024, 1024)),
        "L1_attn_b_VO": np.ascontiguousarray(inp["L1_attn_b_VO"]),
        "moe_wgT": np.ascontiguousarray(inp["L1_ffn_w_g"].T),
    }
    maps = []
    for i in range(NCORES):
        mprev = np.zeros((8, 1), np.float32)
        mprev[:i] = 1.0
        maps.append({**base, "x_in": np.ascontiguousarray(xs[i]),
                     "xh_in": np.ascontiguousarray(xh[i]),
                     "route_in": np.ascontiguousarray(route[i]),
                     "moe_mycnt": allc[i:i + 1].copy(),
                     "moe_mprev": mprev})
    return maps



_BUILT = {}


def _nc_for(name, builder):
    if name not in _BUILT:
        _BUILT[name] = builder()
    return _BUILT[name]


def kernel(**inputs):
    inp = {k: np.asarray(v) for k, v in inputs.items()}
    x = np.ascontiguousarray(np.asarray(inp["L0_attn_a_z"], dtype=np.float32))
    cid = list(range(NCORES))
    resA = run_bass_kernel_spmd(_nc_for("A", build_A_launch), a_inmaps(inp, x), core_ids=cid)
    x5 = np.stack([r["x_out"] for r in resA.results])
    xh = np.stack([r["xh_out"] for r in resA.results])
    route = np.stack([r["route_out"] for r in resA.results])
    cnts = np.stack([r["cnt_out"] for r in resA.results])
    resB = run_bass_kernel_spmd(_nc_for("B", build_B_launch),
                                b_inmaps(inp, x5, xh, route, cnts), core_ids=cid)
    x7 = np.stack([r["x_out"] for r in resB.results])
    xh2 = np.stack([r["xh_out"] for r in resB.results])
    route2 = np.stack([r["route_out"] for r in resB.results])
    cnts2 = np.stack([r["cnt_out"] for r in resB.results])
    resC = run_bass_kernel_spmd(_nc_for("C", build_moemain_launch),
                                moemain_inmaps(inp, x7, xh2, route2, cnts2), core_ids=cid)
    return np.ascontiguousarray(np.stack([r["x_out"] for r in resC.results]).astype(np.float32))


# version 03
# speedup vs baseline: 1.1859x; 1.1809x over previous
import contextlib
import numpy as np
import ml_dtypes
import concourse.bass as bass
import concourse.mybir as mybir
from concourse.bass_utils import run_bass_kernel_spmd

F32 = mybir.dt.float32
BF16 = mybir.dt.bfloat16
U32 = mybir.dt.uint32
AF = mybir.ActivationFunctionType
OP = mybir.AluOpType
BF = ml_dtypes.bfloat16

T = 4096
D = 1024
NT = T // 128
ND = D // 128
EPS = 1e-5
NCORES = 8
EXP_SHIFT = 8.0


class Sem:
    def __init__(self, h):
        self.h = h
        self.n = 0


class EQ:
    def __init__(self):
        self.ops = []

    def __call__(self, fn):
        self.ops.append(fn)

    def wait(self, sem, val=None):
        v = sem.n if val is None else val
        if v > 0:
            self.ops.append(lambda e, s=sem.h, v=v: e.wait_ge(s, v))

    def inc(self, sem, k, fn):
        def g(e, fn=fn, s=sem.h, k=k):
            fn(e).then_inc(s, k)
        self.ops.append(g)
        sem.n += k

    def inc_prev(self, sem, k):
        fn = self.ops.pop()
        def g(e, fn=fn, s=sem.h, k=k):
            fn(e).then_inc(s, k)
        self.ops.append(g)
        sem.n += k

    def drain(self):
        self.ops.append(lambda e: e.drain())

    def flush(self, block, which):
        ops = self.ops
        self.ops = []
        if ops:
            getattr(block, which)(lambda e: [f(e) for f in ops])


class Ctx:
    def __init__(self, nc, stack):
        self.nc = nc
        self.stack = stack
        self.sems = {}
        self.dram = {}
        self.q = {n: EQ() for n in ("sync", "vector", "tensor", "scalar", "gpsimd")}

    def sem(self, name):
        if name not in self.sems:
            h = self.stack.enter_context(self.nc.semaphore(name))
            self.sems[name] = Sem(h)
        return self.sems[name]

    @contextlib.contextmanager
    def scope(self):
        outer = self.stack
        with contextlib.ExitStack() as st:
            self.stack = st
            try:
                yield
            finally:
                self.stack = outer

    def sb(self, name, shape, dtype):
        return self.stack.enter_context(self.nc.sbuf_tensor(name, shape, dtype))

    def ps(self, name, shape, dtype=F32):
        return self.stack.enter_context(self.nc.psum_tensor(name, shape, dtype))

    def din(self, name, shape, dtype):
        ap = self.nc.dram_tensor(name, list(shape), dtype, kind="ExternalInput").ap()
        self.dram[name] = ap
        return ap

    def dout(self, name, shape, dtype):
        ap = self.nc.dram_tensor(name, list(shape), dtype, kind="ExternalOutput").ap()
        self.dram[name] = ap
        return ap

    def dtmp(self, name, shape, dtype):
        ap = self.nc.dram_tensor(name, list(shape), dtype).ap()
        self.dram[name] = ap
        return ap

    def flush(self, name=None):
        with self.nc.Block(name) as block:
            self.q["sync"].flush(block, "sync")
            self.q["vector"].flush(block, "vector")
            self.q["tensor"].flush(block, "tensor")
            self.q["scalar"].flush(block, "scalar")
            self.q["gpsimd"].flush(block, "gpsimd")

    @property
    def sy(self):
        return self.q["sync"]

    @property
    def vq(self):
        return self.q["vector"]

    @property
    def tq(self):
        return self.q["tensor"]

    @property
    def aq(self):
        return self.q["scalar"]

    @property
    def gq(self):
        return self.q["gpsimd"]


def make_nc():
    return bass.Bass("TRN2", target_bir_lowering=False, debug=False)


def host_constants():
    return {
        "c_iden_bf": np.eye(128, dtype=BF),
        "c_iden_f32": np.eye(128, dtype=np.float32),
        "c_ones_row": np.ones((1, 512), dtype=np.float32),
        "c_ones_col": np.ones((128, 1), dtype=np.float32),
        "c_ones_col_bf": np.ones((128, 1), dtype=BF),
        "c_trib": np.kron(np.eye(2, dtype=np.float32), np.triu(np.ones((64, 64), dtype=np.float32))),
        "c_utm": np.tile(np.triu(np.ones((64, 64), dtype=np.float32)), (2, 1)),
        "c_utm_u8": np.tile(np.triu(np.ones((64, 64), dtype=np.uint8)), (2, 1)),
        "c_iota8": np.tile(np.arange(8, dtype=np.float32)[None, :], (128, 1)),
        "c_ltx": np.triu(np.ones((128, 128), dtype=np.float32), k=1),
        "c_i128": np.arange(128, dtype=np.float32)[:, None],
    }


def declare_consts(ctx):
    ctx.din("c_iden_bf", (128, 128), BF16)
    ctx.din("c_iden_f32", (128, 128), F32)
    ctx.din("c_ones_row", (1, 512), F32)
    ctx.din("c_ones_col", (128, 1), F32)
    ctx.din("c_ones_col_bf", (128, 1), BF16)
    ctx.din("c_trib", (128, 128), F32)
    ctx.din("c_utm", (128, 64), F32)
    ctx.din("c_utm_u8", (128, 64), mybir.dt.uint8)
    ctx.din("c_iota8", (128, 8), F32)
    ctx.din("c_ltx", (128, 128), F32)
    ctx.din("c_i128", (128, 1), F32)


CONST_VALUES = [EPS, -EXP_SHIFT, 1.0]


def emit_load_consts(ctx, const_values=CONST_VALUES):
    ctx.iden_bf = ctx.sb("iden_bf", [128, 128], BF16)
    ctx.iden_f32 = ctx.sb("iden_f32", [128, 128], F32)
    ctx.ones_row = ctx.sb("ones_row", [1, 512], F32)
    ctx.onescol_f32 = ctx.sb("ones_col", [128, 1], F32)
    ctx.onescol_bf = ctx.sb("ones_col_bf", [128, 1], BF16)
    ctx.trib = ctx.sb("trib", [128, 128], F32)
    ctx.utm = ctx.sb("utm", [128, 64], F32)
    ctx.utm_u8 = ctx.sb("utm_u8", [128, 64], mybir.dt.uint8)
    ctx.iota8 = ctx.sb("iota8", [128, 8], F32)
    ctx.ltx = ctx.sb("ltx", [128, 128], F32)
    ctx.i128col = ctx.sb("i128col", [128, 1], F32)
    ld = ctx.sem("ld")
    ctx.sy.inc(ld, 16, lambda e: e.dma_start(out=ctx.iden_bf[:, :], in_=ctx.dram["c_iden_bf"][:, :]))
    ctx.sy.inc(ld, 16, lambda e: e.dma_start(out=ctx.iden_f32[:, :], in_=ctx.dram["c_iden_f32"][:, :]))
    ctx.sy.inc(ld, 16, lambda e: e.dma_start(out=ctx.ones_row[:, :], in_=ctx.dram["c_ones_row"][:, :]))
    ctx.sy.inc(ld, 16, lambda e: e.dma_start(out=ctx.onescol_f32[:, :], in_=ctx.dram["c_ones_col"][:, :]))
    ctx.sy.inc(ld, 16, lambda e: e.dma_start(out=ctx.onescol_bf[:, :], in_=ctx.dram["c_ones_col_bf"][:, :]))
    ctx.sy.inc(ld, 16, lambda e: e.dma_start(out=ctx.trib[:, :], in_=ctx.dram["c_trib"][:, :]))
    ctx.sy.inc(ld, 16, lambda e: e.dma_start(out=ctx.utm[:, :], in_=ctx.dram["c_utm"][:, :]))
    ctx.sy.inc(ld, 16, lambda e: e.dma_start(out=ctx.utm_u8[:, :], in_=ctx.dram["c_utm_u8"][:, :]))
    ctx.sy.inc(ld, 16, lambda e: e.dma_start(out=ctx.iota8[:, :], in_=ctx.dram["c_iota8"][:, :]))
    ctx.sy.inc(ld, 16, lambda e: e.dma_start(out=ctx.ltx[:, :], in_=ctx.dram["c_ltx"][:, :]))
    ctx.sy.inc(ld, 16, lambda e: e.dma_start(out=ctx.i128col[:, :], in_=ctx.dram["c_i128"][:, :]))
    cp = ctx.sb("constpool", [128, max(1, len(const_values))], F32)
    for i, v in enumerate(const_values):
        ctx.vq(lambda e, i=i, v=v: e.memset(cp[:, i:i + 1], float(v)))
        ctx.nc.const_aps.register(F32, float(v), cp[:, i:i + 1])
    ctx.vq.drain()
    ctx.sy.wait(ld)



def emit_ln_stats(ctx, x_ap, A, B, name="ln", z_of=None):
    ve, ac = ctx.sem("ve"), ctx.sem("ac")
    ldx = [ctx.sem("ldx0"), ctx.sem("ldx1")]
    sy, vq, aq = ctx.sy, ctx.vq, ctx.aq
    with ctx.scope():
        xb = [ctx.sb(f"{name}_x{i}", [128, 1024], F32) for i in range(2)]
        zb = ctx.sb(f"{name}_zb", [128, 1024], F32) if z_of else None
        bns = ctx.sb(f"{name}_bns", [128, 12], F32)
        mv = ctx.sb(f"{name}_mv", [128, 2], F32)
        var = ctx.sb(f"{name}_var", [128, NT], F32)
        ve_done = []
        for i in range(NT):
            if i >= 2:
                sy.wait(ve, ve_done[i - 2])
            sy.inc(ldx[i % 2], 16, lambda e, i=i: e.dma_start(out=xb[i % 2][:, :], in_=x_ap[i * 128:(i + 1) * 128, :]))
            vq.wait(ldx[i % 2])
            src = xb
            if z_of:
                A1, B1 = z_of
                vq(lambda e, i=i: e.tensor_scalar(out=zb[:, :], in0=xb[i % 2][:, :], scalar1=B1[:, i:i + 1],
                                                  scalar2=A1[:, i:i + 1], op0=OP.mult, op1=OP.subtract))
                vq.drain()
            tgt = (lambda i: zb[:, :]) if z_of else (lambda i: xb[i % 2][:, :])
            vq(lambda e, i=i: e.bn_stats(bns[:, 0:6], tgt(i)[:, 0:512]))
            vq(lambda e, i=i: e.bn_stats(bns[:, 6:12], tgt(i)[:, 512:1024]))
            vq.drain()
            vq(lambda e: e.bn_aggr(mv[:, :], bns[:, :]))
            vq.drain()
            vq(lambda e, i=i: e.tensor_copy(out=B[:, i:i + 1], in_=mv[:, 0:1]))
            vq.inc(ve, 1, lambda e, i=i: e.tensor_copy(out=var[:, i:i + 1], in_=mv[:, 1:2]))
            ve_done.append(ve.n)
        aq.wait(ve)
        aq.inc(ac, 1, lambda e: e.activation(out=var[:, :], in_=var[:, :], func=AF.Sqrt, bias=EPS))
        vq.wait(ac)
        vq(lambda e: e.reciprocal(out=var[:, :], in_=var[:, :]))
        vq.drain()
        vq(lambda e: e.tensor_tensor(out=A[:, :], in0=B[:, :], in1=var[:, :], op=OP.mult))
        vq.drain()
        vq.inc(ve, 1, lambda e: e.tensor_copy(out=B[:, :], in_=var[:, :]))
        ctx.flush(name)



class TileNorm:

    def __init__(self, ctx, name, x_ap, A, B, nbuf=2):
        self.ctx = ctx
        self.x_ap = x_ap
        self.A, self.B = A, B
        self.xb = [ctx.sb(f"{name}_x{i}", [128, 1024], F32) for i in range(2)]
        self.zh = [ctx.sb(f"{name}_z{i}", [128, 1024], BF16) for i in range(2)]
        self.xT = [ctx.sb(f"{name}_t{i}", [128, 1024], BF16) for i in range(2)]
        self.pTf = ctx.ps(f"{name}_pTf", [128, 1024], F32)
        self.ve_done = []
        self.pe_done = []
        self.ac_done = []
        self.consumed = []
        self.zh_consumed = []

    def record(self, i):
        ctx = self.ctx
        ve, pe, ac = ctx.sem("ve"), ctx.sem("pe"), ctx.sem("ac")
        ldx = [ctx.sem("ldx0"), ctx.sem("ldx1")]
        sy, vq, tq, aq = ctx.sy, ctx.vq, ctx.tq, ctx.aq
        if i >= 2:
            sy.wait(ve, self.ve_done[i - 2])
        sy.inc(ldx[i % 2], 16, lambda e, i=i: e.dma_start(out=self.xb[i % 2][:, :],
                                                  in_=self.x_ap[i * 128:(i + 1) * 128, :]))
        vq.wait(ldx[i % 2])
        if i >= 2:
            s, v = self.zh_consumed[i - 2]
            vq.wait(s, v)
        if i >= 1:
            vq.wait(pe, self.pe_done[i - 1])
        vq.inc(ve, 1, lambda e, i=i: e.tensor_scalar(
            out=self.zh[i % 2][:, :], in0=self.xb[i % 2][:, :],
            scalar1=self.B[:, i:i + 1], scalar2=self.A[:, i:i + 1],
            op0=OP.mult, op1=OP.subtract))
        self.ve_done.append(ve.n)
        tq.wait(ve)
        if i >= 1:
            tq.wait(ac, self.ac_done[i - 1])
        for j in range(8):
            tq(lambda e, i=i, j=j: e.matmul(
                out=self.pTf[:, j * 128:(j + 1) * 128],
                lhsT=self.zh[i % 2][:, j * 128:(j + 1) * 128],
                rhs=ctx.iden_bf[:, :], start=True, stop=True))
        tq.inc_prev(pe, 1)
        self.pe_done.append(pe.n)
        self.zh_consumed.append((pe, pe.n))
        aq.wait(pe)
        if i >= 2:
            s, v = self.consumed[i - 2]
            aq.wait(s, v)
        aq.inc(ac, 1, lambda e, i=i: e.copy(out=self.xT[i % 2][:, :], in_=self.pTf[:, :]))
        self.ac_done.append(ac.n)

    def note_consumed(self, sem, val):
        self.consumed.append((sem, val))

    def cur(self, i):
        return self.xT[i % 2]



L0A_PHASES = 9


def emit_l0a(ctx, x_in, x_out):
    nc = ctx.nc
    ld, st_, pe, ve, ac = (ctx.sem(n) for n in ("ld", "st", "pe", "ve", "ac"))
    sy, vq, tq, aq, gq = ctx.sy, ctx.vq, ctx.tq, ctx.aq, ctx.gq

    W_qkv = ctx.dram["L0_attn_a_W_qkv"]
    b_qkv = ctx.dram["L0_attn_a_b_qkv_r"]
    W_proj = ctx.dram["L0_attn_a_W_proj"]
    b_proj = ctx.dram["L0_attn_a_b_proj_r"]
    ln_w = ctx.dram["L0_attn_a_ln_w_r"]
    ln_b = ctx.dram["L0_attn_a_ln_b_r"]
    qkp = ctx.dtmp("l0a_qkp", (16, T, 128), BF16)
    v3 = ctx.dtmp("l0a_v3", (16, T, 64), BF16)

    with ctx.scope():
        A1 = ctx.sb("l0a_A1", [128, NT], F32)
        B1 = ctx.sb("l0a_B1", [128, NT], F32)
        A2 = ctx.sb("l0a_A2", [128, NT], F32)
        B2 = ctx.sb("l0a_B2", [128, NT], F32)
        emit_ln_stats(ctx, x_in, A1, B1, name="l0a_s1")
        emit_ln_stats(ctx, x_in, A2, B2, name="l0a_s2", z_of=(A1, B1))
        Acm = ctx.sb("l0a_Acm", [128, NT], F32)
        Bc = ctx.sb("l0a_Bc", [128, NT], F32)
        Bp = ctx.sb("l0a_Bp", [128, NT], F32)
        vq(lambda e: e.tensor_tensor(out=Bc[:, :], in0=B1[:, :], in1=B2[:, :], op=OP.mult))
        vq(lambda e: e.tensor_tensor(out=Acm[:, :], in0=A1[:, :], in1=B2[:, :], op=OP.mult))
        vq.drain()
        vq(lambda e: e.tensor_tensor(out=Acm[:, :], in0=Acm[:, :], in1=A2[:, :], op=OP.add))
        vq(lambda e: e.tensor_scalar(out=Bp[:, :], in0=B1[:, :], scalar1=1.0, scalar2=None, op0=OP.add))
        vq.drain()
        ctx.flush("l0a_tables")

        if L0A_PHASES < 2:
            return
        wqkv_scope = ctx.scope()
        wqkv_scope.__enter__()
        Wb = ctx.sb("l0a_Wb", [128, 8, 3072], BF16)
        bias_eff = ctx.sb("l0a_bias", [1, 3072], F32)
        with ctx.scope():
            wf = [ctx.sb(f"l0a_wf{i}", [128, 3072], F32) for i in range(2)]
            lnw = ctx.sb("l0a_lnw", [128, 8], F32)
            lnb = ctx.sb("l0a_lnb", [128, 8], F32)
            bq = ctx.sb("l0a_bq", [1, 3072], F32)
            pbias = ctx.ps("l0a_pbias", [1, 3072], F32)
            ldc = ctx.sem("ldc")
            ldw = [ctx.sem("ldw0"), ctx.sem("ldw1")]
            sy.inc(ldc, 16, lambda e: e.dma_start(out=lnw[:, :], in_=ln_w[:, :]))
            sy.inc(ldc, 16, lambda e: e.dma_start(out=lnb[:, :], in_=ln_b[:, :]))
            sy.inc(ldc, 16, lambda e: e.dma_start(out=bq[:, :], in_=b_qkv[:, :]))
            pe_done = []
            for j in range(8):
                if j >= 2:
                    sy.wait(pe, pe_done[j - 2])
                sy.inc(ldw[j % 2], 16, lambda e, j=j: e.dma_start(out=wf[j % 2][:, :],
                                                          in_=W_qkv[j * 128:(j + 1) * 128, :]))
                vq.wait(ldw[j % 2])
                vq.wait(ldc)
                vq.inc(ve, 1, lambda e, j=j: e.tensor_scalar(out=Wb[:, j, :], in0=wf[j % 2][:, :],
                                                             scalar1=lnw[:, j:j + 1], scalar2=None, op0=OP.mult))
                tq.wait(ve)
                for c in range(6):
                    tq(lambda e, j=j, c=c: e.matmul(out=pbias[:, c * 512:(c + 1) * 512], lhsT=lnb[:, j:j + 1],
                                                    rhs=wf[j % 2][:, c * 512:(c + 1) * 512],
                                                    start=(j == 0), stop=(j == 7)))
                tq.inc_prev(pe, 1)
                pe_done.append(pe.n)
            vq.wait(pe)
            vq.inc(ve, 1, lambda e: e.tensor_tensor(out=bias_eff[:, :], in0=pbias[:, :], in1=bq[:, :], op=OP.add))
            ctx.flush("l0a_wprep")

        if L0A_PHASES < 3:
            wqkv_scope.__exit__(None, None, None)
            return
        with ctx.scope():
            tn = TileNorm(ctx, "l0a_q", x_in, Acm, Bc)
            qo = [ctx.sb(f"l0a_qo{i}", [128, 3072], BF16) for i in range(2)]
            pq = ctx.ps("l0a_pq", [128, 3072], F32)
            SUB = 4
            for i in range(NT):
                tn.record(i)
                if SUB < 1:
                    tn.note_consumed(ctx.sem("ac"), ac.n)
                    continue
                tq.wait(ac)
                for c in range(6):
                    for j in range(8):
                        tq(lambda e, i=i, c=c, j=j: e.matmul(
                            out=pq[:, c * 512:(c + 1) * 512], lhsT=tn.cur(i)[:, j * 128:(j + 1) * 128],
                            rhs=Wb[:, j, c * 512:(c + 1) * 512], start=(j == 0), stop=(j == 7) if SUB < 2 else False))
                    if SUB >= 2:
                        tq(lambda e, c=c: e.matmul(out=pq[:, c * 512:(c + 1) * 512], lhsT=ctx.ones_row[0:1, 0:128],
                                                   rhs=bias_eff[:, c * 512:(c + 1) * 512], start=False, stop=True))
                tq.inc_prev(pe, 1)
                tn.note_consumed(pe, pe.n)
                if SUB < 3:
                    continue
                aq.wait(pe)
                stq = [ctx.sem("stq0"), ctx.sem("stq1")]
                if i >= 2:
                    aq.wait(stq[i % 2])
                aq.inc(ac, 1, lambda e, i=i: e.copy(out=qo[i % 2][:, :], in_=pq[:, :]))
                if SUB < 4:
                    continue
                gq.wait(ac)
                gq.inc(stq[i % 2], 16, lambda e, i=i: e.dma_start(
                    out=qkp[:, i * 128:(i + 1) * 128, :].rearrange("s p c -> p s c"),
                    in_=qo[i % 2][:, 0:2048].rearrange("p (s c) -> p s c", c=128)))
                gq.inc(stq[i % 2], 16, lambda e, i=i: e.dma_start(
                    out=v3[:, i * 128:(i + 1) * 128, :].rearrange("s p c -> p s c"),
                    in_=qo[i % 2][:, 2048:3072].rearrange("p (s c) -> p s c", c=64)))
            gq.wait(ctx.sem("stq0"))
            gq.wait(ctx.sem("stq1"))
            ctx.flush("l0a_qkv")
        wqkv_scope.__exit__(None, None, None)

        if L0A_PHASES < 4:
            return
        msaT = [ctx.sb(f"l0a_msaT{p}", [128, T], BF16) for p in range(8)]
        with ctx.scope():
            qTp = [ctx.sb(f"l0a_qT{i}", [128, T], BF16) for i in range(2)]
            kTp = [ctx.sb(f"l0a_kT{i}", [128, T], BF16) for i in range(2)]
            qktm = [ctx.sb(f"l0a_qktm{i}", [128, NT, 128], BF16) for i in range(2)]
            psQT = ctx.ps("l0a_psQT", [128, 512], F32)
            vA = [ctx.sb(f"l0a_v{i}", [128, 2 * NT, 66], BF16) for i in range(2)]
            PT = [ctx.sb(f"l0a_PT{i}", [128, 1024], BF16) for i in range(2)]
            mU = ctx.sb("l0a_mU", [64, 512], BF16)
            rrow = ctx.sb("l0a_rrow", [1, 512], F32)
            psS = [ctx.ps(f"l0a_psS{i}", [128, 1024], F32) for i in range(2)]
            psM = [ctx.ps(f"l0a_psM{i}", [66, 512], F32) for i in range(2)]
            psB = ctx.ps("l0a_psB", [64, 512], F32)
            for b in range(2):
                vq(lambda e, b=b: e.memset(vA[b][:, :, 64:65], 1.0))
                vq(lambda e, b=b: e.memset(vA[b][:, :, 65:66], 0.0))
            vq.drain()
            vq.inc_prev(ve, 1)
            pair_use = []
            for p in range(8):
                b = p % 2
                if p >= 2:
                    sy.wait(pe, pair_use[p - 2])
                else:
                    sy.wait(ve)
                ldv = [ctx.sem("ldv0"), ctx.sem("ldv1")]
                ldq = [ctx.sem("ldq0"), ctx.sem("ldq1")]
                sy.inc(ldv[b], 16, lambda e, p=p, b=b: e.dma_start(
                    out=vA[b][:, 0:NT, 0:64], in_=v3[2 * p, :, :].rearrange("(n p) c -> p n c", p=128)))
                sy.inc(ldv[b], 16, lambda e, p=p, b=b: e.dma_start(
                    out=vA[b][:, NT:2 * NT, 0:64], in_=v3[2 * p + 1, :, :].rearrange("(n p) c -> p n c", p=128)))
                for which, dst in ((0, qTp), (1, kTp)):
                    if which == 1:
                        sy.wait(pe)
                    sy.inc(ldq[b], 16, lambda e, p=p, b=b, which=which: e.dma_start(
                        out=qktm[b][:, :, :],
                        in_=qkp[which * 8 + p, :, :].rearrange("(n p) c -> p n c", p=128)))
                    tq.wait(ldq[b])
                    for g4 in range(8):
                        if not (which == 0 and g4 == 0):
                            tq.wait(ac)
                        for j in range(4):
                            blk = g4 * 4 + j
                            tq(lambda e, b=b, blk=blk, j=j: e.matmul(
                                out=psQT[:, j * 128:(j + 1) * 128],
                                lhsT=qktm[b][:, blk, :],
                                rhs=ctx.iden_bf[:, :], start=True, stop=True))
                        tq.inc_prev(pe, 1)
                        aq.wait(pe)
                        aq.inc(ac, 1, lambda e, dst=dst, b=b, g4=g4: e.copy(
                            out=dst[b][:, g4 * 512:(g4 + 1) * 512], in_=psQT[:, :]))
                tq.wait(ac)
                tq.wait(ldv[b])
                for half in range(2):
                    h = 2 * p + half
                    pb = 64 * half
                    vbase = NT * half
                    exp_done = {}
                    for g in range(8):
                        gb = g % 2
                        for u2 in range(16):
                            if (g, u2) != (0, 0):
                                prev = (g, u2 - 2) if u2 >= 2 else ((g - 1, 14 + u2) if g >= 1 else None)
                                if prev in exp_done:
                                    tq.wait(ac, exp_done[prev])
                            for s in range(2):
                                u = 2 * u2 + s
                                tq(lambda e, b=b, g=g, u=u, u2=u2, s=s, pb=pb: e.matmul(
                                    out=psS[u2 % 2][:, s * 512:(s + 1) * 512],
                                    lhsT=kTp[b][pb:pb + 64, u * 128:(u + 1) * 128],
                                    rhs=qTp[b][pb:pb + 64, g * 512:(g + 1) * 512], start=True, stop=True))
                            tq.inc_prev(pe, 1)
                            aq.wait(pe)
                            aq.inc(ac, 1, lambda e, u2=u2: e.activation(
                                out=PT[u2 % 2][:, :], in_=psS[u2 % 2][:, :], func=AF.Exp,
                                scale=0.125, bias=-EXP_SHIFT))
                            exp_done[(g, u2)] = ac.n
                            tq.wait(ac)
                            for s in range(2):
                                u = 2 * u2 + s
                                tq(lambda e, b=b, gb=gb, u=u, u2=u2, s=s, vbase=vbase: e.matmul(
                                    out=psM[gb][:, :], lhsT=vA[b][:, vbase + u, :],
                                    rhs=PT[u2 % 2][:, s * 512:(s + 1) * 512],
                                    start=(u == 0), stop=(u == 31)))
                            tq.inc_prev(pe, 1)
                        vq.wait(pe)
                        vq.inc(ve, 1, lambda e, gb=gb: e.reciprocal(out=rrow[:, :], in_=psM[gb][64:65, :]))
                        tq.wait(ve)
                        tq.inc(pe, 1, lambda e: e.matmul(out=psB[:, :], lhsT=ctx.ones_row[0:1, 0:64],
                                                         rhs=rrow[:, :], start=True, stop=True))
                        aq.wait(pe)
                        aq.inc(ac, 1, lambda e, gb=gb: e.copy(out=mU[:, :], in_=psM[gb][0:64, :]))
                        vq.wait(ac)
                        vq.inc(ve, 1, lambda e, p=p, half=half, g=g: e.tensor_tensor(
                            out=msaT[p][half * 64:half * 64 + 64, g * 512:(g + 1) * 512],
                            in0=mU[:, :], in1=psB[:, :], op=OP.mult))
                pair_use.append(pe.n)
            ctx.flush("l0a_attn")

        with ctx.scope():
            Wpb = ctx.sb("l0a_Wpb", [128, 8, 1024], BF16)
            bp = ctx.sb("l0a_bp", [1, 1024], F32)
            xb = [ctx.sb(f"l0a_px{i}", [128, 1024], F32) for i in range(2)]
            t1 = [ctx.sb(f"l0a_t1{i}", [128, 1024], F32) for i in range(2)]
            ot = [ctx.sb(f"l0a_ot{i}", [128, 1024], F32) for i in range(2)]
            psP = [ctx.ps(f"l0a_psP{i}", [128, 1024], F32) for i in range(2)]
            gldw = ctx.sem("gldw")
            ldx = [ctx.sem("ldx0"), ctx.sem("ldx1")]
            sto = [ctx.sem("sto0"), ctx.sem("sto1")]
            ldc = ctx.sem("ldc")
            for j in range(8):
                gq.inc(gldw, 16, lambda e, j=j: e.dma_start(out=Wpb[:, j, :], in_=W_proj[j * 128:(j + 1) * 128, :]))
            sy.inc(ldc, 16, lambda e: e.dma_start(out=bp[:, :], in_=b_proj[:, :]))
            tq.wait(gldw)
            tq.wait(ldc)
            ve_done = []
            for i in range(NT):
                ib = i % 2
                if i >= 2:
                    sy.wait(ve, ve_done[i - 2])
                sy.inc(ldx[ib], 16, lambda e, i=i, ib=ib: e.dma_start(out=xb[ib][:, :], in_=x_in[i * 128:(i + 1) * 128, :]))
                if i >= 2:
                    tq.wait(ve, ve_done[i - 2])
                for half in range(2):
                    for p in range(8):
                        tq(lambda e, i=i, ib=ib, half=half, p=p: e.matmul(
                            out=psP[ib][:, half * 512:(half + 1) * 512],
                            lhsT=msaT[p][:, i * 128:(i + 1) * 128],
                            rhs=Wpb[:, p, half * 512:(half + 1) * 512], start=(p == 0), stop=False))
                    tq(lambda e, ib=ib, half=half: e.matmul(
                        out=psP[ib][:, half * 512:(half + 1) * 512], lhsT=ctx.ones_row[0:1, 0:128],
                        rhs=bp[:, half * 512:(half + 1) * 512], start=False, stop=True))
                tq.inc_prev(pe, 1)
                vq.wait(ldx[ib])
                vq.wait(pe)
                if i >= 2:
                    vq.wait(sto[ib])
                vq(lambda e, i=i, ib=ib: e.tensor_scalar(out=t1[ib][:, :], in0=xb[ib][:, :],
                                                         scalar1=Bp[:, i:i + 1], scalar2=A1[:, i:i + 1],
                                                         op0=OP.mult, op1=OP.subtract))
                vq.drain()
                vq.inc(ve, 1, lambda e, ib=ib: e.tensor_tensor(out=ot[ib][:, :], in0=t1[ib][:, :],
                                                               in1=psP[ib][:, :], op=OP.add))
                ve_done.append(ve.n)
                gq.wait(ve)
                gq.inc(sto[ib], 16, lambda e, i=i, ib=ib: e.dma_start(out=x_out[i * 128:(i + 1) * 128, :],
                                                                      in_=ot[ib][:, :]))
            gq.wait(ctx.sem("sto0"))
            gq.wait(ctx.sem("sto1"))
            ctx.flush("l0a_proj")



def build_l0a_launch():
    nc = make_nc()
    stack = contextlib.ExitStack()
    ctx = Ctx(nc, stack)
    with stack:
        declare_consts(ctx)
        x_in = ctx.din("x_in", (T, D), F32)
        x_out = ctx.dout("x_out", (T, D), F32)
        ctx.din("L0_attn_a_W_qkv", (1024, 3072), F32)
        ctx.din("L0_attn_a_b_qkv_r", (1, 3072), F32)
        ctx.din("L0_attn_a_W_proj", (1024, 1024), F32)
        ctx.din("L0_attn_a_b_proj_r", (1, 1024), F32)
        ctx.din("L0_attn_a_ln_w_r", (128, 8), F32)
        ctx.din("L0_attn_a_ln_b_r", (128, 8), F32)
        emit_load_consts(ctx)
        emit_l0a(ctx, x_in, x_out)
        ctx.flush("fin")
    return nc


def l0a_inmaps(inp, xs):
    c = host_constants()
    base = {
        **{k: v for k, v in c.items()},
        "L0_attn_a_W_qkv": np.ascontiguousarray(inp["L0_attn_a_W_qkv"]),
        "L0_attn_a_b_qkv_r": np.ascontiguousarray(inp["L0_attn_a_b_qkv"].reshape(1, -1)),
        "L0_attn_a_W_proj": np.ascontiguousarray(inp["L0_attn_a_W_proj"]),
        "L0_attn_a_b_proj_r": np.ascontiguousarray(inp["L0_attn_a_b_proj"].reshape(1, -1)),
        "L0_attn_a_ln_w_r": np.ascontiguousarray(inp["L0_attn_a_ln_w"].reshape(8, 128).T),
        "L0_attn_a_ln_b_r": np.ascontiguousarray(inp["L0_attn_a_ln_b"].reshape(8, 128).T),
    }
    return [{**base, "x_in": np.ascontiguousarray(xs[i])} for i in range(NCORES)]


def run_l0a(inp, xs):
    nc = build_l0a_launch()
    res = run_bass_kernel_spmd(nc, l0a_inmaps(inp, xs), core_ids=list(range(NCORES)))
    return np.stack([r["x_out"] for r in res.results])



def emit_tern_prep(ctx, W_T, Wt_sb, nchunk, nfree, name):
    nc = ctx.nc
    ve, ac, pe = ctx.sem("ve"), ctx.sem("ac"), ctx.sem("pe")
    ldw = [ctx.sem("ldw0"), ctx.sem("ldw1")]
    sy, vq, tq, aq = ctx.sy, ctx.vq, ctx.tq, ctx.aq
    nelem = float(nchunk * 128 * nfree)
    with ctx.scope():
        wf = [ctx.sb(f"{name}_wf{i}", [128, nfree], F32) for i in range(2)]
        absw = ctx.sb(f"{name}_absw", [128, nfree], F32)
        parts = ctx.sb(f"{name}_parts", [128, nchunk], F32)
        thr = ctx.sb(f"{name}_thr", [128, 1], F32)
        srow = ctx.sb(f"{name}_srow", [1, nchunk], F32)
        sgn = ctx.sb(f"{name}_sgn", [128, nfree], F32)
        psr = ctx.ps(f"{name}_psr", [1, nchunk], F32)
        psb = ctx.ps(f"{name}_psb", [128, 1], F32)
        ve_done = []
        for j in range(nchunk):
            if j >= 2:
                sy.wait(ve, ve_done[j - 2])
            sy.inc(ldw[j % 2], 16, lambda e, j=j: e.dma_start(out=wf[j % 2][:, :],
                                                             in_=W_T[j * 128:(j + 1) * 128, :]))
            vq.wait(ldw[j % 2])
            vq.inc(ve, 1, lambda e, j=j: e.tensor_reduce(out=parts[:, j:j + 1], in_=wf[j % 2][:, :],
                                                         axis=mybir.AxisListType.X, op=OP.add,
                                                         apply_absolute_value=True))
            ve_done.append(ve.n)
        tq.wait(ve)
        tq.inc(pe, 1, lambda e: e.matmul(out=psr[:, :], lhsT=ctx.onescol_f32[:, 0:1], rhs=parts[:, :],
                                         start=True, stop=True))
        vq.wait(pe)
        vq(lambda e: e.tensor_reduce(out=srow[:, 0:1], in_=psr[:, :], axis=mybir.AxisListType.X, op=OP.add))
        vq.drain()
        vq(lambda e: e.tensor_scalar(out=srow[:, 0:1], in0=srow[:, 0:1], scalar1=1.0 / nelem,
                                     scalar2=1e-8, op0=OP.mult, op1=OP.max))
        vq.drain()
        vq.inc(ve, 1, lambda e: e.tensor_scalar(out=srow[:, 0:1], in0=srow[:, 0:1], scalar1=0.5,
                                                scalar2=None, op0=OP.mult))
        tq.wait(ve)
        tq.inc(pe, 1, lambda e: e.matmul(out=psb[:, :], lhsT=ctx.ones_row[0:1, 0:128], rhs=srow[:, 0:1],
                                         start=True, stop=True))
        aq.wait(pe)
        aq.inc(ac, 1, lambda e: e.copy(out=thr[:, :], in_=psb[:, :]))
        sy.wait(ac)
        ac_done = []
        for j in range(nchunk):
            if j >= 2:
                sy.wait(ac, ac_done[j - 2])
            sy.inc(ldw[j % 2], 16, lambda e, j=j: e.dma_start(out=wf[j % 2][:, :],
                                                             in_=W_T[j * 128:(j + 1) * 128, :]))
            aq.wait(ldw[j % 2])
            aq.wait(ve)
            aq(lambda e, j=j: e.activation(out=absw[:, :], in_=wf[j % 2][:, :], func=AF.Abs))
            aq.drain()
            aq.inc(ac, 1, lambda e, j=j: e.activation(out=sgn[:, :], in_=wf[j % 2][:, :], func=AF.Sign))
            ac_done.append(ac.n)
            vq.wait(ac)
            vq(lambda e, j=j: e.tensor_scalar(out=absw[:, :], in0=absw[:, :], scalar1=thr[:, 0:1],
                                              scalar2=None, op0=OP.is_gt))
            vq.drain()
            vq.inc(ve, 1, lambda e, j=j: e.tensor_tensor(out=Wt_sb[:, j, :], in0=absw[:, :], in1=sgn[:, :],
                                                         op=OP.mult))
        ctx.flush(f"{name}_prep")


def emit_ffn(ctx, x_in, x_out, tag):
    nc = ctx.nc
    ve, ac, pe = ctx.sem("ve"), ctx.sem("ac"), ctx.sem("pe")
    ldc = ctx.sem("ldc")
    sy, vq, tq, aq, gq = ctx.sy, ctx.vq, ctx.tq, ctx.aq, ctx.gq
    W1T = ctx.dram["L0_ffn_W1T"]
    W2T = ctx.dram["L0_ffn_W2T"]
    b1r = ctx.dram["L0_ffn_b1_r"]
    b2r = ctx.dram["L0_ffn_b2_r"]

    with ctx.scope():
        A1 = ctx.sb(f"{tag}_A1", [128, NT], F32)
        B1 = ctx.sb(f"{tag}_B1", [128, NT], F32)
        emit_ln_stats(ctx, x_in, A1, B1, name=f"{tag}_s1")

        W1t = ctx.sb(f"{tag}_W1t", [128, 8, 4096], BF16)
        W2t = ctx.sb(f"{tag}_W2t", [128, 32, 1024], BF16)
        emit_tern_prep(ctx, W1T, W1t, 8, 4096, f"{tag}_w1")
        emit_tern_prep(ctx, W2T, W2t, 32, 1024, f"{tag}_w2")

        with ctx.scope():
            b1c = ctx.sb(f"{tag}_b1c", [128, 32], F32)
            b2 = ctx.sb(f"{tag}_b2", [1, 1024], F32)
            sy.inc(ldc, 16, lambda e: e.dma_start(out=b1c[:, :], in_=b1r[:, :]))
            sy.inc(ldc, 16, lambda e: e.dma_start(out=b2[:, :], in_=b2r[:, :]))
            sy.wait(ldc)
            ctx.flush(f"{tag}_bias")

            tn = TileNorm(ctx, f"{tag}_m", x_in, A1, B1)
            hb = [ctx.sb(f"{tag}_hb{i}", [128, 128], BF16) for i in range(2)]
            ot = [ctx.sb(f"{tag}_ot{i}", [128, 1024], F32) for i in range(2)]
            psH = [ctx.ps(f"{tag}_psH{i}", [128, 128], F32) for i in range(2)]
            psO = [ctx.ps(f"{tag}_psO{i}", [128, 1024], F32) for i in range(2)]
            sto = [ctx.sem("sto0"), ctx.sem("sto1")]
            ldx = [ctx.sem("ldx0"), ctx.sem("ldx1")]
            ve_done = []
            for i in range(NT):
                ib = i % 2
                tn.record(i)
                tq.wait(ac)
                if i >= 2:
                    tq.wait(ve, ve_done[i - 2])
                for c in range(32):
                    cb = c % 2
                    if c >= 2:
                        tq.wait(ac)
                    for j in range(8):
                        tq(lambda e, i=i, c=c, j=j, cb=cb: e.matmul(
                            out=psH[cb][:, :], lhsT=W1t[:, j, c * 128:(c + 1) * 128],
                            rhs=tn.cur(i)[:, j * 128:(j + 1) * 128], start=(j == 0), stop=(j == 7)))
                    tq.inc_prev(pe, 1)
                    aq.wait(pe)
                    aq.inc(ac, 1, lambda e, c=c, cb=cb: e.activation(
                        out=hb[cb][:, :], in_=psH[cb][:, :], func=AF.Gelu, bias=b1c[:, c:c + 1]))
                    tq.wait(ac)
                    for half in range(2):
                        tq(lambda e, ib=ib, c=c, cb=cb, half=half: e.matmul(
                            out=psO[ib][:, half * 512:(half + 1) * 512], lhsT=hb[cb][:, :],
                            rhs=W2t[:, c, half * 512:(half + 1) * 512], start=(c == 0), stop=False))
                    tq.inc_prev(pe, 1)
                for half in range(2):
                    tq(lambda e, ib=ib, half=half: e.matmul(
                        out=psO[ib][:, half * 512:(half + 1) * 512], lhsT=ctx.ones_row[0:1, 0:128],
                        rhs=b2[:, half * 512:(half + 1) * 512], start=False, stop=True))
                tq.inc_prev(pe, 1)
                tn.note_consumed(pe, pe.n)
                vq.wait(pe)
                if i >= 2:
                    vq.wait(sto[ib])
                vq.inc(ve, 1, lambda e, i=i, ib=ib: e.tensor_tensor(
                    out=ot[ib][:, :], in0=tn.xb[ib][:, :], in1=psO[ib][:, :], op=OP.add))
                tn.ve_done[i] = ve.n
                ve_done.append(ve.n)
                gq.wait(ve)
                gq.inc(sto[ib], 16, lambda e, i=i, ib=ib: e.dma_start(
                    out=x_out[i * 128:(i + 1) * 128, :], in_=ot[ib][:, :]))
            gq.wait(ctx.sem("sto0"))
            gq.wait(ctx.sem("sto1"))
            ctx.flush(f"{tag}_main")


def build_ffn_launch():
    nc = make_nc()
    stack = contextlib.ExitStack()
    ctx = Ctx(nc, stack)
    with stack:
        declare_consts(ctx)
        x_in = ctx.din("x_in", (T, D), F32)
        x_out = ctx.dout("x_out", (T, D), F32)
        ctx.din("L0_ffn_W1T", (1024, 4096), F32)
        ctx.din("L0_ffn_W2T", (4096, 1024), F32)
        ctx.din("L0_ffn_b1_r", (128, 32), F32)
        ctx.din("L0_ffn_b2_r", (1, 1024), F32)
        emit_load_consts(ctx)
        emit_ffn(ctx, x_in, x_out, "ffn")
        ctx.flush("fin")
    return nc


def ffn_inmaps(inp, xs):
    c = host_constants()
    base = {
        **c,
        "L0_ffn_W1T": np.ascontiguousarray(inp["L0_ffn_W1"].T),
        "L0_ffn_W2T": np.ascontiguousarray(inp["L0_ffn_W2"].T),
        "L0_ffn_b1_r": np.ascontiguousarray(inp["L0_ffn_b1"].reshape(32, 128).T),
        "L0_ffn_b2_r": np.ascontiguousarray(inp["L0_ffn_b2"].reshape(1, -1)),
    }
    return [{**base, "x_in": np.ascontiguousarray(xs[i])} for i in range(NCORES)]


def run_ffn(inp, xs):
    nc = build_ffn_launch()
    res = run_bass_kernel_spmd(nc, ffn_inmaps(inp, xs), core_ids=list(range(NCORES)))
    return np.stack([r["x_out"] for r in res.results])



def emit_ret(ctx, x_in, x_out, tag="ret"):
    nc = ctx.nc
    ve, ac, pe = ctx.sem("ve"), ctx.sem("ac"), ctx.sem("pe")
    sy, vq, tq, aq, gq = ctx.sy, ctx.vq, ctx.tq, ctx.aq, ctx.gq
    WQ, WK, WV = (ctx.dram[f"L0_attn_b_W_{x}"] for x in "QKV")
    sc_q, ss_q = ctx.dram["ret_sc_q"], ctx.dram["ret_ss_q"]
    sc_k, ss_k = ctx.dram["ret_sc_k"], ctx.dram["ret_ss_k"]
    Dt = ctx.dram["ret_Dt"]
    QTd = ctx.dtmp(f"{tag}_QTd", (8, 128, T), BF16)
    KTd = ctx.dtmp(f"{tag}_KTd", (8, 128, T), BF16)
    Vd = ctx.dtmp(f"{tag}_Vd", (T, D), BF16)

    with ctx.scope():
        A1 = ctx.sb(f"{tag}_A1", [128, NT], F32)
        B1 = ctx.sb(f"{tag}_B1", [128, NT], F32)
        emit_ln_stats(ctx, x_in, A1, B1, name=f"{tag}_s1")

        with ctx.scope():
            Wb = ctx.sb(f"{tag}_Wb", [128, 8, 3072], BF16)
            gldw = ctx.sem("gldw")
            for m, W in enumerate((WQ, WK, WV)):
                for j in range(8):
                    gq.inc(gldw, 16, lambda e, m=m, j=j, W=W: e.dma_start(
                        out=Wb[:, j, m * 1024:(m + 1) * 1024], in_=W[j * 128:(j + 1) * 128, :]))
            gq.wait(gldw)
            ctx.flush(f"{tag}_wload")

            tn = TileNorm(ctx, f"{tag}_m", x_in, A1, B1)
            ldt = [ctx.sem("ldt0"), ctx.sem("ldt1")]
            stq = [ctx.sem("stq0"), ctx.sem("stq1")]
            tab = [ctx.sb(f"{tag}_tab{i}", [128, 4, 1024], F32) for i in range(2)]
            Qf = ctx.sb(f"{tag}_Qf", [128, 1024], F32)
            rb = ctx.sb(f"{tag}_rb", [128, 1024], F32)
            ta = ctx.sb(f"{tag}_ta", [128, 1024], F32)
            qkb = [ctx.sb(f"{tag}_qkb{i}", [128, 2048], BF16) for i in range(2)]
            vb = [ctx.sb(f"{tag}_vb{i}", [128, 1024], BF16) for i in range(2)]
            qkT = [ctx.sb(f"{tag}_qkT{i}", [128, 2048], BF16) for i in range(2)]
            psA = ctx.ps(f"{tag}_psA", [128, 1024], F32)
            psT = ctx.ps(f"{tag}_psT", [128, 2048], F32)
            ve_done2 = []
            ac_done2 = []
            tp_done = []
            for i in range(NT):
                ib = i % 2
                tn.record(i)
                if i >= 2:
                    sy.wait(ve, ve_done2[-1])
                sy.inc(ldt[ib], 16, lambda e, i=i, ib=ib: e.dma_start(
                    out=tab[ib][:, 0, :], in_=sc_q[i * 128:(i + 1) * 128, :]))
                sy.inc(ldt[ib], 16, lambda e, i=i, ib=ib: e.dma_start(
                    out=tab[ib][:, 1, :], in_=ss_q[i * 128:(i + 1) * 128, :]))
                sy.inc(ldt[ib], 16, lambda e, i=i, ib=ib: e.dma_start(
                    out=tab[ib][:, 2, :], in_=sc_k[i * 128:(i + 1) * 128, :]))
                sy.inc(ldt[ib], 16, lambda e, i=i, ib=ib: e.dma_start(
                    out=tab[ib][:, 3, :], in_=ss_k[i * 128:(i + 1) * 128, :]))
                tq.wait(ac)
                for m in range(3):
                    if not (i == 0 and m == 0):
                        tq.wait(ac)
                    for half in range(2):
                        for j in range(8):
                            tq(lambda e, i=i, m=m, half=half, j=j: e.matmul(
                                out=psA[:, half * 512:(half + 1) * 512],
                                lhsT=tn.cur(i)[:, j * 128:(j + 1) * 128],
                                rhs=Wb[:, j, m * 1024 + half * 512:m * 1024 + (half + 1) * 512],
                                start=(j == 0), stop=(j == 7)))
                    tq.inc_prev(pe, 1)
                    aq.wait(pe)
                    if m < 2:
                        if ve_done2:
                            aq.wait(ve, ve_done2[-1])
                        aq.inc(ac, 1, lambda e: e.copy(out=Qf[:, :], in_=psA[:, :]))
                        vq.wait(ac)
                        vq.wait(ldt[ib])
                        q3o = Qf.rearrange("p (n two) -> p n two", two=2)
                        r3o = rb.rearrange("p (n two) -> p n two", two=2)
                        vq(lambda e, q3=q3o, r3=r3o: e.tensor_scalar(
                            out=r3[:, :, 0], in0=q3[:, :, 1], scalar1=-1.0, scalar2=None, op0=OP.mult))
                        vq(lambda e, q3=q3o, r3=r3o: e.tensor_copy(out=r3[:, :, 1], in_=q3[:, :, 0]))
                        vq.drain()
                        vq(lambda e, ib=ib, m=m: e.tensor_tensor(
                            out=ta[:, :], in0=Qf[:, :], in1=tab[ib][:, 2 * m, :], op=OP.mult))
                        vq(lambda e, ib=ib, m=m: e.tensor_tensor(
                            out=rb[:, :], in0=rb[:, :], in1=tab[ib][:, 2 * m + 1, :], op=OP.mult))
                        vq.drain()
                        if i >= 2 and m == 0:
                            vq.wait(pe, tp_done[i - 2])
                        vq.inc(ve, 1, lambda e, ib=ib, m=m: e.tensor_tensor(
                            out=qkb[ib][:, m * 1024:(m + 1) * 1024], in0=ta[:, :], in1=rb[:, :], op=OP.add))
                        ve_done2.append(ve.n)
                    else:
                        if i >= 2:
                            aq.wait(stq[ib])
                        aq.inc(ac, 1, lambda e, ib=ib: e.copy(out=vb[ib][:, :], in_=psA[:, :]))
                        gq.wait(ac)
                        gq.inc(stq[ib], 16, lambda e, i=i, ib=ib: e.dma_start(
                            out=Vd[i * 128:(i + 1) * 128, :], in_=vb[ib][:, :]))
                tq.wait(ve)
                if ac_done2:
                    tq.wait(ac, ac_done2[-1])
                for j in range(16):
                    tq(lambda e, ib=ib, j=j: e.matmul(
                        out=psT[:, j * 128:(j + 1) * 128],
                        lhsT=qkb[ib][:, j * 128:(j + 1) * 128],
                        rhs=ctx.iden_bf[:, :], start=True, stop=True))
                tq.inc_prev(pe, 1)
                tp_done.append(pe.n)
                tn.note_consumed(pe, pe.n)
                aq.wait(pe)
                if i >= 2:
                    aq.wait(stq[ib])
                aq.inc(ac, 1, lambda e, ib=ib: e.copy(out=qkT[ib][:, :], in_=psT[:, :]))
                ac_done2.append(ac.n)
                gq.wait(ac)
                gq.inc(stq[ib], 16, lambda e, i=i, ib=ib: e.dma_start(
                    out=QTd[:, :, i * 128:(i + 1) * 128].rearrange("j p c -> p j c"),
                    in_=qkT[ib][:, 0:1024].rearrange("p (j c) -> p j c", c=128)))
                gq.inc(stq[ib], 16, lambda e, i=i, ib=ib: e.dma_start(
                    out=KTd[:, :, i * 128:(i + 1) * 128].rearrange("j p c -> p j c"),
                    in_=qkT[ib][:, 1024:2048].rearrange("p (j c) -> p j c", c=128)))
            gq.wait(ctx.sem("stq0"))
            gq.wait(ctx.sem("stq1"))
            ctx.flush(f"{tag}_r1")

        with ctx.scope():
            ldx = [ctx.sem("ldx0"), ctx.sem("ldx1")]
            ldt = [ctx.sem("ldt0"), ctx.sem("ldt1")]
            ldk = [ctx.sem("ldk0"), ctx.sem("ldk1")]
            sto = [ctx.sem("sto0"), ctx.sem("sto1")]
            ldv = ctx.sem("ldv0")
            Vr = ctx.sb(f"{tag}_Vr", [128, NT, 1024], BF16)
            QTg = [ctx.sb(f"{tag}_QTg{i}", [128, 8, 512], BF16) for i in range(2)]
            KTc = [ctx.sb(f"{tag}_KTc{i}", [128, 8, 128], BF16) for i in range(2)]
            Dtb = [ctx.sb(f"{tag}_Dtb{i}", [128, 512], F32) for i in range(2)]
            SDt = ctx.sb(f"{tag}_SDt", [128, NT, 512], BF16)
            xb = [ctx.sb(f"{tag}_rx{i}", [128, 1024], F32) for i in range(2)]
            ot = [ctx.sb(f"{tag}_rot{i}", [128, 1024], F32) for i in range(2)]
            psS = [ctx.ps(f"{tag}_psS{i}", [128, 512], F32) for i in range(2)]
            psO = [ctx.ps(f"{tag}_psO{i}", [128, 1024], F32) for i in range(2)]
            sy.inc(ldv, 16, lambda e: e.dma_start(
                out=Vr[:, :, :], in_=Vd.rearrange("(n p) c -> p n c", p=128)))
            tq.wait(ldv)
            ve_done3 = []
            qg_done = []
            mc_pe = []
            sd_ve = []
            for g in range(8):
                gb = g % 2
                nmc = 4 * (g + 1)
                if g >= 2:
                    sy.wait(pe, qg_done[g - 2])
                for j in range(8):
                    sy.inc(ldt[gb], 16, lambda e, g=g, gb=gb, j=j: e.dma_start(
                        out=QTg[gb][:, j, :], in_=QTd[j, :, g * 512:(g + 1) * 512]))
                tq.wait(ldt[gb])
                for mc in range(nmc):
                    mb = mc % 2
                    if len(mc_pe) >= 2:
                        sy.wait(pe, mc_pe[-2])
                    if len(sd_ve) >= 2:
                        sy.wait(ve, sd_ve[-2])
                    for j in range(8):
                        sy.inc(ldk[mb], 16, lambda e, mc=mc, mb=mb, j=j: e.dma_start(
                            out=KTc[mb][:, j, :], in_=KTd[j, :, mc * 128:(mc + 1) * 128]))
                    sy.inc(ldk[mb], 16, lambda e, g=g, mc=mc, mb=mb: e.dma_start(
                        out=Dtb[mb][:, :], in_=Dt[mc * 128:(mc + 1) * 128, g * 512:(g + 1) * 512]))
                    if mc >= 2:
                        tq.wait(ve)
                    tq.wait(ldk[mb])
                    for j in range(8):
                        tq(lambda e, gb=gb, mb=mb, j=j: e.matmul(
                            out=psS[mb][:, :], lhsT=KTc[mb][:, j, :], rhs=QTg[gb][:, j, :],
                            start=(j == 0), stop=(j == 7)))
                    tq.inc_prev(pe, 1)
                    mc_pe.append(pe.n)
                    vq.wait(pe)
                    vq.inc(ve, 1, lambda e, mc=mc, mb=mb: e.tensor_tensor(
                        out=SDt[:, mc, :], in0=psS[mb][:, :], in1=Dtb[mb][:, :], op=OP.mult))
                    sd_ve.append(ve.n)
                for tt in range(4):
                    i = g * 4 + tt
                    ib = i % 2
                    if ve_done3:
                        sy.wait(ve, ve_done3[-1])
                    sy.inc(ldx[ib], 16, lambda e, i=i, ib=ib: e.dma_start(
                        out=xb[ib][:, :], in_=x_in[i * 128:(i + 1) * 128, :]))
                    tq.wait(ve)
                    if len(ve_done3) >= 2:
                        tq.wait(ve, ve_done3[-2])
                    for half in range(2):
                        for mc in range(nmc):
                            tq(lambda e, tt=tt, mc=mc, ib=ib, half=half, nmc=nmc: e.matmul(
                                out=psO[ib][:, half * 512:(half + 1) * 512],
                                lhsT=SDt[:, mc, tt * 128:(tt + 1) * 128],
                                rhs=Vr[:, mc, half * 512:(half + 1) * 512],
                                start=(mc == 0), stop=(mc == nmc - 1)))
                    tq.inc_prev(pe, 1)
                    if tt == 3:
                        qg_done.append(pe.n)
                    vq.wait(pe)
                    vq.wait(ldx[ib])
                    if len(ve_done3) >= 2:
                        vq.wait(sto[ib])
                    vq.inc(ve, 1, lambda e, ib=ib: e.tensor_tensor(
                        out=ot[ib][:, :], in0=xb[ib][:, :], in1=psO[ib][:, :], op=OP.add))
                    ve_done3.append(ve.n)
                    gq.wait(ve)
                    gq.inc(sto[ib], 16, lambda e, i=i, ib=ib: e.dma_start(
                        out=x_out[i * 128:(i + 1) * 128, :], in_=ot[ib][:, :]))
            gq.wait(ctx.sem("sto0"))
            gq.wait(ctx.sem("sto1"))
            ctx.flush(f"{tag}_r2")


def ret_tables():
    SCALE_BASE = 512.0
    GAMMA = 0.96875
    d = D
    half = d // 2
    scale_vec = (np.arange(0, d, 2, dtype=np.float64) + 0.4 * d) / (1.4 * d)
    pos = np.arange(T, dtype=np.float64)
    scale = scale_vec[None, :] ** (pos[:, None] / SCALE_BASE)
    inv_freq = 1.0 / 10000.0 ** (np.arange(half, dtype=np.float64) / half)
    sinusoid = pos[:, None] * inv_freq[None, :]
    sin_s = np.repeat(np.sin(sinusoid) * scale, 2, axis=-1)
    cos_s = np.repeat(np.cos(sinusoid) * scale, 2, axis=-1)
    sin_d = np.repeat(np.sin(sinusoid) / scale, 2, axis=-1)
    cos_d = np.repeat(np.cos(sinusoid) / scale, 2, axis=-1)
    n = np.arange(T)[:, None]
    m = np.arange(T)[None, :]
    diff = (n - m).astype(np.float64)
    Dm = np.where(diff >= 0, GAMMA ** np.maximum(diff, 0), 0.0)
    return {
        "ret_sc_q": cos_s.astype(np.float32), "ret_ss_q": sin_s.astype(np.float32),
        "ret_sc_k": cos_d.astype(np.float32), "ret_ss_k": sin_d.astype(np.float32),
        "ret_Dt": np.ascontiguousarray(Dm.T.astype(np.float32)),
    }


def build_ret_launch():
    nc = make_nc()
    stack = contextlib.ExitStack()
    ctx = Ctx(nc, stack)
    with stack:
        declare_consts(ctx)
        x_in = ctx.din("x_in", (T, D), F32)
        x_out = ctx.dout("x_out", (T, D), F32)
        for x in "QKV":
            ctx.din(f"L0_attn_b_W_{x}", (1024, 1024), F32)
        for t in ("ret_sc_q", "ret_ss_q", "ret_sc_k", "ret_ss_k"):
            ctx.din(t, (T, D), F32)
        ctx.din("ret_Dt", (T, T), F32)
        emit_load_consts(ctx)
        emit_ret(ctx, x_in, x_out)
        ctx.flush("fin")
    return nc


def ret_inmaps(inp, xs):
    c = host_constants()
    tabs = ret_tables()
    base = {
        **c, **tabs,
        "L0_attn_b_W_Q": np.ascontiguousarray(inp["L0_attn_b_W_Q"]),
        "L0_attn_b_W_K": np.ascontiguousarray(inp["L0_attn_b_W_K"]),
        "L0_attn_b_W_V": np.ascontiguousarray(inp["L0_attn_b_W_V"]),
    }
    return [{**base, "x_in": np.ascontiguousarray(xs[i])} for i in range(NCORES)]


def run_ret(inp, xs):
    nc = build_ret_launch()
    res = run_bass_kernel_spmd(nc, ret_inmaps(inp, xs), core_ids=list(range(NCORES)))
    return np.stack([r["x_out"] for r in res.results])



def emit_gla(ctx, x_in, x_out, tag="gla"):
    nc = ctx.nc
    ve, ac, pe = ctx.sem("ve"), ctx.sem("ac"), ctx.sem("pe")
    sy, vq, tq, aq, gq = ctx.sy, ctx.vq, ctx.tq, ctx.aq, ctx.gq
    Wq, Wk, Wv, Wg = (ctx.dram[f"L1_attn_a_W{x}"] for x in ("q", "k", "v", "g"))
    Wo = ctx.dram["L1_attn_a_Wo"]
    qtTd = ctx.dtmp(f"{tag}_qtTd", (8, 128, T), BF16)
    ktTd = ctx.dtmp(f"{tag}_ktTd", (8, 128, T), BF16)
    ktd = ctx.dtmp(f"{tag}_ktd", (T, D), BF16)
    vtd = ctx.dtmp(f"{tag}_vtd", (T, D), BF16)
    hvd = ctx.dtmp(f"{tag}_hvd", (NT, 128, 16), F32)

    with ctx.scope():
        A1 = ctx.sb(f"{tag}_A1", [128, NT], F32)
        B1 = ctx.sb(f"{tag}_B1", [128, NT], F32)
        emit_ln_stats(ctx, x_in, A1, B1, name=f"{tag}_s1")

        with ctx.scope():
            Wb = ctx.sb(f"{tag}_Wb", [128, 8, 4096], BF16)
            gldw = ctx.sem("gldw")
            for m, W in enumerate((Wq, Wk, Wv, Wg)):
                for j in range(8):
                    gq.inc(gldw, 16, lambda e, m=m, j=j, W=W: e.dma_start(
                        out=Wb[:, j, m * 1024:(m + 1) * 1024], in_=W[j * 128:(j + 1) * 128, :]))
            gq.wait(gldw)
            ctx.flush(f"{tag}_wload")

            tn = TileNorm(ctx, f"{tag}_m", x_in, A1, B1)
            stq = [ctx.sem("stq0"), ctx.sem("stq1")]
            qkf = [ctx.sb(f"{tag}_qkf{i}", [128, 2048], F32) for i in range(2)]
            gf = [ctx.sb(f"{tag}_gf{i}", [128, 1024], F32) for i in range(2)]
            tmp1 = [ctx.sb(f"{tag}_tmp1{i}", [128, 1024], F32) for i in range(2)]
            lna = [ctx.sb(f"{tag}_lna{i}", [128, 1024], F32) for i in range(2)]
            ss = [ctx.sb(f"{tag}_ss{i}", [128, 16], F32) for i in range(2)]
            qkn = [ctx.sb(f"{tag}_qkn{i}", [128, 2048], BF16) for i in range(2)]
            qknT = [ctx.sb(f"{tag}_qknT{i}", [128, 2048], BF16) for i in range(2)]
            lnaT = ctx.sb(f"{tag}_lnaT", [128, 1024], F32)
            csT = ctx.sb(f"{tag}_csT", [128, 8, 128], F32)
            hv = [ctx.sb(f"{tag}_hv{i}", [128, 16], F32) for i in range(2)]
            earg = ctx.sb(f"{tag}_earg", [128, 2048], F32)
            BB = ctx.sb(f"{tag}_BB", [128, 2048], BF16)
            qkt = [ctx.sb(f"{tag}_qkt{i}", [128, 2048], BF16) for i in range(2)]
            kvb = [ctx.sb(f"{tag}_kvb{i}", [128, 2048], BF16) for i in range(2)]
            psA = ctx.ps(f"{tag}_psA", [128, 2048], F32)
            psT = ctx.ps(f"{tag}_psT", [128, 1024], F32)
            done = []
            for i in range(NT):
                ib = i % 2
                d = {}
                tn.record(i)
                tq.wait(ac)
                if i >= 2:
                    tq.wait(ve, done[i - 2]["all_ve"])
                for m in range(2):
                    for half in range(2):
                        for j in range(8):
                            tq(lambda e, i=i, m=m, half=half, j=j: e.matmul(
                                out=psA[:, m * 1024 + half * 512:m * 1024 + (half + 1) * 512],
                                lhsT=tn.cur(i)[:, j * 128:(j + 1) * 128],
                                rhs=Wb[:, j, m * 1024 + half * 512:m * 1024 + (half + 1) * 512],
                                start=(j == 0), stop=(j == 7)))
                tq.inc_prev(pe, 1)
                aq.wait(pe)
                aq.inc(ac, 1, lambda e, ib=ib: e.copy(out=qkf[ib][:, :], in_=psA[:, :]))
                d["qk_evac"] = ac.n
                aq.drain()
                aq.inc(ac, 1, lambda e, ib=ib: e.activation(out=tmp1[ib][:, :], in_=qkf[ib][:, 0:1024], func=AF.Square))
                vq.wait(ac)
                for s in range(8):
                    vq(lambda e, ib=ib, s=s: e.tensor_reduce(out=ss[ib][:, s:s + 1],
                                                             in_=tmp1[ib][:, s * 128:(s + 1) * 128],
                                                             axis=mybir.AxisListType.X, op=OP.add))
                vq.drain()
                vq.inc(ve, 1, lambda e, ib=ib: e.tensor_scalar(
                    out=ss[ib][:, 0:8], in0=ss[ib][:, 0:8], scalar1=1e-24, scalar2=None, op0=OP.max))
                aq.wait(ve)
                aq.inc(ac, 1, lambda e, ib=ib: e.activation(out=tmp1[ib][:, :], in_=qkf[ib][:, 1024:2048], func=AF.Square))
                vq.wait(ac)
                for s in range(8):
                    vq(lambda e, ib=ib, s=s: e.tensor_reduce(out=ss[ib][:, 8 + s:9 + s],
                                                             in_=tmp1[ib][:, s * 128:(s + 1) * 128],
                                                             axis=mybir.AxisListType.X, op=OP.add))
                vq.drain()
                vq.inc(ve, 1, lambda e, ib=ib: e.tensor_scalar(
                    out=ss[ib][:, 8:16], in0=ss[ib][:, 8:16], scalar1=1e-24, scalar2=None, op0=OP.max))
                aq.wait(ve)
                aq.drain()
                aq(lambda e, ib=ib: e.activation(out=ss[ib][:, :], in_=ss[ib][:, :], func=AF.Ln))
                aq.drain()
                aq.inc(ac, 1, lambda e, ib=ib: e.activation(out=ss[ib][:, :], in_=ss[ib][:, :], func=AF.Exp, scale=-0.5))
                d["rstd"] = ac.n
                vq.wait(ac, d["rstd"])
                if i >= 2:
                    vq.wait(pe, done[i - 2]["qkn_tr"])
                for s in range(16):
                    vq(lambda e, ib=ib, s=s: e.tensor_scalar(
                        out=qkn[ib][:, s * 128:(s + 1) * 128], in0=qkf[ib][:, s * 128:(s + 1) * 128],
                        scalar1=ss[ib][:, s:s + 1], scalar2=None, op0=OP.mult))
                vq.drain()
                vq.inc(ve, 1, lambda e, ib=ib: e.tensor_copy(out=qkn[ib][:, 0:1], in_=qkn[ib][:, 0:1]))
                d["qkn"] = ve.n
                tq.wait(ac, d["qk_evac"])
                for m in range(2):
                    for half in range(2):
                        for j in range(8):
                            tq(lambda e, i=i, m=m, half=half, j=j: e.matmul(
                                out=psA[:, m * 1024 + half * 512:m * 1024 + (half + 1) * 512],
                                lhsT=tn.cur(i)[:, j * 128:(j + 1) * 128],
                                rhs=Wb[:, j, (2 + m) * 1024 + half * 512:(2 + m) * 1024 + (half + 1) * 512],
                                start=(j == 0), stop=(j == 7)))
                tq.inc_prev(pe, 1)
                tn.note_consumed(pe, pe.n)
                aq.wait(pe)
                if i >= 2:
                    aq.wait(stq[ib])
                aq.inc(ac, 1, lambda e, ib=ib: e.copy(out=kvb[ib][:, 1024:2048], in_=psA[:, 0:1024]))
                aq.inc(ac, 1, lambda e, ib=ib: e.copy(out=gf[ib][:, :], in_=psA[:, 1024:2048]))
                d["vg_evac"] = ac.n
                aq.drain()
                aq(lambda e, ib=ib: e.activation(out=tmp1[ib][:, :], in_=gf[ib][:, :], func=AF.Abs))
                aq.drain()
                aq(lambda e, ib=ib: e.activation(out=tmp1[ib][:, :], in_=tmp1[ib][:, :], func=AF.Exp, scale=-1.0))
                aq.drain()
                aq.inc(ac, 1, lambda e, ib=ib: e.activation(out=tmp1[ib][:, :], in_=tmp1[ib][:, :], func=AF.Ln, bias=1.0))
                vq.wait(ac)
                vq(lambda e, ib=ib: e.tensor_scalar(out=lna[ib][:, :], in0=gf[ib][:, :], scalar1=0.0,
                                                    scalar2=None, op0=OP.min))
                vq.drain()
                vq.inc(ve, 1, lambda e, ib=ib: e.tensor_tensor(out=lna[ib][:, :], in0=lna[ib][:, :],
                                                               in1=tmp1[ib][:, :], op=OP.subtract))
                d["lna"] = ve.n
                for which in range(2):
                    tq.wait(ve, d["qkn"])
                    tq.wait(ac)
                    for j in range(8):
                        tq(lambda e, ib=ib, which=which, j=j: e.matmul(
                            out=psT[:, j * 128:(j + 1) * 128],
                            lhsT=qkn[ib][:, which * 1024 + j * 128:which * 1024 + (j + 1) * 128],
                            rhs=ctx.iden_bf[:, :], start=True, stop=True))
                    tq.inc_prev(pe, 1)
                    aq.wait(pe)
                    aq.inc(ac, 1, lambda e, ib=ib, which=which: e.copy(
                        out=qknT[ib][:, which * 1024:(which + 1) * 1024], in_=psT[:, :]))
                d["qkn_tr"] = pe.n
                tq.wait(ve, d["lna"])
                tq.wait(ac)
                for j in range(8):
                    tq(lambda e, ib=ib, j=j: e.matmul(
                        out=psT[:, j * 128:(j + 1) * 128],
                        lhsT=lna[ib][:, j * 128:(j + 1) * 128],
                        rhs=ctx.iden_f32[:, :], start=True, stop=True))
                tq.inc_prev(pe, 1)
                d["lna_tr"] = pe.n
                aq.wait(pe)
                aq.inc(ac, 1, lambda e: e.copy(out=lnaT[:, :], in_=psT[:, :]))
                vq.wait(ac)
                for j in range(8):
                    for c in range(2):
                        vq(lambda e, j=j, c=c: e.tensor_tensor_scan(
                            out=csT[:, j, c * 64:(c + 1) * 64],
                            data0=lnaT[:, j * 128 + c * 64:j * 128 + c * 64 + 64],
                            data1=lnaT[:, j * 128 + c * 64:j * 128 + c * 64 + 64],
                            initial=0.0, op0=OP.add, op1=OP.bypass))
                vq.drain()
                if i >= 2:
                    vq.wait(stq[ib])
                vq(lambda e, ib=ib: e.tensor_scalar(
                    out=hv[ib][:, 0:8], in0=csT[:, :, 63:64].rearrange("p j one -> p (j one)"),
                    scalar1=0.5, scalar2=None, op0=OP.mult))
                vq.drain()
                vq.inc(ve, 1, lambda e, ib=ib: e.tensor_scalar(
                    out=hv[ib][:, 8:16], in0=csT[:, :, 127:128].rearrange("p j one -> p (j one)"),
                    scalar1=0.5, scalar2=None, op0=OP.mult))
                d["hv"] = ve.n
                vq.drain()
                for j in range(8):
                    for c in range(2):
                        vq(lambda e, ib=ib, j=j, c=c: e.tensor_scalar(
                            out=earg[:, j * 128 + c * 64:j * 128 + c * 64 + 64],
                            in0=csT[:, j, c * 64:(c + 1) * 64],
                            scalar1=hv[ib][:, c * 8 + j:c * 8 + j + 1], scalar2=85.0,
                            op0=OP.subtract, op1=OP.min))
                        vq(lambda e, ib=ib, j=j, c=c: e.tensor_scalar(
                            out=earg[:, 1024 + j * 128 + c * 64:1024 + j * 128 + c * 64 + 64],
                            in0=csT[:, j, c * 64:(c + 1) * 64],
                            scalar1=hv[ib][:, c * 8 + j:c * 8 + j + 1], scalar2=-85.0,
                            op0=OP.subtract, op1=OP.max))
                vq.drain()
                vq.inc(ve, 1, lambda e: e.tensor_copy(out=earg[:, 0:1], in_=earg[:, 0:1]))
                aq.wait(ve)
                aq(lambda e: e.activation(out=BB[:, 0:1024], in_=earg[:, 0:1024], func=AF.Exp))
                aq.drain()
                aq.inc(ac, 1, lambda e: e.activation(out=BB[:, 1024:2048], in_=earg[:, 1024:2048],
                                                     func=AF.Exp, scale=-1.0))
                vq.wait(ac)
                if i >= 2:
                    vq.wait(pe, done[i - 2]["ktok_tr"])
                vq(lambda e, ib=ib: e.tensor_tensor(out=qkt[ib][:, 0:1024], in0=qknT[ib][:, 0:1024],
                                                    in1=BB[:, 0:1024], op=OP.mult))
                vq.drain()
                vq.inc(ve, 1, lambda e, ib=ib: e.tensor_tensor(out=qkt[ib][:, 1024:2048], in0=qknT[ib][:, 1024:2048],
                                                               in1=BB[:, 1024:2048], op=OP.mult))
                d["qkt"] = ve.n
                tq.wait(ve, d["qkt"])
                tq.wait(ac)
                for j in range(8):
                    tq(lambda e, ib=ib, j=j: e.matmul(
                        out=psT[:, j * 128:(j + 1) * 128],
                        lhsT=qkt[ib][:, 1024 + j * 128:1024 + (j + 1) * 128],
                        rhs=ctx.iden_bf[:, :], start=True, stop=True))
                tq.inc_prev(pe, 1)
                d["ktok_tr"] = pe.n
                aq.wait(pe)
                if i >= 2:
                    aq.wait(stq[ib])
                aq.inc(ac, 1, lambda e, ib=ib: e.copy(out=kvb[ib][:, 0:1024], in_=psT[:, :]))
                d["all_ve"] = ve.n
                gq.wait(ac)
                gq.wait(ve, d["hv"])
                gq.inc(stq[ib], 16, lambda e, i=i, ib=ib: e.dma_start(
                    out=qtTd[:, :, i * 128:(i + 1) * 128].rearrange("j p c -> p j c"),
                    in_=qkt[ib][:, 0:1024].rearrange("p (j c) -> p j c", c=128)))
                gq.inc(stq[ib], 16, lambda e, i=i, ib=ib: e.dma_start(
                    out=ktTd[:, :, i * 128:(i + 1) * 128].rearrange("j p c -> p j c"),
                    in_=qkt[ib][:, 1024:2048].rearrange("p (j c) -> p j c", c=128)))
                gq.inc(stq[ib], 16, lambda e, i=i, ib=ib: e.dma_start(
                    out=ktd[i * 128:(i + 1) * 128, :], in_=kvb[ib][:, 0:1024]))
                gq.inc(stq[ib], 16, lambda e, i=i, ib=ib: e.dma_start(
                    out=vtd[i * 128:(i + 1) * 128, :], in_=kvb[ib][:, 1024:2048]))
                gq.inc(stq[ib], 16, lambda e, i=i, ib=ib: e.dma_start(
                    out=hvd[i, :, :], in_=hv[ib][:, :]))
                done.append(d)
            gq.wait(ctx.sem("stq0"))
            gq.wait(ctx.sem("stq1"))
            ctx.flush(f"{tag}_g1")

        with ctx.scope():
            ldx = [ctx.sem("ldx0"), ctx.sem("ldx1")]
            ldt = [ctx.sem("ldt0"), ctx.sem("ldt1")]
            sto = [ctx.sem("sto0"), ctx.sem("sto1")]
            ldv = ctx.sem("ldv0")
            gldw = ctx.sem("gldw")
            Wob = ctx.sb(f"{tag}_Wob", [128, 8, 1024], BF16)
            for j in range(8):
                gq.inc(gldw, 16, lambda e, j=j: e.dma_start(out=Wob[:, j, :], in_=Wo[j * 128:(j + 1) * 128, :]))
            S = [ctx.sb(f"{tag}_S{h}", [128, 128], F32) for h in range(8)]
            Sb = [ctx.sb(f"{tag}_Sb{h}", [128, 128], BF16) for h in range(8)]
            hva = ctx.sb(f"{tag}_hva", [128, NT, 2, 8], F32)
            benda = ctx.sb(f"{tag}_benda", [128, 64, 8], F32)
            qkTt = [ctx.sb(f"{tag}_qkTt{i}", [128, 2048], BF16) for i in range(2)]
            kvt = [ctx.sb(f"{tag}_kvt{i}", [128, 2048], BF16) for i in range(2)]
            msk = ctx.sb(f"{tag}_msk", [128, 64], BF16)
            zeros64 = ctx.sb(f"{tag}_z64", [128, 64], BF16)
            otile = [ctx.sb(f"{tag}_oti{i}", [128, 1024], BF16) for i in range(2)]
            oT = [ctx.sb(f"{tag}_oT{i}", [128, 1024], BF16) for i in range(2)]
            xb = [ctx.sb(f"{tag}_gx{i}", [128, 1024], F32) for i in range(2)]
            ot = [ctx.sb(f"{tag}_got{i}", [128, 1024], F32) for i in range(2)]
            psO = ctx.ps(f"{tag}_psO", [128, 128], F32)
            psSc = ctx.ps(f"{tag}_psSc", [128, 64], F32)
            psKV = ctx.ps(f"{tag}_psKV", [128, 128], F32)
            psTo = ctx.ps(f"{tag}_psTo", [128, 1024], F32)
            psO2 = ctx.ps(f"{tag}_psO2", [128, 1024], F32)
            for h in range(8):
                vq(lambda e, h=h: e.memset(S[h][:, :], 0.0))
                vq(lambda e, h=h: e.memset(Sb[h][:, :], 0.0))
            vq(lambda e: e.memset(zeros64[:, :], 0.0))
            sy.inc(ldv, 16, lambda e: e.dma_start(
                out=hva[:, :, :, :], in_=hvd.rearrange("n p c -> p n c").rearrange("p n (c j) -> p n c j", c=2)))
            vq.wait(ldv)
            hflat = hva.rearrange("p n c j -> p (n c) j")
            vq(lambda e: e.tensor_tensor(out=benda[:, 0:63, :], in0=hflat[:, 0:63, :],
                                         in1=hflat[:, 1:64, :], op=OP.add))
            vq(lambda e: e.tensor_copy(out=benda[:, 63:64, :], in_=hflat[:, 63:64, :]))
            vq.drain()
            vq.inc(ve, 1, lambda e: e.tensor_copy(out=benda[:, 0:1, 0:1], in_=benda[:, 0:1, 0:1]))
            aq.wait(ve)
            aq.inc(ac, 1, lambda e: e.activation(out=benda[:, :, :], in_=benda[:, :, :], func=AF.Exp))
            tq.wait(gldw)
            tq.wait(ac)
            done2 = []
            for i in range(NT):
                ib = i % 2
                d = {}
                if i >= 2:
                    sy.wait(pe, done2[i - 2]["used"])
                sy.inc(ldt[ib], 16, lambda e, i=i, ib=ib: e.dma_start(
                    out=qkTt[ib][:, 0:1024].rearrange("p (j c) -> p j c", c=128),
                    in_=qtTd[:, :, i * 128:(i + 1) * 128].rearrange("j p c -> p j c")))
                sy.inc(ldt[ib], 16, lambda e, i=i, ib=ib: e.dma_start(
                    out=qkTt[ib][:, 1024:2048].rearrange("p (j c) -> p j c", c=128),
                    in_=ktTd[:, :, i * 128:(i + 1) * 128].rearrange("j p c -> p j c")))
                sy.inc(ldt[ib], 16, lambda e, i=i, ib=ib: e.dma_start(
                    out=kvt[ib][:, 0:1024], in_=ktd[i * 128:(i + 1) * 128, :]))
                sy.inc(ldt[ib], 16, lambda e, i=i, ib=ib: e.dma_start(
                    out=kvt[ib][:, 1024:2048], in_=vtd[i * 128:(i + 1) * 128, :]))
                if i >= 2:
                    sy.wait(ve, done2[i - 2]["ep_ve"])
                sy.inc(ldx[ib], 16, lambda e, i=i, ib=ib: e.dma_start(
                    out=xb[ib][:, :], in_=x_in[i * 128:(i + 1) * 128, :]))
                tq.wait(ldt[ib])
                for c in range(2):
                    p0 = 64 * c
                    mchunk = 2 * i + c
                    for h in range(8):
                        if (c, h) != (0, 0):
                            tq.wait(ac)
                        tq(lambda e, ib=ib, c=c, h=h, p0=p0: e.matmul(
                            out=psO[p0:p0 + 64, :], lhsT=qkTt[ib][:, h * 128 + p0:h * 128 + p0 + 64],
                            rhs=Sb[h][:, :], start=True, stop=False))
                        tq(lambda e, ib=ib, c=c, h=h, p0=p0: e.matmul(
                            out=psSc[p0:p0 + 64, :], lhsT=qkTt[ib][:, 1024 + h * 128 + p0:1024 + h * 128 + p0 + 64],
                            rhs=qkTt[ib][:, h * 128 + p0:h * 128 + p0 + 64], start=True, stop=True))
                        tq.inc_prev(pe, 1)
                        vq.wait(pe)
                        vq.inc(ve, 1, lambda e, c=c, p0=p0: e.select(
                            out=msk[p0:p0 + 64, :], mask=ctx.utm_u8[p0:p0 + 64, :],
                            on_true=psSc[p0:p0 + 64, :], on_false=zeros64[p0:p0 + 64, :],
                            add_drain=True))
                        tq.wait(ve)
                        tq(lambda e, ib=ib, c=c, h=h, p0=p0: e.matmul(
                            out=psO[p0:p0 + 64, :], lhsT=msk[p0:p0 + 64, :],
                            rhs=kvt[ib][p0:p0 + 64, 1024 + h * 128:1024 + (h + 1) * 128],
                            start=False, stop=True))
                        tq(lambda e, ib=ib, c=c, h=h, p0=p0: e.matmul(
                            out=psKV[:, :], lhsT=kvt[ib][p0:p0 + 64, h * 128:(h + 1) * 128],
                            rhs=kvt[ib][p0:p0 + 64, 1024 + h * 128:1024 + (h + 1) * 128],
                            start=True, stop=True))
                        tq.inc_prev(pe, 1)
                        vq.wait(pe)
                        vq(lambda e, h=h: e.tensor_tensor(out=S[h][:, :], in0=S[h][:, :], in1=psKV[:, :], op=OP.add))
                        vq.drain()
                        vq.inc(ve, 1, lambda e, h=h, mchunk=mchunk: e.tensor_scalar(
                            out=S[h][:, :], in0=S[h][:, :], scalar1=benda[:, mchunk, h:h + 1],
                            scalar2=None, op0=OP.mult))
                        aq.wait(ve)
                        aq(lambda e, h=h: e.copy(out=Sb[h][:, :], in_=S[h][:, :]))
                        aq.drain()
                        aq.inc(ac, 1, lambda e, ib=ib, c=c, h=h, p0=p0: e.copy(
                            out=otile[ib][p0:p0 + 64, h * 128:(h + 1) * 128], in_=psO[p0:p0 + 64, :]))
                tq.wait(ac)
                if i >= 2:
                    tq.wait(ac, done2[i - 2]["oT_evac"])
                for j in range(8):
                    tq(lambda e, ib=ib, j=j: e.matmul(
                        out=psTo[:, j * 128:(j + 1) * 128], lhsT=otile[ib][:, j * 128:(j + 1) * 128],
                        rhs=ctx.iden_bf[:, :], start=True, stop=True))
                tq.inc_prev(pe, 1)
                d["used"] = pe.n
                aq.wait(pe)
                aq.inc(ac, 1, lambda e, ib=ib: e.copy(out=oT[ib][:, :], in_=psTo[:, :]))
                d["oT_evac"] = ac.n
                tq.wait(ac)
                if i >= 1:
                    tq.wait(ve, done2[i - 1]["ep_ve"])
                for half in range(2):
                    for j in range(8):
                        tq(lambda e, ib=ib, half=half, j=j: e.matmul(
                            out=psO2[:, half * 512:(half + 1) * 512], lhsT=oT[ib][:, j * 128:(j + 1) * 128],
                            rhs=Wob[:, j, half * 512:(half + 1) * 512], start=(j == 0), stop=(j == 7)))
                tq.inc_prev(pe, 1)
                vq.wait(pe)
                vq.wait(ldx[ib])
                if i >= 2:
                    vq.wait(sto[ib])
                vq.inc(ve, 1, lambda e, ib=ib: e.tensor_tensor(
                    out=ot[ib][:, :], in0=xb[ib][:, :], in1=psO2[:, :], op=OP.add))
                d["ep_ve"] = ve.n
                gq.wait(ve)
                gq.inc(sto[ib], 16, lambda e, i=i, ib=ib: e.dma_start(
                    out=x_out[i * 128:(i + 1) * 128, :], in_=ot[ib][:, :]))
                done2.append(d)
            gq.wait(ctx.sem("sto0"))
            gq.wait(ctx.sem("sto1"))
            ctx.flush(f"{tag}_g2")


def build_gla_launch():
    nc = make_nc()
    stack = contextlib.ExitStack()
    ctx = Ctx(nc, stack)
    with stack:
        declare_consts(ctx)
        x_in = ctx.din("x_in", (T, D), F32)
        x_out = ctx.dout("x_out", (T, D), F32)
        for x in ("q", "k", "v", "g", "o"):
            ctx.din(f"L1_attn_a_W{x}", (1024, 1024), F32)
        emit_load_consts(ctx)
        emit_gla(ctx, x_in, x_out)
        ctx.flush("fin")
    return nc


def gla_inmaps(inp, xs):
    c = host_constants()
    base = {**c}
    for x in ("q", "k", "v", "g", "o"):
        base[f"L1_attn_a_W{x}"] = np.ascontiguousarray(inp[f"L1_attn_a_W{x}"])
    return [{**base, "x_in": np.ascontiguousarray(xs[i])} for i in range(NCORES)]


def run_gla(inp, xs):
    nc = build_gla_launch()
    res = run_bass_kernel_spmd(nc, gla_inmaps(inp, xs), core_ids=list(range(NCORES)))
    return np.stack([r["x_out"] for r in res.results])



def emit_lra(ctx, x_in, x_out, tag="lra"):
    nc = ctx.nc
    ve, ac, pe = ctx.sem("ve"), ctx.sem("ac"), ctx.sem("pe")
    sy, vq, tq, aq, gq = ctx.sy, ctx.vq, ctx.tq, ctx.aq, ctx.gq
    Qt, Kt = ctx.dram["lra_Qt"], ctx.dram["lra_Kt"]
    VO = ctx.dram["L1_attn_b_VO"]
    xqTd = ctx.dtmp(f"{tag}_xqTd", (8, 128, T), BF16)
    xkTd = ctx.dtmp(f"{tag}_xkTd", (8, 128, T), BF16)
    xhTd = ctx.dtmp(f"{tag}_xhTd", (8, 128, T), BF16)
    PTds = [ctx.dtmp(f"{tag}_PTd{i}", (4, T, T), BF16) for i in range(4)]
    PTd_of = lambda h: PTds[h // 4][h % 4]

    with ctx.scope():
        A1 = ctx.sb(f"{tag}_A1", [128, NT], F32)
        B1 = ctx.sb(f"{tag}_B1", [128, NT], F32)
        emit_ln_stats(ctx, x_in, A1, B1, name=f"{tag}_s1")

        with ctx.scope():
            QKb = ctx.sb(f"{tag}_QKb", [128, 8, 2048], BF16)
            gldw = ctx.sem("gldw")
            for m, W in enumerate((Qt, Kt)):
                for j in range(8):
                    gq.inc(gldw, 16, lambda e, m=m, j=j, W=W: e.dma_start(
                        out=QKb[:, j, m * 1024:(m + 1) * 1024], in_=W[j * 128:(j + 1) * 128, :]))
            gq.wait(gldw)
            ctx.flush(f"{tag}_wload")

            tn = TileNorm(ctx, f"{tag}_m", x_in, A1, B1)
            stq = [ctx.sem("stq0"), ctx.sem("stq1")]
            qko = [ctx.sb(f"{tag}_qko{i}", [128, 2048], BF16) for i in range(2)]
            psQ = ctx.ps(f"{tag}_psQ", [128, 2048], F32)
            ac_done = []
            for i in range(NT):
                ib = i % 2
                tn.record(i)
                tq.wait(ac)
                for m in range(2):
                    for blk in range(8):
                        for j in range(8):
                            tq(lambda e, i=i, m=m, blk=blk, j=j: e.matmul(
                                out=psQ[:, m * 1024 + blk * 128:m * 1024 + (blk + 1) * 128],
                                lhsT=QKb[:, j, m * 1024 + blk * 128:m * 1024 + (blk + 1) * 128],
                                rhs=tn.cur(i)[:, j * 128:(j + 1) * 128],
                                start=(j == 0), stop=(j == 7)))
                tq.inc_prev(pe, 1)
                aq.wait(pe)
                if i >= 2:
                    aq.wait(stq[ib])
                aq.inc(ac, 1, lambda e, ib=ib: e.copy(out=qko[ib][:, :], in_=psQ[:, :]))
                ac_done.append(ac.n)
                gq.wait(ac)
                gq.inc(stq[ib], 16, lambda e, i=i, ib=ib: e.dma_start(
                    out=xqTd[:, :, i * 128:(i + 1) * 128].rearrange("j p c -> p j c"),
                    in_=qko[ib][:, 0:1024].rearrange("p (j c) -> p j c", c=128)))
                gq.inc(stq[ib], 16, lambda e, i=i, ib=ib: e.dma_start(
                    out=xkTd[:, :, i * 128:(i + 1) * 128].rearrange("j p c -> p j c"),
                    in_=qko[ib][:, 1024:2048].rearrange("p (j c) -> p j c", c=128)))
                gq.inc(stq[ib], 16, lambda e, i=i, ib=ib: e.dma_start(
                    out=xhTd[:, :, i * 128:(i + 1) * 128].rearrange("j p c -> p j c"),
                    in_=tn.cur(i).rearrange("p (j c) -> p j c", c=128)))
                tn.note_consumed(ctx.sems["stq0" if ib == 0 else "stq1"], ctx.sems["stq0" if ib == 0 else "stq1"].n)
            gq.wait(ctx.sem("stq0"))
            gq.wait(ctx.sem("stq1"))
            ctx.flush(f"{tag}_b1")

        with ctx.scope():
            ldx = [ctx.sem("ldx0"), ctx.sem("ldx1")]
            ldt = [ctx.sem("ldt0"), ctx.sem("ldt1")]
            ldk = [ctx.sem("ldk0"), ctx.sem("ldk1")]
            ldp = [ctx.sem("ldp0"), ctx.sem("ldp1")]
            ldv = ctx.sem("ldv0")
            sto = [ctx.sem("sto0"), ctx.sem("sto1")]
            stq = [ctx.sem("stq0"), ctx.sem("stq1")]
            gldw = ctx.sem("gldw")
            acc = ctx.sb(f"{tag}_acc", [128, NT, 512], F32)
            Yres = ctx.sb(f"{tag}_Yres", [128, NT, 512], BF16)
            xqh = ctx.sb(f"{tag}_xqh", [64, T], BF16)
            xkh = ctx.sb(f"{tag}_xkh", [64, T], BF16)
            VOb = [ctx.sb(f"{tag}_VOb{i}", [128, 8, 512], BF16) for i in range(2)]
            xhb = [ctx.sb(f"{tag}_xhb{i}", [128, 8, 128], BF16) for i in range(2)]
            PT = [ctx.sb(f"{tag}_PT{i}", [128, 512], BF16) for i in range(2)]
            dnrow = ctx.sb(f"{tag}_dnrow", [1, 512], F32)
            rden = ctx.sb(f"{tag}_rden", [128, 16, 8, 4], F32)
            xb = [ctx.sb(f"{tag}_lx{i}", [128, 512], F32) for i in range(2)]
            ot = [ctx.sb(f"{tag}_lot{i}", [128, 512], F32) for i in range(2)]
            psSY = [ctx.ps(f"{tag}_psSY{i}", [128, 512], F32) for i in range(2)]
            psO = [ctx.ps(f"{tag}_psO{i}", [128, 512], F32) for i in range(4)]
            psDen = ctx.ps(f"{tag}_psDen", [1, 512], F32)
            psDen2 = ctx.ps(f"{tag}_psDen2", [128, 4], F32)
            for eh in range(2):
                if eh == 1:
                    sy.wait(stq[0])
                    sy.wait(stq[1])
                vq.drain()
                vq(lambda e: e.memset(acc[:, :, :], 0.0))
                vq.drain()
                vq.inc(ve, 1, lambda e: e.tensor_copy(out=acc[:, 0:1, 0:1], in_=acc[:, 0:1, 0:1]))
                acc_zero = ve.n
                tq.wait(ve, acc_zero)
                for h in range(16):
                    hb = h % 2
                    sy.wait(pe)
                    sy.inc(ldk[0], 16, lambda e, h=h: e.dma_start(
                        out=xqh[:, :], in_=xqTd[h // 2, (h % 2) * 64:(h % 2) * 64 + 64, :]))
                    sy.inc(ldk[0], 16, lambda e, h=h: e.dma_start(
                        out=xkh[:, :], in_=xkTd[h // 2, (h % 2) * 64:(h % 2) * 64 + 64, :]))
                    gq.wait(pe)
                    for j in range(8):
                        gq.inc(gldw, 16, lambda e, eh=eh, h=h, hb=hb, j=j: e.dma_start(
                            out=VOb[hb][:, j, :], in_=VO[h, j * 128:(j + 1) * 128, eh * 512:(eh + 1) * 512]))
                    tq.wait(gldw)
                    ac_y = []
                    for u in range(NT):
                        ub = u % 2
                        if u >= 2:
                            sy.wait(pe)
                        sy.inc(ldt[ub], 16, lambda e, u=u, ub=ub: e.dma_start(
                            out=xhb[ub][:, :, :],
                            in_=xhTd[:, :, u * 128:(u + 1) * 128].rearrange("j p c -> p j c")))
                        tq.wait(ldt[ub])
                        if u >= 2:
                            tq.wait(ac, ac_y[u - 2])
                        for j in range(8):
                            tq(lambda e, ub=ub, j=j, hb=hb: e.matmul(
                                out=psSY[ub][:, :], lhsT=xhb[ub][:, j, :], rhs=VOb[hb][:, j, :],
                                start=(j == 0), stop=(j == 7)))
                        tq.inc_prev(pe, 1)
                        aq.wait(pe)
                        aq.inc(ac, 1, lambda e, u=u, ub=ub: e.copy(out=Yres[:, u, :], in_=psSY[ub][:, :]))
                        ac_y.append(ac.n)
                    tq.wait(ac)
                    tq.wait(ctx.sems["ldk0"])
                    pt_pe = []
                    for tg in range(8):
                        if eh == 0:
                            pes = []
                            acx = []
                            for u in (0, 1):
                                ub = u % 2
                                tq(lambda e, ub=ub, u=u, tg=tg: e.matmul(
                                    out=psSY[ub][:, :], lhsT=xkh[:, u * 128:(u + 1) * 128],
                                    rhs=xqh[:, tg * 512:(tg + 1) * 512], start=True, stop=True))
                                tq.inc_prev(pe, 1)
                                pes.append(pe.n)
                            for u in range(NT):
                                ub = u % 2
                                aq.wait(pe, pes[u])
                                aq.wait(stq[ub])
                                aq.inc(ac, 1, lambda e, ub=ub: e.activation(
                                    out=PT[ub][:, :], in_=psSY[ub][:, :], func=AF.Exp,
                                    scale=1.0 / 32.0, bias=-EXP_SHIFT))
                                acx.append(ac.n)
                                gq.wait(ac)
                                gq.inc(stq[ub], 16, lambda e, h=h, u=u, tg=tg, ub=ub: e.dma_start(
                                    out=PTd_of(h)[u * 128:(u + 1) * 128, tg * 512:(tg + 1) * 512],
                                    in_=PT[ub][:, :]))
                                tq.wait(ac, acx[u])
                                for ts in range(4):
                                    tq(lambda e, ub=ub, ts=ts, u=u, h=h, tg=tg: e.matmul(
                                        out=psO[ts][:, :], lhsT=PT[ub][:, ts * 128:(ts + 1) * 128],
                                        rhs=Yres[:, u, :], start=(u == 0), stop=(u == NT - 1)))
                                tq(lambda e, ub=ub, u=u: e.matmul(
                                    out=psDen[:, :], lhsT=ctx.onescol_bf[:, 0:1], rhs=PT[ub][:, :],
                                    start=(u == 0), stop=(u == NT - 1)))
                                tq.inc_prev(pe, 1)
                                pt_pe.append(pe.n)
                                if u + 2 < NT:
                                    u2 = u + 2
                                    tq(lambda e, u2=u2, tg=tg: e.matmul(
                                        out=psSY[u2 % 2][:, :], lhsT=xkh[:, u2 * 128:(u2 + 1) * 128],
                                        rhs=xqh[:, tg * 512:(tg + 1) * 512], start=True, stop=True))
                                    tq.inc_prev(pe, 1)
                                    pes.append(pe.n)
                        else:
                            for u in range(NT):
                                ub = u % 2
                                if len(pt_pe) >= 2:
                                    sy.wait(pe, pt_pe[-2])
                                sy.inc(ldp[ub], 16, lambda e, h=h, u=u, tg=tg, ub=ub: e.dma_start(
                                    out=PT[ub][:, :], in_=PTd_of(h)[u * 128:(u + 1) * 128, tg * 512:(tg + 1) * 512]))
                                tq.wait(ldp[ub])
                                for ts in range(4):
                                    tq(lambda e, ub=ub, ts=ts, u=u, h=h, tg=tg: e.matmul(
                                        out=psO[ts][:, :], lhsT=PT[ub][:, ts * 128:(ts + 1) * 128],
                                        rhs=Yres[:, u, :], start=(u == 0), stop=(u == NT - 1)))
                                tq.inc_prev(pe, 1)
                                pt_pe.append(pe.n)
                        if eh == 0:
                            aq.wait(pe)
                            aq.inc(ac, 1, lambda e: e.copy(out=dnrow[:, :], in_=psDen[:, :]))
                            tq.wait(ac)
                            for ts in range(4):
                                tq(lambda e, ts=ts: e.transpose(out=psDen2[:, ts:ts + 1],
                                                                in_=dnrow[:, ts * 128:(ts + 1) * 128],
                                                                identity=ctx.iden_f32[0:1, 0:1]))
                            tq.inc_prev(pe, 1)
                            vq.wait(pe)
                            vq.inc(ve, 1, lambda e, h=h, tg=tg: e.reciprocal(
                                out=rden[:, h, tg, :], in_=psDen2[:, :]))
                        vq.wait(pe)
                        if eh == 0:
                            vq.wait(ve)
                        for ts in range(4):
                            vq(lambda e, h=h, tg=tg, ts=ts: e.scalar_tensor_tensor(
                                out=acc[:, tg * 4 + ts, :], in0=psO[ts][:, :],
                                scalar=rden[:, h, tg, ts:ts + 1], op0=OP.mult,
                                op1=OP.add, in1=acc[:, tg * 4 + ts, :]))
                        vq.drain()
                        vq.inc(ve, 1, lambda e: e.tensor_copy(out=acc[:, 0:1, 0:1], in_=acc[:, 0:1, 0:1]))
                        tq.wait(ve)
                vq.drain()
                for i in range(NT):
                    ib = i % 2
                    if i >= 2:
                        sy.wait(ve)
                    sy.inc(ldx[ib], 16, lambda e, i=i, ib=ib, eh=eh: e.dma_start(
                        out=xb[ib][:, :], in_=x_in[i * 128:(i + 1) * 128, eh * 512:(eh + 1) * 512]))
                    vq.wait(ldx[ib])
                    vq.wait(sto[ib])
                    vq.inc(ve, 1, lambda e, i=i, ib=ib: e.tensor_tensor(
                        out=ot[ib][:, :], in0=xb[ib][:, :], in1=acc[:, i, :], op=OP.add))
                    gq.wait(ve)
                    gq.inc(sto[ib], 16, lambda e, i=i, ib=ib, eh=eh: e.dma_start(
                        out=x_out[i * 128:(i + 1) * 128, eh * 512:(eh + 1) * 512], in_=ot[ib][:, :]))
            gq.wait(ctx.sem("sto0"))
            gq.wait(ctx.sem("sto1"))
            ctx.flush(f"{tag}_b2")


def build_lra_launch():
    nc = make_nc()
    stack = contextlib.ExitStack()
    ctx = Ctx(nc, stack)
    with stack:
        declare_consts(ctx)
        x_in = ctx.din("x_in", (T, D), F32)
        x_out = ctx.dout("x_out", (T, D), F32)
        ctx.din("lra_Qt", (1024, 1024), F32)
        ctx.din("lra_Kt", (1024, 1024), F32)
        ctx.din("L1_attn_b_VO", (16, 1024, 1024), F32)
        emit_load_consts(ctx)
        emit_lra(ctx, x_in, x_out)
        ctx.flush("fin")
    return nc


def lra_inmaps(inp, xs):
    c = host_constants()
    Q = inp["L1_attn_b_Q"]
    K = inp["L1_attn_b_K"]
    base = {
        **c,
        "lra_Qt": np.ascontiguousarray(Q.transpose(2, 0, 1).reshape(1024, 1024)),
        "lra_Kt": np.ascontiguousarray(K.transpose(2, 0, 1).reshape(1024, 1024)),
        "L1_attn_b_VO": np.ascontiguousarray(inp["L1_attn_b_VO"]),
    }
    return [{**base, "x_in": np.ascontiguousarray(xs[i])} for i in range(NCORES)]


def run_lra(inp, xs):
    nc = build_lra_launch()
    res = run_bass_kernel_spmd(nc, lra_inmaps(inp, xs), core_ids=list(range(NCORES)))
    return np.stack([r["x_out"] for r in res.results])



def emit_moe_pre(ctx, x_in, xh_out, route_out, cnt_out, tag):
    nc = ctx.nc
    ve, ac, pe = ctx.sem("ve"), ctx.sem("ac"), ctx.sem("pe")
    ldc = ctx.sem("ldc")
    sy, vq, tq, aq, gq = ctx.sy, ctx.vq, ctx.tq, ctx.aq, ctx.gq
    wgT = ctx.dram["moe_wgT"]

    with ctx.scope():
        A1 = ctx.sb(f"{tag}_A1", [128, NT], F32)
        B1 = ctx.sb(f"{tag}_B1", [128, NT], F32)
        emit_ln_stats(ctx, x_in, A1, B1, name=f"{tag}_s1")

        with ctx.scope():
            wg = ctx.sb(f"{tag}_wg", [128, 8, 8], F32)
            runrow = ctx.sb(f"{tag}_run", [1, 16], F32)
            sy.inc(ldc, 16, lambda e: e.dma_start(out=wg[:, :, :],
                                                  in_=wgT.rearrange("(j p) c -> p j c", p=128)))
            vq(lambda e: e.memset(runrow[:, :], 0.0))
            vq.drain()
            vq.inc(ve, 1, lambda e: e.tensor_copy(out=runrow[:, 0:1], in_=runrow[:, 0:1]))
            ctx.flush(f"{tag}_prep")

            tn = TileNorm(ctx, f"{tag}_m", x_in, A1, B1)
            stq = [ctx.sem("stq0"), ctx.sem("stq1")]
            zf = [ctx.sb(f"{tag}_zf{i}", [128, 1024], F32) for i in range(2)]
            xTf = ctx.sb(f"{tag}_xTf", [128, 1024], F32)
            lg = ctx.sb(f"{tag}_lg", [128, 8], F32)
            w1 = ctx.sb(f"{tag}_w1", [128, 8], F32)
            w2 = ctx.sb(f"{tag}_w2", [128, 8], F32)
            w3 = ctx.sb(f"{tag}_w3", [128, 8], F32)
            oh = ctx.sb(f"{tag}_oh", [128, 16], F32)
            mcol = ctx.sb(f"{tag}_mcol", [128, 8], F32)
            rt = [ctx.sb(f"{tag}_rt{i}", [128, 8], F32) for i in range(2)]
            psL = ctx.ps(f"{tag}_psL", [128, 8], F32)
            psR = ctx.ps(f"{tag}_psR", [128, 16], F32)
            psC = ctx.ps(f"{tag}_psC", [1, 16], F32)
            tq.wait(ldc)
            done = []
            for i in range(NT):
                ib = i % 2
                d = {}
                tn.record(i)
                tn.note_consumed(ctx.sem("ac"), 0)
                vq.wait(ctx.sems["ldx0" if ib == 0 else "ldx1"])
                if i >= 2:
                    vq.wait(pe, done[i - 2]["ztr"])
                vq.inc(ve, 1, lambda e, i=i, ib=ib: e.tensor_scalar(
                    out=zf[ib][:, :], in0=tn.xb[ib][:, :], scalar1=B1[:, i:i + 1],
                    scalar2=A1[:, i:i + 1], op0=OP.mult, op1=OP.subtract))
                tn.ve_done[i] = ve.n
                tq.wait(ve)
                tq.wait(ac)
                for j in range(8):
                    tq(lambda e, ib=ib, j=j: e.matmul(
                        out=tn.pTf[:, j * 128:(j + 1) * 128], lhsT=zf[ib][:, j * 128:(j + 1) * 128],
                        rhs=ctx.iden_f32[:, :], start=True, stop=True))
                tq.inc_prev(pe, 1)
                d["ztr"] = pe.n
                aq.wait(pe)
                aq.inc(ac, 1, lambda e: e.copy(out=xTf[:, :], in_=tn.pTf[:, :]))
                d["xTf"] = ac.n
                tn.ac_done[i] = ac.n
                tq.wait(ac)
                tq.wait(ve)
                for j in range(8):
                    tq(lambda e, j=j: e.matmul(out=psL[:, :], lhsT=xTf[:, j * 128:(j + 1) * 128],
                                               rhs=wg[:, j, :], start=(j == 0), stop=(j == 7)))
                tq.inc_prev(pe, 1)
                vq.wait(pe)
                vq(lambda e: e.tensor_reduce(out=mcol[:, 0:1], in_=psL[:, :], axis=mybir.AxisListType.X, op=OP.max))
                vq.drain()
                vq(lambda e: e.tensor_copy(out=lg[:, :], in_=psL[:, :]))
                vq.drain()
                if i >= 1:
                    vq.wait(pe, done[i - 1]["rkmm"])
                vq.wait(ctx.sems["stq0" if ib == 0 else "stq1"])
                vq(lambda e: e.tensor_scalar(out=w1[:, :], in0=lg[:, :], scalar1=mcol[:, 0:1],
                                             scalar2=None, op0=OP.is_ge))
                vq.drain()
                vq(lambda e: e.tensor_copy(out=w2[:, :], in_=w1[:, :]))
                vq.drain()
                vq(lambda e: e.tensor_tensor(out=w2[:, 1:8], in0=w2[:, 1:8], in1=w1[:, 0:7], op=OP.add))
                vq.drain()
                vq(lambda e: e.tensor_copy(out=w3[:, :], in_=w2[:, :]))
                vq.drain()
                vq(lambda e: e.tensor_tensor(out=w3[:, 2:8], in0=w3[:, 2:8], in1=w2[:, 0:6], op=OP.add))
                vq.drain()
                vq(lambda e: e.tensor_copy(out=w2[:, :], in_=w3[:, :]))
                vq.drain()
                vq(lambda e: e.tensor_tensor(out=w2[:, 4:8], in0=w2[:, 4:8], in1=w3[:, 0:4], op=OP.add))
                vq.drain()
                vq(lambda e: e.tensor_scalar(out=w2[:, :], in0=w2[:, :], scalar1=1.0, scalar2=None, op0=OP.is_le))
                vq.drain()
                vq(lambda e: e.tensor_tensor(out=oh[:, 0:8], in0=w1[:, :], in1=w2[:, :], op=OP.mult))
                vq.drain()
                vq(lambda e: e.tensor_scalar(out=w1[:, :], in0=oh[:, 0:8], scalar1=-1e30, scalar2=None, op0=OP.mult))
                vq.drain()
                vq(lambda e: e.tensor_tensor(out=w1[:, :], in0=w1[:, :], in1=lg[:, :], op=OP.add))
                vq.drain()
                vq(lambda e: e.tensor_reduce(out=mcol[:, 1:2], in_=w1[:, :], axis=mybir.AxisListType.X, op=OP.max))
                vq.drain()
                vq(lambda e: e.tensor_scalar(out=w2[:, :], in0=w1[:, :], scalar1=mcol[:, 1:2],
                                             scalar2=None, op0=OP.is_ge))
                vq.drain()
                vq(lambda e: e.tensor_copy(out=w3[:, :], in_=w2[:, :]))
                vq.drain()
                vq(lambda e: e.tensor_tensor(out=w3[:, 1:8], in0=w3[:, 1:8], in1=w2[:, 0:7], op=OP.add))
                vq.drain()
                vq(lambda e: e.tensor_copy(out=w1[:, :], in_=w3[:, :]))
                vq.drain()
                vq(lambda e: e.tensor_tensor(out=w1[:, 2:8], in0=w1[:, 2:8], in1=w3[:, 0:6], op=OP.add))
                vq.drain()
                vq(lambda e: e.tensor_copy(out=w3[:, :], in_=w1[:, :]))
                vq.drain()
                vq(lambda e: e.tensor_tensor(out=w3[:, 4:8], in0=w3[:, 4:8], in1=w1[:, 0:4], op=OP.add))
                vq.drain()
                vq(lambda e: e.tensor_scalar(out=w3[:, :], in0=w3[:, :], scalar1=1.0, scalar2=None, op0=OP.is_le))
                vq.drain()
                vq.inc(ve, 1, lambda e: e.tensor_tensor(out=oh[:, 8:16], in0=w2[:, :], in1=w3[:, :], op=OP.mult))
                d["oh"] = ve.n
                vq(lambda e: e.tensor_tensor(out=w1[:, :], in0=oh[:, 0:8], in1=ctx.iota8[:, :], op=OP.mult))
                vq.drain()
                vq(lambda e: e.tensor_reduce(out=mcol[:, 4:5], in_=w1[:, :], axis=mybir.AxisListType.X, op=OP.add))
                vq.drain()
                vq(lambda e: e.tensor_tensor(out=w1[:, :], in0=oh[:, 8:16], in1=ctx.iota8[:, :], op=OP.mult))
                vq.drain()
                vq(lambda e: e.tensor_reduce(out=mcol[:, 5:6], in_=w1[:, :], axis=mybir.AxisListType.X, op=OP.add))
                vq.drain()
                vq.inc(ve, 1, lambda e: e.tensor_scalar(out=mcol[:, 6:7], in0=mcol[:, 0:1],
                                                        scalar1=mcol[:, 1:2], scalar2=None, op0=OP.subtract))
                aq.wait(ve)
                aq.inc(ac, 1, lambda e: e.activation(out=mcol[:, 2:3], in_=mcol[:, 6:7], func=AF.Sigmoid))
                vq.wait(ac)
                vq.inc(ve, 1, lambda e: e.tensor_scalar(out=mcol[:, 3:4], in0=mcol[:, 2:3], scalar1=1.0,
                                                        scalar2=-1.0, op0=OP.subtract, op1=OP.mult))
                tq.wait(ve, d["oh"])
                if i >= 1:
                    tq.wait(ve)
                tq(lambda e: e.matmul(out=psR[:, 0:8], lhsT=ctx.ltx[:, :], rhs=oh[:, 0:8], start=True, stop=False))
                tq(lambda e: e.matmul(out=psR[:, 8:16], lhsT=ctx.ltx[:, :], rhs=oh[:, 8:16], start=False, stop=False))
                tq(lambda e: e.matmul(out=psR[:, 0:16], lhsT=ctx.ones_row[0:1, 0:128], rhs=runrow[:, :],
                                      start=False, stop=True))
                tq.inc_prev(pe, 1)
                tq(lambda e: e.matmul(out=psC[:, 0:8], lhsT=ctx.onescol_f32[:, :], rhs=oh[:, 0:8],
                                      start=True, stop=False))
                tq(lambda e: e.matmul(out=psC[:, 8:16], lhsT=ctx.onescol_f32[:, :], rhs=oh[:, 8:16],
                                      start=False, stop=True))
                tq.inc_prev(pe, 1)
                d["rkmm"] = pe.n
                vq.wait(pe)
                vq(lambda e: e.tensor_tensor(out=w1[:, :], in0=oh[:, 0:8], in1=psR[:, 0:8], op=OP.mult))
                vq.drain()
                vq(lambda e, ib=ib: e.tensor_reduce(out=rt[ib][:, 4:5], in_=w1[:, :], axis=mybir.AxisListType.X, op=OP.add))
                vq.drain()
                vq(lambda e: e.tensor_tensor(out=w1[:, :], in0=oh[:, 8:16], in1=psR[:, 8:16], op=OP.mult))
                vq.drain()
                vq(lambda e, ib=ib: e.tensor_reduce(out=rt[ib][:, 5:6], in_=w1[:, :], axis=mybir.AxisListType.X, op=OP.add))
                vq.drain()
                vq(lambda e: e.tensor_tensor(out=runrow[:, :], in0=runrow[:, :], in1=psC[:, :], op=OP.add))
                vq.drain()
                vq(lambda e, ib=ib: e.tensor_copy(out=rt[ib][:, 0:1], in_=mcol[:, 4:5]))
                vq(lambda e, ib=ib: e.tensor_copy(out=rt[ib][:, 1:2], in_=mcol[:, 5:6]))
                vq(lambda e, ib=ib: e.tensor_copy(out=rt[ib][:, 2:3], in_=mcol[:, 2:3]))
                vq(lambda e, ib=ib: e.tensor_copy(out=rt[ib][:, 3:4], in_=mcol[:, 3:4]))
                vq(lambda e, ib=ib: e.memset(rt[ib][:, 6:8], 0.0))
                vq.drain()
                vq.inc(ve, 1, lambda e, ib=ib: e.tensor_copy(out=rt[ib][:, 0:1], in_=rt[ib][:, 0:1]))
                d["rt"] = ve.n
                gq.wait(ve)
                if i >= 2:
                    gq.wait(stq[ib])
                gq.inc(stq[ib], 16, lambda e, i=i, ib=ib: e.dma_start(
                    out=xh_out[i * 128:(i + 1) * 128, :], in_=tn.zh[ib][:, :]))
                gq.inc(stq[ib], 16, lambda e, i=i, ib=ib: e.dma_start(
                    out=route_out[i * 128:(i + 1) * 128, :], in_=rt[ib][:, :]))
                tn.zh_consumed[i] = (ctx.sems["stq0" if ib == 0 else "stq1"],
                                     ctx.sems["stq0" if ib == 0 else "stq1"].n)
                done.append(d)
            gq.wait(ve)
            gq.inc(ctx.sem("sto0"), 16, lambda e: e.dma_start(out=cnt_out[:, :], in_=runrow[:, :]))
            gq.wait(ctx.sem("sto0"))
            gq.wait(ctx.sem("stq0"))
            gq.wait(ctx.sem("stq1"))
            ctx.flush(f"{tag}_main")


def build_moepre_launch():
    nc = make_nc()
    stack = contextlib.ExitStack()
    ctx = Ctx(nc, stack)
    with stack:
        declare_consts(ctx)
        x_in = ctx.din("x_in", (T, D), F32)
        ctx.din("moe_wgT", (1024, 8), F32)
        xh_out = ctx.dout("xh_out", (T, D), BF16)
        route_out = ctx.dout("route_out", (T, 8), F32)
        cnt_out = ctx.dout("cnt_out", (1, 16), F32)
        emit_load_consts(ctx)
        emit_moe_pre(ctx, x_in, xh_out, route_out, cnt_out, "mp")
        ctx.flush("fin")
    return nc


def moepre_inmaps(inp, xs):
    c = host_constants()
    base = {**c, "moe_wgT": np.ascontiguousarray(inp["L1_ffn_w_g"].T)}
    return [{**base, "x_in": np.ascontiguousarray(xs[i])} for i in range(NCORES)]



NGRAN = 15
GRAN = 1024


def emit_moe_main(ctx, x_in, x_out, xh_in, route_in, tag):
    nc = ctx.nc
    ve, ac, pe = ctx.sem("ve"), ctx.sem("ac"), ctx.sem("pe")
    ldc = ctx.sem("ldc")
    sy, vq, tq, aq, gq = ctx.sy, ctx.vq, ctx.tq, ctx.aq, ctx.gq
    allcnt = ctx.dram["moe_allcnt"]
    mycnt = ctx.dram["moe_mycnt"]
    mprev = ctx.dram["moe_mprev"]
    c_jp8 = ctx.dram["c_jp8"]
    c_jp32 = ctx.dram["c_jp32"]
    c_i128 = ctx.dram["c_i128"]
    cfc = ctx.dram["L1_ffn_c_fc"]
    cpj = ctx.dram["L1_ffn_c_proj"]
    cfcv = cfc.rearrange("e d f -> (e d) f")
    cpjv = cpj.rearrange("e f d -> (e f) d")
    xpkd = ctx.dtmp(f"{tag}_xpkd", (NGRAN * GRAN + 2 * T, 1024), BF16)
    opk = ctx.dtmp(f"{tag}_opk", (NGRAN * GRAN, 1024), BF16)
    dstd = ctx.dtmp(f"{tag}_dstd", (T, 2), U32)
    pmd = ctx.dtmp(f"{tag}_pmd", (T, 2), F32)

    with ctx.scope():
        srowb = ctx.sb(f"{tag}_srowb", [128, 48], F32)
        ebc = ctx.sb(f"{tag}_ebc", [128, 15], F32)
        idxfc = ctx.sb(f"{tag}_idxfc", [128, NGRAN, 8], U32)
        idxpj = ctx.sb(f"{tag}_idxpj", [128, NGRAN, 32], U32)
        zz = ctx.sb(f"{tag}_zz", [128, 1024], BF16)
        pfx = ctx.sem("pfx")
        with ctx.scope():
            ac8 = ctx.sb(f"{tag}_ac8", [8, 16], F32)
            mc1 = ctx.sb(f"{tag}_mc1", [1, 16], F32)
            mpv = ctx.sb(f"{tag}_mpv", [8, 1], F32)
            srow = ctx.sb(f"{tag}_srow", [1, 64], F32)
            w8 = ctx.sb(f"{tag}_w8", [1, 8], F32)
            w8b = ctx.sb(f"{tag}_w8b", [1, 8], F32)
            expr = ctx.sb(f"{tag}_expr", [1, 128], F32)
            ecol = ctx.sb(f"{tag}_ecol", [128, 1], F32)
            jp8 = ctx.sb(f"{tag}_jp8", [128, 8], F32)
            jp32 = ctx.sb(f"{tag}_jp32", [128, 32], F32)
            psS = ctx.ps(f"{tag}_psS", [1, 32], F32)
            psB = ctx.ps(f"{tag}_psB", [128, 48], F32)
            psE = ctx.ps(f"{tag}_psE", [128, 15], F32)
            psT1 = ctx.ps(f"{tag}_psT1", [128, 128], F32)
            sy.inc(ldc, 16, lambda e: e.dma_start(out=ac8[:, :], in_=allcnt[:, :]))
            sy.inc(ldc, 16, lambda e: e.dma_start(out=mc1[:, :], in_=mycnt[:, :]))
            sy.inc(ldc, 16, lambda e: e.dma_start(out=mpv[:, :], in_=mprev[:, :]))
            sy.inc(ldc, 16, lambda e: e.dma_start(out=jp8[:, :], in_=c_jp8[:, :]))
            sy.inc(ldc, 16, lambda e: e.dma_start(out=jp32[:, :], in_=c_jp32[:, :]))
            vq(lambda e: e.memset(zz[:, :], 0.0))
            vq.drain()
            vq.inc(ve, 1, lambda e: e.tensor_copy(out=zz[:, 0:1], in_=zz[:, 0:1]))
            gq.wait(ve)
            for z in range(NGRAN * GRAN // 128):
                gq.inc(pfx, 16, lambda e, z=z: e.dma_start(
                    out=xpkd[z * 128:(z + 1) * 128, :], in_=zz[:, :]))
            tq.wait(ldc)
            tq(lambda e: e.matmul(out=psS[:, 0:16], lhsT=mpv[:, :], rhs=ac8[:, :], start=True, stop=False))
            tq(lambda e: e.matmul(out=psS[:, 16:32], lhsT=ctx.onescol_f32[0:8, :], rhs=ac8[:, :],
                                  start=False, stop=True))
            tq.inc_prev(pe, 1)
            vq.wait(pe)
            vq(lambda e: e.memset(srow[:, 40:48], 0.0))
            vq(lambda e: e.tensor_copy(out=srow[:, 0:16], in_=psS[:, 0:16]))
            vq.drain()
            vq(lambda e: e.tensor_tensor(out=srow[:, 8:16], in0=srow[:, 8:16], in1=psS[:, 16:24], op=OP.add))
            vq.drain()
            vq(lambda e: e.tensor_scalar(out=srow[:, 16:32], in0=srow[:, 0:16], scalar1=10240.0, scalar2=-1.0,
                                         op0=OP.subtract, op1=OP.mult))
            vq.drain()
            vq(lambda e: e.tensor_scalar(out=srow[:, 16:32], in0=srow[:, 16:32], scalar1=0.0, scalar2=None, op0=OP.max))
            vq.drain()
            vq(lambda e: e.tensor_tensor(out=srow[:, 16:32], in0=srow[:, 16:32], in1=mc1[:, :], op=OP.min))
            vq.drain()
            vq(lambda e: e.tensor_tensor(out=w8[:, :], in0=srow[:, 16:24], in1=srow[:, 24:32], op=OP.add))
            vq.drain()
            vq(lambda e: e.tensor_scalar(out=w8b[:, :], in0=w8[:, :], scalar1=0.0, scalar2=None, op0=OP.is_gt))
            vq.drain()
            for mth in range(1, 8):
                vq(lambda e, mth=mth: e.tensor_scalar(out=srow[:, 48:56], in0=w8[:, :],
                                                      scalar1=float(mth * 1024), scalar2=None, op0=OP.is_gt))
                vq.drain()
                vq(lambda e: e.tensor_tensor(out=w8b[:, :], in0=w8b[:, :], in1=srow[:, 48:56], op=OP.add))
                vq.drain()
            vq(lambda e: e.tensor_copy(out=srow[:, 48:56], in_=w8b[:, :]))
            vq.drain()
            vq(lambda e: e.tensor_tensor(out=srow[:, 49:56], in0=srow[:, 49:56], in1=w8b[:, 0:7], op=OP.add))
            vq.drain()
            vq(lambda e: e.tensor_copy(out=srow[:, 56:64], in_=srow[:, 48:56]))
            vq.drain()
            vq(lambda e: e.tensor_tensor(out=srow[:, 58:64], in0=srow[:, 58:64], in1=srow[:, 48:54], op=OP.add))
            vq.drain()
            vq(lambda e: e.tensor_copy(out=srow[:, 48:56], in_=srow[:, 56:64]))
            vq.drain()
            vq(lambda e: e.tensor_tensor(out=srow[:, 52:56], in0=srow[:, 52:56], in1=srow[:, 56:60], op=OP.add))
            vq.drain()
            vq(lambda e: e.tensor_tensor(out=srow[:, 32:40], in0=srow[:, 48:56], in1=w8b[:, :], op=OP.subtract))
            vq.drain()
            vq.inc(ve, 1, lambda e: e.tensor_scalar(out=srow[:, 32:40], in0=srow[:, 32:40],
                                                    scalar1=1024.0, scalar2=None, op0=OP.mult))
            tq.wait(ve)
            tq.inc(pe, 1, lambda e: e.matmul(out=psB[:, :], lhsT=ctx.ones_row[0:1, 0:128],
                                             rhs=srow[:, 0:48], start=True, stop=True))
            aq.wait(pe)
            aq.inc(ac, 1, lambda e: e.copy(out=srowb[:, :], in_=psB[:, :]))
            tq.wait(ve)
            tq.inc(pe, 1, lambda e: e.matmul(out=psE[:, 0:8], lhsT=ctx.ones_row[0:1, 0:128],
                                             rhs=srow[:, 48:56], start=True, stop=True))
            vq.wait(pe)
            vq(lambda e: e.tensor_scalar(out=ebc[:, 0:8], in0=psE[:, 0:8], scalar1=ctx.i128col[:, 0:1],
                                         scalar2=None, op0=OP.is_le))
            vq.drain()
            vq.inc(ve, 1, lambda e: e.tensor_reduce(out=ecol[:, :], in_=ebc[:, 0:8],
                                                    axis=mybir.AxisListType.X, op=OP.add))
            tq.wait(ve)
            tq.inc(pe, 1, lambda e: e.transpose(out=psT1[0:1, :],
                                                in_=ecol[:, :], identity=ctx.iden_f32[:, :]))
            aq.wait(pe)
            aq.inc(ac, 1, lambda e: e.copy(out=expr[:, :], in_=psT1[0:1, 0:128]))
            tq.wait(ac)
            tq.inc(pe, 1, lambda e: e.matmul(out=psE[:, 0:15], lhsT=ctx.ones_row[0:1, 0:128],
                                             rhs=expr[:, 0:15], start=True, stop=True))
            aq.wait(pe)
            aq.inc(ac, 1, lambda e: e.copy(out=ebc[:, :], in_=psE[:, :]))
            vq.wait(ac)
            vq(lambda e: e.tensor_scalar(out=ebc[:, :], in0=ebc[:, :],
                                         scalar1=7.0, scalar2=None, op0=OP.min))
            vq.drain()
            for g in range(NGRAN):
                vq(lambda e, g=g: e.tensor_scalar(out=ebc[:, g:g + 1], in0=ebc[:, g:g + 1],
                                                  scalar1=1024.0, scalar2=None, op0=OP.mult))
            vq.drain()
            for g in range(NGRAN):
                vq(lambda e, g=g: e.tensor_scalar(out=idxfc[:, g, :], in0=jp8[:, :],
                                                  scalar1=ebc[:, g:g + 1], scalar2=None, op0=OP.add))
            vq.drain()
            for g in range(NGRAN):
                vq(lambda e, g=g: e.tensor_scalar(out=ebc[:, g:g + 1], in0=ebc[:, g:g + 1],
                                                  scalar1=4.0, scalar2=None, op0=OP.mult))
            vq.drain()
            for g in range(NGRAN):
                vq(lambda e, g=g: e.tensor_scalar(out=idxpj[:, g, :], in0=jp32[:, :],
                                                  scalar1=ebc[:, g:g + 1], scalar2=None, op0=OP.add))
            vq.drain()
            vq.inc(ve, 1, lambda e: e.tensor_copy(out=ebc[:, 0:1], in_=ebc[:, 0:1]))
            ctx.flush(f"{tag}_m0")

        with ctx.scope():
            ldx = [ctx.sem("ldx0"), ctx.sem("ldx1")]
            ldh = [ctx.sem("ldh0"), ctx.sem("ldh1")]
            rt = [ctx.sb(f"{tag}_rt{i}", [128, 8], F32) for i in range(2)]
            xhb = [ctx.sb(f"{tag}_xhb{i}", [128, 1024], BF16) for i in range(2)]
            ohb = ctx.sb(f"{tag}_ohb", [128, 8], F32)
            wk = ctx.sb(f"{tag}_wk", [128, 8], F32)
            pk = ctx.sb(f"{tag}_pk", [128, 6], F32)
            pm2 = [ctx.sb(f"{tag}_pm2{i}", [128, 2], F32) for i in range(2)]
            dstf = ctx.sb(f"{tag}_dstf", [128, 2], F32)
            dstg = ctx.sb(f"{tag}_dstg", [128, 2], F32)
            dst = ctx.sb(f"{tag}_dst", [128, 2], U32)
            dsc = ctx.sb(f"{tag}_dsc", [128, 2], U32)
            stv = [ctx.sem("stq0"), ctx.sem("stq1")]
            gq.wait(pfx)
            for i in range(NT):
                ib = i % 2
                sy.wait(ve)
                sy.wait(ctx.sems["stq0"])
                sy.wait(ctx.sems["stq1"])
                sy.inc(ldx[ib], 16, lambda e, i=i, ib=ib: e.dma_start(
                    out=rt[ib][:, :], in_=route_in[i * 128:(i + 1) * 128, :]))
                sy.inc(ldh[ib], 16, lambda e, i=i, ib=ib: e.dma_start(
                    out=xhb[ib][:, :], in_=xh_in[i * 128:(i + 1) * 128, :]))
                vq.wait(ldx[ib])
                vq.wait(ctx.sems["stq0"])
                vq.wait(ctx.sems["stq1"])
                for k in range(2):
                    lrcol = 4 + k
                    vq(lambda e, ib=ib, k=k: e.tensor_scalar(out=ohb[:, :], in0=ctx.iota8[:, :],
                                                             scalar1=rt[ib][:, k:k + 1], scalar2=None,
                                                             op0=OP.is_equal))
                    vq.drain()
                    vq(lambda e, k=k: e.tensor_tensor(out=wk[:, :], in0=ohb[:, :],
                                                      in1=srowb[:, k * 8:(k + 1) * 8], op=OP.mult))
                    vq.drain()
                    vq(lambda e: e.tensor_reduce(out=pk[:, 0:1], in_=wk[:, :], axis=mybir.AxisListType.X, op=OP.add))
                    vq.drain()
                    vq(lambda e: e.tensor_tensor(out=wk[:, :], in0=ohb[:, :], in1=srowb[:, 32:40], op=OP.mult))
                    vq.drain()
                    vq(lambda e: e.tensor_reduce(out=pk[:, 1:2], in_=wk[:, :], axis=mybir.AxisListType.X, op=OP.add))
                    vq.drain()
                    vq(lambda e: e.tensor_tensor(out=wk[:, :], in0=ohb[:, :], in1=srowb[:, 16:24], op=OP.mult))
                    vq.drain()
                    vq(lambda e: e.tensor_reduce(out=pk[:, 2:3], in_=wk[:, :], axis=mybir.AxisListType.X, op=OP.add))
                    vq.drain()
                    vq(lambda e, k=k: e.tensor_tensor(out=wk[:, :], in0=ohb[:, :],
                                                      in1=srowb[:, (2 + k) * 8:(3 + k) * 8], op=OP.mult))
                    vq.drain()
                    vq(lambda e: e.tensor_reduce(out=pk[:, 3:4], in_=wk[:, :], axis=mybir.AxisListType.X, op=OP.add))
                    vq.drain()
                    vq(lambda e, ib=ib, k=k, lrcol=lrcol: e.tensor_scalar(
                        out=pk[:, 4:5], in0=rt[ib][:, lrcol:lrcol + 1], scalar1=pk[:, 0:1],
                        scalar2=None, op0=OP.add))
                    vq.drain()
                    vq(lambda e: e.tensor_scalar(out=pk[:, 5:6], in0=pk[:, 4:5], scalar1=10240.0,
                                                 scalar2=None, op0=OP.is_lt))
                    vq.drain()
                    vq(lambda e, ib=ib, k=k: e.tensor_scalar(out=pm2[ib][:, k:k + 1],
                                                             in0=rt[ib][:, 2 + k:3 + k],
                                                             scalar1=pk[:, 5:6], scalar2=None, op0=OP.mult))
                    vq.drain()
                    if k == 1:
                        vq(lambda e: e.tensor_scalar(out=pk[:, 1:2], in0=pk[:, 2:3], scalar1=pk[:, 1:2],
                                                     scalar2=None, op0=OP.add))
                        vq.drain()
                    vq(lambda e, ib=ib, k=k, lrcol=lrcol: e.tensor_scalar(
                        out=dstf[:, k:k + 1], in0=rt[ib][:, lrcol:lrcol + 1], scalar1=pk[:, 1:2],
                        scalar2=None, op0=OP.add))
                    vq.drain()
                    vq(lambda e, ib=ib, lrcol=lrcol: e.tensor_scalar(
                        out=pk[:, 3:4], in0=rt[ib][:, lrcol:lrcol + 1], scalar1=pk[:, 3:4],
                        scalar2=None, op0=OP.is_ge))
                    vq(lambda e, i=i, k=k: e.tensor_scalar(
                        out=pk[:, 4:5], in0=ctx.i128col[:, :], scalar1=2.0,
                        scalar2=float(NGRAN * GRAN + k + 256 * i), op0=OP.mult, op1=OP.add))
                    vq.drain()
                    vq(lambda e, k=k: e.tensor_tensor(out=pk[:, 4:5], in0=pk[:, 4:5],
                                                      in1=dstf[:, k:k + 1], op=OP.subtract))
                    vq.drain()
                    vq(lambda e, k=k: e.scalar_tensor_tensor(
                        out=dstf[:, k:k + 1], in0=pk[:, 4:5], scalar=pk[:, 3:4], op0=OP.mult,
                        op1=OP.add, in1=dstf[:, k:k + 1]))
                    vq.drain()
                vq(lambda e: e.tensor_scalar(out=dstf[:, :], in0=dstf[:, :],
                                             scalar1=float(NGRAN * GRAN + 2 * T - 1), scalar2=None, op0=OP.min))
                vq.drain()
                vq(lambda e: e.tensor_scalar(out=dstg[:, :], in0=dstf[:, :],
                                             scalar1=float(NGRAN * GRAN - 1), scalar2=None, op0=OP.min))
                vq.drain()
                vq(lambda e: e.tensor_scalar(out=dst[:, :], in0=dstf[:, :], scalar1=0.0,
                                             scalar2=None, op0=OP.add))
                vq.drain()
                vq.inc(ve, 1, lambda e: e.tensor_scalar(out=dsc[:, :], in0=dstg[:, :], scalar1=0.0,
                                                        scalar2=None, op0=OP.add))
                gq.wait(ve)
                gq.wait(ldh[ib])
                for k in range(2):
                    gq.inc(stv[k], 16, lambda e, k=k, ib=ib: e.indirect_dma_start(
                        out=xpkd[:, :],
                        out_offset=bass.IndirectOffsetOnAxis(ap=dst[:, k:k + 1], axis=0),
                        in_=xhb[ib][:, :], in_offset=None,
                        bounds_check=None))
                gq.inc(stv[ib], 16, lambda e, i=i: e.dma_start(
                    out=dstd[i * 128:(i + 1) * 128, :], in_=dsc[:, :]))
                gq.inc(stv[ib], 16, lambda e, i=i, ib=ib: e.dma_start(
                    out=pmd[i * 128:(i + 1) * 128, :], in_=pm2[ib][:, :]))
            gq.wait(ctx.sem("stq0"))
            gq.wait(ctx.sem("stq1"))
            ctx.flush(f"{tag}_m1")

        with ctx.scope():
            gldA = ctx.sem("gldA")
            gldB = ctx.sem("gldB")
            gldX = ctx.sem("gldX")
            stq = [ctx.sem("stq0"), ctx.sem("stq1")]
            wfcb = ctx.sb(f"{tag}_wfcb", [128, 8, 4096], BF16)
            wpjb = ctx.sb(f"{tag}_wpjb", [128, 32, 1024], BF16)
            xgT = ctx.sb(f"{tag}_xgT", [128, 8, 1024], BF16)
            xgs = [ctx.sb(f"{tag}_xgs{i}", [128, 1024], BF16) for i in range(2)]
            hbT = ctx.sb(f"{tag}_hbT", [128, 32, 512], BF16)
            hrl = ctx.sb(f"{tag}_hrl", [128, 512], F32)
            osb = [ctx.sb(f"{tag}_osb{i}", [128, 1024], BF16) for i in range(2)]
            psH = [ctx.ps(f"{tag}_psH{i}", [128, 512], F32) for i in range(2)]
            psO = [ctx.ps(f"{tag}_psO{i}", [128, 1024], F32) for i in range(2)]
            psT = ctx.ps(f"{tag}_psT2", [128, 1024], F32)
            done = []
            for g in range(NGRAN):
                d = {}
                for s in range(8):
                    sb2 = s % 2
                    gq.wait(pe)
                    gq.inc(gldX, 16, lambda e, g=g, s=s, sb2=sb2: e.dma_start(
                        out=xgs[sb2][:, :],
                        in_=xpkd[g * 1024 + s * 128:g * 1024 + (s + 1) * 128, :]))
                    tq.wait(gldX)
                    tq.wait(ac)
                    for j in range(8):
                        tq(lambda e, sb2=sb2, j=j: e.matmul(
                            out=psT[:, j * 128:(j + 1) * 128], lhsT=xgs[sb2][:, j * 128:(j + 1) * 128],
                            rhs=ctx.iden_bf[:, :], start=True, stop=True))
                    tq.inc_prev(pe, 1)
                    aq.wait(pe)
                    aq.inc(ac, 1, lambda e, s=s: e.copy(
                        out=xgT[:, :, s * 128:(s + 1) * 128],
                        in_=psT[:, :].rearrange("p (j c) -> p j c", c=128)))
                d["xtr"] = pe.n
                gq.wait(pe, done[-1]["h1_done"] if done else 0)
                for j in range(8):
                    for half in range(2):
                        gq.inc(gldA, 16, lambda e, g=g, j=j, half=half: e.indirect_dma_start(
                            out=wfcb[:, j, half * 2048:(half + 1) * 2048], out_offset=None,
                            in_=cfcv[:, :],
                            in_offset=bass.IndirectOffsetOnAxis(ap=idxfc[:, g, j:j + 1], axis=0),
                            element_offset=half * 2048,
                            bounds_check=None))
                ach = []
                for th in range(2):
                    tq.wait(gldA)
                    for fc in range(32):
                        fb = fc % 2
                        hx = th * 32 + fc
                        if hx >= 2:
                            tq.wait(ac, ach[hx - 2])
                        for j in range(8):
                            tq(lambda e, g=g, th=th, fc=fc, fb=fb, j=j: e.matmul(
                                out=psH[fb][:, :], lhsT=wfcb[:, j, fc * 128:(fc + 1) * 128],
                                rhs=xgT[:, j, th * 512:(th + 1) * 512], start=(j == 0), stop=(j == 7)))
                        tq.inc_prev(pe, 1)
                        aq.wait(pe)
                        aq(lambda e, fb=fb: e.activation(out=hrl[:, :], in_=psH[fb][:, :], func=AF.Relu))
                        aq.drain()
                        aq.inc(ac, 1, lambda e, fc=fc: e.activation(out=hbT[:, fc, :], in_=hrl[:, :],
                                                                    func=AF.Square))
                        aq.drain()
                        ach.append(ac.n)
                    if th == 0:
                        d["h0_done"] = pe.n
                    else:
                        d["h1_done"] = pe.n
                    if th == 0:
                        gq.wait(pe, done[-1]["o_done"] if done else 0)
                        for jj in range(32):
                            gq.inc(gldB, 16, lambda e, g=g, jj=jj: e.indirect_dma_start(
                                out=wpjb[:, jj, :], out_offset=None,
                                in_=cpjv[:, :],
                                in_offset=bass.IndirectOffsetOnAxis(ap=idxpj[:, g, jj:jj + 1], axis=0),
                                bounds_check=None))
                    tq.wait(ac)
                    tq.wait(gldB)
                    for tt in range(4):
                        tb = tt % 2
                        if not (th == 0 and tt < 2):
                            tq.wait(ac)
                        for half in range(2):
                            for fc in range(32):
                                tq(lambda e, tt=tt, tb=tb, th=th, fc=fc, half=half: e.matmul(
                                    out=psO[tb][:, half * 512:(half + 1) * 512],
                                    lhsT=hbT[:, fc, tt * 128:(tt + 1) * 128],
                                    rhs=wpjb[:, fc, half * 512:(half + 1) * 512],
                                    start=(fc == 0), stop=(fc == 31)))
                        tq.inc_prev(pe, 1)
                        aq.wait(pe)
                        aq.wait(stq[tb])
                        aq.inc(ac, 1, lambda e, tb=tb: e.copy(out=osb[tb][:, :], in_=psO[tb][:, :]))
                        gq.wait(ac)
                        gq.inc(stq[tb], 16, lambda e, g=g, th=th, tt=tt, tb=tb: e.dma_start(
                            out=opk[g * 1024 + th * 512 + tt * 128:
                                    g * 1024 + th * 512 + (tt + 1) * 128, :],
                            in_=osb[tb][:, :]))
                d["o_done"] = pe.n
                done.append(d)
            gq.wait(ctx.sem("stq0"))
            gq.wait(ctx.sem("stq1"))
            ctx.flush(f"{tag}_m2")

        with ctx.scope():
            ldx = [ctx.sem("ldx0"), ctx.sem("ldx1")]
            ldd = [ctx.sem("ldd0"), ctx.sem("ldd1")]
            glo = [ctx.sem("glo0"), ctx.sem("glo1")]
            sto = [ctx.sem("sto0"), ctx.sem("sto1")]
            xb = [ctx.sb(f"{tag}_cx{i}", [128, 1024], F32) for i in range(2)]
            o0 = [ctx.sb(f"{tag}_co0{i}", [128, 1024], BF16) for i in range(2)]
            o1 = [ctx.sb(f"{tag}_co1{i}", [128, 1024], BF16) for i in range(2)]
            dsb = [ctx.sb(f"{tag}_dsb{i}", [128, 2], U32) for i in range(2)]
            pmb = [ctx.sb(f"{tag}_cpm{i}", [128, 2], F32) for i in range(2)]
            t1 = [ctx.sb(f"{tag}_ct1{i}", [128, 1024], F32) for i in range(2)]
            ot = [ctx.sb(f"{tag}_cot{i}", [128, 1024], F32) for i in range(2)]
            ve_done = []
            glo_done = []
            for i in range(NT):
                ib = i % 2
                if i >= 2:
                    sy.wait(ve, ve_done[i - 2])
                    sy.wait(glo[ib], glo_done[i - 2])
                sy.inc(ldx[ib], 16, lambda e, i=i, ib=ib: e.dma_start(
                    out=xb[ib][:, :], in_=x_in[i * 128:(i + 1) * 128, :]))
                sy.inc(ldd[ib], 16, lambda e, i=i, ib=ib: e.dma_start(
                    out=dsb[ib][:, :], in_=dstd[i * 128:(i + 1) * 128, :]))
                sy.inc(ldd[ib], 16, lambda e, i=i, ib=ib: e.dma_start(
                    out=pmb[ib][:, :], in_=pmd[i * 128:(i + 1) * 128, :]))
                gq.wait(ldd[ib])
                if i >= 2:
                    gq.wait(ve, ve_done[i - 2])
                gq.inc(glo[ib], 16, lambda e, ib=ib: e.indirect_dma_start(
                    out=o0[ib][:, :], out_offset=None,
                    in_=opk[:, :],
                    in_offset=bass.IndirectOffsetOnAxis(ap=dsb[ib][:, 0:1], axis=0),
                    bounds_check=None))
                gq.inc(glo[ib], 16, lambda e, ib=ib: e.indirect_dma_start(
                    out=o1[ib][:, :], out_offset=None,
                    in_=opk[:, :],
                    in_offset=bass.IndirectOffsetOnAxis(ap=dsb[ib][:, 1:2], axis=0),
                    bounds_check=None))
                glo_done.append(glo[ib].n)
                vq.wait(ldx[ib])
                vq.wait(glo[ib])
                vq.wait(ldd[ib])
                if i >= 2:
                    vq.wait(sto[ib])
                vq(lambda e, ib=ib: e.scalar_tensor_tensor(
                    out=t1[ib][:, :], in0=o0[ib][:, :], scalar=pmb[ib][:, 0:1], op0=OP.mult,
                    op1=OP.add, in1=xb[ib][:, :]))
                vq.drain()
                vq.inc(ve, 1, lambda e, ib=ib: e.scalar_tensor_tensor(
                    out=ot[ib][:, :], in0=o1[ib][:, :], scalar=pmb[ib][:, 1:2], op0=OP.mult,
                    op1=OP.add, in1=t1[ib][:, :]))
                ve_done.append(ve.n)
                gq.wait(ve)
                gq.inc(sto[ib], 16, lambda e, i=i, ib=ib: e.dma_start(
                    out=x_out[i * 128:(i + 1) * 128, :], in_=ot[ib][:, :]))
            gq.wait(ctx.sem("sto0"))
            gq.wait(ctx.sem("sto1"))
            ctx.flush(f"{tag}_m3")


def moe_consts():
    jp8 = (np.arange(8, dtype=np.float32)[None, :] * 128 + np.arange(128, dtype=np.float32)[:, None])
    jp32 = (np.arange(32, dtype=np.float32)[None, :] * 128 + np.arange(128, dtype=np.float32)[:, None])
    return {
        "c_jp8": jp8.astype(np.float32),
        "c_jp32": jp32.astype(np.float32),
    }


def build_moemain_launch():
    nc = make_nc()
    stack = contextlib.ExitStack()
    ctx = Ctx(nc, stack)
    with stack:
        declare_consts(ctx)
        x_in = ctx.din("x_in", (T, D), F32)
        xh_in = ctx.din("xh_in", (T, D), BF16)
        route_in = ctx.din("route_in", (T, 8), F32)
        ctx.din("moe_allcnt", (8, 16), F32)
        ctx.din("moe_mycnt", (1, 16), F32)
        ctx.din("moe_mprev", (8, 1), F32)
        ctx.din("c_jp8", (128, 8), F32)
        ctx.din("c_jp32", (128, 32), F32)
        ctx.din("L1_ffn_c_fc", (8, 1024, 4096), F32)
        ctx.din("L1_ffn_c_proj", (8, 4096, 1024), F32)
        x_out = ctx.dout("x_out", (T, D), F32)
        emit_load_consts(ctx)
        emit_moe_main(ctx, x_in, x_out, xh_in, route_in, "mm")
        ctx.flush("fin")
    return nc


def moemain_inmaps(inp, xs, xh, route, cnts):
    c = host_constants()
    mc = moe_consts()
    allc = np.ascontiguousarray(cnts.reshape(8, 16))
    base = {
        **c, **mc,
        "moe_allcnt": allc,
        "L1_ffn_c_fc": np.ascontiguousarray(inp["L1_ffn_c_fc"]),
        "L1_ffn_c_proj": np.ascontiguousarray(inp["L1_ffn_c_proj"]),
    }
    maps = []
    for i in range(NCORES):
        mprev = np.zeros((8, 1), np.float32)
        mprev[:i] = 1.0
        maps.append({**base, "x_in": np.ascontiguousarray(xs[i]),
                     "xh_in": np.ascontiguousarray(xh[i]),
                     "route_in": np.ascontiguousarray(route[i]),
                     "moe_mycnt": allc[i:i + 1].copy(),
                     "moe_mprev": mprev})
    return maps


def run_moe(inp, xs):
    nc1 = build_moepre_launch()
    res1 = run_bass_kernel_spmd(nc1, moepre_inmaps(inp, xs), core_ids=list(range(NCORES)))
    xh = np.stack([r["xh_out"] for r in res1.results])
    route = np.stack([r["route_out"] for r in res1.results])
    cnts = np.stack([r["cnt_out"] for r in res1.results])
    nc2 = build_moemain_launch()
    res2 = run_bass_kernel_spmd(nc2, moemain_inmaps(inp, xs, xh, route, cnts), core_ids=list(range(NCORES)))
    return np.stack([r["x_out"] for r in res2.results])



def build_A_launch():
    nc = make_nc()
    stack = contextlib.ExitStack()
    ctx = Ctx(nc, stack)
    with stack:
        declare_consts(ctx)
        x_in = ctx.din("x_in", (T, D), F32)
        ctx.din("L0_attn_a_W_qkv", (1024, 3072), F32)
        ctx.din("L0_attn_a_b_qkv_r", (1, 3072), F32)
        ctx.din("L0_attn_a_W_proj", (1024, 1024), F32)
        ctx.din("L0_attn_a_b_proj_r", (1, 1024), F32)
        ctx.din("L0_attn_a_ln_w_r", (128, 8), F32)
        ctx.din("L0_attn_a_ln_b_r", (128, 8), F32)
        ctx.din("L0_ffn_W1T", (1024, 4096), F32)
        ctx.din("L0_ffn_W2T", (4096, 1024), F32)
        ctx.din("L0_ffn_b1_r", (128, 32), F32)
        ctx.din("L0_ffn_b2_r", (1, 1024), F32)
        for x in "QKV":
            ctx.din(f"L0_attn_b_W_{x}", (1024, 1024), F32)
        for t in ("ret_sc_q", "ret_ss_q", "ret_sc_k", "ret_ss_k"):
            ctx.din(t, (T, D), F32)
        ctx.din("ret_Dt", (T, T), F32)
        for x in ("q", "k", "v", "g", "o"):
            ctx.din(f"L1_attn_a_W{x}", (1024, 1024), F32)
        ctx.din("moe_wgT", (1024, 8), F32)
        x_out = ctx.dout("x_out", (T, D), F32)
        xh_out = ctx.dout("xh_out", (T, D), BF16)
        route_out = ctx.dout("route_out", (T, 8), F32)
        cnt_out = ctx.dout("cnt_out", (1, 16), F32)
        emit_load_consts(ctx)
        xm1 = ctx.dtmp("xm1", (T, D), F32)
        xm2 = ctx.dtmp("xm2", (T, D), F32)
        xm3 = ctx.dtmp("xm3", (T, D), F32)
        xm4 = ctx.dtmp("xm4", (T, D), F32)
        emit_l0a(ctx, x_in, xm1)
        emit_ffn(ctx, xm1, xm2, "f1")
        emit_ret(ctx, xm2, xm3)
        emit_ffn(ctx, xm3, xm4, "f2")
        emit_gla(ctx, xm4, x_out)
        emit_moe_pre(ctx, x_out, xh_out, route_out, cnt_out, "mp")
        ctx.flush("fin")
    return nc


def a_inmaps(inp, xs):
    c = host_constants()
    tabs = ret_tables()
    base = {
        **c, **tabs,
        "L0_attn_a_W_qkv": np.ascontiguousarray(inp["L0_attn_a_W_qkv"]),
        "L0_attn_a_b_qkv_r": np.ascontiguousarray(inp["L0_attn_a_b_qkv"].reshape(1, -1)),
        "L0_attn_a_W_proj": np.ascontiguousarray(inp["L0_attn_a_W_proj"]),
        "L0_attn_a_b_proj_r": np.ascontiguousarray(inp["L0_attn_a_b_proj"].reshape(1, -1)),
        "L0_attn_a_ln_w_r": np.ascontiguousarray(inp["L0_attn_a_ln_w"].reshape(8, 128).T),
        "L0_attn_a_ln_b_r": np.ascontiguousarray(inp["L0_attn_a_ln_b"].reshape(8, 128).T),
        "L0_ffn_W1T": np.ascontiguousarray(inp["L0_ffn_W1"].T),
        "L0_ffn_W2T": np.ascontiguousarray(inp["L0_ffn_W2"].T),
        "L0_ffn_b1_r": np.ascontiguousarray(inp["L0_ffn_b1"].reshape(32, 128).T),
        "L0_ffn_b2_r": np.ascontiguousarray(inp["L0_ffn_b2"].reshape(1, -1)),
        "L0_attn_b_W_Q": np.ascontiguousarray(inp["L0_attn_b_W_Q"]),
        "L0_attn_b_W_K": np.ascontiguousarray(inp["L0_attn_b_W_K"]),
        "L0_attn_b_W_V": np.ascontiguousarray(inp["L0_attn_b_W_V"]),
        "moe_wgT": np.ascontiguousarray(inp["L1_ffn_w_g"].T),
    }
    for x in ("q", "k", "v", "g", "o"):
        base[f"L1_attn_a_W{x}"] = np.ascontiguousarray(inp[f"L1_attn_a_W{x}"])
    return [{**base, "x_in": np.ascontiguousarray(xs[i])} for i in range(NCORES)]


def build_B_launch():
    nc = make_nc()
    stack = contextlib.ExitStack()
    ctx = Ctx(nc, stack)
    with stack:
        declare_consts(ctx)
        x_in = ctx.din("x_in", (T, D), F32)
        xh_in = ctx.din("xh_in", (T, D), BF16)
        route_in = ctx.din("route_in", (T, 8), F32)
        ctx.din("moe_allcnt", (8, 16), F32)
        ctx.din("moe_mycnt", (1, 16), F32)
        ctx.din("moe_mprev", (8, 1), F32)
        ctx.din("c_jp8", (128, 8), F32)
        ctx.din("c_jp32", (128, 32), F32)
        ctx.din("L1_ffn_c_fc", (8, 1024, 4096), F32)
        ctx.din("L1_ffn_c_proj", (8, 4096, 1024), F32)
        ctx.din("lra_Qt", (1024, 1024), F32)
        ctx.din("lra_Kt", (1024, 1024), F32)
        ctx.din("L1_attn_b_VO", (16, 1024, 1024), F32)
        ctx.din("moe_wgT", (1024, 8), F32)
        x_out = ctx.dout("x_out", (T, D), F32)
        xh_out = ctx.dout("xh_out", (T, D), BF16)
        route_out = ctx.dout("route_out", (T, 8), F32)
        cnt_out = ctx.dout("cnt_out", (1, 16), F32)
        emit_load_consts(ctx)
        xm6 = ctx.dtmp("xm6", (T, D), F32)
        emit_moe_main(ctx, x_in, xm6, xh_in, route_in, "mm")
        emit_lra(ctx, xm6, x_out)
        emit_moe_pre(ctx, x_out, xh_out, route_out, cnt_out, "mp")
        ctx.flush("fin")
    return nc


def b_inmaps(inp, xs, xh, route, cnts):
    c = host_constants()
    mc = moe_consts()
    allc = np.ascontiguousarray(cnts.reshape(8, 16))
    Q = inp["L1_attn_b_Q"]
    K = inp["L1_attn_b_K"]
    base = {
        **c, **mc,
        "moe_allcnt": allc,
        "L1_ffn_c_fc": np.ascontiguousarray(inp["L1_ffn_c_fc"]),
        "L1_ffn_c_proj": np.ascontiguousarray(inp["L1_ffn_c_proj"]),
        "lra_Qt": np.ascontiguousarray(Q.transpose(2, 0, 1).reshape(1024, 1024)),
        "lra_Kt": np.ascontiguousarray(K.transpose(2, 0, 1).reshape(1024, 1024)),
        "L1_attn_b_VO": np.ascontiguousarray(inp["L1_attn_b_VO"]),
        "moe_wgT": np.ascontiguousarray(inp["L1_ffn_w_g"].T),
    }
    maps = []
    for i in range(NCORES):
        mprev = np.zeros((8, 1), np.float32)
        mprev[:i] = 1.0
        maps.append({**base, "x_in": np.ascontiguousarray(xs[i]),
                     "xh_in": np.ascontiguousarray(xh[i]),
                     "route_in": np.ascontiguousarray(route[i]),
                     "moe_mycnt": allc[i:i + 1].copy(),
                     "moe_mprev": mprev})
    return maps



_BUILT = {}


def _nc_for(name, builder):
    if name not in _BUILT:
        _BUILT[name] = builder()
    return _BUILT[name]


def kernel(**inputs):
    inp = {k: np.asarray(v) for k, v in inputs.items()}
    x = np.ascontiguousarray(np.asarray(inp["L0_attn_a_z"], dtype=np.float32))
    cid = list(range(NCORES))
    resA = run_bass_kernel_spmd(_nc_for("A", build_A_launch), a_inmaps(inp, x), core_ids=cid)
    x5 = np.stack([r["x_out"] for r in resA.results])
    xh = np.stack([r["xh_out"] for r in resA.results])
    route = np.stack([r["route_out"] for r in resA.results])
    cnts = np.stack([r["cnt_out"] for r in resA.results])
    resB = run_bass_kernel_spmd(_nc_for("B", build_B_launch),
                                b_inmaps(inp, x5, xh, route, cnts), core_ids=cid)
    x7 = np.stack([r["x_out"] for r in resB.results])
    xh2 = np.stack([r["xh_out"] for r in resB.results])
    route2 = np.stack([r["route_out"] for r in resB.results])
    cnts2 = np.stack([r["cnt_out"] for r in resB.results])
    resC = run_bass_kernel_spmd(_nc_for("C", build_moemain_launch),
                                moemain_inmaps(inp, x7, xh2, route2, cnts2), core_ids=cid)
    return np.ascontiguousarray(np.stack([r["x_out"] for r in resC.results]).astype(np.float32))
